# version 13
# speedup vs baseline: 1.2265x; 1.0132x over previous
import sys
sys.path.insert(0, "/opt/trn_rl_repo")
import numpy as np
import ml_dtypes
import concourse.bass as bass
import concourse.mybir as mybir
from concourse.bass_utils import run_bass_kernel_spmd

F32 = mybir.dt.float32
BF16 = mybir.dt.bfloat16
AF = mybir.ActivationFunctionType
ALU = mybir.AluOpType
AX = mybir.AxisListType

T = 2048
D = 1024
PADT = 2052
NB = 4
NEG = -3.0e38

ENGS = ("tensor", "vector", "scalar", "gpsimd", "sync")


class Prog:

    OWN_SEM = {"tensor": "PE", "vector": "EV", "scalar": "ES"}

    def __init__(self, nc, semnames):
        self.nc = nc
        self.semnames = list(semnames)
        self.sems = {}
        self.cnt = {s: 0 for s in semnames}
        self.q = {e: [] for e in ENGS}
        self.ev = {e: [] for e in ENGS}
        self.self_waited = {e: 0 for e in ENGS}

    def wait(self, eng, sem, val):
        if val <= 0:
            return
        self.q[eng].append(("w", sem, val, None))
        self.ev[eng].append(("w", sem, val))

    def emit(self, eng, fn, inc=None, n=1):
        own = self.OWN_SEM.get(eng)
        if own is not None and self.cnt[own] > self.self_waited[eng]:
            self.self_waited[eng] = self.cnt[own]
            self.wait(eng, own, self.cnt[own])
        if inc is None and eng in ("vector", "scalar"):
            inc = own
        val = None
        if inc is not None:
            self.cnt[inc] += n
            val = self.cnt[inc]
        self.q[eng].append(("i", inc, n, fn))
        self.ev[eng].append(("i", inc, val))
        return val

    def waits(self, eng, pairs):
        best = {}
        for s, v in pairs:
            if v and v > best.get(s, 0):
                best[s] = v
        for s, v in best.items():
            self.wait(eng, s, v)

    def barrier(self, tag="", engines=("tensor", "vector", "scalar")):
        sem_of = {"tensor": "PE", "vector": "EV", "scalar": "ES",
                  "gpsimd": "EG", "sync": "EY"}
        vals = {}
        for e in engines:
            vals[e] = self.emit(e, lambda en: en.nop(), inc=sem_of[e])
        for e in engines:
            for o in engines:
                if o != e:
                    self.wait(e, sem_of[o], vals[o])

    def check_deadlock(self):
        ptr = {e: 0 for e in ENGS}
        cnt = {s: 0 for s in self.semnames}
        progress = True
        while progress:
            progress = False
            for e in ENGS:
                while ptr[e] < len(self.ev[e]):
                    kind, sem, val = self.ev[e][ptr[e]]
                    if kind == "w":
                        if cnt[sem] >= val:
                            ptr[e] += 1
                            progress = True
                        else:
                            break
                    else:
                        if sem is not None:
                            cnt[sem] = val if val > cnt[sem] else cnt[sem]
                        ptr[e] += 1
                        progress = True
        stuck = {e: ptr[e] for e in ENGS if ptr[e] < len(self.ev[e])}
        if stuck:
            info = []
            for e, p in stuck.items():
                kind, sem, val = self.ev[e][p]
                info.append(f"{e}@{p}/{len(self.ev[e])} waiting {sem}>={val} (now {cnt[sem]})")
            raise RuntimeError("DEADLOCK in schedule: " + "; ".join(info))

    def replay(self, block):
        def runner(stream):
            def go(eng):
                for item in stream:
                    kind, a, b, fn = item
                    if kind == "w":
                        eng.wait_ge(self.sems[a], b)
                    else:
                        ins = fn(eng)
                        if a is not None:
                            ins.then_inc(self.sems[a], b)
            return go
        block.tensor(runner(self.q["tensor"]))
        block.vector(runner(self.q["vector"]))
        block.scalar(runner(self.q["scalar"]))
        block.gpsimd(runner(self.q["gpsimd"]))
        block.sync(runner(self.q["sync"]))


class Banks:

    def __init__(self, prog):
        self.p = prog
        self.wrote = [0] * 8
        self.reads = [dict() for _ in range(8)]

    def begin_write(self, *banks):
        pairs = []
        for b in banks:
            pairs += list(self.reads[b].items())
            self.reads[b] = {}
        self.p.waits("tensor", pairs)

    def wrote_banks(self, pe_val, *banks):
        for b in banks:
            self.wrote[b] = pe_val

    def read(self, eng, sem, banks, fn, inc_n=1):
        self.p.waits(eng, [("PE", max(self.wrote[b] for b in banks))])
        v = self.p.emit(eng, fn, inc=sem, n=inc_n)
        for b in banks:
            self.reads[b][sem] = v
        return v



def _pc(vec, nchunk):
    return np.ascontiguousarray(vec.reshape(nchunk, 128).T).astype(np.float32)


def host_prepare(inputs):
    f = {k: np.asarray(v, np.float32) for k, v in inputs.items()}
    c = {}
    for nm in ("Wq", "Wk", "Wv", "Wo"):
        c["a_w" + nm[-1].lower()] = np.ascontiguousarray(f["L0_attn_" + nm].T)
    c["a_bq_pc"] = _pc(f["L0_attn_bq"], 8)
    c["a_bo_pc"] = _pc(f["L0_attn_bo"], 8)
    c["a_bkv_row"] = np.concatenate([f["L0_attn_bk"], f["L0_attn_bv"]])[None, :]

    HID = 2730
    HIDP = 2816
    NBLK = 44
    w1 = f["L0_ffn_w1"]
    b1 = f["L0_ffn_b1"]
    w2 = f["L0_ffn_w2"]
    w1a = np.zeros((2, 1024, HIDP), np.float32)
    w1g = np.zeros((2, 1024, HIDP), np.float32)
    w1a[:, :, :HID] = w1[:, :, :HID]
    w1g[:, :, :HID] = w1[:, :, HID:]
    b1a = np.zeros((2, HIDP), np.float32)
    b1g = np.zeros((2, HIDP), np.float32)
    b1a[:, :HID] = b1[:, :HID]
    b1g[:, :HID] = b1[:, HID:]
    w1p = np.zeros((2, 1024, 2 * HIDP), np.float32)
    b1_pc = np.zeros((2, 128, NBLK), np.float32)
    for j in range(NBLK):
        w1p[:, :, 128 * j:128 * j + 64] = w1a[:, :, 64 * j:64 * j + 64]
        w1p[:, :, 128 * j + 64:128 * j + 128] = w1g[:, :, 64 * j:64 * j + 64]
        b1_pc[:, 0:64, j] = b1a[:, 64 * j:64 * j + 64]
        b1_pc[:, 64:128, j] = b1g[:, 64 * j:64 * j + 64]
    w2p = np.zeros((2, HIDP, 1024), np.float32)
    w2p[:, :HID, :] = w2
    c["f_wqkv"] = f["L0_ffn_wqkv"]
    c["f_wo"] = f["L0_ffn_wo"]
    c["f_w1p"] = w1p
    c["f_w2p"] = w2p
    c["f_b1_pc"] = b1_pc
    c["f_b2_pc"] = np.stack([_pc(f["L0_ffn_b2"][l], 8) for l in range(2)])
    c["f_lng_pc"] = np.stack([_pc(f["L0_ffn_ln_g"][l], 8) for l in range(2)])
    c["f_lnb_pc"] = np.stack([_pc(f["L0_ffn_ln_b"][l], 8) for l in range(2)])
    c["f_qs_row"] = f["L0_ffn_q_scale"][:, None, :]
    c["f_ks_row"] = f["L0_ffn_k_scale"][:, None, :]

    DIM_HEAD, WINDOW, QK_SCALE = 64, 512, 8.0
    half = np.arange(0, DIM_HEAD, 2, dtype=np.float64)
    inv_freq = 1.0 / 10000.0 ** (half / DIM_HEAD)
    t = np.arange(T, dtype=np.float64)
    freqs = t[:, None] * inv_freq[None, :]
    freqs = np.concatenate([freqs, freqs], -1)
    base = (half + 0.4 * DIM_HEAD) / (1.4 * DIM_HEAD)
    power = (t - T // 2) / float(WINDOW // 2)
    scale = base[None, :] ** power[:, None]
    scale = np.concatenate([scale, scale], -1)
    cos, sin = np.cos(freqs), np.sin(freqs)
    qc = cos * scale * QK_SCALE
    qs_t = sin * scale * QK_SCALE
    kc = cos / scale
    ks_t = sin / scale
    def sgn(tab):
        out = tab.copy()
        out[:, :32] = -tab[:, :32]
        return out
    def tokmaj(tab):
        return np.ascontiguousarray(
            tab.reshape(16, 128, 64).transpose(1, 0, 2).reshape(128, 16 * 64)
        ).astype(np.float32)
    c["rt_qc"] = tokmaj(qc)
    c["rt_qs"] = tokmaj(sgn(qs_t))
    c["rt_kc"] = tokmaj(kc)
    c["rt_ks"] = tokmaj(sgn(ks_t))

    p = np.arange(128)[:, None]
    j = np.arange(640)[None, :]
    diff = p + 512 - j
    c["mask_l0"] = np.where((diff >= 0) & (diff <= 512), 0.0, NEG).astype(ml_dtypes.bfloat16)

    for nm in ("Wq", "Wk", "Wv", "Wo"):
        c["l1_w" + nm[-1].lower()] = np.ascontiguousarray(f["L1_attn_" + nm].T)
    BLK = 64
    qi = np.arange(128)[:, None]
    kj = np.arange(128)[None, :]
    local = np.where((qi // BLK == kj // BLK) & (kj <= qi), 0.0, NEG)
    m_a = np.concatenate([local, np.zeros((128, 16))], axis=1)
    gb = np.zeros((128, 16))
    gb[0:64, :] = NEG
    m_b = np.concatenate([local, gb], axis=1)
    c["mask_l1"] = np.concatenate([m_a, m_b], axis=1).astype(ml_dtypes.bfloat16)

    c["m_gate"] = f["L1_ffn_gate"]
    c["m_w1"] = f["L1_ffn_W1"]
    c["m_w2"] = f["L1_ffn_W2"]
    c["m_b1_pc2"] = np.concatenate(
        [_pc(f["L1_ffn_b1"][e], 16) for e in range(8)], axis=1)
    c["c_identf"] = np.eye(128, dtype=np.float32)
    c["c_iota256"] = np.tile(np.arange(256, dtype=np.float32), (128, 1))
    c["m_b2"] = f["L1_ffn_b2"]

    c["c_ident"] = np.eye(128, dtype=np.float32).astype(ml_dtypes.bfloat16)
    c["c_ones_row"] = np.ones((1, 128), dtype=ml_dtypes.bfloat16)
    c["c_ones_col"] = np.ones((128, 1), np.float32)
    c["c_ones_row_f"] = np.ones((1, 128), np.float32)
    c["c_ones_col_b"] = np.ones((128, 1), dtype=ml_dtypes.bfloat16)
    sel = np.zeros((2, 128), np.float32)
    sel[0, 0:64] = 1.0
    sel[1, 64:128] = 1.0
    c["c_sel2"] = sel.astype(ml_dtypes.bfloat16)

    x = np.asarray(inputs["L0_attn_x"], np.float32)
    shards = []
    for core in range(8):
        xb = x[core * NB:(core + 1) * NB]
        shards.append(np.ascontiguousarray(xb.transpose(0, 2, 1)))
    return c, shards



SEMS = ("CL", "CG", "XL", "W0", "W1", "W2", "W3", "W4", "W5", "GR",
        "SO0", "SO1", "ST", "PE", "EV", "ES", "EG", "EY")


class MK:

    def __init__(self, nc, stages, fused=True):
        self.nc = nc
        self.stages = stages
        self.fused = fused
        self.prog = Prog(nc, SEMS)
        self.dram = {}
        self.buf = {}
        self.psum = []
        self.ctxs = []
        self.wa_pe = 0
        self.moe_b1_pe = 0
        self.moe_b1_ev = 0
        self.moe_b1_es = 0
        self.f_wqkv_pe = 0
        self.f_wo_pe = 0
        self.f_rtf_ev = 0
        self.f_qsks_pe = 0
        self.f_qr_pe = 0
        self.f_pp_pe = 0
        self.f_pp_es = 0
        self.f_pp_ev = 0
        self.f_xin_st = 0

    def declare(self):
        nc = self.nc
        dr = self.dram
        dr["x_t"] = nc.dram_tensor("x_t", [NB, D, T], F32, kind="ExternalInput").ap()
        dr["gt_d"] = nc.dram_tensor("gt_d", [8, T], BF16, kind="Internal").ap()
        dr["xin_d"] = nc.dram_tensor("xin_d", [8, 128, T], BF16, kind="Internal").ap()
        dr["out_t"] = nc.dram_tensor("out_t", [NB, D, T], F32, kind="ExternalOutput").ap()
        shapes = {
            "a_wq": [D, D], "a_wk": [D, D], "a_wv": [D, D], "a_wo": [D, D],
            "a_bq_pc": [128, 8], "a_bo_pc": [128, 8], "a_bkv_row": [1, 2 * D],
            "f_wqkv": [2, D, 1536], "f_wo": [2, 512, D],
            "f_w1p": [2, D, 5632], "f_w2p": [2, 2816, D],
            "f_b1_pc": [2, 128, 44], "f_b2_pc": [2, 128, 8],
            "f_lng_pc": [2, 128, 8], "f_lnb_pc": [2, 128, 8],
            "f_qs_row": [2, 1, 64], "f_ks_row": [2, 1, 64],
            "rt_qc": [128, 1024], "rt_qs": [128, 1024],
            "rt_kc": [128, 1024], "rt_ks": [128, 1024],
            "l1_wq": [D, D], "l1_wk": [D, D], "l1_wv": [D, D], "l1_wo": [D, D],
            "m_gate": [D, 8], "m_w1": [8, D, 2048], "m_w2": [8, 2048, D],
            "m_b1_pc2": [128, 128], "m_b2": [8, D],
            "c_identf": [128, 128], "c_iota256": [128, 256],
            "c_ones_col": [128, 1], "c_ones_row_f": [1, 128],
        }
        bf_shapes = {
            "c_ident": [128, 128], "c_ones_row": [1, 128],
            "c_ones_col_b": [128, 1], "c_sel2": [2, 128],
            "mask_l0": [128, 640], "mask_l1": [128, 288],
        }
        for k, s in shapes.items():
            dr[k] = nc.dram_tensor(k, s, F32, kind="ExternalInput").ap()
        for k, s in bf_shapes.items():
            dr[k] = nc.dram_tensor(k, s, BF16, kind="ExternalInput").ap()

    def alloc(self):
        nc = self.nc
        def sb(name, shape, dt=BF16):
            cm = nc.sbuf_tensor(name, shape, dt)
            t = cm.__enter__()
            self.ctxs.append(cm)
            return t
        b = self.buf
        b["xs"] = sb("xs", [128, 8 * PADT])
        b["wa"] = sb("wa", [128, 24576])
        b["aa"] = sb("aa", [128, 34816])
        b["fa"] = sb("fa", [128, 5632], F32)
        b["small"] = sb("small", [128, 256], F32)
        b["ident"] = sb("ident", [128, 128])
        b["identf"] = sb("identf", [128, 128], F32)
        b["iota256"] = sb("iota256", [128, 256], F32)
        b["ones_row"] = sb("ones_row", [1, 128])
        b["ones_col"] = sb("ones_col", [128, 1], F32)
        b["ones_row_f"] = sb("ones_row_f", [1, 128], F32)
        b["ones_col_b"] = sb("ones_col_b", [128, 1])
        b["sel2"] = sb("sel2", [2, 128])
        b["mask_l0"] = sb("s_mask_l0", [128, 640])
        b["mask_l1"] = sb("s_mask_l1", [128, 288])
        b["rt"] = sb("rt", [128, 4096], F32)
        b["rtf"] = sb("rtf", [128, 4096])
        b["pp"] = sb("pp", [128, 160], F32)
        b["rows"] = sb("rows", [1, 2048])
        b["mb2"] = sb("mb2", [8, D])
        b["qsks"] = sb("qsks", [1, 128], F32)
        cm = nc.psum_tensor("pbig", [128, 3584], F32)
        pf = cm.__enter__()
        self.ctxs.append(cm)
        self.pfull = pf
        for i in range(7):
            self.psum.append(pf[:, 512 * i:512 * i + 512])
        cm = nc.psum_tensor("pb7", [128, 1024], BF16)
        self.psum.append(cm.__enter__())
        self.ctxs.append(cm)
        self.banks = Banks(self.prog)

    def close(self):
        for cm in reversed(self.ctxs):
            cm.__exit__(None, None, None)

    def ld_const(self):
        p, dr, b = self.prog, self.dram, self.buf
        def g(eng, dst, src):
            return p.emit(eng, lambda e, d=dst, s=src: e.dma_start(out=d, in_=s),
                          inc="CL" if eng == "sync" else "CG", n=16)
        g("sync", b["ident"][:, :], dr["c_ident"][:, :])
        g("sync", b["ones_row"][0:1, :], dr["c_ones_row"][0:1, :])
        g("sync", b["ones_col"][:, :], dr["c_ones_col"][:, :])
        g("sync", b["ones_row_f"][0:1, :], dr["c_ones_row_f"][0:1, :])
        g("sync", b["identf"][:, :], dr["c_identf"][:, :])
        g("sync", b["iota256"][:, :], dr["c_iota256"][:, :])
        g("sync", b["ones_col_b"][:, :], dr["c_ones_col_b"][:, :])
        g("sync", b["sel2"][0:2, :], dr["c_sel2"][0:2, :])
        g("sync", b["mask_l0"][:, :], dr["mask_l0"][:, :])
        g("sync", b["mask_l1"][:, :], dr["mask_l1"][:, :])
        rt = b["rt"]
        g("sync", rt[:, 0:1024], dr["rt_qc"][:, :])
        g("sync", rt[:, 1024:2048], dr["rt_qs"][:, :])
        g("sync", rt[:, 2048:3072], dr["rt_kc"][:, :])
        g("sync", rt[:, 3072:4096], dr["rt_ks"][:, :])
        pp = b["pp"]
        g("sync", pp[:, 0:8], dr["a_bq_pc"][:, :])
        g("sync", pp[:, 8:16], dr["a_bo_pc"][:, :])
        g("gpsimd", b["rows"][0:1, 0:2048], dr["a_bkv_row"][0:1, :])
        g("gpsimd", b["mb2"][0:8, :], dr["m_b2"][0:8, :])
        self.ld_sem_vals = (self.prog.cnt["CL"], self.prog.cnt["CG"])
        p.emit("vector", lambda e: e.memset(self.buf["small"][:, 255:256], 1e-5), inc="EV")
        for c in range(8):
            p.emit("vector", lambda e, c=c: e.memset(self.buf["xs"][:, c * PADT:c * PADT + 1], 0.0), inc="EV")

    def ld_x(self, bi):
        p, dr, b = self.prog, self.dram, self.buf
        vals = []
        for c in range(8):
            v = p.emit("gpsimd",
                       lambda e, c=c, bi=bi: e.dma_start(
                           out=b["xs"][:, c * PADT + 1: c * PADT + 1 + T],
                           in_=dr["x_t"][bi, c * 128:(c + 1) * 128, :]),
                       inc="XL", n=16)
            vals.append(v)
        return vals[-1]

    def st_out(self, bi, src_f32):
        p, dr = self.prog, self.dram
        vals = []
        for c in range(8):
            v = p.emit("sync",
                       lambda e, c=c, bi=bi: e.dma_start(
                           out=dr["out_t"][bi, c * 128:(c + 1) * 128, :],
                           in_=src_f32(c)),
                       inc="SO0", n=16)
            vals.append(v)
        return vals[-1]

    def xsl(self, c, j0, n):
        return self.buf["xs"][:, c * PADT + 1 + j0: c * PADT + 1 + j0 + n]

    def xsl_shift(self, c, j0, n):
        return self.buf["xs"][:, c * PADT + j0: c * PADT + j0 + n]

    def _module_ln(self, tc, bi, final, st_vals, sb_base=12288, out_base=8192,
                   stats_only=False):
        p, b, bk, pb = self.prog, self.buf, self.banks, self.psum
        aa, fa = b["aa"], b["fa"]
        t0 = 512 * tc
        SB, QB = 5, 6
        B1, B0 = 3, 4
        mu = aa[0:1, sb_base:sb_base + 1024].bitcast(F32)
        var = aa[0:1, sb_base + 1024:sb_base + 2048].bitcast(F32)
        rstd = aa[0:1, sb_base + 2048:sb_base + 3072].bitcast(F32)
        nm = aa[0:1, sb_base + 3072:sb_base + 4096].bitcast(F32)

        bk.begin_write(SB, QB)
        sqb = aa[:, out_base:out_base + 4096]
        sqv = p.emit("scalar", lambda e, sqb=sqb: e.activation(
            out=sqb, in_=fa[:, 0:4096], func=AF.Square), inc="ES")
        p.waits("tensor", [("ES", sqv)])
        for oc in range(8):
            p.emit("tensor", lambda e, oc=oc: e.matmul(
                pb[QB][0:1, 0:512], b["ones_col"][:, 0:1],
                fa[:, 512 * oc:512 * oc + 512], start=(oc == 0), stop=(oc == 7)))
            p.emit("tensor", lambda e, oc=oc, sqb=sqb: e.matmul(
                pb[SB][0:1, 0:512], b["ones_col_b"][:, 0:1],
                sqb[:, 512 * oc:512 * oc + 512], start=(oc == 0), stop=(oc == 7)),
                inc="PE" if oc == 7 else None)
        smm = p.cnt["PE"]
        bk.wrote_banks(smm, SB, QB)
        bk.read("vector", "EV", [QB], lambda e: e.tensor_scalar(
            out=mu, in0=pb[QB][0:1, 0:512], scalar1=1.0 / D, scalar2=None, op0=ALU.mult))
        bk.read("vector", "EV", [SB], lambda e: e.tensor_scalar(
            out=var, in0=pb[SB][0:1, 0:512], scalar1=1.0 / D, scalar2=None, op0=ALU.mult))
        p.emit("vector", lambda e: e.scalar_tensor_tensor(
            out=nm, in0=mu, scalar=-1.0, in1=mu, op0=ALU.mult, op1=ALU.mult), inc="EV")
        vv = p.emit("vector", lambda e: e.tensor_add(out=var, in0=var, in1=nm), inc="EV")
        p.waits("scalar", [("EV", vv)])
        p.emit("scalar", lambda e: e.activation(out=nm, in_=var, func=AF.Ln,
                                                bias=b["small"][0:1, 255:256]), inc="ES")
        sv = p.emit("scalar", lambda e: e.activation(out=rstd, in_=nm, func=AF.Exp, scale=-0.5), inc="ES")
        p.waits("vector", [("ES", sv)])
        nv = p.emit("vector", lambda e: e.scalar_tensor_tensor(
            out=nm, in0=mu, scalar=-1.0, in1=rstd, op0=ALU.mult, op1=ALU.mult), inc="EV")
        if stats_only is True:
            return
        p.waits("tensor", [("EV", nv)])
        bk.begin_write(B1, B0)
        b1mm = p.emit("tensor", lambda e: e.matmul(
            pb[B1][:, :], b["ones_row_f"][0:1, 0:128], rstd, start=True, stop=True),
            inc="PE")
        b0mm = p.emit("tensor", lambda e: e.matmul(
            pb[B0][:, :], b["ones_row_f"][0:1, 0:128], nm, start=True, stop=True),
            inc="PE")
        bk.wrote_banks(b1mm, B1)
        bk.wrote_banks(b0mm, B0)
        if stats_only == "bcast":
            bk.read("scalar", "ES", [B1], lambda e: e.copy(
                out=fa[:, 4096:4608], in_=pb[B1][:, :]))
            bk.read("scalar", "ES", [B0], lambda e: e.copy(
                out=fa[:, 4608:5120], in_=pb[B0][:, :]))
            return
        for oc in range(8):
            tslot = 4096 + 512 * (oc % 2)
            t1v = bk.read("vector", "EV", [B1], lambda e, oc=oc, tslot=tslot: e.tensor_mul(
                out=fa[:, tslot:tslot + 512], in0=fa[:, 512 * oc:512 * oc + 512],
                in1=pb[B1][:, :]))
            if not final:
                wv = bk.read("vector", "EV", [B0], lambda e, oc=oc, tslot=tslot, t0=t0: e.tensor_add(
                    out=self.xsl(oc, t0, 512), in0=fa[:, tslot:tslot + 512],
                    in1=pb[B0][:, :]))
            else:
                oslot = out_base + 2048 * (oc % 2)
                wv = bk.read("vector", "EV", [B0], lambda e, oc=oc, tslot=tslot, oslot=oslot: e.tensor_add(
                    out=aa[:, oslot:oslot + 2048].bitcast(F32)[:, 0:512],
                    in0=fa[:, tslot:tslot + 512], in1=pb[B0][:, :]))
                p.waits("sync", [("EV", wv)])
                lane = "SO0" if oc % 2 == 0 else "SO1"
                sval = p.emit("sync", lambda e, oc=oc, t0=t0, oslot=oslot, bi=bi: e.dma_start(
                    out=self.dram["out_t"][bi, oc * 128:(oc + 1) * 128, t0:t0 + 512],
                    in_=aa[:, oslot:oslot + 2048].bitcast(F32)[:, 0:512]),
                    inc=lane, n=16)
                st_vals.append((lane, sval))
                p.waits("vector", [(lane, sval)])
        return p.cnt["EV"]

    def stage_a(self, bi, final=False, upto=None):
        p, b, dr, bk = self.prog, self.buf, self.dram, self.banks
        aa, fa, wa, sm, pb = b["aa"], b["fa"], b["wa"], b["small"], self.psum
        KK, VV = 0, 16384
        KVS, KSOF = 33024, 34048
        QK, YF, REC = 0, 4096, 8192

        def wload(name, off, lane):
            return p.emit("gpsimd", lambda e, n=name, o=off: e.dma_start(
                out=wa[:, o:o + 8192].rearrange("p (c m) -> p c m", c=8),
                in_=dr[n].rearrange("(c p) m -> p c m", p=128)), inc=lane, n=16)
        p.waits("gpsimd", [("PE", self.wa_pe)])
        wk_v = wload("a_wk", 0, "W0")
        wv_v = wload("a_wv", 8192, "W1")

        if True:
            p.emit("vector", lambda e: e.memset(
                aa[:, VV:VV + 16640].rearrange("p (t h s) -> p t h s", t=16, h=16)[:, :, :, 64:65],
                1.0), inc="EV")
            p.emit("vector", lambda e: e.memset(aa[:, KSOF:KSOF + 16], 0.0), inc="EV")
            p.emit("vector", lambda e: e.memset(aa[:, KVS:KVS + 1024], 0.0), inc="EV")
        vx_ready = p.cnt["EV"]

        p.waits("tensor", [("W0", wk_v), ("W1", wv_v), ("XL", self.x_ready),
                           ("EV", vx_ready)])
        exp_done = 0
        vcp_done = 0
        red_prev = 0
        for t in range(16):
            kb = (0, 1) if t % 2 == 0 else (2, 3)
            vb = (4, 5)
            bk.begin_write(*vb)
            for h2 in range(2):
                for c in range(8):
                    p.emit("tensor", lambda e, t=t, c=c, h2=h2, vb=vb: e.matmul(
                        pb[vb[h2]][:, :], self.xsl(c, 128 * t, 128),
                        wa[:, 8192 + c * 1024 + h2 * 512: 8192 + c * 1024 + (h2 + 1) * 512],
                        start=(c == 0), stop=False))
                vmm = p.emit("tensor", lambda e, h2=h2, vb=vb: e.matmul(
                    pb[vb[h2]][:, :], b["ones_row"][0:1, 0:128],
                    b["rows"][0:1, 1024 + h2 * 512: 1024 + (h2 + 1) * 512],
                    start=False, stop=True), inc="PE")
            bk.wrote_banks(vmm, *vb)
            bk.begin_write(*kb)
            for h2 in range(2):
                for c in range(8):
                    p.emit("tensor", lambda e, t=t, c=c, h2=h2, kb=kb: e.matmul(
                        pb[kb[h2]][:, :], self.xsl(c, 128 * t, 128),
                        wa[:, c * 1024 + h2 * 512: c * 1024 + (h2 + 1) * 512],
                        start=(c == 0), stop=False))
                kmm = p.emit("tensor", lambda e, h2=h2, kb=kb: e.matmul(
                    pb[kb[h2]][:, :], b["ones_row"][0:1, 0:128],
                    b["rows"][0:1, h2 * 512:(h2 + 1) * 512],
                    start=False, stop=True), inc="PE")
            bk.wrote_banks(kmm, *kb)

            for i, bvv in enumerate(vb):
                vcp_done = bk.read("vector", "EV", [bvv], lambda e, t=t, i=i, bvv=bvv: e.tensor_copy(
                    out=aa[:, VV + 1040 * t: VV + 1040 * t + 1040].rearrange(
                        "p (h s) -> p h s", h=16)[:, 8 * i:8 * i + 8, 0:64],
                    in_=pb[bvv][:, :].rearrange("p (h s) -> p h s", h=8)))
            p.waits("vector", [("ES", exp_done)])
            red = bk.read("vector", "EV", list(kb), lambda e, kb=kb: e.reduce_max(
                out=sm[:, 0:16],
                in_=self.pfull[:, 512 * kb[0]:512 * kb[0] + 1024].rearrange(
                    "p (h d) -> p h d", h=16),
                axis=AX.X, negate=True))
            sub = bk.read("vector", "EV", list(kb), lambda e, kb=kb: e.scalar_tensor_tensor(
                out=fa[:, 4096:5120].rearrange("p (h d) -> p h d", h=16),
                in0=self.pfull[:, 512 * kb[0]:512 * kb[0] + 1024].rearrange(
                    "p (h d) -> p h d", h=16),
                scalar=1.0,
                in1=sm[:, 0:16].unsqueeze(2).broadcast_to([128, 16, 64]),
                op0=ALU.mult, op1=ALU.add))
            p.waits("scalar", [("EV", sub)])
            exp_done = p.emit("scalar", lambda e, t=t: e.activation(
                out=aa[:, KK + 1024 * t: KK + 1024 * t + 1024],
                in_=fa[:, 4096:5120], func=AF.Exp))

        if upto == "p1":
            return []
        p.waits("tensor", [("ES", exp_done), ("EV", vcp_done)])
        kv_es, kv_ev = 0, 0
        for H in range(16):
            bno = H % 7
            bk.begin_write(bno)
            for t in range(16):
                kvm = p.emit("tensor", lambda e, t=t, H=H, bno=bno: e.matmul(
                    pb[bno][0:64, 0:65],
                    aa[:, KK + 1024 * t + 64 * H: KK + 1024 * t + 64 * H + 64],
                    aa[:, VV + 1040 * t + 65 * H: VV + 1040 * t + 65 * H + 65],
                    start=(t == 0), stop=(t == 15)),
                    inc="PE" if t == 15 else None)
            bk.wrote_banks(kvm, bno)
            pr, ro = H // 2, 64 * (H % 2)
            kv_es = bk.read("scalar", "ES", [bno], lambda e, pr=pr, ro=ro, bno=bno: e.copy(
                out=aa[ro:ro + 64, KVS + 128 * pr + ro: KVS + 128 * pr + ro + 64],
                in_=pb[bno][0:64, 0:64]))
            kv_ev = bk.read("vector", "EV", [bno], lambda e, H=H, ro=ro, bno=bno: e.tensor_copy(
                out=aa[ro:ro + 64, KSOF + H: KSOF + H + 1],
                in_=pb[bno][0:64, 64:65]))

        p.waits("gpsimd", [("PE", p.cnt["PE"])])
        wq_v = wload("a_wq", 0, "W0")
        wo_v = wload("a_wo", 8192, "W1")

        if upto == "p15":
            return []
        p.waits("tensor", [("ES", kv_es), ("EV", kv_ev), ("W0", wq_v), ("W1", wo_v)])
        st_vals = []
        for tc in range(4):
            t0 = 512 * tc
            for c in range(8):
                qb = c % 2
                bk.begin_write(qb)
                for kc in range(8):
                    qmm = p.emit("tensor", lambda e, c=c, kc=kc, t0=t0, qb=qb: e.matmul(
                        pb[qb][:, :], wa[:, kc * 1024 + c * 128: kc * 1024 + c * 128 + 128],
                        self.xsl(kc, t0, 512), start=(kc == 0), stop=(kc == 7)),
                        inc="PE" if kc == 7 else None)
                bk.wrote_banks(qmm, qb)
                bk.read("scalar", "ES", [qb], lambda e, c=c, qb=qb: e.activation(
                    out=aa[:, QK + 512 * c: QK + 512 * c + 512], in_=pb[qb][:, :],
                    func=AF.Exp, bias=b["pp"][:, c:c + 1], scale=1.0))
            qk_done = p.cnt["ES"]
            p.waits("tensor", [("ES", qk_done)])
            for c in range(8):
                db = 2 + (c % 2)
                bk.begin_write(db)
                dmm = p.emit("tensor", lambda e, c=c, db=db: e.matmul(
                    pb[db][0:2, 0:512], aa[:, KSOF + 2 * c: KSOF + 2 * c + 2],
                    aa[:, QK + 512 * c: QK + 512 * c + 512], start=True, stop=True),
                    inc="PE")
                bk.wrote_banks(dmm, db)
                bk.read("vector", "EV", [db], lambda e, c=c, db=db: e.tensor_scalar(
                    out=fa[0:2, 512 * (c % 2): 512 * (c % 2) + 512], in0=pb[db][0:2, 0:512],
                    scalar1=1e-6, scalar2=None, op0=ALU.add))
                p.emit("vector", lambda e, c=c: e.reciprocal(
                    out=fa[0:2, 1024 + 512 * (c % 2): 1024 + 512 * (c % 2) + 512],
                    in_=fa[0:2, 512 * (c % 2): 512 * (c % 2) + 512]))
                p.emit("vector", lambda e, c=c: e.tensor_copy(
                    out=aa[0:2, REC + 512 * c: REC + 512 * c + 512],
                    in_=fa[0:2, 1024 + 512 * (c % 2): 1024 + 512 * (c % 2) + 512]), inc="EV")
            rec_done = p.cnt["EV"]
            p.waits("tensor", [("EV", rec_done)])
            for c in range(8):
                yb = 4 + (c % 2)
                bb6 = 6
                bk.begin_write(yb)
                ymm = p.emit("tensor", lambda e, c=c, yb=yb: e.matmul(
                    pb[yb][:, :], aa[:, KVS + 128 * c: KVS + 128 * c + 128],
                    aa[:, QK + 512 * c: QK + 512 * c + 512], start=True, stop=True),
                    inc="PE")
                bk.wrote_banks(ymm, yb)
                bk.begin_write(bb6)
                bmm = p.emit("tensor", lambda e, c=c, bb6=bb6: e.matmul(
                    pb[bb6][:, :], b["sel2"][0:2, :],
                    aa[0:2, REC + 512 * c: REC + 512 * c + 512],
                    start=True, stop=True), inc="PE")
                bk.wrote_banks(bmm, bb6)
                bc_ev = bk.read("scalar", "ES", [bb6], lambda e, c=c, bb6=bb6: e.copy(
                    out=fa[:, 4096 + 512 * (c % 2): 4096 + 512 * (c % 2) + 512],
                    in_=pb[bb6][:, :]))
                p.waits("vector", [("ES", bc_ev)])
                bk.read("vector", "EV", [yb], lambda e, c=c, yb=yb: e.tensor_mul(
                    out=aa[:, YF + 512 * c: YF + 512 * c + 512],
                    in0=pb[yb][:, :],
                    in1=fa[:, 4096 + 512 * (c % 2): 4096 + 512 * (c % 2) + 512]))
            yf_done = p.cnt["EV"]
            p.waits("tensor", [("EV", yf_done)])
            for oc in range(8):
                ob = (0, 1)[oc % 2]
                bk.begin_write(ob)
                for kc in range(8):
                    omm = p.emit("tensor", lambda e, oc=oc, kc=kc, ob=ob: e.matmul(
                        pb[ob][:, :], wa[:, 8192 + kc * 1024 + oc * 128: 8192 + kc * 1024 + oc * 128 + 128],
                        aa[:, YF + 512 * kc: YF + 512 * kc + 512],
                        start=(kc == 0), stop=(kc == 7)),
                        inc="PE" if kc == 7 else None)
                bk.wrote_banks(omm, ob)
                bk.read("vector", "EV", [ob], lambda e, oc=oc, ob=ob, t0=t0: e.scalar_tensor_tensor(
                    out=fa[:, 512 * oc:512 * oc + 512], in0=pb[ob][:, :],
                    scalar=b["pp"][:, 8 + oc:9 + oc], in1=self.xsl(oc, t0, 512),
                    op0=ALU.add, op1=ALU.add))
            h_done = p.cnt["EV"]
            if upto == "p2h":
                return []
            p.waits("scalar", [("EV", h_done)])
            p.waits("tensor", [("EV", h_done)])
            so = {"p2s": True, "p2b": "bcast"}.get(upto, False)
            self._module_ln(tc, bi, final, st_vals, stats_only=so)
            if upto in ("p2s", "p2b"):
                return []
        self.wa_pe = p.cnt["PE"]
        return st_vals

    def stage_l1a(self, bi, final=False, upto=None):
        p, b, dr, bk = self.prog, self.buf, self.dram, self.banks
        aa, fa, wa, sm, pb = b["aa"], b["fa"], b["wa"], b["small"], self.psum
        QT, KT, VS = 0, 4096, 8192
        AT, ATT, ATG = 12288, 14592, 15616
        KG, VG, YT = 16640, 16768, 17792

        def wload(name, off, lane):
            return p.emit("gpsimd", lambda e, n=name, o=off: e.dma_start(
                out=wa[:, o:o + 8192].rearrange("p (c m) -> p c m", c=8),
                in_=dr[n].rearrange("(c p) m -> p c m", p=128)), inc=lane, n=16)
        p.waits("gpsimd", [("PE", self.wa_pe)])
        wq_v = wload("l1_wq", 0, "W0")
        wk_v = wload("l1_wk", 8192, "W1")
        wv_v = wload("l1_wv", 16384, "W2")
        p.waits("tensor", [("W0", wq_v), ("W1", wk_v), ("W2", wv_v),
                           ("XL", self.x_ready)])

        atl_pe = [0, 0]
        self._l1a_att2_pe = 0
        pend2 = None

        def l1a_tail(st):
            g_, tp_, t0_, at0_, par_, sc_ev_ = st
            p.waits("tensor", [("EV", sc_ev_)])
            bk.begin_write(7)
            for h8 in range(8):
                mmv = p.emit("tensor", lambda e, h8=h8, at0_=at0_: e.transpose(
                    out=pb[7][:, 128 * h8:128 * h8 + 128],
                    in_=aa[:, at0_ + 144 * h8: at0_ + 144 * h8 + 128],
                    identity=b["ident"][:, :]), inc="PE" if h8 == 7 else None)
            bk.wrote_banks(mmv, 7)
            p.waits("vector", [("PE", self._l1a_att2_pe)])
            bk.read("vector", "EV", [7], lambda e: e.tensor_copy(
                out=aa[:, ATT:ATT + 1024], in_=pb[7][:, :]))
            bk.begin_write(7)
            for h8 in range(8):
                mmv = p.emit("tensor", lambda e, h8=h8, at0_=at0_: e.transpose(
                    out=pb[7][0:16, 128 * h8:128 * h8 + 128],
                    in_=aa[:, at0_ + 144 * h8 + 128: at0_ + 144 * h8 + 144],
                    identity=b["ident"][:, :]), inc="PE" if h8 == 7 else None)
            bk.wrote_banks(mmv, 7)
            atl_pe[par_] = mmv
            tg_ev = bk.read("vector", "EV", [7], lambda e: e.tensor_copy(
                out=aa[0:16, ATG:ATG + 1024], in_=pb[7][0:16, :]))
            p.waits("tensor", [("EV", tg_ev)])
            for h8 in range(8):
                h = 8 * g_ + h8
                yb = 4 + h8 % 2
                bk.begin_write(yb)
                p.emit("tensor", lambda e, h=h, h8=h8, tp_=tp_, yb=yb: e.matmul(
                    pb[yb][0:64, 0:128], aa[:, VS + 1024 * tp_ + 64 * h: VS + 1024 * tp_ + 64 * h + 64],
                    aa[:, ATT + 128 * h8: ATT + 128 * h8 + 128], start=True, stop=False))
                mmv = p.emit("tensor", lambda e, h=h, h8=h8, yb=yb: e.matmul(
                    pb[yb][0:64, 0:128], aa[0:16, VG + 64 * h: VG + 64 * h + 64],
                    aa[0:16, ATG + 128 * h8: ATG + 128 * h8 + 128],
                    start=False, stop=True), inc="PE")
                bk.wrote_banks(mmv, yb)
                self._l1a_att2_pe = mmv
                co, ro = h // 2, 64 * (h % 2)
                bk.read("scalar", "ES", [yb], lambda e, co=co, ro=ro, t0_=t0_, tp_=tp_, yb=yb: e.copy(
                    out=aa[ro:ro + 64, YT + 2048 * co + t0_ + 128 * tp_: YT + 2048 * co + t0_ + 128 * tp_ + 128],
                    in_=pb[yb][0:64, 0:128]))

        last_qkv_mm = 0
        for tc in range(4):
            t0 = 512 * tc
            for which, base, wofs in (("q", QT, 0), ("k", KT, 8192)):
                for c in range(8):
                    qb = c % 2
                    bk.begin_write(qb)
                    for kc in range(8):
                        mmv = p.emit("tensor", lambda e, c=c, kc=kc, t0=t0, qb=qb, wofs=wofs: e.matmul(
                            pb[qb][:, :], wa[:, wofs + kc * 1024 + c * 128: wofs + kc * 1024 + c * 128 + 128],
                            self.xsl(kc, t0, 512), start=(kc == 0), stop=(kc == 7)),
                            inc="PE" if kc == 7 else None)
                    bk.wrote_banks(mmv, qb)
                    last_qkv_mm = mmv
                    if which == "q":
                        bk.read("vector", "EV", [qb], lambda e, c=c, qb=qb, base=base: e.tensor_scalar(
                            out=aa[:, base + 512 * c: base + 512 * c + 512],
                            in0=pb[qb][:, :], scalar1=0.125, scalar2=None, op0=ALU.mult))
                    else:
                        bk.read("scalar", "ES", [qb], lambda e, c=c, qb=qb, base=base: e.copy(
                            out=aa[:, base + 512 * c: base + 512 * c + 512],
                            in_=pb[qb][:, :]))
            for tt in range(4):
                for half in range(2):
                    vb2 = 2 + (2 * tt + half) % 2
                    bk.begin_write(vb2)
                    for kc in range(8):
                        mmv = p.emit("tensor", lambda e, tt=tt, half=half, kc=kc, t0=t0, vb2=vb2: e.matmul(
                            pb[vb2][:, :], self.xsl(kc, t0 + 128 * tt, 128),
                            wa[:, 16384 + kc * 1024 + half * 512: 16384 + kc * 1024 + half * 512 + 512],
                            start=(kc == 0), stop=(kc == 7)),
                            inc="PE" if kc == 7 else None)
                    bk.wrote_banks(mmv, vb2)
                    last_qkv_mm = mmv
                    bk.read("vector", "EV", [vb2], lambda e, tt=tt, half=half, vb2=vb2: e.tensor_copy(
                        out=aa[:, VS + 1024 * tt + 512 * half: VS + 1024 * tt + 512 * half + 512],
                        in_=pb[vb2][:, :]))
            qkv_ev, qkv_es = p.cnt["EV"], p.cnt["ES"]
            if tc == 0:
                p.waits("vector", [("ES", qkv_es)])
                p.emit("vector", lambda e: e.tensor_copy(
                    out=aa[:, KG:KG + 128].rearrange("p (c s) -> p c s", c=8),
                    in_=aa[:, KT:KT + 4096].rearrange("p (c s) -> p c s", c=8)[:, :, 0:16]))
                qkv_ev = p.emit("vector", lambda e: e.tensor_copy(
                    out=aa[0:16, VG:VG + 1024], in_=aa[0:16, VS:VS + 1024]), inc="EV")

            for tp in range(4):
                pglob = tc * 4 + tp
                mvar = 144 if pglob == 0 else 0
                for g in range(2):
                    par = (2 * pglob + g) % 2
                    at0 = AT + 1152 * par
                    p.waits("tensor", [("EV", qkv_ev), ("ES", qkv_es)])
                    bk.begin_write(0, 1, 2, 3)
                    for h8 in (0, 2, 4, 6, 1, 3, 5, 7):
                        h = 8 * g + h8
                        c, ro = h // 2, 64 * (h % 2)
                        bankno, so = h8 // 2, 256 * (h8 % 2)
                        p.emit("tensor", lambda e, c=c, ro=ro, bankno=bankno, so=so, t0=t0, tp=tp: e.matmul(
                            pb[bankno][:, so:so + 128],
                            aa[ro:ro + 64, QT + 512 * c + 128 * tp: QT + 512 * c + 128 * tp + 128],
                            aa[ro:ro + 64, KT + 512 * c + 128 * tp: KT + 512 * c + 128 * tp + 128],
                            start=True, stop=True))
                        mmv = p.emit("tensor", lambda e, c=c, ro=ro, bankno=bankno, so=so, tp=tp: e.matmul(
                            pb[bankno][:, so + 128:so + 144],
                            aa[ro:ro + 64, QT + 512 * c + 128 * tp: QT + 512 * c + 128 * tp + 128],
                            aa[ro:ro + 64, KG + 16 * c: KG + 16 * c + 16],
                            start=True, stop=True), inc="PE" if h8 == 7 else None)
                    bk.wrote_banks(mmv, 0, 1, 2, 3)
                    if pend2 is not None:
                        l1a_tail(pend2)
                        pend2 = None
                    mk_ev = bk.read("vector", "EV", [0, 1, 2, 3], lambda e, mvar=mvar: e.tensor_tensor(
                        out=fa[:, 0:1152].rearrange("p (h s) -> p h s", h=8),
                        in0=self.pfull[:, 0:2048].rearrange("p (h s) -> p h s", h=8)[:, :, 0:144],
                        in1=b["mask_l1"][:, mvar:mvar + 144].unsqueeze(1).broadcast_to([128, 8, 144]),
                        op=ALU.add))
                    rx = p.emit("vector", lambda e: e.reduce_max(
                        out=sm[:, 0:8], in_=fa[:, 0:1152].rearrange("p (h s) -> p h s", h=8),
                        axis=AX.X, negate=True), inc="EV")
                    p.waits("scalar", [("EV", rx), ("PE", atl_pe[par])])
                    exp_last = 0
                    for h8 in range(8):
                        exp_last = p.emit("scalar", lambda e, h8=h8, at0=at0: e.activation(
                            out=aa[:, at0 + 144 * h8: at0 + 144 * h8 + 144],
                            in_=fa[:, 144 * h8: 144 * h8 + 144], func=AF.Exp,
                            bias=sm[:, h8:h8 + 1], scale=1.0,
                            accum_out=sm[:, 8 + h8:9 + h8]), inc="ES")
                    p.waits("vector", [("ES", exp_last)])
                    p.emit("vector", lambda e: e.reciprocal(out=sm[:, 16:24], in_=sm[:, 8:16]))
                    sc_ev = p.emit("vector", lambda e, at0=at0: e.tensor_mul(
                        out=aa[:, at0:at0 + 1152].rearrange("p (h s) -> p h s", h=8),
                        in0=aa[:, at0:at0 + 1152].rearrange("p (h s) -> p h s", h=8),
                        in1=sm[:, 16:24].unsqueeze(2).broadcast_to([128, 8, 144])), inc="EV")
                    pend2 = (g, tp, t0, at0, par, sc_ev)
            l1a_tail(pend2)
            pend2 = None

        if upto is not None:
            return []
        p.waits("gpsimd", [("PE", p.cnt["PE"])])
        wo_v = wload("l1_wo", 0, "W0")
        p.waits("tensor", [("W0", wo_v), ("ES", p.cnt["ES"])])
        st_vals = []
        for tc in range(4):
            t0 = 512 * tc
            for oc in range(8):
                ob = oc % 2
                bk.begin_write(ob)
                for kc in range(8):
                    omm = p.emit("tensor", lambda e, oc=oc, kc=kc, t0=t0, ob=ob: e.matmul(
                        pb[ob][:, :], wa[:, kc * 1024 + oc * 128: kc * 1024 + oc * 128 + 128],
                        aa[:, YT + 2048 * kc + t0: YT + 2048 * kc + t0 + 512],
                        start=(kc == 0), stop=(kc == 7)),
                        inc="PE" if kc == 7 else None)
                bk.wrote_banks(omm, ob)
                bk.read("vector", "EV", [ob], lambda e, oc=oc, ob=ob, t0=t0: e.tensor_add(
                    out=fa[:, 512 * oc:512 * oc + 512], in0=pb[ob][:, :],
                    in1=self.xsl(oc, t0, 512)))
            h_done = p.cnt["EV"]
            p.waits("scalar", [("EV", h_done)])
            p.waits("tensor", [("EV", h_done)])
            self._module_ln(tc, bi, final, st_vals)
        self.wa_pe = p.cnt["PE"]
        return st_vals


    def stage_moe(self, bi, final=False):
        p, b, dr, bk = self.prog, self.buf, self.dram, self.banks
        aa, fa, wa, sm, pb = b["aa"], b["fa"], b["wa"], b["small"], self.psum
        ACC = 0
        XTM, XG, HP = 8192, 12288, 14336
        OG, OGT, PU, PG, PT = 18432, 20480, 22528, 23552, 24576
        B1C, GT = 25600, 26624
        GROWG, GTMF, POSM, MTM = 28672, 28928, 29184, 29248
        OUTB, STATS = 8192, 12288
        accf = aa[:, 0:8192].bitcast(F32)
        C = 256

        p.waits("tensor", [("XL", self.x_ready), ("CG", self.ld_sem_vals[1])])
        p.waits("gpsimd", [("PE", self.wa_pe)])
        gv = p.emit("gpsimd", lambda e: e.dma_start(
            out=wa[:, 0:64].rearrange("p (c m) -> p c m", c=8),
            in_=dr["m_gate"].rearrange("(c p) m -> p c m", p=128)), inc="W3", n=16)
        p.waits("sync", [("PE", p.cnt["PE"]), ("EV", p.cnt["EV"]),
                         ("ES", p.cnt["ES"])])
        b1v = p.emit("sync", lambda e: e.dma_start(
            out=aa[:, B1C:B1C + 512].bitcast(F32)[:, 0:128],
            in_=dr["m_b1_pc2"][:, :]), inc="W5", n=16)
        p.waits("tensor", [("W3", gv)])
        for t in range(16):
            gb2 = t % 2
            bk.begin_write(gb2)
            for kc in range(8):
                mmv = p.emit("tensor", lambda e, t=t, kc=kc, gb2=gb2: e.matmul(
                    pb[gb2][:, 0:8], self.xsl(kc, 128 * t, 128),
                    wa[:, 8 * kc: 8 * kc + 8],
                    start=(kc == 0), stop=(kc == 7)), inc="PE" if kc == 7 else None)
            bk.wrote_banks(mmv, gb2)
            bk.read("vector", "EV", [gb2], lambda e, gb2=gb2: e.tensor_copy(
                out=fa[:, 0:8], in_=pb[gb2][:, 0:8]))
            p.emit("vector", lambda e: e.reduce_max(
                out=sm[:, 40:41], in_=fa[:, 0:8], axis=AX.X, negate=True))
            p.emit("vector", lambda e: e.tensor_scalar(
                out=sm[:, 41:42], in0=sm[:, 40:41], scalar1=-1.0, scalar2=None, op0=ALU.mult))
            p.emit("vector", lambda e: e.tensor_scalar(
                out=fa[:, 8:16], in0=fa[:, 0:8], scalar1=sm[:, 41:42], scalar2=None, op0=ALU.is_equal))
            p.emit("vector", lambda e: e.scalar_tensor_tensor(
                out=fa[:, 8:16], in0=fa[:, 8:16], scalar=-1e30, in1=fa[:, 0:8],
                op0=ALU.mult, op1=ALU.add))
            p.emit("vector", lambda e: e.reduce_max(
                out=sm[:, 42:43], in_=fa[:, 8:16], axis=AX.X))
            msk_v = p.emit("vector", lambda e: e.tensor_scalar(
                out=fa[:, 8:16], in0=fa[:, 0:8], scalar1=sm[:, 42:43], scalar2=None, op0=ALU.is_ge))
            p.waits("scalar", [("EV", msk_v)])
            ex_v = p.emit("scalar", lambda e: e.activation(
                out=fa[:, 16:24], in_=fa[:, 0:8], func=AF.Exp, bias=sm[:, 40:41]))
            p.waits("vector", [("ES", ex_v)])
            p.emit("vector", lambda e: e.tensor_mul(
                out=fa[:, 16:24], in0=fa[:, 16:24], in1=fa[:, 8:16]))
            p.emit("vector", lambda e: e.reduce_sum(
                out=sm[:, 43:44], in_=fa[:, 16:24], axis=AX.X))
            p.emit("vector", lambda e: e.reciprocal(
                out=sm[:, 44:45], in_=sm[:, 43:44]))
            p.emit("vector", lambda e, t=t: e.tensor_scalar(
                out=aa[:, GTMF:GTMF + 512].bitcast(F32)[:, 8 * t:8 * t + 8],
                in0=fa[:, 16:24], scalar1=sm[:, 44:45], scalar2=None, op0=ALU.mult))
            gt_v = p.emit("vector", lambda e, t=t: e.tensor_copy(
                out=fa[:, 24:32].bitcast(BF16)[:, 0:8],
                in_=aa[:, GTMF:GTMF + 512].bitcast(F32)[:, 8 * t:8 * t + 8]))
            p.waits("tensor", [("EV", gt_v)])
            bk.begin_write(7)
            tv = p.emit("tensor", lambda e: e.transpose(
                out=pb[7][0:8, 0:128], in_=fa[:, 24:32].bitcast(BF16)[:, 0:8],
                identity=b["ident"][:, :]), inc="PE")
            bk.wrote_banks(tv, 7)
            bk.read("vector", "EV", [7], lambda e, t=t: e.tensor_copy(
                out=aa[0:8, GT + 128 * t: GT + 128 * t + 128], in_=pb[7][0:8, 0:128]))
        gt_done = p.cnt["EV"]

        pslot_pe = [p.cnt["PE"], 0, 0]
        pi = 0
        st_vals = []
        xtm_pe = 0
        pup_pe = [0, 0]
        ogt_pe = 0
        ptbuf_pe = 0
        for tc in range(4):
            t0 = 512 * tc
            p.waits("tensor", [("EV", gt_done)])
            for tt in range(4):
                bk.begin_write(7)
                for dc in range(8):
                    tv = p.emit("tensor", lambda e, tt=tt, dc=dc, t0=t0: e.transpose(
                        out=pb[7][:, 128 * dc:128 * dc + 128],
                        in_=self.xsl(dc, t0 + 128 * tt, 128),
                        identity=b["ident"][:, :]), inc="PE" if dc == 7 else None)
                bk.wrote_banks(tv, 7)
                p.waits("vector", [("PE", xtm_pe)])
                bk.read("vector", "EV", [7], lambda e, tt=tt: e.tensor_copy(
                    out=aa[:, XTM + 1024 * tt: XTM + 1024 * tt + 1024], in_=pb[7][:, :]))
            mp_ev = p.emit("vector", lambda e, tc=tc: e.tensor_scalar(
                out=fa[0:8, 0:512], in0=aa[0:8, GT + 512 * tc: GT + 512 * tc + 512],
                scalar1=0.0, scalar2=None, op0=ALU.is_gt))
            mp_ev = p.emit("vector", lambda e: e.tensor_tensor_scan(
                out=fa[0:8, 512:1024], data0=fa[0:8, 0:512],
                data1=b["ones_col"][0:8, 0:1].broadcast_to([8, 512]),
                initial=0.0, op0=ALU.add, op1=ALU.mult))
            mp_ev = p.emit("vector", lambda e: e.tensor_sub(
                out=fa[0:8, 512:1024], in0=fa[0:8, 512:1024], in1=fa[0:8, 0:512]))
            mt_ev = p.emit("vector", lambda e, tc=tc: e.tensor_scalar(
                out=aa[:, MTM:MTM + 128].bitcast(F32)[:, 0:32],
                in0=aa[:, GTMF:GTMF + 512].bitcast(F32)[:, 32 * tc:32 * tc + 32],
                scalar1=0.0, scalar2=None, op0=ALU.is_gt))
            p.waits("tensor", [("EV", mp_ev)])
            bk.begin_write(6)
            for tt in range(4):
                tv = p.emit("tensor", lambda e, tt=tt: e.transpose(
                    out=pb[6].bitcast(F32)[0:0 + 128, 8 * tt:8 * tt + 8] if False else pb[6][:, 8 * tt:8 * tt + 8],
                    in_=fa[0:8, 512 + 128 * tt:512 + 128 * tt + 128],
                    identity=b["identf"][0:8, 0:8]), inc="PE" if tt == 3 else None)
            bk.wrote_banks(tv, 6)
            ps_ev = bk.read("vector", "EV", [6], lambda e: e.tensor_copy(
                out=aa[:, POSM:POSM + 64].bitcast(F32)[:, 0:32], in_=pb[6][:, 0:32]))
            for oc in range(8):
                ib = 2 + (oc % 2)
                bk.begin_write(ib)
                mmv = p.emit("tensor", lambda e, oc=oc, t0=t0, ib=ib: e.matmul(
                    pb[ib][:, :], b["mb2"][0:8, 128 * oc:128 * oc + 128],
                    aa[0:8, GT + t0: GT + t0 + 512],
                    start=True, stop=True), inc="PE")
                bk.wrote_banks(mmv, ib)
                bk.read("vector", "EV", [ib], lambda e, oc=oc, ib=ib: e.tensor_copy(
                    out=accf[:, 512 * oc: 512 * oc + 512], in_=pb[ib][:, :]))
            for e_i in range(8):
                pls = []
                for pnl in range(3):
                    slot = pi % 3
                    lane = f"W{slot}"
                    p.waits("gpsimd", [("PE", pslot_pe[slot])])
                    if pnl < 2:
                        v = p.emit("gpsimd", lambda e, e_i=e_i, pnl=pnl, slot=slot: e.dma_start(
                            out=wa[:, 8192 * slot:8192 * slot + 8192].rearrange("p (c m) -> p c m", c=8),
                            in_=dr["m_w1"][e_i].rearrange("(c p) m -> p c m", p=128)[:, :, 1024 * pnl:1024 * pnl + 1024]),
                            inc=lane, n=16)
                    else:
                        v = p.emit("gpsimd", lambda e, e_i=e_i, pnl=pnl, slot=slot: e.dma_start(
                            out=wa[:, 8192 * slot:8192 * slot + 8192].rearrange("p (c m) -> p c m", c=16),
                            in_=dr["m_w2"][e_i].rearrange("(c p) m -> p c m", p=128)[:, :, 0:512]),
                            inc=lane, n=16)
                    pls.append((slot, lane, v))
                    pi += 1
                p.waits("vector", [("EV", ps_ev), ("PE", max(pup_pe))])
                pv = 0
                for tt in range(4):
                    gtt = 4 * tc + tt
                    pv = p.emit("vector", lambda e, tt=tt, e_i=e_i: e.tensor_scalar(
                        out=aa[:, PU + 256 * tt: PU + 256 * tt + 256],
                        in0=b["iota256"][:, :],
                        scalar1=aa[:, POSM:POSM + 64].bitcast(F32)[:, 8 * tt + e_i:8 * tt + e_i + 1],
                        scalar2=aa[:, MTM:MTM + 128].bitcast(F32)[:, 8 * tt + e_i:8 * tt + e_i + 1],
                        op0=ALU.is_equal, op1=ALU.mult))
                    pv = p.emit("vector", lambda e, tt=tt, e_i=e_i, gtt=gtt: e.tensor_scalar(
                        out=aa[:, PG + 256 * tt: PG + 256 * tt + 256],
                        in0=b["iota256"][:, :],
                        scalar1=aa[:, POSM:POSM + 64].bitcast(F32)[:, 8 * tt + e_i:8 * tt + e_i + 1],
                        scalar2=aa[:, GTMF:GTMF + 512].bitcast(F32)[:, 8 * gtt + e_i:8 * gtt + e_i + 1],
                        op0=ALU.is_equal, op1=ALU.mult))
                p.waits("tensor", [("EV", pv)])
                for dc in range(8):
                    gb3 = dc % 2
                    bk.begin_write(gb3)
                    for tt in range(4):
                        mmv = p.emit("tensor", lambda e, dc=dc, tt=tt, gb3=gb3: e.matmul(
                            pb[gb3][:, 0:256],
                            aa[:, XTM + 1024 * tt + 128 * dc: XTM + 1024 * tt + 128 * dc + 128],
                            aa[:, PU + 256 * tt: PU + 256 * tt + 256],
                            start=(tt == 0), stop=(tt == 3)),
                            inc="PE" if tt == 3 else None)
                    bk.wrote_banks(mmv, gb3)
                    if dc >= 6:
                        xtm_pe = mmv
                    pup_pe[0] = mmv
                    eng = ("vector", "EV") if dc % 2 == 0 else ("scalar", "ES")
                    if eng[0] == "vector":
                        bk.read("vector", "EV", [gb3], lambda e, dc=dc, gb3=gb3: e.tensor_copy(
                            out=aa[:, XG + 256 * dc: XG + 256 * dc + 256], in_=pb[gb3][:, 0:256]))
                    else:
                        bk.read("scalar", "ES", [gb3], lambda e, dc=dc, gb3=gb3: e.copy(
                            out=aa[:, XG + 256 * dc: XG + 256 * dc + 256], in_=pb[gb3][:, 0:256]))
                xg_ev, xg_es = p.cnt["EV"], p.cnt["ES"]
                p.waits("tensor", [("EV", pv)])
                p.waits("vector", [("PE", ptbuf_pe)])
                for st2 in range(2):
                    bk.begin_write(7)
                    for tt in range(4):
                        tv = p.emit("tensor", lambda e, st2=st2, tt=tt: e.transpose(
                            out=pb[7][:, 128 * tt:128 * tt + 128],
                            in_=aa[:, PG + 256 * tt + 128 * st2: PG + 256 * tt + 128 * st2 + 128],
                            identity=b["ident"][:, :]), inc="PE" if tt == 3 else None)
                    bk.wrote_banks(tv, 7)
                    pup_pe[1] = tv
                    bk.read("vector", "EV", [7], lambda e, st2=st2: e.tensor_copy(
                        out=aa[:, PT + 512 * st2: PT + 512 * st2 + 512], in_=pb[7][:, 0:512]))
                p.waits("tensor", [(pls[0][1], pls[0][2]), ("EV", xg_ev), ("ES", xg_es),
                                   ("W5", b1v)])
                for hdc in range(16):
                    hb = hdc % 4
                    slot = pls[0][0] if hdc < 8 else pls[1][0]
                    if hdc == 8:
                        p.waits("tensor", [(pls[1][1], pls[1][2])])
                    bk.begin_write(hb)
                    for kc in range(8):
                        mmv = p.emit("tensor", lambda e, hdc=hdc, kc=kc, slot=slot, hb=hb: e.matmul(
                            pb[hb][:, 0:256],
                            wa[:, 8192 * slot + 1024 * kc + 128 * (hdc % 8): 8192 * slot + 1024 * kc + 128 * (hdc % 8) + 128],
                            aa[:, XG + 256 * kc: XG + 256 * kc + 256],
                            start=(kc == 0), stop=(kc == 7)),
                            inc="PE" if kc == 7 else None)
                    bk.wrote_banks(mmv, hb)
                    if hdc == 7:
                        pslot_pe[pls[0][0]] = mmv
                    if hdc == 15:
                        pslot_pe[pls[1][0]] = mmv
                    if hdc % 2 == 0:
                        bk.read("vector", "EV", [hb], lambda e, hdc=hdc, hb=hb, e_i=e_i: e.tensor_scalar(
                            out=aa[:, HP + 256 * hdc: HP + 256 * hdc + 256],
                            in0=pb[hb][:, 0:256],
                            scalar1=aa[:, B1C:B1C + 512].bitcast(F32)[:, 16 * e_i + hdc:16 * e_i + hdc + 1],
                            scalar2=0.0, op0=ALU.add, op1=ALU.max))
                    else:
                        bk.read("scalar", "ES", [hb], lambda e, hdc=hdc, hb=hb, e_i=e_i: e.activation(
                            out=aa[:, HP + 256 * hdc: HP + 256 * hdc + 256],
                            in_=pb[hb][:, 0:256], func=AF.Relu,
                            bias=aa[:, B1C:B1C + 512].bitcast(F32)[:, 16 * e_i + hdc:16 * e_i + hdc + 1]))
                hp_ev, hp_es = p.cnt["EV"], p.cnt["ES"]
                slot = pi % 3
                lane = f"W{slot}"
                p.waits("gpsimd", [("PE", pslot_pe[slot])])
                v = p.emit("gpsimd", lambda e, e_i=e_i, slot=slot: e.dma_start(
                    out=wa[:, 8192 * slot:8192 * slot + 8192].rearrange("p (c m) -> p c m", c=16),
                    in_=dr["m_w2"][e_i].rearrange("(c p) m -> p c m", p=128)[:, :, 512:1024]),
                    inc=lane, n=16)
                pls.append((slot, lane, v))
                pi += 1
                ogv = {}

                def emit_ogt(oc2):
                    p.waits("tensor", [ogv[oc2]])
                    bk.begin_write(7)
                    for st2 in range(2):
                        tv = p.emit("tensor", lambda e, oc2=oc2, st2=st2: e.transpose(
                            out=pb[7][:, 512 * (oc2 % 2) + 128 * st2: 512 * (oc2 % 2) + 128 * st2 + 128],
                            in_=aa[:, OG + 256 * oc2 + 128 * st2: OG + 256 * oc2 + 128 * st2 + 128],
                            identity=b["ident"][:, :]), inc="PE" if st2 == 1 else None)
                    bk.wrote_banks(tv, 7)
                    if oc2 == 0:
                        p.waits("vector", [("PE", ogt_pe)])
                    bk.read("vector", "EV", [7], lambda e, oc2=oc2: e.tensor_copy(
                        out=aa[:, OGT:OGT + 2048].rearrange("p (s n) -> p s n", s=2)[:, :, 128 * oc2:128 * oc2 + 128],
                        in_=pb[7][:, 512 * (oc2 % 2):512 * (oc2 % 2) + 256].rearrange("p (s n) -> p s n", s=2)))

                p.waits("tensor", [(pls[2][1], pls[2][2]), ("EV", hp_ev), ("ES", hp_es)])
                for oc in range(8):
                    ob = 2 + (oc % 4)
                    pl = pls[2] if oc < 4 else pls[3]
                    if oc == 4:
                        p.waits("tensor", [(pls[3][1], pls[3][2])])
                    bk.begin_write(ob)
                    for kc in range(16):
                        mmv = p.emit("tensor", lambda e, oc=oc, kc=kc, slot=pl[0], ob=ob: e.matmul(
                            pb[ob][:, 0:256],
                            wa[:, 8192 * slot + 512 * kc + 128 * (oc % 4): 8192 * slot + 512 * kc + 128 * (oc % 4) + 128],
                            aa[:, HP + 256 * kc: HP + 256 * kc + 256],
                            start=(kc == 0), stop=(kc == 15)),
                            inc="PE" if kc == 15 else None)
                    bk.wrote_banks(mmv, ob)
                    if oc == 3:
                        pslot_pe[pls[2][0]] = mmv
                    if oc == 7:
                        pslot_pe[pls[3][0]] = mmv
                    eng = ("vector", "EV") if oc % 2 == 0 else ("scalar", "ES")
                    if eng[0] == "vector":
                        ogv[oc] = ("EV", bk.read("vector", "EV", [ob], lambda e, oc=oc, ob=ob: e.tensor_copy(
                            out=aa[:, OG + 256 * oc: OG + 256 * oc + 256], in_=pb[ob][:, 0:256])))
                    else:
                        ogv[oc] = ("ES", bk.read("scalar", "ES", [ob], lambda e, oc=oc, ob=ob: e.copy(
                            out=aa[:, OG + 256 * oc: OG + 256 * oc + 256], in_=pb[ob][:, 0:256])))
                    if oc >= 1:
                        emit_ogt(oc - 1)
                emit_ogt(7)
                ogt_last_ev = p.cnt["EV"]
                p.waits("tensor", [("EV", ogt_last_ev)])
                for dc in range(8):
                    sb3 = 4 + (dc % 2)
                    bk.begin_write(sb3)
                    for st2 in range(2):
                        mmv = p.emit("tensor", lambda e, dc=dc, st2=st2, sb3=sb3: e.matmul(
                            pb[sb3][:, :],
                            aa[:, OGT + 1024 * st2 + 128 * dc: OGT + 1024 * st2 + 128 * dc + 128],
                            aa[:, PT + 512 * st2: PT + 512 * st2 + 512],
                            start=(st2 == 0), stop=(st2 == 1)),
                            inc="PE" if st2 == 1 else None)
                    bk.wrote_banks(mmv, sb3)
                    ogt_pe = mmv
                    ptbuf_pe = mmv
                    bk.read("vector", "EV", [sb3], lambda e, dc=dc, sb3=sb3: e.tensor_add(
                        out=accf[:, 512 * dc: 512 * dc + 512],
                        in0=pb[sb3][:, :],
                        in1=accf[:, 512 * dc: 512 * dc + 512]))
            acc_done = p.cnt["EV"]
            for oc in range(8):
                hv = p.emit("vector", lambda e, oc=oc, t0=t0: e.tensor_add(
                    out=fa[:, 512 * oc:512 * oc + 512],
                    in0=accf[:, 512 * oc: 512 * oc + 512],
                    in1=self.xsl(oc, t0, 512)))
            p.waits("scalar", [("EV", hv)])
            p.waits("tensor", [("EV", hv)])
            self._module_ln(tc, bi, final, st_vals, sb_base=STATS, out_base=OUTB)
        self.wa_pe = p.cnt["PE"]
        self.moe_b1_pe = p.cnt["PE"]
        self.moe_b1_ev = p.cnt["EV"]
        self.moe_b1_es = p.cnt["ES"]
        return st_vals

    def stage_f(self, bi, final=False, upto=None):
        p, b, dr, bk = self.prog, self.buf, self.dram, self.banks
        aa, fa, wa, sm, pb = b["aa"], b["fa"], b["wa"], b["small"], self.psum
        VS, QT, KT, YW = 0, 8192, 16384, 24576
        ATN, ATT, QR = 26624, 29184, 30464
        DG = 32512
        H2N, GL, H3 = 0, 4096, 8192
        XIN, STATS2, OUTB2 = 20480, 26624, 30720
        pp = b["pp"]

        p.waits("sync", [("XL", self.x_ready), ("ST", getattr(self, "f_xin_st", 0)),
                         ("EV", p.cnt["EV"])])
        for c in range(8):
            xin_st = p.emit("sync", lambda e, c=c: e.dma_start(
                out=dr["xin_d"][c, :, :], in_=self.xsl(c, 0, T)), inc="ST", n=16)
        self.f_xin_st = xin_st

        for l in range(2):
            p.waits("gpsimd", [("PE", max(self.f_wqkv_pe, self.wa_pe))])
            wqkv_v = p.emit("gpsimd", lambda e, l=l: e.dma_start(
                out=wa[:, 0:12288].rearrange("p (c m) -> p c m", c=8),
                in_=dr["f_wqkv"][l].rearrange("(c p) m -> p c m", p=128)), inc="W0", n=16)
            p.waits("gpsimd", [("PE", self.f_wo_pe)])
            wo_v = p.emit("gpsimd", lambda e, l=l: e.dma_start(
                out=wa[:, 12288:16384].rearrange("p (c m) -> p c m", c=4),
                in_=dr["f_wo"][l].rearrange("(c p) m -> p c m", p=128)), inc="W1", n=16)
            p.waits("sync", [("PE", self.f_pp_pe), ("ES", self.f_pp_es),
                             ("EV", self.f_pp_ev)])
            p.emit("sync", lambda e, l=l: e.dma_start(
                out=pp[:, 16:60], in_=dr["f_b1_pc"][l]), inc="W5", n=16)
            p.emit("sync", lambda e, l=l: e.dma_start(
                out=pp[:, 60:68], in_=dr["f_b2_pc"][l]), inc="W5", n=16)
            p.emit("sync", lambda e, l=l: e.dma_start(
                out=pp[:, 68:76], in_=dr["f_lng_pc"][l]), inc="W5", n=16)
            pp_v = p.emit("sync", lambda e, l=l: e.dma_start(
                out=pp[:, 76:84], in_=dr["f_lnb_pc"][l]), inc="W5", n=16)
            qs_v = p.emit("sync", lambda e, l=l: e.dma_start(
                out=b["qsks"][0:1, 0:64], in_=dr["f_qs_row"][l][0:1, :]), inc="W5", n=16)
            ks_v = p.emit("sync", lambda e, l=l: e.dma_start(
                out=b["qsks"][0:1, 64:128], in_=dr["f_ks_row"][l][0:1, :]), inc="W5", n=16)
            p.waits("gpsimd", [("EV", self.f_rtf_ev)])
            for ti, nm in enumerate(("rt_qc", "rt_qs", "rt_kc", "rt_ks")):
                rtl_v = p.emit("gpsimd", lambda e, nm=nm, ti=ti: e.dma_start(
                    out=b["rtf"][:, 1024 * ti:1024 * ti + 1024],
                    in_=dr[nm][:, :]), inc="W4", n=16)
            p.waits("tensor", [("W5", ks_v), ("PE", self.f_qsks_pe)])
            sb6 = 6
            bk.begin_write(sb6)
            scv = p.emit("tensor", lambda e: e.matmul(
                pb[sb6][:, 0:128], b["ones_row_f"][0:1, 0:128], b["qsks"][0:1, 0:128],
                start=True, stop=True), inc="PE")
            bk.wrote_banks(scv, sb6)
            self.f_qsks_pe = scv
            sc_ev = bk.read("vector", "EV", [sb6], lambda e: e.tensor_copy(
                out=fa[:, 5120:5248], in_=pb[sb6][:, 0:128]))
            p.waits("vector", [("W4", rtl_v)])
            for ti in range(4):
                off = 0 if ti < 2 else 64
                p.emit("vector", lambda e, ti=ti, off=off: e.tensor_mul(
                    out=b["rtf"][:, 1024 * ti:1024 * ti + 1024].rearrange("p (a d) -> p a d", a=16),
                    in0=b["rtf"][:, 1024 * ti:1024 * ti + 1024].rearrange("p (a d) -> p a d", a=16),
                    in1=fa[:, 5120 + off:5120 + off + 64].unsqueeze(1).broadcast_to([128, 16, 64])))
            rtf_ev = p.cnt["EV"]

            def rtfs(ti, t):
                return b["rtf"][:, 1024 * ti + 64 * t: 1024 * ti + 64 * t + 64]

            p.waits("tensor", [("W0", wqkv_v), ("XL", self.x_ready), ("EV", rtf_ev),
                               ("W5", ks_v)])
            pend_tr = []
            qr_pe = [0, 0]
            for t in range(16):
                bq3 = (0, 1, 2) if t % 2 == 0 else (3, 4, 5)
                for pi3 in range(3):
                    bno = bq3[pi3]
                    bk.begin_write(bno)
                    for kc in range(8):
                        src = self.xsl(kc, 128 * t, 128) if kc < 4 else self.xsl_shift(kc, 128 * t, 128)
                        mmv = p.emit("tensor", lambda e, kc=kc, pi3=pi3, bno=bno, src=src: e.matmul(
                            pb[bno][:, :], src,
                            wa[:, 1536 * kc + 512 * pi3: 1536 * kc + 512 * pi3 + 512],
                            start=(kc == 0), stop=(kc == 7)),
                            inc="PE" if kc == 7 else None)
                    bk.wrote_banks(mmv, bno)
                for qro, base_, t_, qr_v in pend_tr:
                    p.waits("tensor", [("EV", qr_v)])
                    bk.begin_write(7)
                    for dc in range(4):
                        tv = p.emit("tensor", lambda e, dc=dc, qro=qro: e.transpose(
                            out=pb[7][:, 128 * dc:128 * dc + 128],
                            in_=aa[:, qro + 128 * dc: qro + 128 * dc + 128],
                            identity=b["ident"][:, :]), inc="PE" if dc == 3 else None)
                    bk.wrote_banks(tv, 7)
                    qr_pe[t_ % 2] = tv
                    bk.read("vector", "EV", [7], lambda e, base_=base_, t_=t_: e.tensor_copy(
                        out=aa[:, base_:base_ + 8192].rearrange("p (c n) -> p c n", c=4)[:, :, 128 * t_:128 * t_ + 128],
                        in_=pb[7][:, 0:512].rearrange("p (c n) -> p c n", c=4)))
                pend_tr = []
                vb = bq3[2]
                bk.read("vector", "EV", [vb], lambda e, t=t, vb=vb: e.tensor_copy(
                    out=aa[:, VS + 512 * t: VS + 512 * t + 512], in_=pb[vb][:, :]))
                for pi3, (base, tq, ts_) in enumerate(((QT, 0, 1), (KT, 2, 3))):
                    bno = bq3[pi3]
                    ft = 512 * pi3
                    sq_v = bk.read("scalar", "ES", [bno], lambda e, bno=bno, ft=ft: e.activation(
                        out=fa[:, 1024 + ft:1024 + ft + 512], in_=pb[bno][:, :],
                        func=AF.Square))
                    p.waits("vector", [("ES", sq_v)])
                    p.emit("vector", lambda e, ft=ft: e.reduce_sum(
                        out=sm[:, 48:56], in_=fa[:, 1024 + ft:1024 + ft + 512].rearrange(
                            "p (h d) -> p h d", h=8), axis=AX.X))
                    ssc_v = p.emit("vector", lambda e: e.tensor_scalar(
                        out=sm[:, 48:56], in0=sm[:, 48:56], scalar1=1e-24, scalar2=None,
                        op0=ALU.max))
                    p.waits("scalar", [("EV", ssc_v)])
                    p.emit("scalar", lambda e: e.activation(
                        out=sm[:, 56:64], in_=sm[:, 48:56], func=AF.Ln))
                    rsq_v = p.emit("scalar", lambda e: e.activation(
                        out=sm[:, 56:64], in_=sm[:, 56:64], func=AF.Exp, scale=-0.5))
                    p.waits("vector", [("ES", rsq_v)])
                    bk.read("vector", "EV", [bno], lambda e, bno=bno, ft=ft: e.tensor_mul(
                        out=fa[:, 2048 + ft:2048 + ft + 512].rearrange("p (h d) -> p h d", h=8),
                        in0=pb[bno][:, :].rearrange("p (h d) -> p h d", h=8),
                        in1=sm[:, 56:64].unsqueeze(2).broadcast_to([128, 8, 64])))
                    qn = fa[:, 2048 + ft:2048 + ft + 512]
                    t1 = fa[:, 3072:3584]
                    uu = fa[:, 3584:4096]
                    p.waits("vector", [("PE", self.f_qr_pe)])
                    p.emit("vector", lambda e, qn=qn, t1=t1, tq=tq, t=t: e.tensor_mul(
                        out=t1.rearrange("p (h d) -> p h d", h=8),
                        in0=qn.rearrange("p (h d) -> p h d", h=8),
                        in1=rtfs(tq, t).unsqueeze(1).broadcast_to([128, 8, 64])))
                    p.emit("vector", lambda e, qn=qn, uu=uu, ts_=ts_, t=t: e.tensor_mul(
                        out=uu.rearrange("p (h d) -> p h d", h=8)[:, :, 0:32],
                        in0=qn.rearrange("p (h d) -> p h d", h=8)[:, :, 32:64],
                        in1=rtfs(ts_, t)[:, 0:32].unsqueeze(1).broadcast_to([128, 8, 32])))
                    p.emit("vector", lambda e, qn=qn, uu=uu, ts_=ts_, t=t: e.tensor_mul(
                        out=uu.rearrange("p (h d) -> p h d", h=8)[:, :, 32:64],
                        in0=qn.rearrange("p (h d) -> p h d", h=8)[:, :, 0:32],
                        in1=rtfs(ts_, t)[:, 32:64].unsqueeze(1).broadcast_to([128, 8, 32])))
                    qro = QR + 1024 * (t % 2) + 512 * pi3
                    p.waits("vector", [("PE", qr_pe[t % 2])])
                    qr_v = p.emit("vector", lambda e, t1=t1, uu=uu, qro=qro: e.tensor_add(
                        out=aa[:, qro:qro + 512], in0=t1, in1=uu))
                    pend_tr.append((qro, base, t, qr_v))
            for qro, base_, t_, qr_v in pend_tr:
                p.waits("tensor", [("EV", qr_v)])
                bk.begin_write(7)
                for dc in range(4):
                    tv = p.emit("tensor", lambda e, dc=dc, qro=qro: e.transpose(
                        out=pb[7][:, 128 * dc:128 * dc + 128],
                        in_=aa[:, qro + 128 * dc: qro + 128 * dc + 128],
                        identity=b["ident"][:, :]), inc="PE" if dc == 3 else None)
                bk.wrote_banks(tv, 7)
                qr_pe[t_ % 2] = tv
                bk.read("vector", "EV", [7], lambda e, base_=base_, t_=t_: e.tensor_copy(
                    out=aa[:, base_:base_ + 8192].rearrange("p (c n) -> p c n", c=4)[:, :, 128 * t_:128 * t_ + 128],
                    in_=pb[7][:, 0:512].rearrange("p (c n) -> p c n", c=4)))
            pend_tr = []
            qkv_ev, qkv_es = p.cnt["EV"], p.cnt["ES"]
            rtf_last_ev = qkv_ev
            self.f_rtf_ev = rtf_last_ev
            if upto == "qkv":
                return []

            p.waits("tensor", [("W1", wo_v), ("EV", qkv_ev), ("ES", qkv_es)])
            p.waits("vector", [("ST", self.f_xin_st)])
            atn_pe = [0, 0]
            att_pe = [0, 0]
            yw_pe = 0
            for w in range(4):
                p.waits("scalar", [("PE", yw_pe)])
                pend = None

                def emit_tail(st):
                    hp_, par_, atn0_, W_, nkt_, k_lo_, qt_, exdg = st
                    p.waits("tensor", [("ES", exdg[0]), ("EV", exdg[1])])
                    for j in range(2):
                        hh = 2 * hp_ + j
                        ats = ATT + 640 * j
                        bk.begin_write(4, 5)
                        for k in range(nkt_):
                            tv = p.emit("tensor", lambda e, k=k, j=j, atn0_=atn0_, par_=par_: e.matmul(
                                self.pfull[:, 2048 + 128 * k:2048 + 128 * k + 128],
                                aa[:, atn0_ + 640 * j + 128 * k: atn0_ + 640 * j + 128 * k + 128],
                                aa[:, DG + 256 * par_ + 128 * j: DG + 256 * par_ + 128 * j + 128],
                                start=True, stop=True),
                                inc="PE" if k == nkt_ - 1 else None)
                        bk.wrote_banks(tv, 4, 5)
                        atn_pe[par_] = tv
                        p.waits("scalar", [("PE", att_pe[j])])
                        att_v = bk.read("scalar", "ES", [4, 5], lambda e, nkt_=nkt_, ats=ats: e.copy(
                            out=aa[:, ats:ats + 128 * nkt_],
                            in_=self.pfull[:, 2048:2048 + 128 * nkt_]))
                        p.waits("tensor", [("ES", att_v)])
                        bk.begin_write(6)
                        for k in range(nkt_):
                            ktg = k_lo_ // 128 + k
                            mmv = p.emit("tensor", lambda e, k=k, ktg=ktg, hh=hh, j=j, nkt_=nkt_, ats=ats: e.matmul(
                                pb[6][0:64, 128 * j:128 * j + 128],
                                aa[:, VS + 512 * ktg + 64 * hh: VS + 512 * ktg + 64 * hh + 64],
                                aa[:, ats + 128 * k: ats + 128 * k + 128],
                                start=(k == 0), stop=(k == nkt_ - 1)),
                                inc="PE" if k == nkt_ - 1 else None)
                        bk.wrote_banks(mmv, 6)
                        att_pe[j] = mmv
                        bk.read("scalar", "ES", [6], lambda e, hh=hh, j=j, qt_=qt_: e.copy(
                            out=aa[64 * (hh % 2):64 * (hh % 2) + 64,
                                   YW + 512 * (hh // 2) + 128 * qt_: YW + 512 * (hh // 2) + 128 * qt_ + 128],
                            in_=pb[6][0:64, 128 * j:128 * j + 128]))

                for qt in range(4):
                    qi = 4 * w + qt
                    q0 = 128 * qi
                    k_lo = max(0, q0 - 512)
                    W = q0 + 128 - k_lo
                    nkt = W // 128
                    mof = 640 - W
                    for hp in range(4):
                        par = (qi * 4 + hp) % 2
                        atn0 = ATN + 1280 * par
                        bk.begin_write(0, 1, 2, 3)
                        for j in range(2):
                            hh = 2 * hp + j
                            ro = 64 * j
                            for j0 in range(0, W, 512):
                                nb2 = 2 * j + (1 if j0 else 0)
                                nn = min(512, W - j0)
                                mmv = p.emit("tensor", lambda e, hp=hp, ro=ro, q0=q0, k_lo=k_lo, j0=j0, nn=nn, nb2=nb2: e.matmul(
                                    pb[nb2][:, 0:nn],
                                    aa[ro:ro + 64, QT + 2048 * hp + q0: QT + 2048 * hp + q0 + 128],
                                    aa[ro:ro + 64, KT + 2048 * hp + k_lo + j0: KT + 2048 * hp + k_lo + j0 + nn],
                                    start=True, stop=True), inc="PE" if (j == 1 and j0 + 512 >= W) else None)
                        bk.wrote_banks(mmv, 0, 1, 2, 3)
                        if pend is not None:
                            emit_tail(pend)
                            pend = None
                        mk1 = bk.read("vector", "EV", [0, 1, 2, 3], lambda e, W=W, mof=mof: e.tensor_tensor(
                            out=fa[:, 0:1280].rearrange("p (h s) -> p h s", h=2)[:, :, 0:W],
                            in0=self.pfull[:, 0:2048].rearrange("p (h s) -> p h s", h=2)[:, :, 0:W],
                            in1=b["mask_l0"][:, mof:mof + W].unsqueeze(1).broadcast_to([128, 2, W]),
                            op=ALU.add))
                        rx_v = p.emit("vector", lambda e, W=W: e.reduce_max(
                            out=sm[:, 64:66], in_=fa[:, 0:1280].rearrange("p (h s) -> p h s", h=2)[:, :, 0:W],
                            axis=AX.X, negate=True))
                        p.waits("scalar", [("EV", rx_v), ("PE", atn_pe[par])])
                        ex_v = 0
                        for j in range(2):
                            ex_v = p.emit("scalar", lambda e, j=j, W=W, atn0=atn0: e.activation(
                                out=aa[:, atn0 + 640 * j: atn0 + 640 * j + W],
                                in_=fa[:, 640 * j:640 * j + W], func=AF.Exp,
                                bias=sm[:, 64 + j:65 + j], accum_out=sm[:, 66 + j:67 + j]))
                        p.waits("vector", [("ES", ex_v), ("PE", atn_pe[par])])
                        rc_v = p.emit("vector", lambda e: e.reciprocal(
                            out=sm[:, 68:70], in_=sm[:, 66:68]))
                        dg_v = 0
                        for j in range(2):
                            dg_v = p.emit("vector", lambda e, j=j, par=par: e.tensor_scalar(
                                out=aa[:, DG + 256 * par + 128 * j: DG + 256 * par + 128 * j + 128],
                                in0=b["ident"][:, :], scalar1=sm[:, 68 + j:69 + j],
                                scalar2=None, op0=ALU.mult))
                        pend = (hp, par, atn0, W, nkt, k_lo, qt, (ex_v, dg_v))
                emit_tail(pend)
                ow_es = p.cnt["ES"]
                p.waits("tensor", [("ES", ow_es)])
                for oc in range(8):
                    ob = oc % 2
                    bk.begin_write(ob)
                    for kc in range(4):
                        mmv = p.emit("tensor", lambda e, oc=oc, kc=kc, ob=ob: e.matmul(
                            pb[ob][:, :],
                            wa[:, 12288 + 1024 * kc + 128 * oc: 12288 + 1024 * kc + 128 * oc + 128],
                            aa[:, YW + 512 * kc: YW + 512 * kc + 512],
                            start=(kc == 0), stop=(kc == 3)),
                            inc="PE" if kc == 3 else None)
                    bk.wrote_banks(mmv, ob)
                    yw_pe = mmv
                    bk.read("vector", "EV", [ob], lambda e, oc=oc, w=w, ob=ob: e.tensor_add(
                        out=self.xsl(oc, 512 * w, 512), in0=pb[ob][:, :],
                        in1=self.xsl(oc, 512 * w, 512)))
            self.f_wqkv_pe = p.cnt["PE"]
            self.f_wo_pe = p.cnt["PE"]
            resid_ev = p.cnt["EV"]
            if upto == "attn":
                return []

            SB2, QB2, B1B, B0B = 5, 6, 3, 4
            pn1_pe = [self.f_wqkv_pe, self.f_wqkv_pe]
            pn2_pe = [self.f_wo_pe, self.f_wo_pe]
            h2n_pe = 0
            h3_pe = 0
            w2_ev = resid_ev
            mu2 = aa[0:1, 19456:20480].bitcast(F32)
            var2 = aa[0:1, 20480:21504].bitcast(F32)
            rstd2 = aa[0:1, 21504:22528].bitcast(F32)
            nm2 = aa[0:1, 22528:23552].bitcast(F32)
            for tc in range(4):
                t0 = 512 * tc
                def h2ap(oc, t0=t0):
                    return self.xsl(oc, t0, 512) if oc < 4 else self.xsl_shift(oc, t0, 512)
                w1l_v = [0] * 11
                for pnl in range(2):
                    p.waits("gpsimd", [("PE", pn1_pe[pnl % 2])])
                    w1l_v[pnl] = p.emit("gpsimd", lambda e, l=l, pnl=pnl: e.dma_start(
                        out=wa[:, 4096 * (pnl % 2):4096 * (pnl % 2) + 4096].rearrange("p (c m) -> p c m", c=8),
                        in_=dr["f_w1p"][l].rearrange("(c p) m -> p c m", p=128)[:, :, 512 * pnl:512 * pnl + 512]),
                        inc=f"W{pnl % 2}", n=16)
                p.waits("tensor", [("EV", w2_ev)])
                bk.begin_write(SB2, QB2)
                mmv2 = []
                for oc in range(8):
                    sq = 4096 + 512 * (oc % 2)
                    if oc >= 2:
                        p.waits("scalar", [("PE", mmv2[oc - 2])])
                    sqv = p.emit("scalar", lambda e, oc=oc, sq=sq, h2ap=h2ap: e.activation(
                        out=fa[:, sq:sq + 512], in_=h2ap(oc), func=AF.Square))
                    p.waits("tensor", [("ES", sqv)])
                    p.emit("tensor", lambda e, oc=oc, h2ap=h2ap: e.matmul(
                        pb[QB2][0:1, 0:512], b["ones_col_b"][:, 0:1], h2ap(oc),
                        start=(oc == 0), stop=(oc == 7)))
                    mmv2.append(p.emit("tensor", lambda e, oc=oc, sq=sq: e.matmul(
                        pb[SB2][0:1, 0:512], b["ones_col"][:, 0:1], fa[:, sq:sq + 512],
                        start=(oc == 0), stop=(oc == 7)), inc="PE"))
                smm = p.cnt["PE"]
                bk.wrote_banks(smm, SB2, QB2)
                bk.read("vector", "EV", [QB2], lambda e: e.tensor_scalar(
                    out=mu2, in0=pb[QB2][0:1, 0:512], scalar1=1.0 / D, scalar2=None, op0=ALU.mult))
                bk.read("vector", "EV", [SB2], lambda e: e.tensor_scalar(
                    out=var2, in0=pb[SB2][0:1, 0:512], scalar1=1.0 / D, scalar2=None, op0=ALU.mult))
                p.emit("vector", lambda e: e.scalar_tensor_tensor(
                    out=nm2, in0=mu2, scalar=-1.0, in1=mu2, op0=ALU.mult, op1=ALU.mult))
                vv = p.emit("vector", lambda e: e.tensor_add(out=var2, in0=var2, in1=nm2))
                p.waits("scalar", [("EV", vv)])
                p.emit("scalar", lambda e: e.activation(out=nm2, in_=var2, func=AF.Ln,
                                                        bias=b["small"][0:1, 255:256]))
                sv = p.emit("scalar", lambda e: e.activation(out=rstd2, in_=nm2, func=AF.Exp, scale=-0.5))
                p.waits("vector", [("ES", sv)])
                nv = p.emit("vector", lambda e: e.scalar_tensor_tensor(
                    out=nm2, in0=mu2, scalar=-1.0, in1=rstd2, op0=ALU.mult, op1=ALU.mult))
                p.waits("tensor", [("EV", nv)])
                bk.begin_write(B1B, B0B)
                b1mm = p.emit("tensor", lambda e: e.matmul(
                    pb[B1B][:, :], b["ones_row_f"][0:1, 0:128], rstd2, start=True, stop=True), inc="PE")
                b0mm = p.emit("tensor", lambda e: e.matmul(
                    pb[B0B][:, :], b["ones_row_f"][0:1, 0:128], nm2, start=True, stop=True), inc="PE")
                bk.wrote_banks(b1mm, B1B)
                bk.wrote_banks(b0mm, B0B)
                p.waits("vector", [("PE", max(h2n_pe, b0mm))])
                for oc in range(8):
                    tslot = 4096 + 512 * (oc % 2)
                    bk.read("vector", "EV", [B1B], lambda e, oc=oc, tslot=tslot, h2ap=h2ap: e.tensor_mul(
                        out=fa[:, tslot:tslot + 512], in0=h2ap(oc), in1=pb[B1B][:, :]))
                    bk.read("vector", "EV", [B0B], lambda e, oc=oc, tslot=tslot: e.tensor_add(
                        out=fa[:, tslot:tslot + 512], in0=fa[:, tslot:tslot + 512], in1=pb[B0B][:, :]))
                    p.emit("vector", lambda e, oc=oc, tslot=tslot: e.tensor_scalar(
                        out=aa[:, H2N + 512 * oc: H2N + 512 * oc + 512],
                        in0=fa[:, tslot:tslot + 512], scalar1=pp[:, 68 + oc:69 + oc],
                        scalar2=pp[:, 76 + oc:77 + oc], op0=ALU.mult, op1=ALU.add))
                h2n_ev = p.cnt["EV"]
                p.waits("tensor", [("EV", h2n_ev), ("W0", w1l_v[0]), ("PE", h3_pe)])
                gl_par = 0
                gl_ev = []
                for j in range(44):
                    pnl = j // 4
                    if j % 4 == 0 and 1 <= pnl < 10:
                        slot = (pnl + 1) % 2
                        p.waits("gpsimd", [("PE", pn1_pe[slot])])
                        w1l_v[pnl + 1] = p.emit("gpsimd", lambda e, l=l, pnl=pnl, slot=slot: e.dma_start(
                            out=wa[:, 4096 * slot:4096 * slot + 4096].rearrange("p (c m) -> p c m", c=8),
                            in_=dr["f_w1p"][l].rearrange("(c p) m -> p c m", p=128)[:, :, 512 * (pnl + 1):512 * (pnl + 1) + 512]),
                            inc=f"W{slot}", n=16)
                    if j % 4 == 0 and pnl > 0:
                        p.waits("tensor", [(f"W{pnl % 2}", w1l_v[pnl])])
                    hb = j % 2
                    slot = pnl % 2
                    bk.begin_write(hb)
                    for kc in range(8):
                        mmv = p.emit("tensor", lambda e, j=j, kc=kc, slot=slot, hb=hb: e.matmul(
                            pb[hb][:, :],
                            wa[:, 4096 * slot + 512 * kc + 128 * (j % 4): 4096 * slot + 512 * kc + 128 * (j % 4) + 128],
                            aa[:, H2N + 512 * kc: H2N + 512 * kc + 512],
                            start=(kc == 0), stop=(kc == 7)),
                            inc="PE" if kc == 7 else None)
                    bk.wrote_banks(mmv, hb)
                    if j % 4 == 3 or j == 43:
                        pn1_pe[slot] = mmv
                    glo = GL + 512 * gl_par
                    if j >= 2:
                        p.waits("scalar", [("EV", gl_ev[j - 2])])
                    gv2 = bk.read("scalar", "ES", [hb], lambda e, j=j, hb=hb, glo=glo: e.activation(
                        out=aa[0:64, glo:glo + 512], in_=pb[hb][64:128, :],
                        func=AF.Gelu, bias=pp[64:128, 16 + j:17 + j]))
                    p.waits("vector", [("ES", gv2)])
                    ro2 = 64 * (j % 2)
                    gl_ev.append(bk.read("vector", "EV", [hb], lambda e, j=j, hb=hb, glo=glo, ro2=ro2: e.scalar_tensor_tensor(
                        out=aa[ro2:ro2 + 64, H3 + 512 * (j // 2): H3 + 512 * (j // 2) + 512],
                        in0=pb[hb][0:64, :], scalar=pp[0:64, 16 + j:17 + j],
                        in1=aa[0:64, glo:glo + 512], op0=ALU.add, op1=ALU.mult)))
                    gl_par ^= 1
                h3_ev, h3_es = p.cnt["EV"], p.cnt["ES"]
                for pnl in range(2):
                    p.waits("gpsimd", [("PE", pn2_pe[pnl])])
                    w2l0 = p.emit("gpsimd", lambda e, l=l, pnl=pnl: e.dma_start(
                        out=wa[:, 8192 + 5632 * pnl:8192 + 5632 * pnl + 5632].rearrange("p (c m) -> p c m", c=22),
                        in_=dr["f_w2p"][l].rearrange("(c p) m -> p c m", p=128)[:, :, 256 * pnl:256 * pnl + 256]),
                        inc=f"W{2 + pnl}", n=16)
                    if pnl == 0:
                        w2l_v0 = w2l0
                    else:
                        w2l_v1 = w2l0
                p.waits("tensor", [("EV", h3_ev), ("ES", h3_es), ("W2", w2l_v0)])
                for oc in range(8):
                    pnl = oc // 4
                    pq = oc // 2
                    slot = pq % 2
                    lane = f"W{2 + slot}"
                    if oc == 2:
                        pass
                    ob = 2 + (oc % 2)
                    if pq >= 2 and oc % 2 == 0:
                        p.waits("gpsimd", [("PE", pn2_pe[slot])])
                        wv2 = p.emit("gpsimd", lambda e, l=l, pq=pq, slot=slot: e.dma_start(
                            out=wa[:, 8192 + 5632 * slot:8192 + 5632 * slot + 5632].rearrange("p (c m) -> p c m", c=22),
                            in_=dr["f_w2p"][l].rearrange("(c p) m -> p c m", p=128)[:, :, 256 * pq:256 * pq + 256]),
                            inc=lane, n=16)
                        p.waits("tensor", [(lane, wv2)])
                    elif oc == 2:
                        p.waits("tensor", [("W3", w2l_v1)])
                    bk.begin_write(ob)
                    for kc in range(22):
                        mmv = p.emit("tensor", lambda e, oc=oc, kc=kc, slot=slot, ob=ob: e.matmul(
                            pb[ob][:, :],
                            wa[:, 8192 + 5632 * slot + 256 * kc + 128 * (oc % 2): 8192 + 5632 * slot + 256 * kc + 128 * (oc % 2) + 128],
                            aa[:, H3 + 512 * kc: H3 + 512 * kc + 512],
                            start=(kc == 0), stop=(kc == 21)),
                            inc="PE" if kc == 21 else None)
                    bk.wrote_banks(mmv, ob)
                    if oc % 2 == 1:
                        pn2_pe[slot] = mmv
                    bk.read("vector", "EV", [ob], lambda e, oc=oc, t0=t0, ob=ob: e.scalar_tensor_tensor(
                        out=self.xsl(oc, t0, 512), in0=pb[ob][:, :],
                        scalar=pp[:, 60 + oc:61 + oc], in1=self.xsl(oc, t0, 512),
                        op0=ALU.add, op1=ALU.add))
                w2_ev = p.cnt["EV"]
                h2n_pe = p.cnt["PE"]
                h3_pe = h2n_pe
            self.f_pp_pe, self.f_pp_es, self.f_pp_ev = p.cnt["PE"], p.cnt["ES"], p.cnt["EV"]
            self.f_wqkv_pe = p.cnt["PE"]
            self.f_wo_pe = p.cnt["PE"]
            if upto == "ffn" + str(l):
                return []

        st_vals = []
        xin_pe = p.cnt["PE"]
        for tc in range(4):
            t0 = 512 * tc
            p.waits("sync", [("ST", self.f_xin_st), ("PE", xin_pe),
                             ("EV", w2_ev), ("ES", p.cnt["ES"] if tc == 0 else 0)])
            xi_v = p.emit("sync", lambda e, t0=t0: e.dma_start(
                out=aa[:, XIN:XIN + 4096].rearrange("p (c n) -> p c n", c=8),
                in_=dr["xin_d"].rearrange("c p n -> p c n")[:, :, t0:t0 + 512]), inc="GR", n=16)
            p.waits("vector", [("GR", xi_v), ("EV", w2_ev)])
            for oc in range(8):
                hv = p.emit("vector", lambda e, oc=oc, t0=t0: e.tensor_add(
                    out=fa[:, 512 * oc:512 * oc + 512],
                    in0=aa[:, XIN + 512 * oc: XIN + 512 * oc + 512],
                    in1=self.xsl(oc, t0, 512)))
            p.waits("scalar", [("EV", hv)])
            p.waits("tensor", [("EV", hv)])
            self._module_ln(tc, bi, final, st_vals, sb_base=STATS2, out_base=OUTB2)
            xin_pe = self.prog.cnt["PE"]
        self.wa_pe = p.cnt["PE"]
        return st_vals



def build_nc(stage_list, nb=NB, dump=False, upto=None):
    nc = bass.Bass("TRN2", target_bir_lowering=False, debug=False)
    mk = MK(nc, stage_list)
    mk.declare()
    if dump:
        mk.dram["dbg_aa"] = nc.dram_tensor("dbg_aa", [128, 34816], BF16, kind="ExternalOutput").ap()
        mk.dram["dbg_fa"] = nc.dram_tensor("dbg_fa", [128, 5632], F32, kind="ExternalOutput").ap()
        mk.dram["dbg_wa"] = nc.dram_tensor("dbg_wa", [128, 24576], BF16, kind="ExternalOutput").ap()
    for name in SEMS:
        cm = nc.semaphore(name)
        mk.prog.sems[name] = cm.__enter__()
        mk.ctxs.append(cm)
    mk.alloc()
    mk.ld_const()
    cl0, cg0, ev0 = mk.prog.cnt["CL"], mk.prog.cnt["CG"], mk.prog.cnt["EV"]
    for eng in ("tensor", "vector", "scalar"):
        mk.prog.waits(eng, [("CL", cl0), ("CG", cg0)])
    mk.prog.waits("tensor", [("EV", ev0)])

    for bi in range(nb):
        if bi > 0:
            mk.prog.waits("gpsimd", [("EV", mk.prog.cnt["EV"]),
                                     ("ES", mk.prog.cnt["ES"]),
                                     ("PE", mk.prog.cnt["PE"])])
        mk.x_ready = mk.ld_x(bi)
        st_vals = []
        for si, stg in enumerate(stage_list):
            final = (si == len(stage_list) - 1)
            if stg == "a":
                st_vals = mk.stage_a(bi, final=final, upto=upto)
            elif stg == "f":
                st_vals = mk.stage_f(bi, final=final, upto=upto)
            elif stg == "l1a":
                st_vals = mk.stage_l1a(bi, final=final, upto=upto)
            elif stg == "moe":
                st_vals = mk.stage_moe(bi, final=final)
            else:
                raise ValueError(stg)
            if si != len(stage_list) - 1:
                mk.prog.barrier()
        mk.prog.barrier()
    if dump:
        for nm2, bf2 in (("dbg_aa", "aa"), ("dbg_fa", "fa"), ("dbg_wa", "wa")):
            mk.prog.emit("sync", lambda e, nm2=nm2, bf2=bf2: e.dma_start(
                out=mk.dram[nm2][:, :], in_=mk.buf[bf2][:, :]), inc="ST", n=16)
    mk.prog.waits("sync", [(s, mk.prog.cnt[s]) for s in
                           ("SO0", "SO1", "ST") if mk.prog.cnt[s]])
    mk.prog.check_deadlock()
    with nc.Block() as block:
        mk.prog.replay(block)
    mk.close()
    return nc


def run_stages(inputs, stage_list, nc=None):
    common, shards = host_prepare(inputs)
    if nc is None:
        nc = build_nc(stage_list)
    in_maps = []
    for core in range(8):
        m = dict(common)
        m["x_t"] = shards[core]
        in_maps.append(m)
    res = run_bass_kernel_spmd(nc, in_maps, core_ids=list(range(8)))
    outs = []
    for core in range(8):
        ot = res.results[core]["out_t"]
        outs.append(np.ascontiguousarray(ot.transpose(0, 2, 1)))
    return np.concatenate(outs, axis=0)


def kernel(**inputs):
    return run_stages(inputs, ["a", "f", "l1a", "moe"]).astype(np.float32)



# version 14
# speedup vs baseline: 1.2399x; 1.0109x over previous
import sys
sys.path.insert(0, "/opt/trn_rl_repo")
import numpy as np
import ml_dtypes
import concourse.bass as bass
import concourse.mybir as mybir
from concourse.bass_utils import run_bass_kernel_spmd

F32 = mybir.dt.float32
BF16 = mybir.dt.bfloat16
AF = mybir.ActivationFunctionType
ALU = mybir.AluOpType
AX = mybir.AxisListType

T = 2048
D = 1024
PADT = 2052
NB = 4
NEG = -3.0e38

ENGS = ("tensor", "vector", "scalar", "gpsimd", "sync")


class Prog:

    OWN_SEM = {"tensor": "PE", "vector": "EV", "scalar": "ES"}

    def __init__(self, nc, semnames):
        self.nc = nc
        self.semnames = list(semnames)
        self.sems = {}
        self.cnt = {s: 0 for s in semnames}
        self.q = {e: [] for e in ENGS}
        self.ev = {e: [] for e in ENGS}
        self.self_waited = {e: 0 for e in ENGS}

    def wait(self, eng, sem, val):
        if val <= 0:
            return
        self.q[eng].append(("w", sem, val, None))
        self.ev[eng].append(("w", sem, val))

    def emit(self, eng, fn, inc=None, n=1):
        own = self.OWN_SEM.get(eng)
        if own is not None and self.cnt[own] > self.self_waited[eng]:
            self.self_waited[eng] = self.cnt[own]
            self.wait(eng, own, self.cnt[own])
        if inc is None and eng in ("vector", "scalar"):
            inc = own
        val = None
        if inc is not None:
            self.cnt[inc] += n
            val = self.cnt[inc]
        self.q[eng].append(("i", inc, n, fn))
        self.ev[eng].append(("i", inc, val))
        return val

    def waits(self, eng, pairs):
        best = {}
        for s, v in pairs:
            if v and v > best.get(s, 0):
                best[s] = v
        for s, v in best.items():
            self.wait(eng, s, v)

    def barrier(self, tag="", engines=("tensor", "vector", "scalar")):
        sem_of = {"tensor": "PE", "vector": "EV", "scalar": "ES",
                  "gpsimd": "EG", "sync": "EY"}
        vals = {}
        for e in engines:
            vals[e] = self.emit(e, lambda en: en.nop(), inc=sem_of[e])
        for e in engines:
            for o in engines:
                if o != e:
                    self.wait(e, sem_of[o], vals[o])

    def check_deadlock(self):
        ptr = {e: 0 for e in ENGS}
        cnt = {s: 0 for s in self.semnames}
        progress = True
        while progress:
            progress = False
            for e in ENGS:
                while ptr[e] < len(self.ev[e]):
                    kind, sem, val = self.ev[e][ptr[e]]
                    if kind == "w":
                        if cnt[sem] >= val:
                            ptr[e] += 1
                            progress = True
                        else:
                            break
                    else:
                        if sem is not None:
                            cnt[sem] = val if val > cnt[sem] else cnt[sem]
                        ptr[e] += 1
                        progress = True
        stuck = {e: ptr[e] for e in ENGS if ptr[e] < len(self.ev[e])}
        if stuck:
            info = []
            for e, p in stuck.items():
                kind, sem, val = self.ev[e][p]
                info.append(f"{e}@{p}/{len(self.ev[e])} waiting {sem}>={val} (now {cnt[sem]})")
            raise RuntimeError("DEADLOCK in schedule: " + "; ".join(info))

    def replay(self, block):
        def runner(stream):
            def go(eng):
                for item in stream:
                    kind, a, b, fn = item
                    if kind == "w":
                        eng.wait_ge(self.sems[a], b)
                    else:
                        ins = fn(eng)
                        if a is not None:
                            ins.then_inc(self.sems[a], b)
            return go
        block.tensor(runner(self.q["tensor"]))
        block.vector(runner(self.q["vector"]))
        block.scalar(runner(self.q["scalar"]))
        block.gpsimd(runner(self.q["gpsimd"]))
        block.sync(runner(self.q["sync"]))


class Banks:

    def __init__(self, prog):
        self.p = prog
        self.wrote = [0] * 8
        self.reads = [dict() for _ in range(8)]

    def begin_write(self, *banks):
        pairs = []
        for b in banks:
            pairs += list(self.reads[b].items())
            self.reads[b] = {}
        self.p.waits("tensor", pairs)

    def wrote_banks(self, pe_val, *banks):
        for b in banks:
            self.wrote[b] = pe_val

    def read(self, eng, sem, banks, fn, inc_n=1):
        self.p.waits(eng, [("PE", max(self.wrote[b] for b in banks))])
        v = self.p.emit(eng, fn, inc=sem, n=inc_n)
        for b in banks:
            self.reads[b][sem] = v
        return v



def _pc(vec, nchunk):
    return np.ascontiguousarray(vec.reshape(nchunk, 128).T).astype(np.float32)


def host_prepare(inputs):
    f = {k: np.asarray(v, np.float32) for k, v in inputs.items()}
    c = {}
    for nm in ("Wq", "Wk", "Wv", "Wo"):
        c["a_w" + nm[-1].lower()] = np.ascontiguousarray(f["L0_attn_" + nm].T)
    c["a_bq_pc"] = _pc(f["L0_attn_bq"], 8)
    c["a_bo_pc"] = _pc(f["L0_attn_bo"], 8)
    c["a_bkv_row"] = np.concatenate([f["L0_attn_bk"], f["L0_attn_bv"]])[None, :]

    HID = 2730
    HIDP = 2816
    NBLK = 44
    w1 = f["L0_ffn_w1"]
    b1 = f["L0_ffn_b1"]
    w2 = f["L0_ffn_w2"]
    w1a = np.zeros((2, 1024, HIDP), np.float32)
    w1g = np.zeros((2, 1024, HIDP), np.float32)
    w1a[:, :, :HID] = w1[:, :, :HID]
    w1g[:, :, :HID] = w1[:, :, HID:]
    b1a = np.zeros((2, HIDP), np.float32)
    b1g = np.zeros((2, HIDP), np.float32)
    b1a[:, :HID] = b1[:, :HID]
    b1g[:, :HID] = b1[:, HID:]
    w1p = np.zeros((2, 1024, 2 * HIDP), np.float32)
    b1_pc = np.zeros((2, 128, NBLK), np.float32)
    for j in range(NBLK):
        w1p[:, :, 128 * j:128 * j + 64] = w1a[:, :, 64 * j:64 * j + 64]
        w1p[:, :, 128 * j + 64:128 * j + 128] = w1g[:, :, 64 * j:64 * j + 64]
        b1_pc[:, 0:64, j] = b1a[:, 64 * j:64 * j + 64]
        b1_pc[:, 64:128, j] = b1g[:, 64 * j:64 * j + 64]
    w2p = np.zeros((2, HIDP, 1024), np.float32)
    w2p[:, :HID, :] = w2
    c["f_wqkv"] = f["L0_ffn_wqkv"]
    c["f_wo"] = f["L0_ffn_wo"]
    c["f_w1p"] = w1p
    c["f_w2p"] = w2p
    c["f_b1_pc"] = b1_pc
    c["f_b2_pc"] = np.stack([_pc(f["L0_ffn_b2"][l], 8) for l in range(2)])
    c["f_lng_pc"] = np.stack([_pc(f["L0_ffn_ln_g"][l], 8) for l in range(2)])
    c["f_lnb_pc"] = np.stack([_pc(f["L0_ffn_ln_b"][l], 8) for l in range(2)])
    c["f_qs_row"] = f["L0_ffn_q_scale"][:, None, :]
    c["f_ks_row"] = f["L0_ffn_k_scale"][:, None, :]

    DIM_HEAD, WINDOW, QK_SCALE = 64, 512, 8.0
    half = np.arange(0, DIM_HEAD, 2, dtype=np.float64)
    inv_freq = 1.0 / 10000.0 ** (half / DIM_HEAD)
    t = np.arange(T, dtype=np.float64)
    freqs = t[:, None] * inv_freq[None, :]
    freqs = np.concatenate([freqs, freqs], -1)
    base = (half + 0.4 * DIM_HEAD) / (1.4 * DIM_HEAD)
    power = (t - T // 2) / float(WINDOW // 2)
    scale = base[None, :] ** power[:, None]
    scale = np.concatenate([scale, scale], -1)
    cos, sin = np.cos(freqs), np.sin(freqs)
    qc = cos * scale * QK_SCALE
    qs_t = sin * scale * QK_SCALE
    kc = cos / scale
    ks_t = sin / scale
    def sgn(tab):
        out = tab.copy()
        out[:, :32] = -tab[:, :32]
        return out
    def tokmaj(tab):
        return np.ascontiguousarray(
            tab.reshape(16, 128, 64).transpose(1, 0, 2).reshape(128, 16 * 64)
        ).astype(np.float32)
    c["rt_qc"] = tokmaj(qc)
    c["rt_qs"] = tokmaj(sgn(qs_t))
    c["rt_kc"] = tokmaj(kc)
    c["rt_ks"] = tokmaj(sgn(ks_t))

    p = np.arange(128)[:, None]
    j = np.arange(640)[None, :]
    diff = p + 512 - j
    c["mask_l0"] = np.where((diff >= 0) & (diff <= 512), 0.0, NEG).astype(ml_dtypes.bfloat16)

    for nm in ("Wq", "Wk", "Wv", "Wo"):
        c["l1_w" + nm[-1].lower()] = np.ascontiguousarray(f["L1_attn_" + nm].T)
    BLK = 64
    qi = np.arange(128)[:, None]
    kj = np.arange(128)[None, :]
    local = np.where((qi // BLK == kj // BLK) & (kj <= qi), 0.0, NEG)
    m_a = np.concatenate([local, np.zeros((128, 16))], axis=1)
    gb = np.zeros((128, 16))
    gb[0:64, :] = NEG
    m_b = np.concatenate([local, gb], axis=1)
    c["mask_l1"] = np.concatenate([m_a, m_b], axis=1).astype(ml_dtypes.bfloat16)

    c["m_gate"] = f["L1_ffn_gate"]
    c["m_w1"] = f["L1_ffn_W1"]
    c["m_w2"] = f["L1_ffn_W2"]
    c["m_b1_pc2"] = np.concatenate(
        [_pc(f["L1_ffn_b1"][e], 16) for e in range(8)], axis=1)
    c["c_identf"] = np.eye(128, dtype=np.float32)
    c["c_iota256"] = np.tile(np.arange(256, dtype=np.float32), (128, 1))
    c["m_b2"] = f["L1_ffn_b2"]

    c["c_ident"] = np.eye(128, dtype=np.float32).astype(ml_dtypes.bfloat16)
    c["c_ones_row"] = np.ones((1, 128), dtype=ml_dtypes.bfloat16)
    c["c_ones_col"] = np.ones((128, 1), np.float32)
    c["c_ones_row_f"] = np.ones((1, 128), np.float32)
    c["c_ones_col_b"] = np.ones((128, 1), dtype=ml_dtypes.bfloat16)
    sel = np.zeros((2, 128), np.float32)
    sel[0, 0:64] = 1.0
    sel[1, 64:128] = 1.0
    c["c_sel2"] = sel.astype(ml_dtypes.bfloat16)

    x = np.asarray(inputs["L0_attn_x"], np.float32)
    shards = []
    for core in range(8):
        xb = x[core * NB:(core + 1) * NB]
        shards.append(np.ascontiguousarray(xb.transpose(0, 2, 1)))
    return c, shards



SEMS = ("CL", "CG", "XL", "W0", "W1", "W2", "W3", "W4", "W5", "GR",
        "SO0", "SO1", "ST", "PE", "EV", "ES", "EG", "EY")


class MK:

    def __init__(self, nc, stages, fused=True):
        self.nc = nc
        self.stages = stages
        self.fused = fused
        self.prog = Prog(nc, SEMS)
        self.dram = {}
        self.buf = {}
        self.psum = []
        self.ctxs = []
        self.wa_pe = 0
        self.moe_b1_pe = 0
        self.moe_b1_ev = 0
        self.moe_b1_es = 0
        self.f_wqkv_pe = 0
        self.f_wo_pe = 0
        self.f_rtf_ev = 0
        self.f_qsks_pe = 0
        self.f_qr_pe = 0
        self.f_pp_pe = 0
        self.f_pp_es = 0
        self.f_pp_ev = 0
        self.f_xin_st = 0

    def declare(self):
        nc = self.nc
        dr = self.dram
        dr["x_t"] = nc.dram_tensor("x_t", [NB, D, T], F32, kind="ExternalInput").ap()
        dr["gt_d"] = nc.dram_tensor("gt_d", [8, T], BF16, kind="Internal").ap()
        dr["xin_d"] = nc.dram_tensor("xin_d", [8, 128, T], BF16, kind="Internal").ap()
        dr["out_t"] = nc.dram_tensor("out_t", [NB, D, T], F32, kind="ExternalOutput").ap()
        shapes = {
            "a_wq": [D, D], "a_wk": [D, D], "a_wv": [D, D], "a_wo": [D, D],
            "a_bq_pc": [128, 8], "a_bo_pc": [128, 8], "a_bkv_row": [1, 2 * D],
            "f_wqkv": [2, D, 1536], "f_wo": [2, 512, D],
            "f_w1p": [2, D, 5632], "f_w2p": [2, 2816, D],
            "f_b1_pc": [2, 128, 44], "f_b2_pc": [2, 128, 8],
            "f_lng_pc": [2, 128, 8], "f_lnb_pc": [2, 128, 8],
            "f_qs_row": [2, 1, 64], "f_ks_row": [2, 1, 64],
            "rt_qc": [128, 1024], "rt_qs": [128, 1024],
            "rt_kc": [128, 1024], "rt_ks": [128, 1024],
            "l1_wq": [D, D], "l1_wk": [D, D], "l1_wv": [D, D], "l1_wo": [D, D],
            "m_gate": [D, 8], "m_w1": [8, D, 2048], "m_w2": [8, 2048, D],
            "m_b1_pc2": [128, 128], "m_b2": [8, D],
            "c_identf": [128, 128], "c_iota256": [128, 256],
            "c_ones_col": [128, 1], "c_ones_row_f": [1, 128],
        }
        bf_shapes = {
            "c_ident": [128, 128], "c_ones_row": [1, 128],
            "c_ones_col_b": [128, 1], "c_sel2": [2, 128],
            "mask_l0": [128, 640], "mask_l1": [128, 288],
        }
        for k, s in shapes.items():
            dr[k] = nc.dram_tensor(k, s, F32, kind="ExternalInput").ap()
        for k, s in bf_shapes.items():
            dr[k] = nc.dram_tensor(k, s, BF16, kind="ExternalInput").ap()

    def alloc(self):
        nc = self.nc
        def sb(name, shape, dt=BF16):
            cm = nc.sbuf_tensor(name, shape, dt)
            t = cm.__enter__()
            self.ctxs.append(cm)
            return t
        b = self.buf
        b["xs"] = sb("xs", [128, 8 * PADT])
        b["wa"] = sb("wa", [128, 24576])
        b["aa"] = sb("aa", [128, 34816])
        b["fa"] = sb("fa", [128, 5632], F32)
        b["small"] = sb("small", [128, 256], F32)
        b["ident"] = sb("ident", [128, 128])
        b["identf"] = sb("identf", [128, 128], F32)
        b["iota256"] = sb("iota256", [128, 256], F32)
        b["ones_row"] = sb("ones_row", [1, 128])
        b["ones_col"] = sb("ones_col", [128, 1], F32)
        b["ones_row_f"] = sb("ones_row_f", [1, 128], F32)
        b["ones_col_b"] = sb("ones_col_b", [128, 1])
        b["sel2"] = sb("sel2", [2, 128])
        b["mask_l0"] = sb("s_mask_l0", [128, 640])
        b["mask_l1"] = sb("s_mask_l1", [128, 288])
        b["rt"] = sb("rt", [128, 4096], F32)
        b["rtf"] = sb("rtf", [128, 4096])
        b["pp"] = sb("pp", [128, 160], F32)
        b["rows"] = sb("rows", [1, 2048])
        b["mb2"] = sb("mb2", [8, D])
        b["qsks"] = sb("qsks", [1, 128], F32)
        cm = nc.psum_tensor("pbig", [128, 3584], F32)
        pf = cm.__enter__()
        self.ctxs.append(cm)
        self.pfull = pf
        for i in range(7):
            self.psum.append(pf[:, 512 * i:512 * i + 512])
        cm = nc.psum_tensor("pb7", [128, 1024], BF16)
        self.psum.append(cm.__enter__())
        self.ctxs.append(cm)
        self.banks = Banks(self.prog)

    def close(self):
        for cm in reversed(self.ctxs):
            cm.__exit__(None, None, None)

    def ld_const(self):
        p, dr, b = self.prog, self.dram, self.buf
        def g(eng, dst, src):
            return p.emit(eng, lambda e, d=dst, s=src: e.dma_start(out=d, in_=s),
                          inc="CL" if eng == "sync" else "CG", n=16)
        g("sync", b["ident"][:, :], dr["c_ident"][:, :])
        g("sync", b["ones_row"][0:1, :], dr["c_ones_row"][0:1, :])
        g("sync", b["ones_col"][:, :], dr["c_ones_col"][:, :])
        g("sync", b["ones_row_f"][0:1, :], dr["c_ones_row_f"][0:1, :])
        g("sync", b["identf"][:, :], dr["c_identf"][:, :])
        g("sync", b["iota256"][:, :], dr["c_iota256"][:, :])
        g("sync", b["ones_col_b"][:, :], dr["c_ones_col_b"][:, :])
        g("sync", b["sel2"][0:2, :], dr["c_sel2"][0:2, :])
        g("sync", b["mask_l0"][:, :], dr["mask_l0"][:, :])
        g("sync", b["mask_l1"][:, :], dr["mask_l1"][:, :])
        rt = b["rt"]
        g("sync", rt[:, 0:1024], dr["rt_qc"][:, :])
        g("sync", rt[:, 1024:2048], dr["rt_qs"][:, :])
        g("sync", rt[:, 2048:3072], dr["rt_kc"][:, :])
        g("sync", rt[:, 3072:4096], dr["rt_ks"][:, :])
        pp = b["pp"]
        g("sync", pp[:, 0:8], dr["a_bq_pc"][:, :])
        g("sync", pp[:, 8:16], dr["a_bo_pc"][:, :])
        g("gpsimd", b["rows"][0:1, 0:2048], dr["a_bkv_row"][0:1, :])
        g("gpsimd", b["mb2"][0:8, :], dr["m_b2"][0:8, :])
        self.ld_sem_vals = (self.prog.cnt["CL"], self.prog.cnt["CG"])
        p.emit("vector", lambda e: e.memset(self.buf["small"][:, 255:256], 1e-5), inc="EV")
        for c in range(8):
            p.emit("vector", lambda e, c=c: e.memset(self.buf["xs"][:, c * PADT:c * PADT + 1], 0.0), inc="EV")

    def ld_x(self, bi):
        p, dr, b = self.prog, self.dram, self.buf
        vals = []
        for c in range(8):
            v = p.emit("gpsimd",
                       lambda e, c=c, bi=bi: e.dma_start(
                           out=b["xs"][:, c * PADT + 1: c * PADT + 1 + T],
                           in_=dr["x_t"][bi, c * 128:(c + 1) * 128, :]),
                       inc="XL", n=16)
            vals.append(v)
        return vals[-1]

    def st_out(self, bi, src_f32):
        p, dr = self.prog, self.dram
        vals = []
        for c in range(8):
            v = p.emit("sync",
                       lambda e, c=c, bi=bi: e.dma_start(
                           out=dr["out_t"][bi, c * 128:(c + 1) * 128, :],
                           in_=src_f32(c)),
                       inc="SO0", n=16)
            vals.append(v)
        return vals[-1]

    def xsl(self, c, j0, n):
        return self.buf["xs"][:, c * PADT + 1 + j0: c * PADT + 1 + j0 + n]

    def xsl_shift(self, c, j0, n):
        return self.buf["xs"][:, c * PADT + j0: c * PADT + j0 + n]

    def _module_ln(self, tc, bi, final, st_vals, sb_base=12288, out_base=8192,
                   stats_only=False):
        p, b, bk, pb = self.prog, self.buf, self.banks, self.psum
        aa, fa = b["aa"], b["fa"]
        t0 = 512 * tc
        SB, QB = 5, 6
        B1, B0 = 3, 4
        mu = aa[0:1, sb_base:sb_base + 1024].bitcast(F32)
        var = aa[0:1, sb_base + 1024:sb_base + 2048].bitcast(F32)
        rstd = aa[0:1, sb_base + 2048:sb_base + 3072].bitcast(F32)
        nm = aa[0:1, sb_base + 3072:sb_base + 4096].bitcast(F32)

        bk.begin_write(SB, QB)
        sqb = aa[:, out_base:out_base + 4096]
        sqv = p.emit("scalar", lambda e, sqb=sqb: e.activation(
            out=sqb, in_=fa[:, 0:4096], func=AF.Square), inc="ES")
        p.waits("tensor", [("ES", sqv)])
        for oc in range(8):
            p.emit("tensor", lambda e, oc=oc: e.matmul(
                pb[QB][0:1, 0:512], b["ones_col"][:, 0:1],
                fa[:, 512 * oc:512 * oc + 512], start=(oc == 0), stop=(oc == 7)))
            p.emit("tensor", lambda e, oc=oc, sqb=sqb: e.matmul(
                pb[SB][0:1, 0:512], b["ones_col_b"][:, 0:1],
                sqb[:, 512 * oc:512 * oc + 512], start=(oc == 0), stop=(oc == 7)),
                inc="PE" if oc == 7 else None)
        smm = p.cnt["PE"]
        bk.wrote_banks(smm, SB, QB)
        bk.read("vector", "EV", [QB], lambda e: e.tensor_scalar(
            out=mu, in0=pb[QB][0:1, 0:512], scalar1=1.0 / D, scalar2=None, op0=ALU.mult))
        bk.read("vector", "EV", [SB], lambda e: e.tensor_scalar(
            out=var, in0=pb[SB][0:1, 0:512], scalar1=1.0 / D, scalar2=None, op0=ALU.mult))
        p.emit("vector", lambda e: e.scalar_tensor_tensor(
            out=nm, in0=mu, scalar=-1.0, in1=mu, op0=ALU.mult, op1=ALU.mult), inc="EV")
        vv = p.emit("vector", lambda e: e.tensor_add(out=var, in0=var, in1=nm), inc="EV")
        p.waits("scalar", [("EV", vv)])
        p.emit("scalar", lambda e: e.activation(out=nm, in_=var, func=AF.Ln,
                                                bias=b["small"][0:1, 255:256]), inc="ES")
        sv = p.emit("scalar", lambda e: e.activation(out=rstd, in_=nm, func=AF.Exp, scale=-0.5), inc="ES")
        p.waits("vector", [("ES", sv)])
        nv = p.emit("vector", lambda e: e.scalar_tensor_tensor(
            out=nm, in0=mu, scalar=-1.0, in1=rstd, op0=ALU.mult, op1=ALU.mult), inc="EV")
        if stats_only is True:
            return
        p.waits("tensor", [("EV", nv)])
        bk.begin_write(B1, B0)
        b1mm = p.emit("tensor", lambda e: e.matmul(
            pb[B1][:, :], b["ones_row_f"][0:1, 0:128], rstd, start=True, stop=True),
            inc="PE")
        b0mm = p.emit("tensor", lambda e: e.matmul(
            pb[B0][:, :], b["ones_row_f"][0:1, 0:128], nm, start=True, stop=True),
            inc="PE")
        bk.wrote_banks(b1mm, B1)
        bk.wrote_banks(b0mm, B0)
        if stats_only == "bcast":
            bk.read("scalar", "ES", [B1], lambda e: e.copy(
                out=fa[:, 4096:4608], in_=pb[B1][:, :]))
            bk.read("scalar", "ES", [B0], lambda e: e.copy(
                out=fa[:, 4608:5120], in_=pb[B0][:, :]))
            return
        for oc in range(8):
            tslot = 4096 + 512 * (oc % 2)
            t1v = bk.read("vector", "EV", [B1], lambda e, oc=oc, tslot=tslot: e.tensor_mul(
                out=fa[:, tslot:tslot + 512], in0=fa[:, 512 * oc:512 * oc + 512],
                in1=pb[B1][:, :]))
            if not final:
                wv = bk.read("vector", "EV", [B0], lambda e, oc=oc, tslot=tslot, t0=t0: e.tensor_add(
                    out=self.xsl(oc, t0, 512), in0=fa[:, tslot:tslot + 512],
                    in1=pb[B0][:, :]))
            else:
                oslot = out_base + 2048 * (oc % 2)
                wv = bk.read("vector", "EV", [B0], lambda e, oc=oc, tslot=tslot, oslot=oslot: e.tensor_add(
                    out=aa[:, oslot:oslot + 2048].bitcast(F32)[:, 0:512],
                    in0=fa[:, tslot:tslot + 512], in1=pb[B0][:, :]))
                p.waits("sync", [("EV", wv)])
                lane = "SO0" if oc % 2 == 0 else "SO1"
                sval = p.emit("sync", lambda e, oc=oc, t0=t0, oslot=oslot, bi=bi: e.dma_start(
                    out=self.dram["out_t"][bi, oc * 128:(oc + 1) * 128, t0:t0 + 512],
                    in_=aa[:, oslot:oslot + 2048].bitcast(F32)[:, 0:512]),
                    inc=lane, n=16)
                st_vals.append((lane, sval))
                p.waits("vector", [(lane, sval)])
        return p.cnt["EV"]

    def stage_a(self, bi, final=False, upto=None):
        p, b, dr, bk = self.prog, self.buf, self.dram, self.banks
        aa, fa, wa, sm, pb = b["aa"], b["fa"], b["wa"], b["small"], self.psum
        KK, VV = 0, 16384
        KVS, KSOF = 33024, 34048
        QK, YF, REC = 0, 4096, 8192

        def wload(name, off, lane):
            return p.emit("gpsimd", lambda e, n=name, o=off: e.dma_start(
                out=wa[:, o:o + 8192].rearrange("p (c m) -> p c m", c=8),
                in_=dr[n].rearrange("(c p) m -> p c m", p=128)), inc=lane, n=16)
        p.waits("gpsimd", [("PE", self.wa_pe)])
        wk_v = wload("a_wk", 0, "W0")
        wv_v = wload("a_wv", 8192, "W1")

        if True:
            p.emit("vector", lambda e: e.memset(
                aa[:, VV:VV + 16640].rearrange("p (t h s) -> p t h s", t=16, h=16)[:, :, :, 64:65],
                1.0), inc="EV")
            p.emit("vector", lambda e: e.memset(aa[:, KSOF:KSOF + 16], 0.0), inc="EV")
            p.emit("vector", lambda e: e.memset(aa[:, KVS:KVS + 1024], 0.0), inc="EV")
        vx_ready = p.cnt["EV"]

        p.waits("tensor", [("W0", wk_v), ("W1", wv_v), ("XL", self.x_ready),
                           ("EV", vx_ready)])
        exp_done = 0
        vcp_done = 0
        red_prev = 0
        for t in range(16):
            kb = (0, 1) if t % 2 == 0 else (2, 3)
            vb = (4, 5)
            bk.begin_write(*vb)
            for h2 in range(2):
                for c in range(8):
                    p.emit("tensor", lambda e, t=t, c=c, h2=h2, vb=vb: e.matmul(
                        pb[vb[h2]][:, :], self.xsl(c, 128 * t, 128),
                        wa[:, 8192 + c * 1024 + h2 * 512: 8192 + c * 1024 + (h2 + 1) * 512],
                        start=(c == 0), stop=False))
                vmm = p.emit("tensor", lambda e, h2=h2, vb=vb: e.matmul(
                    pb[vb[h2]][:, :], b["ones_row"][0:1, 0:128],
                    b["rows"][0:1, 1024 + h2 * 512: 1024 + (h2 + 1) * 512],
                    start=False, stop=True), inc="PE")
            bk.wrote_banks(vmm, *vb)
            bk.begin_write(*kb)
            for h2 in range(2):
                for c in range(8):
                    p.emit("tensor", lambda e, t=t, c=c, h2=h2, kb=kb: e.matmul(
                        pb[kb[h2]][:, :], self.xsl(c, 128 * t, 128),
                        wa[:, c * 1024 + h2 * 512: c * 1024 + (h2 + 1) * 512],
                        start=(c == 0), stop=False))
                kmm = p.emit("tensor", lambda e, h2=h2, kb=kb: e.matmul(
                    pb[kb[h2]][:, :], b["ones_row"][0:1, 0:128],
                    b["rows"][0:1, h2 * 512:(h2 + 1) * 512],
                    start=False, stop=True), inc="PE")
            bk.wrote_banks(kmm, *kb)

            for i, bvv in enumerate(vb):
                vcp_done = bk.read("vector", "EV", [bvv], lambda e, t=t, i=i, bvv=bvv: e.tensor_copy(
                    out=aa[:, VV + 1040 * t: VV + 1040 * t + 1040].rearrange(
                        "p (h s) -> p h s", h=16)[:, 8 * i:8 * i + 8, 0:64],
                    in_=pb[bvv][:, :].rearrange("p (h s) -> p h s", h=8)))
            p.waits("vector", [("ES", exp_done)])
            red = bk.read("vector", "EV", list(kb), lambda e, kb=kb: e.reduce_max(
                out=sm[:, 0:16],
                in_=self.pfull[:, 512 * kb[0]:512 * kb[0] + 1024].rearrange(
                    "p (h d) -> p h d", h=16),
                axis=AX.X, negate=True))
            sub = bk.read("vector", "EV", list(kb), lambda e, kb=kb: e.scalar_tensor_tensor(
                out=fa[:, 4096:5120].rearrange("p (h d) -> p h d", h=16),
                in0=self.pfull[:, 512 * kb[0]:512 * kb[0] + 1024].rearrange(
                    "p (h d) -> p h d", h=16),
                scalar=1.0,
                in1=sm[:, 0:16].unsqueeze(2).broadcast_to([128, 16, 64]),
                op0=ALU.mult, op1=ALU.add))
            p.waits("scalar", [("EV", sub)])
            exp_done = p.emit("scalar", lambda e, t=t: e.activation(
                out=aa[:, KK + 1024 * t: KK + 1024 * t + 1024],
                in_=fa[:, 4096:5120], func=AF.Exp))

        if upto == "p1":
            return []
        p.waits("tensor", [("ES", exp_done), ("EV", vcp_done)])
        kv_es, kv_ev = 0, 0
        for H in range(16):
            bno = H % 7
            bk.begin_write(bno)
            for t in range(16):
                kvm = p.emit("tensor", lambda e, t=t, H=H, bno=bno: e.matmul(
                    pb[bno][0:64, 0:65],
                    aa[:, KK + 1024 * t + 64 * H: KK + 1024 * t + 64 * H + 64],
                    aa[:, VV + 1040 * t + 65 * H: VV + 1040 * t + 65 * H + 65],
                    start=(t == 0), stop=(t == 15)),
                    inc="PE" if t == 15 else None)
            bk.wrote_banks(kvm, bno)
            pr, ro = H // 2, 64 * (H % 2)
            kv_es = bk.read("scalar", "ES", [bno], lambda e, pr=pr, ro=ro, bno=bno: e.copy(
                out=aa[ro:ro + 64, KVS + 128 * pr + ro: KVS + 128 * pr + ro + 64],
                in_=pb[bno][0:64, 0:64]))
            kv_ev = bk.read("vector", "EV", [bno], lambda e, H=H, ro=ro, bno=bno: e.tensor_copy(
                out=aa[ro:ro + 64, KSOF + H: KSOF + H + 1],
                in_=pb[bno][0:64, 64:65]))

        p.waits("gpsimd", [("PE", p.cnt["PE"])])
        wq_v = wload("a_wq", 0, "W0")
        wo_v = wload("a_wo", 8192, "W1")

        if upto == "p15":
            return []
        p.waits("tensor", [("ES", kv_es), ("EV", kv_ev), ("W0", wq_v), ("W1", wo_v)])
        st_vals = []
        for tc in range(4):
            t0 = 512 * tc
            for c in range(8):
                qb = c % 2
                bk.begin_write(qb)
                for kc in range(8):
                    qmm = p.emit("tensor", lambda e, c=c, kc=kc, t0=t0, qb=qb: e.matmul(
                        pb[qb][:, :], wa[:, kc * 1024 + c * 128: kc * 1024 + c * 128 + 128],
                        self.xsl(kc, t0, 512), start=(kc == 0), stop=(kc == 7)),
                        inc="PE" if kc == 7 else None)
                bk.wrote_banks(qmm, qb)
                bk.read("scalar", "ES", [qb], lambda e, c=c, qb=qb: e.activation(
                    out=aa[:, QK + 512 * c: QK + 512 * c + 512], in_=pb[qb][:, :],
                    func=AF.Exp, bias=b["pp"][:, c:c + 1], scale=1.0))
            qk_done = p.cnt["ES"]
            p.waits("tensor", [("ES", qk_done)])
            for c in range(8):
                db = 2 + (c % 2)
                bk.begin_write(db)
                dmm = p.emit("tensor", lambda e, c=c, db=db: e.matmul(
                    pb[db][0:2, 0:512], aa[:, KSOF + 2 * c: KSOF + 2 * c + 2],
                    aa[:, QK + 512 * c: QK + 512 * c + 512], start=True, stop=True),
                    inc="PE")
                bk.wrote_banks(dmm, db)
                bk.read("vector", "EV", [db], lambda e, c=c, db=db: e.tensor_scalar(
                    out=fa[0:2, 512 * (c % 2): 512 * (c % 2) + 512], in0=pb[db][0:2, 0:512],
                    scalar1=1e-6, scalar2=None, op0=ALU.add))
                p.emit("vector", lambda e, c=c: e.reciprocal(
                    out=fa[0:2, 1024 + 512 * (c % 2): 1024 + 512 * (c % 2) + 512],
                    in_=fa[0:2, 512 * (c % 2): 512 * (c % 2) + 512]))
                p.emit("vector", lambda e, c=c: e.tensor_copy(
                    out=aa[0:2, REC + 512 * c: REC + 512 * c + 512],
                    in_=fa[0:2, 1024 + 512 * (c % 2): 1024 + 512 * (c % 2) + 512]), inc="EV")
            rec_done = p.cnt["EV"]
            p.waits("tensor", [("EV", rec_done)])
            for c in range(8):
                yb = 4 + (c % 2)
                bb6 = 6
                bk.begin_write(yb)
                ymm = p.emit("tensor", lambda e, c=c, yb=yb: e.matmul(
                    pb[yb][:, :], aa[:, KVS + 128 * c: KVS + 128 * c + 128],
                    aa[:, QK + 512 * c: QK + 512 * c + 512], start=True, stop=True),
                    inc="PE")
                bk.wrote_banks(ymm, yb)
                bk.begin_write(bb6)
                bmm = p.emit("tensor", lambda e, c=c, bb6=bb6: e.matmul(
                    pb[bb6][:, :], b["sel2"][0:2, :],
                    aa[0:2, REC + 512 * c: REC + 512 * c + 512],
                    start=True, stop=True), inc="PE")
                bk.wrote_banks(bmm, bb6)
                bc_ev = bk.read("scalar", "ES", [bb6], lambda e, c=c, bb6=bb6: e.copy(
                    out=fa[:, 4096 + 512 * (c % 2): 4096 + 512 * (c % 2) + 512],
                    in_=pb[bb6][:, :]))
                p.waits("vector", [("ES", bc_ev)])
                bk.read("vector", "EV", [yb], lambda e, c=c, yb=yb: e.tensor_mul(
                    out=aa[:, YF + 512 * c: YF + 512 * c + 512],
                    in0=pb[yb][:, :],
                    in1=fa[:, 4096 + 512 * (c % 2): 4096 + 512 * (c % 2) + 512]))
            yf_done = p.cnt["EV"]
            p.waits("tensor", [("EV", yf_done)])
            for oc in range(8):
                ob = (0, 1)[oc % 2]
                bk.begin_write(ob)
                for kc in range(8):
                    omm = p.emit("tensor", lambda e, oc=oc, kc=kc, ob=ob: e.matmul(
                        pb[ob][:, :], wa[:, 8192 + kc * 1024 + oc * 128: 8192 + kc * 1024 + oc * 128 + 128],
                        aa[:, YF + 512 * kc: YF + 512 * kc + 512],
                        start=(kc == 0), stop=(kc == 7)),
                        inc="PE" if kc == 7 else None)
                bk.wrote_banks(omm, ob)
                bk.read("vector", "EV", [ob], lambda e, oc=oc, ob=ob, t0=t0: e.scalar_tensor_tensor(
                    out=fa[:, 512 * oc:512 * oc + 512], in0=pb[ob][:, :],
                    scalar=b["pp"][:, 8 + oc:9 + oc], in1=self.xsl(oc, t0, 512),
                    op0=ALU.add, op1=ALU.add))
            h_done = p.cnt["EV"]
            if upto == "p2h":
                return []
            p.waits("scalar", [("EV", h_done)])
            p.waits("tensor", [("EV", h_done)])
            so = {"p2s": True, "p2b": "bcast"}.get(upto, False)
            self._module_ln(tc, bi, final, st_vals, stats_only=so)
            if upto in ("p2s", "p2b"):
                return []
        self.wa_pe = p.cnt["PE"]
        return st_vals

    def stage_l1a(self, bi, final=False, upto=None):
        p, b, dr, bk = self.prog, self.buf, self.dram, self.banks
        aa, fa, wa, sm, pb = b["aa"], b["fa"], b["wa"], b["small"], self.psum
        QT, KT, VS = 0, 4096, 8192
        AT, ATT, ATG = 12288, 14592, 15616
        KG, VG, YT = 16640, 16768, 17792

        def wload(name, off, lane):
            return p.emit("gpsimd", lambda e, n=name, o=off: e.dma_start(
                out=wa[:, o:o + 8192].rearrange("p (c m) -> p c m", c=8),
                in_=dr[n].rearrange("(c p) m -> p c m", p=128)), inc=lane, n=16)
        p.waits("gpsimd", [("PE", self.wa_pe)])
        wq_v = wload("l1_wq", 0, "W0")
        wk_v = wload("l1_wk", 8192, "W1")
        wv_v = wload("l1_wv", 16384, "W2")
        p.waits("tensor", [("W0", wq_v), ("W1", wk_v), ("W2", wv_v),
                           ("XL", self.x_ready)])

        atl_pe = [0, 0]
        self._l1a_att2_pe = 0
        pend2 = None

        def l1a_tail(st):
            g_, tp_, t0_, at0_, par_, sc_ev_ = st
            p.waits("tensor", [("EV", sc_ev_)])
            bk.begin_write(7)
            for h8 in range(8):
                mmv = p.emit("tensor", lambda e, h8=h8, at0_=at0_: e.transpose(
                    out=pb[7][:, 128 * h8:128 * h8 + 128],
                    in_=aa[:, at0_ + 144 * h8: at0_ + 144 * h8 + 128],
                    identity=b["ident"][:, :]), inc="PE" if h8 == 7 else None)
            bk.wrote_banks(mmv, 7)
            p.waits("vector", [("PE", self._l1a_att2_pe)])
            bk.read("vector", "EV", [7], lambda e: e.tensor_copy(
                out=aa[:, ATT:ATT + 1024], in_=pb[7][:, :]))
            bk.begin_write(7)
            for h8 in range(8):
                mmv = p.emit("tensor", lambda e, h8=h8, at0_=at0_: e.transpose(
                    out=pb[7][0:16, 128 * h8:128 * h8 + 128],
                    in_=aa[:, at0_ + 144 * h8 + 128: at0_ + 144 * h8 + 144],
                    identity=b["ident"][:, :]), inc="PE" if h8 == 7 else None)
            bk.wrote_banks(mmv, 7)
            atl_pe[par_] = mmv
            tg_ev = bk.read("vector", "EV", [7], lambda e: e.tensor_copy(
                out=aa[0:16, ATG:ATG + 1024], in_=pb[7][0:16, :]))
            p.waits("tensor", [("EV", tg_ev)])
            for h8 in range(8):
                h = 8 * g_ + h8
                yb = 4 + h8 % 2
                bk.begin_write(yb)
                p.emit("tensor", lambda e, h=h, h8=h8, tp_=tp_, yb=yb: e.matmul(
                    pb[yb][0:64, 0:128], aa[:, VS + 1024 * tp_ + 64 * h: VS + 1024 * tp_ + 64 * h + 64],
                    aa[:, ATT + 128 * h8: ATT + 128 * h8 + 128], start=True, stop=False))
                mmv = p.emit("tensor", lambda e, h=h, h8=h8, yb=yb: e.matmul(
                    pb[yb][0:64, 0:128], aa[0:16, VG + 64 * h: VG + 64 * h + 64],
                    aa[0:16, ATG + 128 * h8: ATG + 128 * h8 + 128],
                    start=False, stop=True), inc="PE")
                bk.wrote_banks(mmv, yb)
                self._l1a_att2_pe = mmv
                co, ro = h // 2, 64 * (h % 2)
                bk.read("scalar", "ES", [yb], lambda e, co=co, ro=ro, t0_=t0_, tp_=tp_, yb=yb: e.copy(
                    out=aa[ro:ro + 64, YT + 2048 * co + t0_ + 128 * tp_: YT + 2048 * co + t0_ + 128 * tp_ + 128],
                    in_=pb[yb][0:64, 0:128]))

        last_qkv_mm = 0
        for tc in range(4):
            t0 = 512 * tc
            for which, base, wofs in (("q", QT, 0), ("k", KT, 8192)):
                for c in range(8):
                    qb = c % 2
                    bk.begin_write(qb)
                    for kc in range(8):
                        mmv = p.emit("tensor", lambda e, c=c, kc=kc, t0=t0, qb=qb, wofs=wofs: e.matmul(
                            pb[qb][:, :], wa[:, wofs + kc * 1024 + c * 128: wofs + kc * 1024 + c * 128 + 128],
                            self.xsl(kc, t0, 512), start=(kc == 0), stop=(kc == 7)),
                            inc="PE" if kc == 7 else None)
                    bk.wrote_banks(mmv, qb)
                    last_qkv_mm = mmv
                    if which == "q":
                        bk.read("vector", "EV", [qb], lambda e, c=c, qb=qb, base=base: e.tensor_scalar(
                            out=aa[:, base + 512 * c: base + 512 * c + 512],
                            in0=pb[qb][:, :], scalar1=0.125, scalar2=None, op0=ALU.mult))
                    else:
                        bk.read("scalar", "ES", [qb], lambda e, c=c, qb=qb, base=base: e.copy(
                            out=aa[:, base + 512 * c: base + 512 * c + 512],
                            in_=pb[qb][:, :]))
            for tt in range(4):
                for half in range(2):
                    vb2 = 2 + (2 * tt + half) % 2
                    bk.begin_write(vb2)
                    for kc in range(8):
                        mmv = p.emit("tensor", lambda e, tt=tt, half=half, kc=kc, t0=t0, vb2=vb2: e.matmul(
                            pb[vb2][:, :], self.xsl(kc, t0 + 128 * tt, 128),
                            wa[:, 16384 + kc * 1024 + half * 512: 16384 + kc * 1024 + half * 512 + 512],
                            start=(kc == 0), stop=(kc == 7)),
                            inc="PE" if kc == 7 else None)
                    bk.wrote_banks(mmv, vb2)
                    last_qkv_mm = mmv
                    bk.read("vector", "EV", [vb2], lambda e, tt=tt, half=half, vb2=vb2: e.tensor_copy(
                        out=aa[:, VS + 1024 * tt + 512 * half: VS + 1024 * tt + 512 * half + 512],
                        in_=pb[vb2][:, :]))
            qkv_ev, qkv_es = p.cnt["EV"], p.cnt["ES"]
            if tc == 0:
                p.waits("vector", [("ES", qkv_es)])
                p.emit("vector", lambda e: e.tensor_copy(
                    out=aa[:, KG:KG + 128].rearrange("p (c s) -> p c s", c=8),
                    in_=aa[:, KT:KT + 4096].rearrange("p (c s) -> p c s", c=8)[:, :, 0:16]))
                qkv_ev = p.emit("vector", lambda e: e.tensor_copy(
                    out=aa[0:16, VG:VG + 1024], in_=aa[0:16, VS:VS + 1024]), inc="EV")

            for tp in range(4):
                pglob = tc * 4 + tp
                mvar = 144 if pglob == 0 else 0
                for g in range(2):
                    par = (2 * pglob + g) % 2
                    at0 = AT + 1152 * par
                    p.waits("tensor", [("EV", qkv_ev), ("ES", qkv_es)])
                    bk.begin_write(0, 1, 2, 3)
                    for h8 in (0, 2, 4, 6, 1, 3, 5, 7):
                        h = 8 * g + h8
                        c, ro = h // 2, 64 * (h % 2)
                        bankno, so = h8 // 2, 256 * (h8 % 2)
                        p.emit("tensor", lambda e, c=c, ro=ro, bankno=bankno, so=so, t0=t0, tp=tp: e.matmul(
                            pb[bankno][:, so:so + 128],
                            aa[ro:ro + 64, QT + 512 * c + 128 * tp: QT + 512 * c + 128 * tp + 128],
                            aa[ro:ro + 64, KT + 512 * c + 128 * tp: KT + 512 * c + 128 * tp + 128],
                            start=True, stop=True))
                        mmv = p.emit("tensor", lambda e, c=c, ro=ro, bankno=bankno, so=so, tp=tp: e.matmul(
                            pb[bankno][:, so + 128:so + 144],
                            aa[ro:ro + 64, QT + 512 * c + 128 * tp: QT + 512 * c + 128 * tp + 128],
                            aa[ro:ro + 64, KG + 16 * c: KG + 16 * c + 16],
                            start=True, stop=True), inc="PE" if h8 == 7 else None)
                    bk.wrote_banks(mmv, 0, 1, 2, 3)
                    if pend2 is not None:
                        l1a_tail(pend2)
                        pend2 = None
                    mk_ev = bk.read("vector", "EV", [0, 1, 2, 3], lambda e, mvar=mvar: e.tensor_tensor(
                        out=fa[:, 0:1152].rearrange("p (h s) -> p h s", h=8),
                        in0=self.pfull[:, 0:2048].rearrange("p (h s) -> p h s", h=8)[:, :, 0:144],
                        in1=b["mask_l1"][:, mvar:mvar + 144].unsqueeze(1).broadcast_to([128, 8, 144]),
                        op=ALU.add))
                    rx = p.emit("vector", lambda e: e.reduce_max(
                        out=sm[:, 0:8], in_=fa[:, 0:1152].rearrange("p (h s) -> p h s", h=8),
                        axis=AX.X, negate=True), inc="EV")
                    p.waits("scalar", [("EV", rx), ("PE", atl_pe[par])])
                    exp_last = 0
                    for h8 in range(8):
                        exp_last = p.emit("scalar", lambda e, h8=h8, at0=at0: e.activation(
                            out=aa[:, at0 + 144 * h8: at0 + 144 * h8 + 144],
                            in_=fa[:, 144 * h8: 144 * h8 + 144], func=AF.Exp,
                            bias=sm[:, h8:h8 + 1], scale=1.0,
                            accum_out=sm[:, 8 + h8:9 + h8]), inc="ES")
                    p.waits("vector", [("ES", exp_last)])
                    p.emit("vector", lambda e: e.reciprocal(out=sm[:, 16:24], in_=sm[:, 8:16]))
                    sc_ev = p.emit("vector", lambda e, at0=at0: e.tensor_mul(
                        out=aa[:, at0:at0 + 1152].rearrange("p (h s) -> p h s", h=8),
                        in0=aa[:, at0:at0 + 1152].rearrange("p (h s) -> p h s", h=8),
                        in1=sm[:, 16:24].unsqueeze(2).broadcast_to([128, 8, 144])), inc="EV")
                    pend2 = (g, tp, t0, at0, par, sc_ev)
            l1a_tail(pend2)
            pend2 = None

        if upto is not None:
            return []
        p.waits("gpsimd", [("PE", p.cnt["PE"])])
        wo_v = wload("l1_wo", 0, "W0")
        p.waits("tensor", [("W0", wo_v), ("ES", p.cnt["ES"])])
        st_vals = []
        for tc in range(4):
            t0 = 512 * tc
            for oc in range(8):
                ob = oc % 2
                bk.begin_write(ob)
                for kc in range(8):
                    omm = p.emit("tensor", lambda e, oc=oc, kc=kc, t0=t0, ob=ob: e.matmul(
                        pb[ob][:, :], wa[:, kc * 1024 + oc * 128: kc * 1024 + oc * 128 + 128],
                        aa[:, YT + 2048 * kc + t0: YT + 2048 * kc + t0 + 512],
                        start=(kc == 0), stop=(kc == 7)),
                        inc="PE" if kc == 7 else None)
                bk.wrote_banks(omm, ob)
                bk.read("vector", "EV", [ob], lambda e, oc=oc, ob=ob, t0=t0: e.tensor_add(
                    out=fa[:, 512 * oc:512 * oc + 512], in0=pb[ob][:, :],
                    in1=self.xsl(oc, t0, 512)))
            h_done = p.cnt["EV"]
            p.waits("scalar", [("EV", h_done)])
            p.waits("tensor", [("EV", h_done)])
            self._module_ln(tc, bi, final, st_vals)
        self.wa_pe = p.cnt["PE"]
        return st_vals


    def stage_moe(self, bi, final=False):
        p, b, dr, bk = self.prog, self.buf, self.dram, self.banks
        aa, fa, wa, sm, pb = b["aa"], b["fa"], b["wa"], b["small"], self.psum
        ACC = 0
        XTM, XG, HP = 8192, 12288, 14336
        OG, OGT, PU, PG, PT = 18432, 20480, 22528, 23552, 24576
        B1C, GT = 25600, 26624
        GROWG, GTMF, POSM, MTM = 28672, 28928, 29184, 29248
        OUTB, STATS = 8192, 12288
        accf = aa[:, 0:8192].bitcast(F32)
        C = 256

        p.waits("tensor", [("XL", self.x_ready), ("CG", self.ld_sem_vals[1])])
        p.waits("gpsimd", [("PE", self.wa_pe)])
        gv = p.emit("gpsimd", lambda e: e.dma_start(
            out=wa[:, 0:64].rearrange("p (c m) -> p c m", c=8),
            in_=dr["m_gate"].rearrange("(c p) m -> p c m", p=128)), inc="W3", n=16)
        p.waits("sync", [("PE", p.cnt["PE"]), ("EV", p.cnt["EV"]),
                         ("ES", p.cnt["ES"])])
        b1v = p.emit("sync", lambda e: e.dma_start(
            out=aa[:, B1C:B1C + 512].bitcast(F32)[:, 0:128],
            in_=dr["m_b1_pc2"][:, :]), inc="W5", n=16)
        p.waits("tensor", [("W3", gv)])
        for t in range(16):
            gb2 = t % 2
            bk.begin_write(gb2)
            for kc in range(8):
                mmv = p.emit("tensor", lambda e, t=t, kc=kc, gb2=gb2: e.matmul(
                    pb[gb2][:, 0:8], self.xsl(kc, 128 * t, 128),
                    wa[:, 8 * kc: 8 * kc + 8],
                    start=(kc == 0), stop=(kc == 7)), inc="PE" if kc == 7 else None)
            bk.wrote_banks(mmv, gb2)
            bk.read("vector", "EV", [gb2], lambda e, gb2=gb2: e.tensor_copy(
                out=fa[:, 0:8], in_=pb[gb2][:, 0:8]))
            p.emit("vector", lambda e: e.reduce_max(
                out=sm[:, 40:41], in_=fa[:, 0:8], axis=AX.X, negate=True))
            p.emit("vector", lambda e: e.tensor_scalar(
                out=sm[:, 41:42], in0=sm[:, 40:41], scalar1=-1.0, scalar2=None, op0=ALU.mult))
            p.emit("vector", lambda e: e.tensor_scalar(
                out=fa[:, 8:16], in0=fa[:, 0:8], scalar1=sm[:, 41:42], scalar2=None, op0=ALU.is_equal))
            p.emit("vector", lambda e: e.scalar_tensor_tensor(
                out=fa[:, 8:16], in0=fa[:, 8:16], scalar=-1e30, in1=fa[:, 0:8],
                op0=ALU.mult, op1=ALU.add))
            p.emit("vector", lambda e: e.reduce_max(
                out=sm[:, 42:43], in_=fa[:, 8:16], axis=AX.X))
            msk_v = p.emit("vector", lambda e: e.tensor_scalar(
                out=fa[:, 8:16], in0=fa[:, 0:8], scalar1=sm[:, 42:43], scalar2=None, op0=ALU.is_ge))
            p.waits("scalar", [("EV", msk_v)])
            ex_v = p.emit("scalar", lambda e: e.activation(
                out=fa[:, 16:24], in_=fa[:, 0:8], func=AF.Exp, bias=sm[:, 40:41]))
            p.waits("vector", [("ES", ex_v)])
            p.emit("vector", lambda e: e.tensor_mul(
                out=fa[:, 16:24], in0=fa[:, 16:24], in1=fa[:, 8:16]))
            p.emit("vector", lambda e: e.reduce_sum(
                out=sm[:, 43:44], in_=fa[:, 16:24], axis=AX.X))
            p.emit("vector", lambda e: e.reciprocal(
                out=sm[:, 44:45], in_=sm[:, 43:44]))
            p.emit("vector", lambda e, t=t: e.tensor_scalar(
                out=aa[:, GTMF:GTMF + 512].bitcast(F32)[:, 8 * t:8 * t + 8],
                in0=fa[:, 16:24], scalar1=sm[:, 44:45], scalar2=None, op0=ALU.mult))
            gt_v = p.emit("vector", lambda e, t=t: e.tensor_copy(
                out=fa[:, 24:32].bitcast(BF16)[:, 0:8],
                in_=aa[:, GTMF:GTMF + 512].bitcast(F32)[:, 8 * t:8 * t + 8]))
            p.waits("tensor", [("EV", gt_v)])
            bk.begin_write(7)
            tv = p.emit("tensor", lambda e: e.transpose(
                out=pb[7][0:8, 0:128], in_=fa[:, 24:32].bitcast(BF16)[:, 0:8],
                identity=b["ident"][:, :]), inc="PE")
            bk.wrote_banks(tv, 7)
            bk.read("vector", "EV", [7], lambda e, t=t: e.tensor_copy(
                out=aa[0:8, GT + 128 * t: GT + 128 * t + 128], in_=pb[7][0:8, 0:128]))
        gt_done = p.cnt["EV"]

        pslot_pe = [p.cnt["PE"], 0, 0]
        pi = 0
        st_vals = []
        xtm_pe = 0
        pup_pe = [0, 0]
        ogt_pe = 0
        ptbuf_pe = 0
        for tc in range(4):
            t0 = 512 * tc
            p.waits("tensor", [("EV", gt_done)])
            for tt in range(4):
                bk.begin_write(7)
                for dc in range(8):
                    tv = p.emit("tensor", lambda e, tt=tt, dc=dc, t0=t0: e.transpose(
                        out=pb[7][:, 128 * dc:128 * dc + 128],
                        in_=self.xsl(dc, t0 + 128 * tt, 128),
                        identity=b["ident"][:, :]), inc="PE" if dc == 7 else None)
                bk.wrote_banks(tv, 7)
                p.waits("vector", [("PE", xtm_pe)])
                bk.read("vector", "EV", [7], lambda e, tt=tt: e.tensor_copy(
                    out=aa[:, XTM + 1024 * tt: XTM + 1024 * tt + 1024], in_=pb[7][:, :]))
            mp_ev = p.emit("vector", lambda e, tc=tc: e.tensor_scalar(
                out=fa[0:8, 0:512], in0=aa[0:8, GT + 512 * tc: GT + 512 * tc + 512],
                scalar1=0.0, scalar2=None, op0=ALU.is_gt))
            mp_ev = p.emit("vector", lambda e: e.tensor_tensor_scan(
                out=fa[0:8, 512:1024], data0=fa[0:8, 0:512],
                data1=b["ones_col"][0:8, 0:1].broadcast_to([8, 512]),
                initial=0.0, op0=ALU.add, op1=ALU.mult))
            mp_ev = p.emit("vector", lambda e: e.tensor_sub(
                out=fa[0:8, 512:1024], in0=fa[0:8, 512:1024], in1=fa[0:8, 0:512]))
            mt_ev = p.emit("vector", lambda e, tc=tc: e.tensor_scalar(
                out=aa[:, MTM:MTM + 128].bitcast(F32)[:, 0:32],
                in0=aa[:, GTMF:GTMF + 512].bitcast(F32)[:, 32 * tc:32 * tc + 32],
                scalar1=0.0, scalar2=None, op0=ALU.is_gt))
            p.waits("tensor", [("EV", mp_ev)])
            bk.begin_write(6)
            for tt in range(4):
                tv = p.emit("tensor", lambda e, tt=tt: e.transpose(
                    out=pb[6].bitcast(F32)[0:0 + 128, 8 * tt:8 * tt + 8] if False else pb[6][:, 8 * tt:8 * tt + 8],
                    in_=fa[0:8, 512 + 128 * tt:512 + 128 * tt + 128],
                    identity=b["identf"][0:8, 0:8]), inc="PE" if tt == 3 else None)
            bk.wrote_banks(tv, 6)
            ps_ev = bk.read("vector", "EV", [6], lambda e: e.tensor_copy(
                out=aa[:, POSM:POSM + 64].bitcast(F32)[:, 0:32], in_=pb[6][:, 0:32]))
            for oc in range(8):
                ib = 2 + (oc % 2)
                bk.begin_write(ib)
                mmv = p.emit("tensor", lambda e, oc=oc, t0=t0, ib=ib: e.matmul(
                    pb[ib][:, :], b["mb2"][0:8, 128 * oc:128 * oc + 128],
                    aa[0:8, GT + t0: GT + t0 + 512],
                    start=True, stop=True), inc="PE")
                bk.wrote_banks(mmv, ib)
                bk.read("vector", "EV", [ib], lambda e, oc=oc, ib=ib: e.tensor_copy(
                    out=accf[:, 512 * oc: 512 * oc + 512], in_=pb[ib][:, :]))
            for e_i in range(8):
                pls = []
                for pnl in range(3):
                    slot = pi % 3
                    lane = f"W{slot}"
                    p.waits("gpsimd", [("PE", pslot_pe[slot])])
                    if pnl < 2:
                        v = p.emit("gpsimd", lambda e, e_i=e_i, pnl=pnl, slot=slot: e.dma_start(
                            out=wa[:, 8192 * slot:8192 * slot + 8192].rearrange("p (c m) -> p c m", c=8),
                            in_=dr["m_w1"][e_i].rearrange("(c p) m -> p c m", p=128)[:, :, 1024 * pnl:1024 * pnl + 1024]),
                            inc=lane, n=16)
                    else:
                        v = p.emit("gpsimd", lambda e, e_i=e_i, pnl=pnl, slot=slot: e.dma_start(
                            out=wa[:, 8192 * slot:8192 * slot + 8192].rearrange("p (c m) -> p c m", c=16),
                            in_=dr["m_w2"][e_i].rearrange("(c p) m -> p c m", p=128)[:, :, 0:512]),
                            inc=lane, n=16)
                    pls.append((slot, lane, v))
                    pi += 1
                p.waits("vector", [("EV", ps_ev), ("PE", max(pup_pe))])
                pv = 0
                for tt in range(4):
                    gtt = 4 * tc + tt
                    pv = p.emit("vector", lambda e, tt=tt, e_i=e_i: e.tensor_scalar(
                        out=aa[:, PU + 256 * tt: PU + 256 * tt + 256],
                        in0=b["iota256"][:, :],
                        scalar1=aa[:, POSM:POSM + 64].bitcast(F32)[:, 8 * tt + e_i:8 * tt + e_i + 1],
                        scalar2=aa[:, MTM:MTM + 128].bitcast(F32)[:, 8 * tt + e_i:8 * tt + e_i + 1],
                        op0=ALU.is_equal, op1=ALU.mult))
                    pv = p.emit("vector", lambda e, tt=tt, e_i=e_i, gtt=gtt: e.tensor_scalar(
                        out=aa[:, PG + 256 * tt: PG + 256 * tt + 256],
                        in0=b["iota256"][:, :],
                        scalar1=aa[:, POSM:POSM + 64].bitcast(F32)[:, 8 * tt + e_i:8 * tt + e_i + 1],
                        scalar2=aa[:, GTMF:GTMF + 512].bitcast(F32)[:, 8 * gtt + e_i:8 * gtt + e_i + 1],
                        op0=ALU.is_equal, op1=ALU.mult))
                p.waits("tensor", [("EV", pv)])
                for dc in range(8):
                    gb3 = dc % 2
                    bk.begin_write(gb3)
                    for tt in range(4):
                        mmv = p.emit("tensor", lambda e, dc=dc, tt=tt, gb3=gb3: e.matmul(
                            pb[gb3][:, 0:256],
                            aa[:, XTM + 1024 * tt + 128 * dc: XTM + 1024 * tt + 128 * dc + 128],
                            aa[:, PU + 256 * tt: PU + 256 * tt + 256],
                            start=(tt == 0), stop=(tt == 3)),
                            inc="PE" if tt == 3 else None)
                    bk.wrote_banks(mmv, gb3)
                    if dc >= 6:
                        xtm_pe = mmv
                    pup_pe[0] = mmv
                    eng = ("vector", "EV") if dc % 2 == 0 else ("scalar", "ES")
                    if eng[0] == "vector":
                        bk.read("vector", "EV", [gb3], lambda e, dc=dc, gb3=gb3: e.tensor_copy(
                            out=aa[:, XG + 256 * dc: XG + 256 * dc + 256], in_=pb[gb3][:, 0:256]))
                    else:
                        bk.read("scalar", "ES", [gb3], lambda e, dc=dc, gb3=gb3: e.copy(
                            out=aa[:, XG + 256 * dc: XG + 256 * dc + 256], in_=pb[gb3][:, 0:256]))
                xg_ev, xg_es = p.cnt["EV"], p.cnt["ES"]
                p.waits("tensor", [("EV", pv)])
                p.waits("vector", [("PE", ptbuf_pe)])
                for st2 in range(2):
                    bk.begin_write(7)
                    for tt in range(4):
                        tv = p.emit("tensor", lambda e, st2=st2, tt=tt: e.transpose(
                            out=pb[7][:, 128 * tt:128 * tt + 128],
                            in_=aa[:, PG + 256 * tt + 128 * st2: PG + 256 * tt + 128 * st2 + 128],
                            identity=b["ident"][:, :]), inc="PE" if tt == 3 else None)
                    bk.wrote_banks(tv, 7)
                    pup_pe[1] = tv
                    bk.read("vector", "EV", [7], lambda e, st2=st2: e.tensor_copy(
                        out=aa[:, PT + 512 * st2: PT + 512 * st2 + 512], in_=pb[7][:, 0:512]))
                p.waits("tensor", [(pls[0][1], pls[0][2]), ("EV", xg_ev), ("ES", xg_es),
                                   ("W5", b1v)])
                for hdc in range(16):
                    hb = hdc % 4
                    slot = pls[0][0] if hdc < 8 else pls[1][0]
                    if hdc == 8:
                        p.waits("tensor", [(pls[1][1], pls[1][2])])
                    bk.begin_write(hb)
                    for kc in range(8):
                        mmv = p.emit("tensor", lambda e, hdc=hdc, kc=kc, slot=slot, hb=hb: e.matmul(
                            pb[hb][:, 0:256],
                            wa[:, 8192 * slot + 1024 * kc + 128 * (hdc % 8): 8192 * slot + 1024 * kc + 128 * (hdc % 8) + 128],
                            aa[:, XG + 256 * kc: XG + 256 * kc + 256],
                            start=(kc == 0), stop=(kc == 7)),
                            inc="PE" if kc == 7 else None)
                    bk.wrote_banks(mmv, hb)
                    if hdc == 7:
                        pslot_pe[pls[0][0]] = mmv
                    if hdc == 15:
                        pslot_pe[pls[1][0]] = mmv
                    if hdc % 2 == 0:
                        bk.read("vector", "EV", [hb], lambda e, hdc=hdc, hb=hb, e_i=e_i: e.tensor_scalar(
                            out=aa[:, HP + 256 * hdc: HP + 256 * hdc + 256],
                            in0=pb[hb][:, 0:256],
                            scalar1=aa[:, B1C:B1C + 512].bitcast(F32)[:, 16 * e_i + hdc:16 * e_i + hdc + 1],
                            scalar2=0.0, op0=ALU.add, op1=ALU.max))
                    else:
                        bk.read("scalar", "ES", [hb], lambda e, hdc=hdc, hb=hb, e_i=e_i: e.activation(
                            out=aa[:, HP + 256 * hdc: HP + 256 * hdc + 256],
                            in_=pb[hb][:, 0:256], func=AF.Relu,
                            bias=aa[:, B1C:B1C + 512].bitcast(F32)[:, 16 * e_i + hdc:16 * e_i + hdc + 1]))
                hp_ev, hp_es = p.cnt["EV"], p.cnt["ES"]
                slot = pi % 3
                lane = f"W{slot}"
                p.waits("gpsimd", [("PE", pslot_pe[slot])])
                v = p.emit("gpsimd", lambda e, e_i=e_i, slot=slot: e.dma_start(
                    out=wa[:, 8192 * slot:8192 * slot + 8192].rearrange("p (c m) -> p c m", c=16),
                    in_=dr["m_w2"][e_i].rearrange("(c p) m -> p c m", p=128)[:, :, 512:1024]),
                    inc=lane, n=16)
                pls.append((slot, lane, v))
                pi += 1
                p.waits("tensor", [(pls[2][1], pls[2][2]), ("EV", hp_ev), ("ES", hp_es)])
                for cb in range(4):
                    nh, st2 = cb // 2, cb % 2
                    ob = 2 + cb
                    pl = pls[2] if nh == 0 else pls[3]
                    if cb == 2:
                        p.waits("tensor", [(pls[3][1], pls[3][2])])
                    bk.begin_write(ob)
                    for kc in range(16):
                        mmv = p.emit("tensor", lambda e, kc=kc, st2=st2, slot=pl[0], ob=ob: e.matmul(
                            pb[ob][:, 0:512],
                            aa[:, HP + 256 * kc + 128 * st2: HP + 256 * kc + 128 * st2 + 128],
                            wa[:, 8192 * slot + 512 * kc: 8192 * slot + 512 * kc + 512],
                            start=(kc == 0), stop=(kc == 15)),
                            inc="PE" if kc == 15 else None)
                    bk.wrote_banks(mmv, ob)
                    if cb == 1:
                        pslot_pe[pls[2][0]] = mmv
                    if cb == 3:
                        pslot_pe[pls[3][0]] = mmv
                    if cb == 0:
                        p.waits("vector", [("PE", ogt_pe)])
                    if cb == 1:
                        p.waits("scalar", [("PE", ogt_pe)])
                    if cb % 2 == 0:
                        bk.read("vector", "EV", [ob], lambda e, nh=nh, st2=st2, ob=ob: e.tensor_copy(
                            out=aa[:, OGT + 1024 * st2 + 512 * nh: OGT + 1024 * st2 + 512 * nh + 512],
                            in_=pb[ob][:, 0:512]))
                    else:
                        bk.read("scalar", "ES", [ob], lambda e, nh=nh, st2=st2, ob=ob: e.copy(
                            out=aa[:, OGT + 1024 * st2 + 512 * nh: OGT + 1024 * st2 + 512 * nh + 512],
                            in_=pb[ob][:, 0:512]))
                ogt_ev, ogt_es = p.cnt["EV"], p.cnt["ES"]
                p.waits("tensor", [("EV", ogt_ev), ("ES", ogt_es)])
                for dc in range(8):
                    sb3 = 4 + (dc % 2)
                    bk.begin_write(sb3)
                    for st2 in range(2):
                        mmv = p.emit("tensor", lambda e, dc=dc, st2=st2, sb3=sb3: e.matmul(
                            pb[sb3][:, :],
                            aa[:, OGT + 1024 * st2 + 128 * dc: OGT + 1024 * st2 + 128 * dc + 128],
                            aa[:, PT + 512 * st2: PT + 512 * st2 + 512],
                            start=(st2 == 0), stop=(st2 == 1)),
                            inc="PE" if st2 == 1 else None)
                    bk.wrote_banks(mmv, sb3)
                    ogt_pe = mmv
                    ptbuf_pe = mmv
                    bk.read("vector", "EV", [sb3], lambda e, dc=dc, sb3=sb3: e.tensor_add(
                        out=accf[:, 512 * dc: 512 * dc + 512],
                        in0=pb[sb3][:, :],
                        in1=accf[:, 512 * dc: 512 * dc + 512]))
            acc_done = p.cnt["EV"]
            for oc in range(8):
                hv = p.emit("vector", lambda e, oc=oc, t0=t0: e.tensor_add(
                    out=fa[:, 512 * oc:512 * oc + 512],
                    in0=accf[:, 512 * oc: 512 * oc + 512],
                    in1=self.xsl(oc, t0, 512)))
            p.waits("scalar", [("EV", hv)])
            p.waits("tensor", [("EV", hv)])
            self._module_ln(tc, bi, final, st_vals, sb_base=STATS, out_base=OUTB)
        self.wa_pe = p.cnt["PE"]
        self.moe_b1_pe = p.cnt["PE"]
        self.moe_b1_ev = p.cnt["EV"]
        self.moe_b1_es = p.cnt["ES"]
        return st_vals

    def stage_f(self, bi, final=False, upto=None):
        p, b, dr, bk = self.prog, self.buf, self.dram, self.banks
        aa, fa, wa, sm, pb = b["aa"], b["fa"], b["wa"], b["small"], self.psum
        VS, QT, KT, YW = 0, 8192, 16384, 24576
        ATN, ATT, QR = 26624, 29184, 30464
        DG = 32512
        H2N, GL, H3 = 0, 4096, 8192
        XIN, STATS2, OUTB2 = 20480, 26624, 30720
        pp = b["pp"]

        p.waits("sync", [("XL", self.x_ready), ("ST", getattr(self, "f_xin_st", 0)),
                         ("EV", p.cnt["EV"])])
        for c in range(8):
            xin_st = p.emit("sync", lambda e, c=c: e.dma_start(
                out=dr["xin_d"][c, :, :], in_=self.xsl(c, 0, T)), inc="ST", n=16)
        self.f_xin_st = xin_st

        for l in range(2):
            p.waits("gpsimd", [("PE", max(self.f_wqkv_pe, self.wa_pe))])
            wqkv_v = p.emit("gpsimd", lambda e, l=l: e.dma_start(
                out=wa[:, 0:12288].rearrange("p (c m) -> p c m", c=8),
                in_=dr["f_wqkv"][l].rearrange("(c p) m -> p c m", p=128)), inc="W0", n=16)
            p.waits("gpsimd", [("PE", self.f_wo_pe)])
            wo_v = p.emit("gpsimd", lambda e, l=l: e.dma_start(
                out=wa[:, 12288:16384].rearrange("p (c m) -> p c m", c=4),
                in_=dr["f_wo"][l].rearrange("(c p) m -> p c m", p=128)), inc="W1", n=16)
            p.waits("sync", [("PE", self.f_pp_pe), ("ES", self.f_pp_es),
                             ("EV", self.f_pp_ev)])
            p.emit("sync", lambda e, l=l: e.dma_start(
                out=pp[:, 16:60], in_=dr["f_b1_pc"][l]), inc="W5", n=16)
            p.emit("sync", lambda e, l=l: e.dma_start(
                out=pp[:, 60:68], in_=dr["f_b2_pc"][l]), inc="W5", n=16)
            p.emit("sync", lambda e, l=l: e.dma_start(
                out=pp[:, 68:76], in_=dr["f_lng_pc"][l]), inc="W5", n=16)
            pp_v = p.emit("sync", lambda e, l=l: e.dma_start(
                out=pp[:, 76:84], in_=dr["f_lnb_pc"][l]), inc="W5", n=16)
            qs_v = p.emit("sync", lambda e, l=l: e.dma_start(
                out=b["qsks"][0:1, 0:64], in_=dr["f_qs_row"][l][0:1, :]), inc="W5", n=16)
            ks_v = p.emit("sync", lambda e, l=l: e.dma_start(
                out=b["qsks"][0:1, 64:128], in_=dr["f_ks_row"][l][0:1, :]), inc="W5", n=16)
            p.waits("gpsimd", [("EV", self.f_rtf_ev)])
            for ti, nm in enumerate(("rt_qc", "rt_qs", "rt_kc", "rt_ks")):
                rtl_v = p.emit("gpsimd", lambda e, nm=nm, ti=ti: e.dma_start(
                    out=b["rtf"][:, 1024 * ti:1024 * ti + 1024],
                    in_=dr[nm][:, :]), inc="W4", n=16)
            p.waits("tensor", [("W5", ks_v), ("PE", self.f_qsks_pe)])
            sb6 = 6
            bk.begin_write(sb6)
            scv = p.emit("tensor", lambda e: e.matmul(
                pb[sb6][:, 0:128], b["ones_row_f"][0:1, 0:128], b["qsks"][0:1, 0:128],
                start=True, stop=True), inc="PE")
            bk.wrote_banks(scv, sb6)
            self.f_qsks_pe = scv
            sc_ev = bk.read("vector", "EV", [sb6], lambda e: e.tensor_copy(
                out=fa[:, 5120:5248], in_=pb[sb6][:, 0:128]))
            p.waits("vector", [("W4", rtl_v)])
            for ti in range(4):
                off = 0 if ti < 2 else 64
                p.emit("vector", lambda e, ti=ti, off=off: e.tensor_mul(
                    out=b["rtf"][:, 1024 * ti:1024 * ti + 1024].rearrange("p (a d) -> p a d", a=16),
                    in0=b["rtf"][:, 1024 * ti:1024 * ti + 1024].rearrange("p (a d) -> p a d", a=16),
                    in1=fa[:, 5120 + off:5120 + off + 64].unsqueeze(1).broadcast_to([128, 16, 64])))
            rtf_ev = p.cnt["EV"]

            def rtfs(ti, t):
                return b["rtf"][:, 1024 * ti + 64 * t: 1024 * ti + 64 * t + 64]

            p.waits("tensor", [("W0", wqkv_v), ("XL", self.x_ready), ("EV", rtf_ev),
                               ("W5", ks_v)])
            pend_tr = []
            qr_pe = [0, 0]
            for t in range(16):
                bq3 = (0, 1, 2) if t % 2 == 0 else (3, 4, 5)
                for pi3 in range(3):
                    bno = bq3[pi3]
                    bk.begin_write(bno)
                    for kc in range(8):
                        src = self.xsl(kc, 128 * t, 128) if kc < 4 else self.xsl_shift(kc, 128 * t, 128)
                        mmv = p.emit("tensor", lambda e, kc=kc, pi3=pi3, bno=bno, src=src: e.matmul(
                            pb[bno][:, :], src,
                            wa[:, 1536 * kc + 512 * pi3: 1536 * kc + 512 * pi3 + 512],
                            start=(kc == 0), stop=(kc == 7)),
                            inc="PE" if kc == 7 else None)
                    bk.wrote_banks(mmv, bno)
                for qro, base_, t_, qr_v in pend_tr:
                    p.waits("tensor", [("EV", qr_v)])
                    bk.begin_write(7)
                    for dc in range(4):
                        tv = p.emit("tensor", lambda e, dc=dc, qro=qro: e.transpose(
                            out=pb[7][:, 128 * dc:128 * dc + 128],
                            in_=aa[:, qro + 128 * dc: qro + 128 * dc + 128],
                            identity=b["ident"][:, :]), inc="PE" if dc == 3 else None)
                    bk.wrote_banks(tv, 7)
                    qr_pe[t_ % 2] = tv
                    bk.read("vector", "EV", [7], lambda e, base_=base_, t_=t_: e.tensor_copy(
                        out=aa[:, base_:base_ + 8192].rearrange("p (c n) -> p c n", c=4)[:, :, 128 * t_:128 * t_ + 128],
                        in_=pb[7][:, 0:512].rearrange("p (c n) -> p c n", c=4)))
                pend_tr = []
                vb = bq3[2]
                bk.read("vector", "EV", [vb], lambda e, t=t, vb=vb: e.tensor_copy(
                    out=aa[:, VS + 512 * t: VS + 512 * t + 512], in_=pb[vb][:, :]))
                for pi3, (base, tq, ts_) in enumerate(((QT, 0, 1), (KT, 2, 3))):
                    bno = bq3[pi3]
                    ft = 512 * pi3
                    sq_v = bk.read("scalar", "ES", [bno], lambda e, bno=bno, ft=ft: e.activation(
                        out=fa[:, 1024 + ft:1024 + ft + 512], in_=pb[bno][:, :],
                        func=AF.Square))
                    p.waits("vector", [("ES", sq_v)])
                    p.emit("vector", lambda e, ft=ft: e.reduce_sum(
                        out=sm[:, 48:56], in_=fa[:, 1024 + ft:1024 + ft + 512].rearrange(
                            "p (h d) -> p h d", h=8), axis=AX.X))
                    ssc_v = p.emit("vector", lambda e: e.tensor_scalar(
                        out=sm[:, 48:56], in0=sm[:, 48:56], scalar1=1e-24, scalar2=None,
                        op0=ALU.max))
                    p.waits("scalar", [("EV", ssc_v)])
                    p.emit("scalar", lambda e: e.activation(
                        out=sm[:, 56:64], in_=sm[:, 48:56], func=AF.Ln))
                    rsq_v = p.emit("scalar", lambda e: e.activation(
                        out=sm[:, 56:64], in_=sm[:, 56:64], func=AF.Exp, scale=-0.5))
                    p.waits("vector", [("ES", rsq_v)])
                    bk.read("vector", "EV", [bno], lambda e, bno=bno, ft=ft: e.tensor_mul(
                        out=fa[:, 2048 + ft:2048 + ft + 512].rearrange("p (h d) -> p h d", h=8),
                        in0=pb[bno][:, :].rearrange("p (h d) -> p h d", h=8),
                        in1=sm[:, 56:64].unsqueeze(2).broadcast_to([128, 8, 64])))
                    qn = fa[:, 2048 + ft:2048 + ft + 512]
                    t1 = fa[:, 3072:3584]
                    uu = fa[:, 3584:4096]
                    p.waits("vector", [("PE", self.f_qr_pe)])
                    p.emit("vector", lambda e, qn=qn, t1=t1, tq=tq, t=t: e.tensor_mul(
                        out=t1.rearrange("p (h d) -> p h d", h=8),
                        in0=qn.rearrange("p (h d) -> p h d", h=8),
                        in1=rtfs(tq, t).unsqueeze(1).broadcast_to([128, 8, 64])))
                    p.emit("vector", lambda e, qn=qn, uu=uu, ts_=ts_, t=t: e.tensor_mul(
                        out=uu.rearrange("p (h d) -> p h d", h=8)[:, :, 0:32],
                        in0=qn.rearrange("p (h d) -> p h d", h=8)[:, :, 32:64],
                        in1=rtfs(ts_, t)[:, 0:32].unsqueeze(1).broadcast_to([128, 8, 32])))
                    p.emit("vector", lambda e, qn=qn, uu=uu, ts_=ts_, t=t: e.tensor_mul(
                        out=uu.rearrange("p (h d) -> p h d", h=8)[:, :, 32:64],
                        in0=qn.rearrange("p (h d) -> p h d", h=8)[:, :, 0:32],
                        in1=rtfs(ts_, t)[:, 32:64].unsqueeze(1).broadcast_to([128, 8, 32])))
                    qro = QR + 1024 * (t % 2) + 512 * pi3
                    p.waits("vector", [("PE", qr_pe[t % 2])])
                    qr_v = p.emit("vector", lambda e, t1=t1, uu=uu, qro=qro: e.tensor_add(
                        out=aa[:, qro:qro + 512], in0=t1, in1=uu))
                    pend_tr.append((qro, base, t, qr_v))
            for qro, base_, t_, qr_v in pend_tr:
                p.waits("tensor", [("EV", qr_v)])
                bk.begin_write(7)
                for dc in range(4):
                    tv = p.emit("tensor", lambda e, dc=dc, qro=qro: e.transpose(
                        out=pb[7][:, 128 * dc:128 * dc + 128],
                        in_=aa[:, qro + 128 * dc: qro + 128 * dc + 128],
                        identity=b["ident"][:, :]), inc="PE" if dc == 3 else None)
                bk.wrote_banks(tv, 7)
                qr_pe[t_ % 2] = tv
                bk.read("vector", "EV", [7], lambda e, base_=base_, t_=t_: e.tensor_copy(
                    out=aa[:, base_:base_ + 8192].rearrange("p (c n) -> p c n", c=4)[:, :, 128 * t_:128 * t_ + 128],
                    in_=pb[7][:, 0:512].rearrange("p (c n) -> p c n", c=4)))
            pend_tr = []
            qkv_ev, qkv_es = p.cnt["EV"], p.cnt["ES"]
            rtf_last_ev = qkv_ev
            self.f_rtf_ev = rtf_last_ev
            if upto == "qkv":
                return []

            p.waits("tensor", [("W1", wo_v), ("EV", qkv_ev), ("ES", qkv_es)])
            p.waits("vector", [("ST", self.f_xin_st)])
            atn_pe = [0, 0]
            att_pe = [0, 0]
            yw_pe = 0
            for w in range(4):
                p.waits("scalar", [("PE", yw_pe)])
                pend = None

                def emit_tail(st):
                    hp_, par_, atn0_, W_, nkt_, k_lo_, qt_, exdg = st
                    p.waits("tensor", [("ES", exdg[0]), ("EV", exdg[1])])
                    for j in range(2):
                        hh = 2 * hp_ + j
                        ats = ATT + 640 * j
                        bk.begin_write(4, 5)
                        for k in range(nkt_):
                            tv = p.emit("tensor", lambda e, k=k, j=j, atn0_=atn0_, par_=par_: e.matmul(
                                self.pfull[:, 2048 + 128 * k:2048 + 128 * k + 128],
                                aa[:, atn0_ + 640 * j + 128 * k: atn0_ + 640 * j + 128 * k + 128],
                                aa[:, DG + 256 * par_ + 128 * j: DG + 256 * par_ + 128 * j + 128],
                                start=True, stop=True),
                                inc="PE" if k == nkt_ - 1 else None)
                        bk.wrote_banks(tv, 4, 5)
                        atn_pe[par_] = tv
                        p.waits("scalar", [("PE", att_pe[j])])
                        att_v = bk.read("scalar", "ES", [4, 5], lambda e, nkt_=nkt_, ats=ats: e.copy(
                            out=aa[:, ats:ats + 128 * nkt_],
                            in_=self.pfull[:, 2048:2048 + 128 * nkt_]))
                        p.waits("tensor", [("ES", att_v)])
                        bk.begin_write(6)
                        for k in range(nkt_):
                            ktg = k_lo_ // 128 + k
                            mmv = p.emit("tensor", lambda e, k=k, ktg=ktg, hh=hh, j=j, nkt_=nkt_, ats=ats: e.matmul(
                                pb[6][0:64, 128 * j:128 * j + 128],
                                aa[:, VS + 512 * ktg + 64 * hh: VS + 512 * ktg + 64 * hh + 64],
                                aa[:, ats + 128 * k: ats + 128 * k + 128],
                                start=(k == 0), stop=(k == nkt_ - 1)),
                                inc="PE" if k == nkt_ - 1 else None)
                        bk.wrote_banks(mmv, 6)
                        att_pe[j] = mmv
                        bk.read("scalar", "ES", [6], lambda e, hh=hh, j=j, qt_=qt_: e.copy(
                            out=aa[64 * (hh % 2):64 * (hh % 2) + 64,
                                   YW + 512 * (hh // 2) + 128 * qt_: YW + 512 * (hh // 2) + 128 * qt_ + 128],
                            in_=pb[6][0:64, 128 * j:128 * j + 128]))

                for qt in range(4):
                    qi = 4 * w + qt
                    q0 = 128 * qi
                    k_lo = max(0, q0 - 512)
                    W = q0 + 128 - k_lo
                    nkt = W // 128
                    mof = 640 - W
                    for hp in range(4):
                        par = (qi * 4 + hp) % 2
                        atn0 = ATN + 1280 * par
                        bk.begin_write(0, 1, 2, 3)
                        for j in range(2):
                            hh = 2 * hp + j
                            ro = 64 * j
                            for j0 in range(0, W, 512):
                                nb2 = 2 * j + (1 if j0 else 0)
                                nn = min(512, W - j0)
                                mmv = p.emit("tensor", lambda e, hp=hp, ro=ro, q0=q0, k_lo=k_lo, j0=j0, nn=nn, nb2=nb2: e.matmul(
                                    pb[nb2][:, 0:nn],
                                    aa[ro:ro + 64, QT + 2048 * hp + q0: QT + 2048 * hp + q0 + 128],
                                    aa[ro:ro + 64, KT + 2048 * hp + k_lo + j0: KT + 2048 * hp + k_lo + j0 + nn],
                                    start=True, stop=True), inc="PE" if (j == 1 and j0 + 512 >= W) else None)
                        bk.wrote_banks(mmv, 0, 1, 2, 3)
                        if pend is not None:
                            emit_tail(pend)
                            pend = None
                        mk1 = bk.read("vector", "EV", [0, 1, 2, 3], lambda e, W=W, mof=mof: e.tensor_tensor(
                            out=fa[:, 0:1280].rearrange("p (h s) -> p h s", h=2)[:, :, 0:W],
                            in0=self.pfull[:, 0:2048].rearrange("p (h s) -> p h s", h=2)[:, :, 0:W],
                            in1=b["mask_l0"][:, mof:mof + W].unsqueeze(1).broadcast_to([128, 2, W]),
                            op=ALU.add))
                        rx_v = p.emit("vector", lambda e, W=W: e.reduce_max(
                            out=sm[:, 64:66], in_=fa[:, 0:1280].rearrange("p (h s) -> p h s", h=2)[:, :, 0:W],
                            axis=AX.X, negate=True))
                        p.waits("scalar", [("EV", rx_v), ("PE", atn_pe[par])])
                        ex_v = 0
                        for j in range(2):
                            ex_v = p.emit("scalar", lambda e, j=j, W=W, atn0=atn0: e.activation(
                                out=aa[:, atn0 + 640 * j: atn0 + 640 * j + W],
                                in_=fa[:, 640 * j:640 * j + W], func=AF.Exp,
                                bias=sm[:, 64 + j:65 + j], accum_out=sm[:, 66 + j:67 + j]))
                        p.waits("vector", [("ES", ex_v), ("PE", atn_pe[par])])
                        rc_v = p.emit("vector", lambda e: e.reciprocal(
                            out=sm[:, 68:70], in_=sm[:, 66:68]))
                        dg_v = 0
                        for j in range(2):
                            dg_v = p.emit("vector", lambda e, j=j, par=par: e.tensor_scalar(
                                out=aa[:, DG + 256 * par + 128 * j: DG + 256 * par + 128 * j + 128],
                                in0=b["ident"][:, :], scalar1=sm[:, 68 + j:69 + j],
                                scalar2=None, op0=ALU.mult))
                        pend = (hp, par, atn0, W, nkt, k_lo, qt, (ex_v, dg_v))
                emit_tail(pend)
                ow_es = p.cnt["ES"]
                p.waits("tensor", [("ES", ow_es)])
                for oc in range(8):
                    ob = oc % 2
                    bk.begin_write(ob)
                    for kc in range(4):
                        mmv = p.emit("tensor", lambda e, oc=oc, kc=kc, ob=ob: e.matmul(
                            pb[ob][:, :],
                            wa[:, 12288 + 1024 * kc + 128 * oc: 12288 + 1024 * kc + 128 * oc + 128],
                            aa[:, YW + 512 * kc: YW + 512 * kc + 512],
                            start=(kc == 0), stop=(kc == 3)),
                            inc="PE" if kc == 3 else None)
                    bk.wrote_banks(mmv, ob)
                    yw_pe = mmv
                    bk.read("vector", "EV", [ob], lambda e, oc=oc, w=w, ob=ob: e.tensor_add(
                        out=self.xsl(oc, 512 * w, 512), in0=pb[ob][:, :],
                        in1=self.xsl(oc, 512 * w, 512)))
            self.f_wqkv_pe = p.cnt["PE"]
            self.f_wo_pe = p.cnt["PE"]
            resid_ev = p.cnt["EV"]
            if upto == "attn":
                return []

            SB2, QB2, B1B, B0B = 5, 6, 3, 4
            pn1_pe = [self.f_wqkv_pe, self.f_wqkv_pe]
            pn2_pe = [self.f_wo_pe, self.f_wo_pe]
            h2n_pe = 0
            h3_pe = 0
            w2_ev = resid_ev
            mu2 = aa[0:1, 19456:20480].bitcast(F32)
            var2 = aa[0:1, 20480:21504].bitcast(F32)
            rstd2 = aa[0:1, 21504:22528].bitcast(F32)
            nm2 = aa[0:1, 22528:23552].bitcast(F32)
            for tc in range(4):
                t0 = 512 * tc
                def h2ap(oc, t0=t0):
                    return self.xsl(oc, t0, 512) if oc < 4 else self.xsl_shift(oc, t0, 512)
                w1l_v = [0] * 11
                for pnl in range(2):
                    p.waits("gpsimd", [("PE", pn1_pe[pnl % 2])])
                    w1l_v[pnl] = p.emit("gpsimd", lambda e, l=l, pnl=pnl: e.dma_start(
                        out=wa[:, 4096 * (pnl % 2):4096 * (pnl % 2) + 4096].rearrange("p (c m) -> p c m", c=8),
                        in_=dr["f_w1p"][l].rearrange("(c p) m -> p c m", p=128)[:, :, 512 * pnl:512 * pnl + 512]),
                        inc=f"W{pnl % 2}", n=16)
                p.waits("tensor", [("EV", w2_ev)])
                bk.begin_write(SB2, QB2)
                mmv2 = []
                for oc in range(8):
                    sq = 4096 + 512 * (oc % 2)
                    if oc >= 2:
                        p.waits("scalar", [("PE", mmv2[oc - 2])])
                    sqv = p.emit("scalar", lambda e, oc=oc, sq=sq, h2ap=h2ap: e.activation(
                        out=fa[:, sq:sq + 512], in_=h2ap(oc), func=AF.Square))
                    p.waits("tensor", [("ES", sqv)])
                    p.emit("tensor", lambda e, oc=oc, h2ap=h2ap: e.matmul(
                        pb[QB2][0:1, 0:512], b["ones_col_b"][:, 0:1], h2ap(oc),
                        start=(oc == 0), stop=(oc == 7)))
                    mmv2.append(p.emit("tensor", lambda e, oc=oc, sq=sq: e.matmul(
                        pb[SB2][0:1, 0:512], b["ones_col"][:, 0:1], fa[:, sq:sq + 512],
                        start=(oc == 0), stop=(oc == 7)), inc="PE"))
                smm = p.cnt["PE"]
                bk.wrote_banks(smm, SB2, QB2)
                bk.read("vector", "EV", [QB2], lambda e: e.tensor_scalar(
                    out=mu2, in0=pb[QB2][0:1, 0:512], scalar1=1.0 / D, scalar2=None, op0=ALU.mult))
                bk.read("vector", "EV", [SB2], lambda e: e.tensor_scalar(
                    out=var2, in0=pb[SB2][0:1, 0:512], scalar1=1.0 / D, scalar2=None, op0=ALU.mult))
                p.emit("vector", lambda e: e.scalar_tensor_tensor(
                    out=nm2, in0=mu2, scalar=-1.0, in1=mu2, op0=ALU.mult, op1=ALU.mult))
                vv = p.emit("vector", lambda e: e.tensor_add(out=var2, in0=var2, in1=nm2))
                p.waits("scalar", [("EV", vv)])
                p.emit("scalar", lambda e: e.activation(out=nm2, in_=var2, func=AF.Ln,
                                                        bias=b["small"][0:1, 255:256]))
                sv = p.emit("scalar", lambda e: e.activation(out=rstd2, in_=nm2, func=AF.Exp, scale=-0.5))
                p.waits("vector", [("ES", sv)])
                nv = p.emit("vector", lambda e: e.scalar_tensor_tensor(
                    out=nm2, in0=mu2, scalar=-1.0, in1=rstd2, op0=ALU.mult, op1=ALU.mult))
                p.waits("tensor", [("EV", nv)])
                bk.begin_write(B1B, B0B)
                b1mm = p.emit("tensor", lambda e: e.matmul(
                    pb[B1B][:, :], b["ones_row_f"][0:1, 0:128], rstd2, start=True, stop=True), inc="PE")
                b0mm = p.emit("tensor", lambda e: e.matmul(
                    pb[B0B][:, :], b["ones_row_f"][0:1, 0:128], nm2, start=True, stop=True), inc="PE")
                bk.wrote_banks(b1mm, B1B)
                bk.wrote_banks(b0mm, B0B)
                p.waits("vector", [("PE", max(h2n_pe, b0mm))])
                for oc in range(8):
                    tslot = 4096 + 512 * (oc % 2)
                    bk.read("vector", "EV", [B1B], lambda e, oc=oc, tslot=tslot, h2ap=h2ap: e.tensor_mul(
                        out=fa[:, tslot:tslot + 512], in0=h2ap(oc), in1=pb[B1B][:, :]))
                    bk.read("vector", "EV", [B0B], lambda e, oc=oc, tslot=tslot: e.tensor_add(
                        out=fa[:, tslot:tslot + 512], in0=fa[:, tslot:tslot + 512], in1=pb[B0B][:, :]))
                    p.emit("vector", lambda e, oc=oc, tslot=tslot: e.tensor_scalar(
                        out=aa[:, H2N + 512 * oc: H2N + 512 * oc + 512],
                        in0=fa[:, tslot:tslot + 512], scalar1=pp[:, 68 + oc:69 + oc],
                        scalar2=pp[:, 76 + oc:77 + oc], op0=ALU.mult, op1=ALU.add))
                h2n_ev = p.cnt["EV"]
                p.waits("tensor", [("EV", h2n_ev), ("W0", w1l_v[0]), ("PE", h3_pe)])
                gl_par = 0
                gl_ev = []
                for j in range(44):
                    pnl = j // 4
                    if j % 4 == 0 and 1 <= pnl < 10:
                        slot = (pnl + 1) % 2
                        p.waits("gpsimd", [("PE", pn1_pe[slot])])
                        w1l_v[pnl + 1] = p.emit("gpsimd", lambda e, l=l, pnl=pnl, slot=slot: e.dma_start(
                            out=wa[:, 4096 * slot:4096 * slot + 4096].rearrange("p (c m) -> p c m", c=8),
                            in_=dr["f_w1p"][l].rearrange("(c p) m -> p c m", p=128)[:, :, 512 * (pnl + 1):512 * (pnl + 1) + 512]),
                            inc=f"W{slot}", n=16)
                    if j % 4 == 0 and pnl > 0:
                        p.waits("tensor", [(f"W{pnl % 2}", w1l_v[pnl])])
                    hb = j % 2
                    slot = pnl % 2
                    bk.begin_write(hb)
                    for kc in range(8):
                        mmv = p.emit("tensor", lambda e, j=j, kc=kc, slot=slot, hb=hb: e.matmul(
                            pb[hb][:, :],
                            wa[:, 4096 * slot + 512 * kc + 128 * (j % 4): 4096 * slot + 512 * kc + 128 * (j % 4) + 128],
                            aa[:, H2N + 512 * kc: H2N + 512 * kc + 512],
                            start=(kc == 0), stop=(kc == 7)),
                            inc="PE" if kc == 7 else None)
                    bk.wrote_banks(mmv, hb)
                    if j % 4 == 3 or j == 43:
                        pn1_pe[slot] = mmv
                    glo = GL + 512 * gl_par
                    if j >= 2:
                        p.waits("scalar", [("EV", gl_ev[j - 2])])
                    gv2 = bk.read("scalar", "ES", [hb], lambda e, j=j, hb=hb, glo=glo: e.activation(
                        out=aa[0:64, glo:glo + 512], in_=pb[hb][64:128, :],
                        func=AF.Gelu, bias=pp[64:128, 16 + j:17 + j]))
                    p.waits("vector", [("ES", gv2)])
                    ro2 = 64 * (j % 2)
                    gl_ev.append(bk.read("vector", "EV", [hb], lambda e, j=j, hb=hb, glo=glo, ro2=ro2: e.scalar_tensor_tensor(
                        out=aa[ro2:ro2 + 64, H3 + 512 * (j // 2): H3 + 512 * (j // 2) + 512],
                        in0=pb[hb][0:64, :], scalar=pp[0:64, 16 + j:17 + j],
                        in1=aa[0:64, glo:glo + 512], op0=ALU.add, op1=ALU.mult)))
                    gl_par ^= 1
                h3_ev, h3_es = p.cnt["EV"], p.cnt["ES"]
                for pnl in range(2):
                    p.waits("gpsimd", [("PE", pn2_pe[pnl])])
                    w2l0 = p.emit("gpsimd", lambda e, l=l, pnl=pnl: e.dma_start(
                        out=wa[:, 8192 + 5632 * pnl:8192 + 5632 * pnl + 5632].rearrange("p (c m) -> p c m", c=22),
                        in_=dr["f_w2p"][l].rearrange("(c p) m -> p c m", p=128)[:, :, 256 * pnl:256 * pnl + 256]),
                        inc=f"W{2 + pnl}", n=16)
                    if pnl == 0:
                        w2l_v0 = w2l0
                    else:
                        w2l_v1 = w2l0
                p.waits("tensor", [("EV", h3_ev), ("ES", h3_es), ("W2", w2l_v0)])
                for oc in range(8):
                    pnl = oc // 4
                    pq = oc // 2
                    slot = pq % 2
                    lane = f"W{2 + slot}"
                    if oc == 2:
                        pass
                    ob = 2 + (oc % 2)
                    if pq >= 2 and oc % 2 == 0:
                        p.waits("gpsimd", [("PE", pn2_pe[slot])])
                        wv2 = p.emit("gpsimd", lambda e, l=l, pq=pq, slot=slot: e.dma_start(
                            out=wa[:, 8192 + 5632 * slot:8192 + 5632 * slot + 5632].rearrange("p (c m) -> p c m", c=22),
                            in_=dr["f_w2p"][l].rearrange("(c p) m -> p c m", p=128)[:, :, 256 * pq:256 * pq + 256]),
                            inc=lane, n=16)
                        p.waits("tensor", [(lane, wv2)])
                    elif oc == 2:
                        p.waits("tensor", [("W3", w2l_v1)])
                    bk.begin_write(ob)
                    for kc in range(22):
                        mmv = p.emit("tensor", lambda e, oc=oc, kc=kc, slot=slot, ob=ob: e.matmul(
                            pb[ob][:, :],
                            wa[:, 8192 + 5632 * slot + 256 * kc + 128 * (oc % 2): 8192 + 5632 * slot + 256 * kc + 128 * (oc % 2) + 128],
                            aa[:, H3 + 512 * kc: H3 + 512 * kc + 512],
                            start=(kc == 0), stop=(kc == 21)),
                            inc="PE" if kc == 21 else None)
                    bk.wrote_banks(mmv, ob)
                    if oc % 2 == 1:
                        pn2_pe[slot] = mmv
                    bk.read("vector", "EV", [ob], lambda e, oc=oc, t0=t0, ob=ob: e.scalar_tensor_tensor(
                        out=self.xsl(oc, t0, 512), in0=pb[ob][:, :],
                        scalar=pp[:, 60 + oc:61 + oc], in1=self.xsl(oc, t0, 512),
                        op0=ALU.add, op1=ALU.add))
                w2_ev = p.cnt["EV"]
                h2n_pe = p.cnt["PE"]
                h3_pe = h2n_pe
            self.f_pp_pe, self.f_pp_es, self.f_pp_ev = p.cnt["PE"], p.cnt["ES"], p.cnt["EV"]
            self.f_wqkv_pe = p.cnt["PE"]
            self.f_wo_pe = p.cnt["PE"]
            if upto == "ffn" + str(l):
                return []

        st_vals = []
        xin_pe = p.cnt["PE"]
        for tc in range(4):
            t0 = 512 * tc
            p.waits("sync", [("ST", self.f_xin_st), ("PE", xin_pe),
                             ("EV", w2_ev), ("ES", p.cnt["ES"] if tc == 0 else 0)])
            xi_v = p.emit("sync", lambda e, t0=t0: e.dma_start(
                out=aa[:, XIN:XIN + 4096].rearrange("p (c n) -> p c n", c=8),
                in_=dr["xin_d"].rearrange("c p n -> p c n")[:, :, t0:t0 + 512]), inc="GR", n=16)
            p.waits("vector", [("GR", xi_v), ("EV", w2_ev)])
            for oc in range(8):
                hv = p.emit("vector", lambda e, oc=oc, t0=t0: e.tensor_add(
                    out=fa[:, 512 * oc:512 * oc + 512],
                    in0=aa[:, XIN + 512 * oc: XIN + 512 * oc + 512],
                    in1=self.xsl(oc, t0, 512)))
            p.waits("scalar", [("EV", hv)])
            p.waits("tensor", [("EV", hv)])
            self._module_ln(tc, bi, final, st_vals, sb_base=STATS2, out_base=OUTB2)
            xin_pe = self.prog.cnt["PE"]
        self.wa_pe = p.cnt["PE"]
        return st_vals



def build_nc(stage_list, nb=NB, dump=False, upto=None):
    nc = bass.Bass("TRN2", target_bir_lowering=False, debug=False)
    mk = MK(nc, stage_list)
    mk.declare()
    if dump:
        mk.dram["dbg_aa"] = nc.dram_tensor("dbg_aa", [128, 34816], BF16, kind="ExternalOutput").ap()
        mk.dram["dbg_fa"] = nc.dram_tensor("dbg_fa", [128, 5632], F32, kind="ExternalOutput").ap()
        mk.dram["dbg_wa"] = nc.dram_tensor("dbg_wa", [128, 24576], BF16, kind="ExternalOutput").ap()
    for name in SEMS:
        cm = nc.semaphore(name)
        mk.prog.sems[name] = cm.__enter__()
        mk.ctxs.append(cm)
    mk.alloc()
    mk.ld_const()
    cl0, cg0, ev0 = mk.prog.cnt["CL"], mk.prog.cnt["CG"], mk.prog.cnt["EV"]
    for eng in ("tensor", "vector", "scalar"):
        mk.prog.waits(eng, [("CL", cl0), ("CG", cg0)])
    mk.prog.waits("tensor", [("EV", ev0)])

    for bi in range(nb):
        if bi > 0:
            mk.prog.waits("gpsimd", [("EV", mk.prog.cnt["EV"]),
                                     ("ES", mk.prog.cnt["ES"]),
                                     ("PE", mk.prog.cnt["PE"])])
        mk.x_ready = mk.ld_x(bi)
        st_vals = []
        for si, stg in enumerate(stage_list):
            final = (si == len(stage_list) - 1)
            if stg == "a":
                st_vals = mk.stage_a(bi, final=final, upto=upto)
            elif stg == "f":
                st_vals = mk.stage_f(bi, final=final, upto=upto)
            elif stg == "l1a":
                st_vals = mk.stage_l1a(bi, final=final, upto=upto)
            elif stg == "moe":
                st_vals = mk.stage_moe(bi, final=final)
            else:
                raise ValueError(stg)
            if si != len(stage_list) - 1:
                mk.prog.barrier()
        mk.prog.barrier()
    if dump:
        for nm2, bf2 in (("dbg_aa", "aa"), ("dbg_fa", "fa"), ("dbg_wa", "wa")):
            mk.prog.emit("sync", lambda e, nm2=nm2, bf2=bf2: e.dma_start(
                out=mk.dram[nm2][:, :], in_=mk.buf[bf2][:, :]), inc="ST", n=16)
    mk.prog.waits("sync", [(s, mk.prog.cnt[s]) for s in
                           ("SO0", "SO1", "ST") if mk.prog.cnt[s]])
    mk.prog.check_deadlock()
    with nc.Block() as block:
        mk.prog.replay(block)
    mk.close()
    return nc


def run_stages(inputs, stage_list, nc=None):
    common, shards = host_prepare(inputs)
    if nc is None:
        nc = build_nc(stage_list)
    in_maps = []
    for core in range(8):
        m = dict(common)
        m["x_t"] = shards[core]
        in_maps.append(m)
    res = run_bass_kernel_spmd(nc, in_maps, core_ids=list(range(8)))
    outs = []
    for core in range(8):
        ot = res.results[core]["out_t"]
        outs.append(np.ascontiguousarray(ot.transpose(0, 2, 1)))
    return np.concatenate(outs, axis=0)


def kernel(**inputs):
    return run_stages(inputs, ["a", "f", "l1a", "moe"]).astype(np.float32)



# version 15
# speedup vs baseline: 1.2768x; 1.0298x over previous
import sys
sys.path.insert(0, "/opt/trn_rl_repo")
import numpy as np
import ml_dtypes
import concourse.bass as bass
import concourse.mybir as mybir
from concourse.bass_utils import run_bass_kernel_spmd

F32 = mybir.dt.float32
BF16 = mybir.dt.bfloat16
AF = mybir.ActivationFunctionType
ALU = mybir.AluOpType
AX = mybir.AxisListType

T = 2048
D = 1024
PADT = 2052
NB = 4
NEG = -3.0e38

ENGS = ("tensor", "vector", "scalar", "gpsimd", "sync")


class Prog:

    OWN_SEM = {"tensor": "PE", "vector": "EV", "scalar": "ES"}

    def __init__(self, nc, semnames):
        self.nc = nc
        self.semnames = list(semnames)
        self.sems = {}
        self.cnt = {s: 0 for s in semnames}
        self.q = {e: [] for e in ENGS}
        self.ev = {e: [] for e in ENGS}
        self.self_waited = {e: 0 for e in ENGS}

    def wait(self, eng, sem, val):
        if val <= 0:
            return
        self.q[eng].append(("w", sem, val, None))
        self.ev[eng].append(("w", sem, val))

    def emit(self, eng, fn, inc=None, n=1):
        own = self.OWN_SEM.get(eng)
        if own is not None and self.cnt[own] > self.self_waited[eng]:
            self.self_waited[eng] = self.cnt[own]
            self.wait(eng, own, self.cnt[own])
        if inc is None and eng in ("vector", "scalar"):
            inc = own
        val = None
        if inc is not None:
            self.cnt[inc] += n
            val = self.cnt[inc]
        self.q[eng].append(("i", inc, n, fn))
        self.ev[eng].append(("i", inc, val))
        return val

    def waits(self, eng, pairs):
        best = {}
        for s, v in pairs:
            if v and v > best.get(s, 0):
                best[s] = v
        for s, v in best.items():
            self.wait(eng, s, v)

    def barrier(self, tag="", engines=("tensor", "vector", "scalar")):
        sem_of = {"tensor": "PE", "vector": "EV", "scalar": "ES",
                  "gpsimd": "EG", "sync": "EY"}
        vals = {}
        for e in engines:
            vals[e] = self.emit(e, lambda en: en.nop(), inc=sem_of[e])
        for e in engines:
            for o in engines:
                if o != e:
                    self.wait(e, sem_of[o], vals[o])

    def check_deadlock(self):
        ptr = {e: 0 for e in ENGS}
        cnt = {s: 0 for s in self.semnames}
        progress = True
        while progress:
            progress = False
            for e in ENGS:
                while ptr[e] < len(self.ev[e]):
                    kind, sem, val = self.ev[e][ptr[e]]
                    if kind == "w":
                        if cnt[sem] >= val:
                            ptr[e] += 1
                            progress = True
                        else:
                            break
                    else:
                        if sem is not None:
                            cnt[sem] = val if val > cnt[sem] else cnt[sem]
                        ptr[e] += 1
                        progress = True
        stuck = {e: ptr[e] for e in ENGS if ptr[e] < len(self.ev[e])}
        if stuck:
            info = []
            for e, p in stuck.items():
                kind, sem, val = self.ev[e][p]
                info.append(f"{e}@{p}/{len(self.ev[e])} waiting {sem}>={val} (now {cnt[sem]})")
            raise RuntimeError("DEADLOCK in schedule: " + "; ".join(info))

    def replay(self, block):
        def runner(stream):
            def go(eng):
                for item in stream:
                    kind, a, b, fn = item
                    if kind == "w":
                        eng.wait_ge(self.sems[a], b)
                    else:
                        ins = fn(eng)
                        if a is not None:
                            ins.then_inc(self.sems[a], b)
            return go
        block.tensor(runner(self.q["tensor"]))
        block.vector(runner(self.q["vector"]))
        block.scalar(runner(self.q["scalar"]))
        block.gpsimd(runner(self.q["gpsimd"]))
        block.sync(runner(self.q["sync"]))


class Banks:

    def __init__(self, prog):
        self.p = prog
        self.wrote = [0] * 8
        self.reads = [dict() for _ in range(8)]

    def begin_write(self, *banks):
        pairs = []
        for b in banks:
            pairs += list(self.reads[b].items())
            self.reads[b] = {}
        self.p.waits("tensor", pairs)

    def wrote_banks(self, pe_val, *banks):
        for b in banks:
            self.wrote[b] = pe_val

    def read(self, eng, sem, banks, fn, inc_n=1):
        self.p.waits(eng, [("PE", max(self.wrote[b] for b in banks))])
        v = self.p.emit(eng, fn, inc=sem, n=inc_n)
        for b in banks:
            self.reads[b][sem] = v
        return v



def _pc(vec, nchunk):
    return np.ascontiguousarray(vec.reshape(nchunk, 128).T).astype(np.float32)


def host_prepare(inputs):
    f = {k: np.asarray(v, np.float32) for k, v in inputs.items()}
    c = {}
    for nm in ("Wq", "Wk", "Wv", "Wo"):
        c["a_w" + nm[-1].lower()] = np.ascontiguousarray(f["L0_attn_" + nm].T)
    c["a_bq_pc"] = _pc(f["L0_attn_bq"], 8)
    c["a_bo_pc"] = _pc(f["L0_attn_bo"], 8)
    c["a_bkv_row"] = np.concatenate([f["L0_attn_bk"], f["L0_attn_bv"]])[None, :]

    HID = 2730
    HIDP = 2816
    NBLK = 44
    w1 = f["L0_ffn_w1"]
    b1 = f["L0_ffn_b1"]
    w2 = f["L0_ffn_w2"]
    w1a = np.zeros((2, 1024, HIDP), np.float32)
    w1g = np.zeros((2, 1024, HIDP), np.float32)
    w1a[:, :, :HID] = w1[:, :, :HID]
    w1g[:, :, :HID] = w1[:, :, HID:]
    b1a = np.zeros((2, HIDP), np.float32)
    b1g = np.zeros((2, HIDP), np.float32)
    b1a[:, :HID] = b1[:, :HID]
    b1g[:, :HID] = b1[:, HID:]
    w1p = np.zeros((2, 1024, 2 * HIDP), np.float32)
    b1_pc = np.zeros((2, 128, NBLK), np.float32)
    for j in range(NBLK):
        w1p[:, :, 128 * j:128 * j + 64] = w1a[:, :, 64 * j:64 * j + 64]
        w1p[:, :, 128 * j + 64:128 * j + 128] = w1g[:, :, 64 * j:64 * j + 64]
        b1_pc[:, 0:64, j] = b1a[:, 64 * j:64 * j + 64]
        b1_pc[:, 64:128, j] = b1g[:, 64 * j:64 * j + 64]
    w2p = np.zeros((2, HIDP, 1024), np.float32)
    w2p[:, :HID, :] = w2
    c["f_wqkv"] = f["L0_ffn_wqkv"]
    c["f_wo"] = f["L0_ffn_wo"]
    c["f_w1p"] = w1p
    c["f_w2p"] = w2p
    c["f_b1_pc"] = b1_pc
    c["f_b2_pc"] = np.stack([_pc(f["L0_ffn_b2"][l], 8) for l in range(2)])
    c["f_lng_pc"] = np.stack([_pc(f["L0_ffn_ln_g"][l], 8) for l in range(2)])
    c["f_lnb_pc"] = np.stack([_pc(f["L0_ffn_ln_b"][l], 8) for l in range(2)])
    c["f_qs_row"] = f["L0_ffn_q_scale"][:, None, :]
    c["f_ks_row"] = f["L0_ffn_k_scale"][:, None, :]

    DIM_HEAD, WINDOW, QK_SCALE = 64, 512, 8.0
    half = np.arange(0, DIM_HEAD, 2, dtype=np.float64)
    inv_freq = 1.0 / 10000.0 ** (half / DIM_HEAD)
    t = np.arange(T, dtype=np.float64)
    freqs = t[:, None] * inv_freq[None, :]
    freqs = np.concatenate([freqs, freqs], -1)
    base = (half + 0.4 * DIM_HEAD) / (1.4 * DIM_HEAD)
    power = (t - T // 2) / float(WINDOW // 2)
    scale = base[None, :] ** power[:, None]
    scale = np.concatenate([scale, scale], -1)
    cos, sin = np.cos(freqs), np.sin(freqs)
    qc = cos * scale * QK_SCALE
    qs_t = sin * scale * QK_SCALE
    kc = cos / scale
    ks_t = sin / scale
    def sgn(tab):
        out = tab.copy()
        out[:, :32] = -tab[:, :32]
        return out
    def tokmaj(tab):
        return np.ascontiguousarray(
            tab.reshape(16, 128, 64).transpose(1, 0, 2).reshape(128, 16 * 64)
        ).astype(np.float32)
    c["rt_qc"] = tokmaj(qc)
    c["rt_qs"] = tokmaj(sgn(qs_t))
    c["rt_kc"] = tokmaj(kc)
    c["rt_ks"] = tokmaj(sgn(ks_t))

    p = np.arange(128)[:, None]
    j = np.arange(640)[None, :]
    diff = p + 512 - j
    c["mask_l0"] = np.where((diff >= 0) & (diff <= 512), 0.0, NEG).astype(ml_dtypes.bfloat16)

    for nm in ("Wq", "Wk", "Wv", "Wo"):
        c["l1_w" + nm[-1].lower()] = np.ascontiguousarray(f["L1_attn_" + nm].T)
    BLK = 64
    qi = np.arange(128)[:, None]
    kj = np.arange(128)[None, :]
    local = np.where((qi // BLK == kj // BLK) & (kj <= qi), 0.0, NEG)
    m_a = np.concatenate([local, np.zeros((128, 16))], axis=1)
    gb = np.zeros((128, 16))
    gb[0:64, :] = NEG
    m_b = np.concatenate([local, gb], axis=1)
    c["mask_l1"] = np.concatenate([m_a, m_b], axis=1).astype(ml_dtypes.bfloat16)

    c["m_gate"] = f["L1_ffn_gate"]
    c["m_w1"] = f["L1_ffn_W1"]
    c["m_w2"] = f["L1_ffn_W2"]
    c["m_b1_pc2"] = np.concatenate(
        [_pc(f["L1_ffn_b1"][e], 16) for e in range(8)], axis=1)
    c["c_identf"] = np.eye(128, dtype=np.float32)
    c["c_iota256"] = np.tile(np.arange(256, dtype=np.float32), (128, 1))
    c["m_b2"] = f["L1_ffn_b2"]

    c["c_ident"] = np.eye(128, dtype=np.float32).astype(ml_dtypes.bfloat16)
    c["c_ones_row"] = np.ones((1, 128), dtype=ml_dtypes.bfloat16)
    c["c_ones_col"] = np.ones((128, 1), np.float32)
    c["c_ones_row_f"] = np.ones((1, 128), np.float32)
    c["c_ones_col_b"] = np.ones((128, 1), dtype=ml_dtypes.bfloat16)
    sel = np.zeros((2, 128), np.float32)
    sel[0, 0:64] = 1.0
    sel[1, 64:128] = 1.0
    c["c_sel2"] = sel.astype(ml_dtypes.bfloat16)

    x = np.asarray(inputs["L0_attn_x"], np.float32)
    shards = []
    for core in range(8):
        xb = x[core * NB:(core + 1) * NB]
        shards.append(np.ascontiguousarray(xb.transpose(0, 2, 1)))
    return c, shards



SEMS = ("CL", "CG", "XL", "W0", "W1", "W2", "W3", "W4", "W5", "GR",
        "SO0", "SO1", "ST", "PE", "EV", "ES", "EG", "EY")


class MK:

    def __init__(self, nc, stages, fused=True):
        self.nc = nc
        self.stages = stages
        self.fused = fused
        self.prog = Prog(nc, SEMS)
        self.dram = {}
        self.buf = {}
        self.psum = []
        self.ctxs = []
        self.wa_pe = 0
        self.moe_b1_pe = 0
        self.moe_b1_ev = 0
        self.moe_b1_es = 0
        self.f_wqkv_pe = 0
        self.f_wo_pe = 0
        self.f_rtf_ev = 0
        self.f_qsks_pe = 0
        self.f_qr_pe = 0
        self.f_pp_pe = 0
        self.f_pp_es = 0
        self.f_pp_ev = 0
        self.f_xin_st = 0

    def declare(self):
        nc = self.nc
        dr = self.dram
        dr["x_t"] = nc.dram_tensor("x_t", [NB, D, T], F32, kind="ExternalInput").ap()
        dr["gt_d"] = nc.dram_tensor("gt_d", [8, T], BF16, kind="Internal").ap()
        dr["xin_d"] = nc.dram_tensor("xin_d", [8, 128, T], BF16, kind="Internal").ap()
        dr["out_t"] = nc.dram_tensor("out_t", [NB, D, T], F32, kind="ExternalOutput").ap()
        shapes = {
            "a_wq": [D, D], "a_wk": [D, D], "a_wv": [D, D], "a_wo": [D, D],
            "a_bq_pc": [128, 8], "a_bo_pc": [128, 8], "a_bkv_row": [1, 2 * D],
            "f_wqkv": [2, D, 1536], "f_wo": [2, 512, D],
            "f_w1p": [2, D, 5632], "f_w2p": [2, 2816, D],
            "f_b1_pc": [2, 128, 44], "f_b2_pc": [2, 128, 8],
            "f_lng_pc": [2, 128, 8], "f_lnb_pc": [2, 128, 8],
            "f_qs_row": [2, 1, 64], "f_ks_row": [2, 1, 64],
            "rt_qc": [128, 1024], "rt_qs": [128, 1024],
            "rt_kc": [128, 1024], "rt_ks": [128, 1024],
            "l1_wq": [D, D], "l1_wk": [D, D], "l1_wv": [D, D], "l1_wo": [D, D],
            "m_gate": [D, 8], "m_w1": [8, D, 2048], "m_w2": [8, 2048, D],
            "m_b1_pc2": [128, 128], "m_b2": [8, D],
            "c_identf": [128, 128], "c_iota256": [128, 256],
            "c_ones_col": [128, 1], "c_ones_row_f": [1, 128],
        }
        bf_shapes = {
            "c_ident": [128, 128], "c_ones_row": [1, 128],
            "c_ones_col_b": [128, 1], "c_sel2": [2, 128],
            "mask_l0": [128, 640], "mask_l1": [128, 288],
        }
        for k, s in shapes.items():
            dr[k] = nc.dram_tensor(k, s, F32, kind="ExternalInput").ap()
        for k, s in bf_shapes.items():
            dr[k] = nc.dram_tensor(k, s, BF16, kind="ExternalInput").ap()

    def alloc(self):
        nc = self.nc
        def sb(name, shape, dt=BF16):
            cm = nc.sbuf_tensor(name, shape, dt)
            t = cm.__enter__()
            self.ctxs.append(cm)
            return t
        b = self.buf
        b["xs"] = sb("xs", [128, 8 * PADT])
        b["wa"] = sb("wa", [128, 24576])
        b["aa"] = sb("aa", [128, 34816])
        b["fa"] = sb("fa", [128, 5632], F32)
        b["small"] = sb("small", [128, 256], F32)
        b["ident"] = sb("ident", [128, 128])
        b["identf"] = sb("identf", [128, 128], F32)
        b["iota256"] = sb("iota256", [128, 256], F32)
        b["ones_row"] = sb("ones_row", [1, 128])
        b["ones_col"] = sb("ones_col", [128, 1], F32)
        b["ones_row_f"] = sb("ones_row_f", [1, 128], F32)
        b["ones_col_b"] = sb("ones_col_b", [128, 1])
        b["sel2"] = sb("sel2", [2, 128])
        b["mask_l0"] = sb("s_mask_l0", [128, 640])
        b["mask_l1"] = sb("s_mask_l1", [128, 288])
        b["rt"] = sb("rt", [128, 4096], F32)
        b["rtf"] = sb("rtf", [128, 4096])
        b["pp"] = sb("pp", [128, 160], F32)
        b["rows"] = sb("rows", [1, 2048])
        b["mb2"] = sb("mb2", [8, D])
        b["qsks"] = sb("qsks", [1, 128], F32)
        cm = nc.psum_tensor("pbig", [128, 3584], F32)
        pf = cm.__enter__()
        self.ctxs.append(cm)
        self.pfull = pf
        for i in range(7):
            self.psum.append(pf[:, 512 * i:512 * i + 512])
        cm = nc.psum_tensor("pb7", [128, 1024], BF16)
        self.psum.append(cm.__enter__())
        self.ctxs.append(cm)
        self.banks = Banks(self.prog)

    def close(self):
        for cm in reversed(self.ctxs):
            cm.__exit__(None, None, None)

    def ld_const(self):
        p, dr, b = self.prog, self.dram, self.buf
        def g(eng, dst, src):
            return p.emit(eng, lambda e, d=dst, s=src: e.dma_start(out=d, in_=s),
                          inc="CL" if eng == "sync" else "CG", n=16)
        g("sync", b["ident"][:, :], dr["c_ident"][:, :])
        g("sync", b["ones_row"][0:1, :], dr["c_ones_row"][0:1, :])
        g("sync", b["ones_col"][:, :], dr["c_ones_col"][:, :])
        g("sync", b["ones_row_f"][0:1, :], dr["c_ones_row_f"][0:1, :])
        g("sync", b["identf"][:, :], dr["c_identf"][:, :])
        g("sync", b["iota256"][:, :], dr["c_iota256"][:, :])
        g("sync", b["ones_col_b"][:, :], dr["c_ones_col_b"][:, :])
        g("sync", b["sel2"][0:2, :], dr["c_sel2"][0:2, :])
        g("sync", b["mask_l0"][:, :], dr["mask_l0"][:, :])
        g("sync", b["mask_l1"][:, :], dr["mask_l1"][:, :])
        rt = b["rt"]
        g("sync", rt[:, 0:1024], dr["rt_qc"][:, :])
        g("sync", rt[:, 1024:2048], dr["rt_qs"][:, :])
        g("sync", rt[:, 2048:3072], dr["rt_kc"][:, :])
        g("sync", rt[:, 3072:4096], dr["rt_ks"][:, :])
        pp = b["pp"]
        g("sync", pp[:, 0:8], dr["a_bq_pc"][:, :])
        g("sync", pp[:, 8:16], dr["a_bo_pc"][:, :])
        g("gpsimd", b["rows"][0:1, 0:2048], dr["a_bkv_row"][0:1, :])
        g("gpsimd", b["mb2"][0:8, :], dr["m_b2"][0:8, :])
        self.ld_sem_vals = (self.prog.cnt["CL"], self.prog.cnt["CG"])
        p.emit("vector", lambda e: e.memset(self.buf["small"][:, 255:256], 1e-5), inc="EV")
        for c in range(8):
            p.emit("vector", lambda e, c=c: e.memset(self.buf["xs"][:, c * PADT:c * PADT + 1], 0.0), inc="EV")

    def ld_x(self, bi):
        p, dr, b = self.prog, self.dram, self.buf
        vals = []
        for c in range(8):
            v = p.emit("gpsimd",
                       lambda e, c=c, bi=bi: e.dma_start(
                           out=b["xs"][:, c * PADT + 1: c * PADT + 1 + T],
                           in_=dr["x_t"][bi, c * 128:(c + 1) * 128, :]),
                       inc="XL", n=16)
            vals.append(v)
        return vals[-1]

    def st_out(self, bi, src_f32):
        p, dr = self.prog, self.dram
        vals = []
        for c in range(8):
            v = p.emit("sync",
                       lambda e, c=c, bi=bi: e.dma_start(
                           out=dr["out_t"][bi, c * 128:(c + 1) * 128, :],
                           in_=src_f32(c)),
                       inc="SO0", n=16)
            vals.append(v)
        return vals[-1]

    def xsl(self, c, j0, n):
        return self.buf["xs"][:, c * PADT + 1 + j0: c * PADT + 1 + j0 + n]

    def xsl_shift(self, c, j0, n):
        return self.buf["xs"][:, c * PADT + j0: c * PADT + j0 + n]

    def _module_ln(self, tc, bi, final, st_vals, sb_base=12288, out_base=8192,
                   stats_only=False):
        p, b, bk, pb = self.prog, self.buf, self.banks, self.psum
        aa, fa = b["aa"], b["fa"]
        t0 = 512 * tc
        SB, QB = 5, 6
        B1, B0 = 3, 4
        mu = aa[0:1, sb_base:sb_base + 1024].bitcast(F32)
        var = aa[0:1, sb_base + 1024:sb_base + 2048].bitcast(F32)
        rstd = aa[0:1, sb_base + 2048:sb_base + 3072].bitcast(F32)
        nm = aa[0:1, sb_base + 3072:sb_base + 4096].bitcast(F32)

        bk.begin_write(SB, QB)
        sqb = aa[:, out_base:out_base + 4096]
        sqv = p.emit("scalar", lambda e, sqb=sqb: e.activation(
            out=sqb, in_=fa[:, 0:4096], func=AF.Square), inc="ES")
        p.waits("tensor", [("ES", sqv)])
        for oc in range(8):
            p.emit("tensor", lambda e, oc=oc: e.matmul(
                pb[QB][0:1, 0:512], b["ones_col"][:, 0:1],
                fa[:, 512 * oc:512 * oc + 512], start=(oc == 0), stop=(oc == 7)))
            p.emit("tensor", lambda e, oc=oc, sqb=sqb: e.matmul(
                pb[SB][0:1, 0:512], b["ones_col_b"][:, 0:1],
                sqb[:, 512 * oc:512 * oc + 512], start=(oc == 0), stop=(oc == 7)),
                inc="PE" if oc == 7 else None)
        smm = p.cnt["PE"]
        bk.wrote_banks(smm, SB, QB)
        bk.read("vector", "EV", [QB], lambda e: e.tensor_scalar(
            out=mu, in0=pb[QB][0:1, 0:512], scalar1=1.0 / D, scalar2=None, op0=ALU.mult))
        bk.read("vector", "EV", [SB], lambda e: e.tensor_scalar(
            out=var, in0=pb[SB][0:1, 0:512], scalar1=1.0 / D, scalar2=None, op0=ALU.mult))
        p.emit("vector", lambda e: e.scalar_tensor_tensor(
            out=nm, in0=mu, scalar=-1.0, in1=mu, op0=ALU.mult, op1=ALU.mult), inc="EV")
        vv = p.emit("vector", lambda e: e.tensor_add(out=var, in0=var, in1=nm), inc="EV")
        p.waits("scalar", [("EV", vv)])
        p.emit("scalar", lambda e: e.activation(out=nm, in_=var, func=AF.Ln,
                                                bias=b["small"][0:1, 255:256]), inc="ES")
        sv = p.emit("scalar", lambda e: e.activation(out=rstd, in_=nm, func=AF.Exp, scale=-0.5), inc="ES")
        p.waits("vector", [("ES", sv)])
        nv = p.emit("vector", lambda e: e.scalar_tensor_tensor(
            out=nm, in0=mu, scalar=-1.0, in1=rstd, op0=ALU.mult, op1=ALU.mult), inc="EV")
        if stats_only is True:
            return
        p.waits("tensor", [("EV", nv)])
        bk.begin_write(B1, B0)
        b1mm = p.emit("tensor", lambda e: e.matmul(
            pb[B1][:, :], b["ones_row_f"][0:1, 0:128], rstd, start=True, stop=True),
            inc="PE")
        b0mm = p.emit("tensor", lambda e: e.matmul(
            pb[B0][:, :], b["ones_row_f"][0:1, 0:128], nm, start=True, stop=True),
            inc="PE")
        bk.wrote_banks(b1mm, B1)
        bk.wrote_banks(b0mm, B0)
        if stats_only == "bcast":
            bk.read("scalar", "ES", [B1], lambda e: e.copy(
                out=fa[:, 4096:4608], in_=pb[B1][:, :]))
            bk.read("scalar", "ES", [B0], lambda e: e.copy(
                out=fa[:, 4608:5120], in_=pb[B0][:, :]))
            return
        for oc in range(8):
            tslot = 4096 + 512 * (oc % 2)
            t1v = bk.read("vector", "EV", [B1], lambda e, oc=oc, tslot=tslot: e.tensor_mul(
                out=fa[:, tslot:tslot + 512], in0=fa[:, 512 * oc:512 * oc + 512],
                in1=pb[B1][:, :]))
            if not final:
                wv = bk.read("vector", "EV", [B0], lambda e, oc=oc, tslot=tslot, t0=t0: e.tensor_add(
                    out=self.xsl(oc, t0, 512), in0=fa[:, tslot:tslot + 512],
                    in1=pb[B0][:, :]))
            else:
                oslot = out_base + 2048 * (oc % 2)
                wv = bk.read("vector", "EV", [B0], lambda e, oc=oc, tslot=tslot, oslot=oslot: e.tensor_add(
                    out=aa[:, oslot:oslot + 2048].bitcast(F32)[:, 0:512],
                    in0=fa[:, tslot:tslot + 512], in1=pb[B0][:, :]))
                p.waits("sync", [("EV", wv)])
                lane = "SO0" if oc % 2 == 0 else "SO1"
                sval = p.emit("sync", lambda e, oc=oc, t0=t0, oslot=oslot, bi=bi: e.dma_start(
                    out=self.dram["out_t"][bi, oc * 128:(oc + 1) * 128, t0:t0 + 512],
                    in_=aa[:, oslot:oslot + 2048].bitcast(F32)[:, 0:512]),
                    inc=lane, n=16)
                st_vals.append((lane, sval))
                p.waits("vector", [(lane, sval)])
        return p.cnt["EV"]

    def stage_a(self, bi, final=False, upto=None):
        p, b, dr, bk = self.prog, self.buf, self.dram, self.banks
        aa, fa, wa, sm, pb = b["aa"], b["fa"], b["wa"], b["small"], self.psum
        KK, VV = 0, 16384
        KVS, KSOF = 33024, 34048
        QK, YF, REC = 0, 4096, 8192

        def wload(name, off, lane):
            return p.emit("gpsimd", lambda e, n=name, o=off: e.dma_start(
                out=wa[:, o:o + 8192].rearrange("p (c m) -> p c m", c=8),
                in_=dr[n].rearrange("(c p) m -> p c m", p=128)), inc=lane, n=16)
        p.waits("gpsimd", [("PE", self.wa_pe)])
        wk_v = wload("a_wk", 0, "W0")
        wv_v = wload("a_wv", 8192, "W1")

        if True:
            p.emit("vector", lambda e: e.memset(
                aa[:, VV:VV + 16640].rearrange("p (t h s) -> p t h s", t=16, h=16)[:, :, :, 64:65],
                1.0), inc="EV")
            p.emit("vector", lambda e: e.memset(aa[:, KSOF:KSOF + 16], 0.0), inc="EV")
            p.emit("vector", lambda e: e.memset(aa[:, KVS:KVS + 1024], 0.0), inc="EV")
        vx_ready = p.cnt["EV"]

        p.waits("tensor", [("W0", wk_v), ("W1", wv_v), ("XL", self.x_ready),
                           ("EV", vx_ready)])
        exp_done = 0
        vcp_done = 0
        red_prev = 0
        for t in range(16):
            kb = (0, 1) if t % 2 == 0 else (2, 3)
            vb = (4, 5)
            bk.begin_write(*vb)
            for h2 in range(2):
                for c in range(8):
                    p.emit("tensor", lambda e, t=t, c=c, h2=h2, vb=vb: e.matmul(
                        pb[vb[h2]][:, :], self.xsl(c, 128 * t, 128),
                        wa[:, 8192 + c * 1024 + h2 * 512: 8192 + c * 1024 + (h2 + 1) * 512],
                        start=(c == 0), stop=False))
                vmm = p.emit("tensor", lambda e, h2=h2, vb=vb: e.matmul(
                    pb[vb[h2]][:, :], b["ones_row"][0:1, 0:128],
                    b["rows"][0:1, 1024 + h2 * 512: 1024 + (h2 + 1) * 512],
                    start=False, stop=True), inc="PE")
            bk.wrote_banks(vmm, *vb)
            bk.begin_write(*kb)
            for h2 in range(2):
                for c in range(8):
                    p.emit("tensor", lambda e, t=t, c=c, h2=h2, kb=kb: e.matmul(
                        pb[kb[h2]][:, :], self.xsl(c, 128 * t, 128),
                        wa[:, c * 1024 + h2 * 512: c * 1024 + (h2 + 1) * 512],
                        start=(c == 0), stop=False))
                kmm = p.emit("tensor", lambda e, h2=h2, kb=kb: e.matmul(
                    pb[kb[h2]][:, :], b["ones_row"][0:1, 0:128],
                    b["rows"][0:1, h2 * 512:(h2 + 1) * 512],
                    start=False, stop=True), inc="PE")
            bk.wrote_banks(kmm, *kb)

            for i, bvv in enumerate(vb):
                vcp_done = bk.read("vector", "EV", [bvv], lambda e, t=t, i=i, bvv=bvv: e.tensor_copy(
                    out=aa[:, VV + 1040 * t: VV + 1040 * t + 1040].rearrange(
                        "p (h s) -> p h s", h=16)[:, 8 * i:8 * i + 8, 0:64],
                    in_=pb[bvv][:, :].rearrange("p (h s) -> p h s", h=8)))
            p.waits("vector", [("ES", exp_done)])
            red = bk.read("vector", "EV", list(kb), lambda e, kb=kb: e.reduce_max(
                out=sm[:, 0:16],
                in_=self.pfull[:, 512 * kb[0]:512 * kb[0] + 1024].rearrange(
                    "p (h d) -> p h d", h=16),
                axis=AX.X, negate=True))
            sub = bk.read("vector", "EV", list(kb), lambda e, kb=kb: e.scalar_tensor_tensor(
                out=fa[:, 4096:5120].rearrange("p (h d) -> p h d", h=16),
                in0=self.pfull[:, 512 * kb[0]:512 * kb[0] + 1024].rearrange(
                    "p (h d) -> p h d", h=16),
                scalar=1.0,
                in1=sm[:, 0:16].unsqueeze(2).broadcast_to([128, 16, 64]),
                op0=ALU.mult, op1=ALU.add))
            p.waits("scalar", [("EV", sub)])
            exp_done = p.emit("scalar", lambda e, t=t: e.activation(
                out=aa[:, KK + 1024 * t: KK + 1024 * t + 1024],
                in_=fa[:, 4096:5120], func=AF.Exp))

        if upto == "p1":
            return []
        p.waits("tensor", [("ES", exp_done), ("EV", vcp_done)])
        kv_es, kv_ev = 0, 0
        for H in range(16):
            bno = H % 7
            bk.begin_write(bno)
            for t in range(16):
                kvm = p.emit("tensor", lambda e, t=t, H=H, bno=bno: e.matmul(
                    pb[bno][0:64, 0:65],
                    aa[:, KK + 1024 * t + 64 * H: KK + 1024 * t + 64 * H + 64],
                    aa[:, VV + 1040 * t + 65 * H: VV + 1040 * t + 65 * H + 65],
                    start=(t == 0), stop=(t == 15)),
                    inc="PE" if t == 15 else None)
            bk.wrote_banks(kvm, bno)
            pr, ro = H // 2, 64 * (H % 2)
            kv_es = bk.read("scalar", "ES", [bno], lambda e, pr=pr, ro=ro, bno=bno: e.copy(
                out=aa[ro:ro + 64, KVS + 128 * pr + ro: KVS + 128 * pr + ro + 64],
                in_=pb[bno][0:64, 0:64]))
            kv_ev = bk.read("vector", "EV", [bno], lambda e, H=H, ro=ro, bno=bno: e.tensor_copy(
                out=aa[ro:ro + 64, KSOF + H: KSOF + H + 1],
                in_=pb[bno][0:64, 64:65]))

        p.waits("gpsimd", [("PE", p.cnt["PE"])])
        wq_v = wload("a_wq", 0, "W0")
        wo_v = wload("a_wo", 8192, "W1")

        if upto == "p15":
            return []
        p.waits("tensor", [("ES", kv_es), ("EV", kv_ev), ("W0", wq_v), ("W1", wo_v)])
        st_vals = []
        for tc in range(4):
            t0 = 512 * tc
            for c in range(8):
                qb = c % 2
                bk.begin_write(qb)
                for kc in range(8):
                    qmm = p.emit("tensor", lambda e, c=c, kc=kc, t0=t0, qb=qb: e.matmul(
                        pb[qb][:, :], wa[:, kc * 1024 + c * 128: kc * 1024 + c * 128 + 128],
                        self.xsl(kc, t0, 512), start=(kc == 0), stop=(kc == 7)),
                        inc="PE" if kc == 7 else None)
                bk.wrote_banks(qmm, qb)
                bk.read("scalar", "ES", [qb], lambda e, c=c, qb=qb: e.activation(
                    out=aa[:, QK + 512 * c: QK + 512 * c + 512], in_=pb[qb][:, :],
                    func=AF.Exp, bias=b["pp"][:, c:c + 1], scale=1.0))
            qk_done = p.cnt["ES"]
            p.waits("tensor", [("ES", qk_done)])
            for c in range(8):
                db = 2 + (c % 2)
                bk.begin_write(db)
                dmm = p.emit("tensor", lambda e, c=c, db=db: e.matmul(
                    pb[db][0:2, 0:512], aa[:, KSOF + 2 * c: KSOF + 2 * c + 2],
                    aa[:, QK + 512 * c: QK + 512 * c + 512], start=True, stop=True),
                    inc="PE")
                bk.wrote_banks(dmm, db)
                bk.read("vector", "EV", [db], lambda e, c=c, db=db: e.tensor_scalar(
                    out=fa[0:2, 512 * (c % 2): 512 * (c % 2) + 512], in0=pb[db][0:2, 0:512],
                    scalar1=1e-6, scalar2=None, op0=ALU.add))
                p.emit("vector", lambda e, c=c: e.reciprocal(
                    out=fa[0:2, 1024 + 512 * (c % 2): 1024 + 512 * (c % 2) + 512],
                    in_=fa[0:2, 512 * (c % 2): 512 * (c % 2) + 512]))
                p.emit("vector", lambda e, c=c: e.tensor_copy(
                    out=aa[0:2, REC + 512 * c: REC + 512 * c + 512],
                    in_=fa[0:2, 1024 + 512 * (c % 2): 1024 + 512 * (c % 2) + 512]), inc="EV")
            rec_done = p.cnt["EV"]
            p.waits("tensor", [("EV", rec_done)])
            for c in range(8):
                yb = 4 + (c % 2)
                bb6 = 6
                bk.begin_write(yb)
                ymm = p.emit("tensor", lambda e, c=c, yb=yb: e.matmul(
                    pb[yb][:, :], aa[:, KVS + 128 * c: KVS + 128 * c + 128],
                    aa[:, QK + 512 * c: QK + 512 * c + 512], start=True, stop=True),
                    inc="PE")
                bk.wrote_banks(ymm, yb)
                bk.begin_write(bb6)
                bmm = p.emit("tensor", lambda e, c=c, bb6=bb6: e.matmul(
                    pb[bb6][:, :], b["sel2"][0:2, :],
                    aa[0:2, REC + 512 * c: REC + 512 * c + 512],
                    start=True, stop=True), inc="PE")
                bk.wrote_banks(bmm, bb6)
                bc_ev = bk.read("scalar", "ES", [bb6], lambda e, c=c, bb6=bb6: e.copy(
                    out=fa[:, 4096 + 512 * (c % 2): 4096 + 512 * (c % 2) + 512],
                    in_=pb[bb6][:, :]))
                p.waits("vector", [("ES", bc_ev)])
                bk.read("vector", "EV", [yb], lambda e, c=c, yb=yb: e.tensor_mul(
                    out=aa[:, YF + 512 * c: YF + 512 * c + 512],
                    in0=pb[yb][:, :],
                    in1=fa[:, 4096 + 512 * (c % 2): 4096 + 512 * (c % 2) + 512]))
            yf_done = p.cnt["EV"]
            p.waits("tensor", [("EV", yf_done)])
            for oc in range(8):
                ob = (0, 1)[oc % 2]
                bk.begin_write(ob)
                for kc in range(8):
                    omm = p.emit("tensor", lambda e, oc=oc, kc=kc, ob=ob: e.matmul(
                        pb[ob][:, :], wa[:, 8192 + kc * 1024 + oc * 128: 8192 + kc * 1024 + oc * 128 + 128],
                        aa[:, YF + 512 * kc: YF + 512 * kc + 512],
                        start=(kc == 0), stop=(kc == 7)),
                        inc="PE" if kc == 7 else None)
                bk.wrote_banks(omm, ob)
                bk.read("vector", "EV", [ob], lambda e, oc=oc, ob=ob, t0=t0: e.scalar_tensor_tensor(
                    out=fa[:, 512 * oc:512 * oc + 512], in0=pb[ob][:, :],
                    scalar=b["pp"][:, 8 + oc:9 + oc], in1=self.xsl(oc, t0, 512),
                    op0=ALU.add, op1=ALU.add))
            h_done = p.cnt["EV"]
            if upto == "p2h":
                return []
            p.waits("scalar", [("EV", h_done)])
            p.waits("tensor", [("EV", h_done)])
            so = {"p2s": True, "p2b": "bcast"}.get(upto, False)
            self._module_ln(tc, bi, final, st_vals, stats_only=so)
            if upto in ("p2s", "p2b"):
                return []
        self.wa_pe = p.cnt["PE"]
        return st_vals

    def stage_l1a(self, bi, final=False, upto=None):
        p, b, dr, bk = self.prog, self.buf, self.dram, self.banks
        aa, fa, wa, sm, pb = b["aa"], b["fa"], b["wa"], b["small"], self.psum
        QT, KT, VS = 0, 4096, 8192
        AT, ATT, ATG = 12288, 14592, 15616
        KG, VG, YT = 16640, 16768, 17792

        def wload(name, off, lane):
            return p.emit("gpsimd", lambda e, n=name, o=off: e.dma_start(
                out=wa[:, o:o + 8192].rearrange("p (c m) -> p c m", c=8),
                in_=dr[n].rearrange("(c p) m -> p c m", p=128)), inc=lane, n=16)
        p.waits("gpsimd", [("PE", self.wa_pe)])
        wq_v = wload("l1_wq", 0, "W0")
        wk_v = wload("l1_wk", 8192, "W1")
        wv_v = wload("l1_wv", 16384, "W2")
        p.waits("tensor", [("W0", wq_v), ("W1", wk_v), ("W2", wv_v),
                           ("XL", self.x_ready)])

        atl_pe = [0, 0]
        self._l1a_att2_pe = 0
        pend2 = None

        def l1a_tail(st):
            g_, tp_, t0_, at0_, par_, sc_ev_ = st
            p.waits("tensor", [("EV", sc_ev_)])
            bk.begin_write(7)
            for h8 in range(8):
                mmv = p.emit("tensor", lambda e, h8=h8, at0_=at0_: e.transpose(
                    out=pb[7][:, 128 * h8:128 * h8 + 128],
                    in_=aa[:, at0_ + 144 * h8: at0_ + 144 * h8 + 128],
                    identity=b["ident"][:, :]), inc="PE" if h8 == 7 else None)
            bk.wrote_banks(mmv, 7)
            p.waits("vector", [("PE", self._l1a_att2_pe)])
            bk.read("vector", "EV", [7], lambda e: e.tensor_copy(
                out=aa[:, ATT:ATT + 1024], in_=pb[7][:, :]))
            bk.begin_write(7)
            for h8 in range(8):
                mmv = p.emit("tensor", lambda e, h8=h8, at0_=at0_: e.transpose(
                    out=pb[7][0:16, 128 * h8:128 * h8 + 128],
                    in_=aa[:, at0_ + 144 * h8 + 128: at0_ + 144 * h8 + 144],
                    identity=b["ident"][:, :]), inc="PE" if h8 == 7 else None)
            bk.wrote_banks(mmv, 7)
            atl_pe[par_] = mmv
            tg_ev = bk.read("vector", "EV", [7], lambda e: e.tensor_copy(
                out=aa[0:16, ATG:ATG + 1024], in_=pb[7][0:16, :]))
            p.waits("tensor", [("EV", tg_ev)])
            for h8 in range(8):
                h = 8 * g_ + h8
                yb = 4 + h8 % 2
                bk.begin_write(yb)
                p.emit("tensor", lambda e, h=h, h8=h8, tp_=tp_, yb=yb: e.matmul(
                    pb[yb][0:64, 0:128], aa[:, VS + 1024 * tp_ + 64 * h: VS + 1024 * tp_ + 64 * h + 64],
                    aa[:, ATT + 128 * h8: ATT + 128 * h8 + 128], start=True, stop=False))
                mmv = p.emit("tensor", lambda e, h=h, h8=h8, yb=yb: e.matmul(
                    pb[yb][0:64, 0:128], aa[0:16, VG + 64 * h: VG + 64 * h + 64],
                    aa[0:16, ATG + 128 * h8: ATG + 128 * h8 + 128],
                    start=False, stop=True), inc="PE")
                bk.wrote_banks(mmv, yb)
                self._l1a_att2_pe = mmv
                co, ro = h // 2, 64 * (h % 2)
                bk.read("scalar", "ES", [yb], lambda e, co=co, ro=ro, t0_=t0_, tp_=tp_, yb=yb: e.copy(
                    out=aa[ro:ro + 64, YT + 2048 * co + t0_ + 128 * tp_: YT + 2048 * co + t0_ + 128 * tp_ + 128],
                    in_=pb[yb][0:64, 0:128]))

        last_qkv_mm = 0
        for tc in range(4):
            t0 = 512 * tc
            for which, base, wofs in (("q", QT, 0), ("k", KT, 8192)):
                for c in range(8):
                    qb = c % 2
                    bk.begin_write(qb)
                    for kc in range(8):
                        mmv = p.emit("tensor", lambda e, c=c, kc=kc, t0=t0, qb=qb, wofs=wofs: e.matmul(
                            pb[qb][:, :], wa[:, wofs + kc * 1024 + c * 128: wofs + kc * 1024 + c * 128 + 128],
                            self.xsl(kc, t0, 512), start=(kc == 0), stop=(kc == 7)),
                            inc="PE" if kc == 7 else None)
                    bk.wrote_banks(mmv, qb)
                    last_qkv_mm = mmv
                    if which == "q":
                        bk.read("vector", "EV", [qb], lambda e, c=c, qb=qb, base=base: e.tensor_scalar(
                            out=aa[:, base + 512 * c: base + 512 * c + 512],
                            in0=pb[qb][:, :], scalar1=0.125, scalar2=None, op0=ALU.mult))
                    else:
                        bk.read("scalar", "ES", [qb], lambda e, c=c, qb=qb, base=base: e.copy(
                            out=aa[:, base + 512 * c: base + 512 * c + 512],
                            in_=pb[qb][:, :]))
            for tt in range(4):
                for half in range(2):
                    vb2 = 2 + (2 * tt + half) % 2
                    bk.begin_write(vb2)
                    for kc in range(8):
                        mmv = p.emit("tensor", lambda e, tt=tt, half=half, kc=kc, t0=t0, vb2=vb2: e.matmul(
                            pb[vb2][:, :], self.xsl(kc, t0 + 128 * tt, 128),
                            wa[:, 16384 + kc * 1024 + half * 512: 16384 + kc * 1024 + half * 512 + 512],
                            start=(kc == 0), stop=(kc == 7)),
                            inc="PE" if kc == 7 else None)
                    bk.wrote_banks(mmv, vb2)
                    last_qkv_mm = mmv
                    bk.read("vector", "EV", [vb2], lambda e, tt=tt, half=half, vb2=vb2: e.tensor_copy(
                        out=aa[:, VS + 1024 * tt + 512 * half: VS + 1024 * tt + 512 * half + 512],
                        in_=pb[vb2][:, :]))
            qkv_ev, qkv_es = p.cnt["EV"], p.cnt["ES"]
            if tc == 0:
                p.waits("vector", [("ES", qkv_es)])
                p.emit("vector", lambda e: e.tensor_copy(
                    out=aa[:, KG:KG + 128].rearrange("p (c s) -> p c s", c=8),
                    in_=aa[:, KT:KT + 4096].rearrange("p (c s) -> p c s", c=8)[:, :, 0:16]))
                qkv_ev = p.emit("vector", lambda e: e.tensor_copy(
                    out=aa[0:16, VG:VG + 1024], in_=aa[0:16, VS:VS + 1024]), inc="EV")

            for tp in range(4):
                pglob = tc * 4 + tp
                mvar = 144 if pglob == 0 else 0
                for g in range(2):
                    par = (2 * pglob + g) % 2
                    at0 = AT + 1152 * par
                    p.waits("tensor", [("EV", qkv_ev), ("ES", qkv_es)])
                    bk.begin_write(0, 1, 2, 3)
                    for h8 in (0, 2, 4, 6, 1, 3, 5, 7):
                        h = 8 * g + h8
                        c, ro = h // 2, 64 * (h % 2)
                        bankno, so = h8 // 2, 256 * (h8 % 2)
                        p.emit("tensor", lambda e, c=c, ro=ro, bankno=bankno, so=so, t0=t0, tp=tp: e.matmul(
                            pb[bankno][:, so:so + 128],
                            aa[ro:ro + 64, QT + 512 * c + 128 * tp: QT + 512 * c + 128 * tp + 128],
                            aa[ro:ro + 64, KT + 512 * c + 128 * tp: KT + 512 * c + 128 * tp + 128],
                            start=True, stop=True))
                        mmv = p.emit("tensor", lambda e, c=c, ro=ro, bankno=bankno, so=so, tp=tp: e.matmul(
                            pb[bankno][:, so + 128:so + 144],
                            aa[ro:ro + 64, QT + 512 * c + 128 * tp: QT + 512 * c + 128 * tp + 128],
                            aa[ro:ro + 64, KG + 16 * c: KG + 16 * c + 16],
                            start=True, stop=True), inc="PE" if h8 == 7 else None)
                    bk.wrote_banks(mmv, 0, 1, 2, 3)
                    if pend2 is not None:
                        l1a_tail(pend2)
                        pend2 = None
                    mk_ev = bk.read("vector", "EV", [0, 1, 2, 3], lambda e, mvar=mvar: e.tensor_tensor(
                        out=fa[:, 0:1152].rearrange("p (h s) -> p h s", h=8),
                        in0=self.pfull[:, 0:2048].rearrange("p (h s) -> p h s", h=8)[:, :, 0:144],
                        in1=b["mask_l1"][:, mvar:mvar + 144].unsqueeze(1).broadcast_to([128, 8, 144]),
                        op=ALU.add))
                    rx = p.emit("vector", lambda e: e.reduce_max(
                        out=sm[:, 0:8], in_=fa[:, 0:1152].rearrange("p (h s) -> p h s", h=8),
                        axis=AX.X, negate=True), inc="EV")
                    p.waits("scalar", [("EV", rx), ("PE", atl_pe[par])])
                    exp_last = 0
                    for h8 in range(8):
                        exp_last = p.emit("scalar", lambda e, h8=h8, at0=at0: e.activation(
                            out=aa[:, at0 + 144 * h8: at0 + 144 * h8 + 144],
                            in_=fa[:, 144 * h8: 144 * h8 + 144], func=AF.Exp,
                            bias=sm[:, h8:h8 + 1], scale=1.0,
                            accum_out=sm[:, 8 + h8:9 + h8]), inc="ES")
                    p.waits("vector", [("ES", exp_last)])
                    p.emit("vector", lambda e: e.reciprocal(out=sm[:, 16:24], in_=sm[:, 8:16]))
                    sc_ev = p.emit("vector", lambda e, at0=at0: e.tensor_mul(
                        out=aa[:, at0:at0 + 1152].rearrange("p (h s) -> p h s", h=8),
                        in0=aa[:, at0:at0 + 1152].rearrange("p (h s) -> p h s", h=8),
                        in1=sm[:, 16:24].unsqueeze(2).broadcast_to([128, 8, 144])), inc="EV")
                    pend2 = (g, tp, t0, at0, par, sc_ev)
            l1a_tail(pend2)
            pend2 = None

        if upto is not None:
            return []
        p.waits("gpsimd", [("PE", p.cnt["PE"])])
        wo_v = wload("l1_wo", 0, "W0")
        p.waits("tensor", [("W0", wo_v), ("ES", p.cnt["ES"])])
        st_vals = []
        for tc in range(4):
            t0 = 512 * tc
            for oc in range(8):
                ob = oc % 2
                bk.begin_write(ob)
                for kc in range(8):
                    omm = p.emit("tensor", lambda e, oc=oc, kc=kc, t0=t0, ob=ob: e.matmul(
                        pb[ob][:, :], wa[:, kc * 1024 + oc * 128: kc * 1024 + oc * 128 + 128],
                        aa[:, YT + 2048 * kc + t0: YT + 2048 * kc + t0 + 512],
                        start=(kc == 0), stop=(kc == 7)),
                        inc="PE" if kc == 7 else None)
                bk.wrote_banks(omm, ob)
                bk.read("vector", "EV", [ob], lambda e, oc=oc, ob=ob, t0=t0: e.tensor_add(
                    out=fa[:, 512 * oc:512 * oc + 512], in0=pb[ob][:, :],
                    in1=self.xsl(oc, t0, 512)))
            h_done = p.cnt["EV"]
            p.waits("scalar", [("EV", h_done)])
            p.waits("tensor", [("EV", h_done)])
            self._module_ln(tc, bi, final, st_vals)
        self.wa_pe = p.cnt["PE"]
        return st_vals


    def stage_moe(self, bi, final=False):
        p, b, dr, bk = self.prog, self.buf, self.dram, self.banks
        aa, fa, wa, sm, pb = b["aa"], b["fa"], b["wa"], b["small"], self.psum
        ACC = 0
        XTM, XG, HP = 8192, 12288, 14336
        OG, OGT, PU, PG, PT = 18432, 20480, 22528, 23552, 24576
        B1C, GT = 25600, 26624
        GROWG, GTMF, POSM, MTM = 28672, 28928, 29184, 29248
        OUTB, STATS = 8192, 12288
        accf = aa[:, 0:8192].bitcast(F32)
        C = 256

        p.waits("tensor", [("XL", self.x_ready), ("CG", self.ld_sem_vals[1])])
        p.waits("gpsimd", [("PE", self.wa_pe)])
        gv = p.emit("gpsimd", lambda e: e.dma_start(
            out=wa[:, 0:64].rearrange("p (c m) -> p c m", c=8),
            in_=dr["m_gate"].rearrange("(c p) m -> p c m", p=128)), inc="W3", n=16)
        p.waits("sync", [("PE", p.cnt["PE"]), ("EV", p.cnt["EV"]),
                         ("ES", p.cnt["ES"])])
        b1v = p.emit("sync", lambda e: e.dma_start(
            out=aa[:, B1C:B1C + 512].bitcast(F32)[:, 0:128],
            in_=dr["m_b1_pc2"][:, :]), inc="W5", n=16)
        p.waits("tensor", [("W3", gv)])
        for t in range(16):
            gb2 = t % 2
            bk.begin_write(gb2)
            for kc in range(8):
                mmv = p.emit("tensor", lambda e, t=t, kc=kc, gb2=gb2: e.matmul(
                    pb[gb2][:, 0:8], self.xsl(kc, 128 * t, 128),
                    wa[:, 8 * kc: 8 * kc + 8],
                    start=(kc == 0), stop=(kc == 7)), inc="PE" if kc == 7 else None)
            bk.wrote_banks(mmv, gb2)
            bk.read("vector", "EV", [gb2], lambda e, gb2=gb2: e.tensor_copy(
                out=fa[:, 0:8], in_=pb[gb2][:, 0:8]))
            p.emit("vector", lambda e: e.reduce_max(
                out=sm[:, 40:41], in_=fa[:, 0:8], axis=AX.X, negate=True))
            p.emit("vector", lambda e: e.tensor_scalar(
                out=sm[:, 41:42], in0=sm[:, 40:41], scalar1=-1.0, scalar2=None, op0=ALU.mult))
            p.emit("vector", lambda e: e.tensor_scalar(
                out=fa[:, 8:16], in0=fa[:, 0:8], scalar1=sm[:, 41:42], scalar2=None, op0=ALU.is_equal))
            p.emit("vector", lambda e: e.scalar_tensor_tensor(
                out=fa[:, 8:16], in0=fa[:, 8:16], scalar=-1e30, in1=fa[:, 0:8],
                op0=ALU.mult, op1=ALU.add))
            p.emit("vector", lambda e: e.reduce_max(
                out=sm[:, 42:43], in_=fa[:, 8:16], axis=AX.X))
            msk_v = p.emit("vector", lambda e: e.tensor_scalar(
                out=fa[:, 8:16], in0=fa[:, 0:8], scalar1=sm[:, 42:43], scalar2=None, op0=ALU.is_ge))
            p.waits("scalar", [("EV", msk_v)])
            ex_v = p.emit("scalar", lambda e: e.activation(
                out=fa[:, 16:24], in_=fa[:, 0:8], func=AF.Exp, bias=sm[:, 40:41]))
            p.waits("vector", [("ES", ex_v)])
            p.emit("vector", lambda e: e.tensor_mul(
                out=fa[:, 16:24], in0=fa[:, 16:24], in1=fa[:, 8:16]))
            p.emit("vector", lambda e: e.reduce_sum(
                out=sm[:, 43:44], in_=fa[:, 16:24], axis=AX.X))
            p.emit("vector", lambda e: e.reciprocal(
                out=sm[:, 44:45], in_=sm[:, 43:44]))
            p.emit("vector", lambda e, t=t: e.tensor_scalar(
                out=aa[:, GTMF:GTMF + 512].bitcast(F32)[:, 8 * t:8 * t + 8],
                in0=fa[:, 16:24], scalar1=sm[:, 44:45], scalar2=None, op0=ALU.mult))
            gt_v = p.emit("vector", lambda e, t=t: e.tensor_copy(
                out=fa[:, 24:32].bitcast(BF16)[:, 0:8],
                in_=aa[:, GTMF:GTMF + 512].bitcast(F32)[:, 8 * t:8 * t + 8]))
            p.waits("tensor", [("EV", gt_v)])
            bk.begin_write(7)
            tv = p.emit("tensor", lambda e: e.transpose(
                out=pb[7][0:8, 0:128], in_=fa[:, 24:32].bitcast(BF16)[:, 0:8],
                identity=b["ident"][:, :]), inc="PE")
            bk.wrote_banks(tv, 7)
            bk.read("vector", "EV", [7], lambda e, t=t: e.tensor_copy(
                out=aa[0:8, GT + 128 * t: GT + 128 * t + 128], in_=pb[7][0:8, 0:128]))
        gt_done = p.cnt["EV"]

        pslot_pe = [p.cnt["PE"], 0, 0]
        pi = 0
        st_vals = []
        xtm_pe = 0
        pup_pe = [0, 0]
        mst = {"ogt_pe": 0, "ptbuf_pe": [0, 0]}
        for tc in range(4):
            t0 = 512 * tc
            p.waits("tensor", [("EV", gt_done)])
            for tt in range(4):
                bk.begin_write(7)
                for dc in range(8):
                    tv = p.emit("tensor", lambda e, tt=tt, dc=dc, t0=t0: e.transpose(
                        out=pb[7][:, 128 * dc:128 * dc + 128],
                        in_=self.xsl(dc, t0 + 128 * tt, 128),
                        identity=b["ident"][:, :]), inc="PE" if dc == 7 else None)
                bk.wrote_banks(tv, 7)
                p.waits("vector", [("PE", xtm_pe)])
                bk.read("vector", "EV", [7], lambda e, tt=tt: e.tensor_copy(
                    out=aa[:, XTM + 1024 * tt: XTM + 1024 * tt + 1024], in_=pb[7][:, :]))
            mp_ev = p.emit("vector", lambda e, tc=tc: e.tensor_scalar(
                out=fa[0:8, 0:512], in0=aa[0:8, GT + 512 * tc: GT + 512 * tc + 512],
                scalar1=0.0, scalar2=None, op0=ALU.is_gt))
            mp_ev = p.emit("vector", lambda e: e.tensor_tensor_scan(
                out=fa[0:8, 512:1024], data0=fa[0:8, 0:512],
                data1=b["ones_col"][0:8, 0:1].broadcast_to([8, 512]),
                initial=0.0, op0=ALU.add, op1=ALU.mult))
            mp_ev = p.emit("vector", lambda e: e.tensor_sub(
                out=fa[0:8, 512:1024], in0=fa[0:8, 512:1024], in1=fa[0:8, 0:512]))
            mt_ev = p.emit("vector", lambda e, tc=tc: e.tensor_scalar(
                out=aa[:, MTM:MTM + 128].bitcast(F32)[:, 0:32],
                in0=aa[:, GTMF:GTMF + 512].bitcast(F32)[:, 32 * tc:32 * tc + 32],
                scalar1=0.0, scalar2=None, op0=ALU.is_gt))
            p.waits("tensor", [("EV", mp_ev)])
            bk.begin_write(6)
            for tt in range(4):
                tv = p.emit("tensor", lambda e, tt=tt: e.transpose(
                    out=pb[6].bitcast(F32)[0:0 + 128, 8 * tt:8 * tt + 8] if False else pb[6][:, 8 * tt:8 * tt + 8],
                    in_=fa[0:8, 512 + 128 * tt:512 + 128 * tt + 128],
                    identity=b["identf"][0:8, 0:8]), inc="PE" if tt == 3 else None)
            bk.wrote_banks(tv, 6)
            ps_ev = bk.read("vector", "EV", [6], lambda e: e.tensor_copy(
                out=aa[:, POSM:POSM + 64].bitcast(F32)[:, 0:32], in_=pb[6][:, 0:32]))
            for oc in range(8):
                ib = 2 + (oc % 2)
                bk.begin_write(ib)
                mmv = p.emit("tensor", lambda e, oc=oc, t0=t0, ib=ib: e.matmul(
                    pb[ib][:, :], b["mb2"][0:8, 128 * oc:128 * oc + 128],
                    aa[0:8, GT + t0: GT + t0 + 512],
                    start=True, stop=True), inc="PE")
                bk.wrote_banks(mmv, ib)
                bk.read("vector", "EV", [ib], lambda e, oc=oc, ib=ib: e.tensor_copy(
                    out=accf[:, 512 * oc: 512 * oc + 512], in_=pb[ib][:, :]))
            pend3 = None

            def emit_scatter(par, oev, oes):
                PTb = PT if par == 0 else OG
                p.waits("tensor", [("EV", oev), ("ES", oes)])
                for dc in range(8):
                    sb3 = 4 + (dc % 2)
                    bk.begin_write(sb3)
                    for st2 in range(2):
                        mmv = p.emit("tensor", lambda e, dc=dc, st2=st2, sb3=sb3, PTb=PTb: e.matmul(
                            pb[sb3][:, :],
                            aa[:, OGT + 1024 * st2 + 128 * dc: OGT + 1024 * st2 + 128 * dc + 128],
                            aa[:, PTb + 512 * st2: PTb + 512 * st2 + 512],
                            start=(st2 == 0), stop=(st2 == 1)),
                            inc="PE" if st2 == 1 else None)
                    bk.wrote_banks(mmv, sb3)
                    mst["ogt_pe"] = mmv
                    mst["ptbuf_pe"][par] = mmv
                    bk.read("vector", "EV", [sb3], lambda e, dc=dc, sb3=sb3: e.tensor_add(
                        out=accf[:, 512 * dc: 512 * dc + 512],
                        in0=pb[sb3][:, :],
                        in1=accf[:, 512 * dc: 512 * dc + 512]))

            for e_i in range(8):
                pls = []
                for pnl in range(3):
                    slot = pi % 3
                    lane = f"W{slot}"
                    p.waits("gpsimd", [("PE", pslot_pe[slot])])
                    if pnl < 2:
                        v = p.emit("gpsimd", lambda e, e_i=e_i, pnl=pnl, slot=slot: e.dma_start(
                            out=wa[:, 8192 * slot:8192 * slot + 8192].rearrange("p (c m) -> p c m", c=8),
                            in_=dr["m_w1"][e_i].rearrange("(c p) m -> p c m", p=128)[:, :, 1024 * pnl:1024 * pnl + 1024]),
                            inc=lane, n=16)
                    else:
                        v = p.emit("gpsimd", lambda e, e_i=e_i, pnl=pnl, slot=slot: e.dma_start(
                            out=wa[:, 8192 * slot:8192 * slot + 8192].rearrange("p (c m) -> p c m", c=16),
                            in_=dr["m_w2"][e_i].rearrange("(c p) m -> p c m", p=128)[:, :, 0:512]),
                            inc=lane, n=16)
                    pls.append((slot, lane, v))
                    pi += 1
                p.waits("vector", [("EV", ps_ev), ("PE", max(pup_pe))])
                pv = 0
                for tt in range(4):
                    gtt = 4 * tc + tt
                    pv = p.emit("vector", lambda e, tt=tt, e_i=e_i: e.tensor_scalar(
                        out=aa[:, PU + 256 * tt: PU + 256 * tt + 256],
                        in0=b["iota256"][:, :],
                        scalar1=aa[:, POSM:POSM + 64].bitcast(F32)[:, 8 * tt + e_i:8 * tt + e_i + 1],
                        scalar2=aa[:, MTM:MTM + 128].bitcast(F32)[:, 8 * tt + e_i:8 * tt + e_i + 1],
                        op0=ALU.is_equal, op1=ALU.mult))
                    pv = p.emit("vector", lambda e, tt=tt, e_i=e_i, gtt=gtt: e.tensor_scalar(
                        out=aa[:, PG + 256 * tt: PG + 256 * tt + 256],
                        in0=b["iota256"][:, :],
                        scalar1=aa[:, POSM:POSM + 64].bitcast(F32)[:, 8 * tt + e_i:8 * tt + e_i + 1],
                        scalar2=aa[:, GTMF:GTMF + 512].bitcast(F32)[:, 8 * gtt + e_i:8 * gtt + e_i + 1],
                        op0=ALU.is_equal, op1=ALU.mult))
                p.waits("tensor", [("EV", pv)])
                for dc in range(8):
                    gb3 = dc % 2
                    bk.begin_write(gb3)
                    for tt in range(4):
                        mmv = p.emit("tensor", lambda e, dc=dc, tt=tt, gb3=gb3: e.matmul(
                            pb[gb3][:, 0:256],
                            aa[:, XTM + 1024 * tt + 128 * dc: XTM + 1024 * tt + 128 * dc + 128],
                            aa[:, PU + 256 * tt: PU + 256 * tt + 256],
                            start=(tt == 0), stop=(tt == 3)),
                            inc="PE" if tt == 3 else None)
                    bk.wrote_banks(mmv, gb3)
                    if dc >= 6:
                        xtm_pe = mmv
                    pup_pe[0] = mmv
                    eng = ("vector", "EV") if dc % 2 == 0 else ("scalar", "ES")
                    if eng[0] == "vector":
                        bk.read("vector", "EV", [gb3], lambda e, dc=dc, gb3=gb3: e.tensor_copy(
                            out=aa[:, XG + 256 * dc: XG + 256 * dc + 256], in_=pb[gb3][:, 0:256]))
                    else:
                        bk.read("scalar", "ES", [gb3], lambda e, dc=dc, gb3=gb3: e.copy(
                            out=aa[:, XG + 256 * dc: XG + 256 * dc + 256], in_=pb[gb3][:, 0:256]))
                xg_ev, xg_es = p.cnt["EV"], p.cnt["ES"]
                par = e_i % 2
                PTb = PT if par == 0 else OG
                p.waits("tensor", [("EV", pv)])
                p.waits("vector", [("PE", mst["ptbuf_pe"][par])])
                for st2 in range(2):
                    bk.begin_write(7)
                    for tt in range(4):
                        tv = p.emit("tensor", lambda e, st2=st2, tt=tt: e.transpose(
                            out=pb[7][:, 128 * tt:128 * tt + 128],
                            in_=aa[:, PG + 256 * tt + 128 * st2: PG + 256 * tt + 128 * st2 + 128],
                            identity=b["ident"][:, :]), inc="PE" if tt == 3 else None)
                    bk.wrote_banks(tv, 7)
                    pup_pe[1] = tv
                    bk.read("vector", "EV", [7], lambda e, st2=st2, PTb=PTb: e.tensor_copy(
                        out=aa[:, PTb + 512 * st2: PTb + 512 * st2 + 512], in_=pb[7][:, 0:512]))
                if pend3 is not None:
                    emit_scatter(*pend3)
                    pend3 = None
                p.waits("tensor", [(pls[0][1], pls[0][2]), ("EV", xg_ev), ("ES", xg_es),
                                   ("W5", b1v)])
                for hdc in range(16):
                    hb = hdc % 4
                    slot = pls[0][0] if hdc < 8 else pls[1][0]
                    if hdc == 8:
                        p.waits("tensor", [(pls[1][1], pls[1][2])])
                    bk.begin_write(hb)
                    for kc in range(8):
                        mmv = p.emit("tensor", lambda e, hdc=hdc, kc=kc, slot=slot, hb=hb: e.matmul(
                            pb[hb][:, 0:256],
                            wa[:, 8192 * slot + 1024 * kc + 128 * (hdc % 8): 8192 * slot + 1024 * kc + 128 * (hdc % 8) + 128],
                            aa[:, XG + 256 * kc: XG + 256 * kc + 256],
                            start=(kc == 0), stop=(kc == 7)),
                            inc="PE" if kc == 7 else None)
                    bk.wrote_banks(mmv, hb)
                    if hdc == 7:
                        pslot_pe[pls[0][0]] = mmv
                    if hdc == 15:
                        pslot_pe[pls[1][0]] = mmv
                    if hdc % 2 == 0:
                        bk.read("vector", "EV", [hb], lambda e, hdc=hdc, hb=hb, e_i=e_i: e.tensor_scalar(
                            out=aa[:, HP + 256 * hdc: HP + 256 * hdc + 256],
                            in0=pb[hb][:, 0:256],
                            scalar1=aa[:, B1C:B1C + 512].bitcast(F32)[:, 16 * e_i + hdc:16 * e_i + hdc + 1],
                            scalar2=0.0, op0=ALU.add, op1=ALU.max))
                    else:
                        bk.read("scalar", "ES", [hb], lambda e, hdc=hdc, hb=hb, e_i=e_i: e.activation(
                            out=aa[:, HP + 256 * hdc: HP + 256 * hdc + 256],
                            in_=pb[hb][:, 0:256], func=AF.Relu,
                            bias=aa[:, B1C:B1C + 512].bitcast(F32)[:, 16 * e_i + hdc:16 * e_i + hdc + 1]))
                hp_ev, hp_es = p.cnt["EV"], p.cnt["ES"]
                slot = pi % 3
                lane = f"W{slot}"
                p.waits("gpsimd", [("PE", pslot_pe[slot])])
                v = p.emit("gpsimd", lambda e, e_i=e_i, slot=slot: e.dma_start(
                    out=wa[:, 8192 * slot:8192 * slot + 8192].rearrange("p (c m) -> p c m", c=16),
                    in_=dr["m_w2"][e_i].rearrange("(c p) m -> p c m", p=128)[:, :, 512:1024]),
                    inc=lane, n=16)
                pls.append((slot, lane, v))
                pi += 1
                p.waits("tensor", [(pls[2][1], pls[2][2]), ("EV", hp_ev), ("ES", hp_es)])
                for cb in range(4):
                    nh, st2 = cb // 2, cb % 2
                    ob = 2 + cb
                    pl = pls[2] if nh == 0 else pls[3]
                    if cb == 2:
                        p.waits("tensor", [(pls[3][1], pls[3][2])])
                    bk.begin_write(ob)
                    for kc in range(16):
                        mmv = p.emit("tensor", lambda e, kc=kc, st2=st2, slot=pl[0], ob=ob: e.matmul(
                            pb[ob][:, 0:512],
                            aa[:, HP + 256 * kc + 128 * st2: HP + 256 * kc + 128 * st2 + 128],
                            wa[:, 8192 * slot + 512 * kc: 8192 * slot + 512 * kc + 512],
                            start=(kc == 0), stop=(kc == 15)),
                            inc="PE" if kc == 15 else None)
                    bk.wrote_banks(mmv, ob)
                    if cb == 1:
                        pslot_pe[pls[2][0]] = mmv
                    if cb == 3:
                        pslot_pe[pls[3][0]] = mmv
                    if cb == 0:
                        p.waits("vector", [("PE", mst["ogt_pe"])])
                    if cb == 1:
                        p.waits("scalar", [("PE", mst["ogt_pe"])])
                    if cb % 2 == 0:
                        bk.read("vector", "EV", [ob], lambda e, nh=nh, st2=st2, ob=ob: e.tensor_copy(
                            out=aa[:, OGT + 1024 * st2 + 512 * nh: OGT + 1024 * st2 + 512 * nh + 512],
                            in_=pb[ob][:, 0:512]))
                    else:
                        bk.read("scalar", "ES", [ob], lambda e, nh=nh, st2=st2, ob=ob: e.copy(
                            out=aa[:, OGT + 1024 * st2 + 512 * nh: OGT + 1024 * st2 + 512 * nh + 512],
                            in_=pb[ob][:, 0:512]))
                ogt_ev, ogt_es = p.cnt["EV"], p.cnt["ES"]
                pend3 = (par, ogt_ev, ogt_es)
            emit_scatter(*pend3)
            pend3 = None
            acc_done = p.cnt["EV"]
            for oc in range(8):
                hv = p.emit("vector", lambda e, oc=oc, t0=t0: e.tensor_add(
                    out=fa[:, 512 * oc:512 * oc + 512],
                    in0=accf[:, 512 * oc: 512 * oc + 512],
                    in1=self.xsl(oc, t0, 512)))
            p.waits("scalar", [("EV", hv)])
            p.waits("tensor", [("EV", hv)])
            self._module_ln(tc, bi, final, st_vals, sb_base=STATS, out_base=OUTB)
        self.wa_pe = p.cnt["PE"]
        self.moe_b1_pe = p.cnt["PE"]
        self.moe_b1_ev = p.cnt["EV"]
        self.moe_b1_es = p.cnt["ES"]
        return st_vals

    def stage_f(self, bi, final=False, upto=None):
        p, b, dr, bk = self.prog, self.buf, self.dram, self.banks
        aa, fa, wa, sm, pb = b["aa"], b["fa"], b["wa"], b["small"], self.psum
        VS, QT, KT, YW = 0, 8192, 16384, 24576
        ATN, ATT, QR = 26624, 29184, 30464
        DG = 32512
        H2N, GL, H3 = 0, 4096, 8192
        XIN, STATS2, OUTB2 = 20480, 26624, 30720
        pp = b["pp"]

        p.waits("sync", [("XL", self.x_ready), ("ST", getattr(self, "f_xin_st", 0)),
                         ("EV", p.cnt["EV"])])
        for c in range(8):
            xin_st = p.emit("sync", lambda e, c=c: e.dma_start(
                out=dr["xin_d"][c, :, :], in_=self.xsl(c, 0, T)), inc="ST", n=16)
        self.f_xin_st = xin_st

        for l in range(2):
            p.waits("gpsimd", [("PE", max(self.f_wqkv_pe, self.wa_pe))])
            wqkv_v = p.emit("gpsimd", lambda e, l=l: e.dma_start(
                out=wa[:, 0:12288].rearrange("p (c m) -> p c m", c=8),
                in_=dr["f_wqkv"][l].rearrange("(c p) m -> p c m", p=128)), inc="W0", n=16)
            p.waits("gpsimd", [("PE", self.f_wo_pe)])
            wo_v = p.emit("gpsimd", lambda e, l=l: e.dma_start(
                out=wa[:, 12288:16384].rearrange("p (c m) -> p c m", c=4),
                in_=dr["f_wo"][l].rearrange("(c p) m -> p c m", p=128)), inc="W1", n=16)
            p.waits("sync", [("PE", self.f_pp_pe), ("ES", self.f_pp_es),
                             ("EV", self.f_pp_ev)])
            p.emit("sync", lambda e, l=l: e.dma_start(
                out=pp[:, 16:60], in_=dr["f_b1_pc"][l]), inc="W5", n=16)
            p.emit("sync", lambda e, l=l: e.dma_start(
                out=pp[:, 60:68], in_=dr["f_b2_pc"][l]), inc="W5", n=16)
            p.emit("sync", lambda e, l=l: e.dma_start(
                out=pp[:, 68:76], in_=dr["f_lng_pc"][l]), inc="W5", n=16)
            pp_v = p.emit("sync", lambda e, l=l: e.dma_start(
                out=pp[:, 76:84], in_=dr["f_lnb_pc"][l]), inc="W5", n=16)
            qs_v = p.emit("sync", lambda e, l=l: e.dma_start(
                out=b["qsks"][0:1, 0:64], in_=dr["f_qs_row"][l][0:1, :]), inc="W5", n=16)
            ks_v = p.emit("sync", lambda e, l=l: e.dma_start(
                out=b["qsks"][0:1, 64:128], in_=dr["f_ks_row"][l][0:1, :]), inc="W5", n=16)
            p.waits("gpsimd", [("EV", self.f_rtf_ev)])
            for ti, nm in enumerate(("rt_qc", "rt_qs", "rt_kc", "rt_ks")):
                rtl_v = p.emit("gpsimd", lambda e, nm=nm, ti=ti: e.dma_start(
                    out=b["rtf"][:, 1024 * ti:1024 * ti + 1024],
                    in_=dr[nm][:, :]), inc="W4", n=16)
            p.waits("tensor", [("W5", ks_v), ("PE", self.f_qsks_pe)])
            sb6 = 6
            bk.begin_write(sb6)
            scv = p.emit("tensor", lambda e: e.matmul(
                pb[sb6][:, 0:128], b["ones_row_f"][0:1, 0:128], b["qsks"][0:1, 0:128],
                start=True, stop=True), inc="PE")
            bk.wrote_banks(scv, sb6)
            self.f_qsks_pe = scv
            sc_ev = bk.read("vector", "EV", [sb6], lambda e: e.tensor_copy(
                out=fa[:, 5120:5248], in_=pb[sb6][:, 0:128]))
            p.waits("vector", [("W4", rtl_v)])
            for ti in range(4):
                off = 0 if ti < 2 else 64
                p.emit("vector", lambda e, ti=ti, off=off: e.tensor_mul(
                    out=b["rtf"][:, 1024 * ti:1024 * ti + 1024].rearrange("p (a d) -> p a d", a=16),
                    in0=b["rtf"][:, 1024 * ti:1024 * ti + 1024].rearrange("p (a d) -> p a d", a=16),
                    in1=fa[:, 5120 + off:5120 + off + 64].unsqueeze(1).broadcast_to([128, 16, 64])))
            rtf_ev = p.cnt["EV"]

            def rtfs(ti, t):
                return b["rtf"][:, 1024 * ti + 64 * t: 1024 * ti + 64 * t + 64]

            p.waits("tensor", [("W0", wqkv_v), ("XL", self.x_ready), ("EV", rtf_ev),
                               ("W5", ks_v)])
            pend_tr = []
            qr_pe = [0, 0]
            for t in range(16):
                bq3 = (0, 1, 2) if t % 2 == 0 else (3, 4, 5)
                for pi3 in range(3):
                    bno = bq3[pi3]
                    bk.begin_write(bno)
                    for kc in range(8):
                        src = self.xsl(kc, 128 * t, 128) if kc < 4 else self.xsl_shift(kc, 128 * t, 128)
                        mmv = p.emit("tensor", lambda e, kc=kc, pi3=pi3, bno=bno, src=src: e.matmul(
                            pb[bno][:, :], src,
                            wa[:, 1536 * kc + 512 * pi3: 1536 * kc + 512 * pi3 + 512],
                            start=(kc == 0), stop=(kc == 7)),
                            inc="PE" if kc == 7 else None)
                    bk.wrote_banks(mmv, bno)
                for qro, base_, t_, qr_v in pend_tr:
                    p.waits("tensor", [("EV", qr_v)])
                    bk.begin_write(7)
                    for dc in range(4):
                        tv = p.emit("tensor", lambda e, dc=dc, qro=qro: e.transpose(
                            out=pb[7][:, 128 * dc:128 * dc + 128],
                            in_=aa[:, qro + 128 * dc: qro + 128 * dc + 128],
                            identity=b["ident"][:, :]), inc="PE" if dc == 3 else None)
                    bk.wrote_banks(tv, 7)
                    qr_pe[t_ % 2] = tv
                    bk.read("vector", "EV", [7], lambda e, base_=base_, t_=t_: e.tensor_copy(
                        out=aa[:, base_:base_ + 8192].rearrange("p (c n) -> p c n", c=4)[:, :, 128 * t_:128 * t_ + 128],
                        in_=pb[7][:, 0:512].rearrange("p (c n) -> p c n", c=4)))
                pend_tr = []
                vb = bq3[2]
                bk.read("vector", "EV", [vb], lambda e, t=t, vb=vb: e.tensor_copy(
                    out=aa[:, VS + 512 * t: VS + 512 * t + 512], in_=pb[vb][:, :]))
                for pi3, (base, tq, ts_) in enumerate(((QT, 0, 1), (KT, 2, 3))):
                    bno = bq3[pi3]
                    ft = 512 * pi3
                    sq_v = bk.read("scalar", "ES", [bno], lambda e, bno=bno, ft=ft: e.activation(
                        out=fa[:, 1024 + ft:1024 + ft + 512], in_=pb[bno][:, :],
                        func=AF.Square))
                    p.waits("vector", [("ES", sq_v)])
                    p.emit("vector", lambda e, ft=ft: e.reduce_sum(
                        out=sm[:, 48:56], in_=fa[:, 1024 + ft:1024 + ft + 512].rearrange(
                            "p (h d) -> p h d", h=8), axis=AX.X))
                    ssc_v = p.emit("vector", lambda e: e.tensor_scalar(
                        out=sm[:, 48:56], in0=sm[:, 48:56], scalar1=1e-24, scalar2=None,
                        op0=ALU.max))
                    p.waits("scalar", [("EV", ssc_v)])
                    p.emit("scalar", lambda e: e.activation(
                        out=sm[:, 56:64], in_=sm[:, 48:56], func=AF.Ln))
                    rsq_v = p.emit("scalar", lambda e: e.activation(
                        out=sm[:, 56:64], in_=sm[:, 56:64], func=AF.Exp, scale=-0.5))
                    p.waits("vector", [("ES", rsq_v)])
                    bk.read("vector", "EV", [bno], lambda e, bno=bno, ft=ft: e.tensor_mul(
                        out=fa[:, 2048 + ft:2048 + ft + 512].rearrange("p (h d) -> p h d", h=8),
                        in0=pb[bno][:, :].rearrange("p (h d) -> p h d", h=8),
                        in1=sm[:, 56:64].unsqueeze(2).broadcast_to([128, 8, 64])))
                    qn = fa[:, 2048 + ft:2048 + ft + 512]
                    t1 = fa[:, 3072:3584]
                    uu = fa[:, 3584:4096]
                    p.waits("vector", [("PE", self.f_qr_pe)])
                    p.emit("vector", lambda e, qn=qn, t1=t1, tq=tq, t=t: e.tensor_mul(
                        out=t1.rearrange("p (h d) -> p h d", h=8),
                        in0=qn.rearrange("p (h d) -> p h d", h=8),
                        in1=rtfs(tq, t).unsqueeze(1).broadcast_to([128, 8, 64])))
                    p.emit("vector", lambda e, qn=qn, uu=uu, ts_=ts_, t=t: e.tensor_mul(
                        out=uu.rearrange("p (h d) -> p h d", h=8)[:, :, 0:32],
                        in0=qn.rearrange("p (h d) -> p h d", h=8)[:, :, 32:64],
                        in1=rtfs(ts_, t)[:, 0:32].unsqueeze(1).broadcast_to([128, 8, 32])))
                    p.emit("vector", lambda e, qn=qn, uu=uu, ts_=ts_, t=t: e.tensor_mul(
                        out=uu.rearrange("p (h d) -> p h d", h=8)[:, :, 32:64],
                        in0=qn.rearrange("p (h d) -> p h d", h=8)[:, :, 0:32],
                        in1=rtfs(ts_, t)[:, 32:64].unsqueeze(1).broadcast_to([128, 8, 32])))
                    qro = QR + 1024 * (t % 2) + 512 * pi3
                    p.waits("vector", [("PE", qr_pe[t % 2])])
                    qr_v = p.emit("vector", lambda e, t1=t1, uu=uu, qro=qro: e.tensor_add(
                        out=aa[:, qro:qro + 512], in0=t1, in1=uu))
                    pend_tr.append((qro, base, t, qr_v))
            for qro, base_, t_, qr_v in pend_tr:
                p.waits("tensor", [("EV", qr_v)])
                bk.begin_write(7)
                for dc in range(4):
                    tv = p.emit("tensor", lambda e, dc=dc, qro=qro: e.transpose(
                        out=pb[7][:, 128 * dc:128 * dc + 128],
                        in_=aa[:, qro + 128 * dc: qro + 128 * dc + 128],
                        identity=b["ident"][:, :]), inc="PE" if dc == 3 else None)
                bk.wrote_banks(tv, 7)
                qr_pe[t_ % 2] = tv
                bk.read("vector", "EV", [7], lambda e, base_=base_, t_=t_: e.tensor_copy(
                    out=aa[:, base_:base_ + 8192].rearrange("p (c n) -> p c n", c=4)[:, :, 128 * t_:128 * t_ + 128],
                    in_=pb[7][:, 0:512].rearrange("p (c n) -> p c n", c=4)))
            pend_tr = []
            qkv_ev, qkv_es = p.cnt["EV"], p.cnt["ES"]
            rtf_last_ev = qkv_ev
            self.f_rtf_ev = rtf_last_ev
            if upto == "qkv":
                return []

            p.waits("tensor", [("W1", wo_v), ("EV", qkv_ev), ("ES", qkv_es)])
            p.waits("vector", [("ST", self.f_xin_st)])
            atn_pe = [0, 0]
            att_pe = [0, 0]
            yw_pe = 0
            for w in range(4):
                p.waits("scalar", [("PE", yw_pe)])
                pend = None

                def emit_tail(st):
                    hp_, par_, atn0_, W_, nkt_, k_lo_, qt_, exdg = st
                    p.waits("tensor", [("ES", exdg[0]), ("EV", exdg[1])])
                    for j in range(2):
                        hh = 2 * hp_ + j
                        ats = ATT + 640 * j
                        bk.begin_write(4, 5)
                        for k in range(nkt_):
                            tv = p.emit("tensor", lambda e, k=k, j=j, atn0_=atn0_, par_=par_: e.matmul(
                                self.pfull[:, 2048 + 128 * k:2048 + 128 * k + 128],
                                aa[:, atn0_ + 640 * j + 128 * k: atn0_ + 640 * j + 128 * k + 128],
                                aa[:, DG + 256 * par_ + 128 * j: DG + 256 * par_ + 128 * j + 128],
                                start=True, stop=True),
                                inc="PE" if k == nkt_ - 1 else None)
                        bk.wrote_banks(tv, 4, 5)
                        atn_pe[par_] = tv
                        p.waits("scalar", [("PE", att_pe[j])])
                        att_v = bk.read("scalar", "ES", [4, 5], lambda e, nkt_=nkt_, ats=ats: e.copy(
                            out=aa[:, ats:ats + 128 * nkt_],
                            in_=self.pfull[:, 2048:2048 + 128 * nkt_]))
                        p.waits("tensor", [("ES", att_v)])
                        bk.begin_write(6)
                        for k in range(nkt_):
                            ktg = k_lo_ // 128 + k
                            mmv = p.emit("tensor", lambda e, k=k, ktg=ktg, hh=hh, j=j, nkt_=nkt_, ats=ats: e.matmul(
                                pb[6][0:64, 128 * j:128 * j + 128],
                                aa[:, VS + 512 * ktg + 64 * hh: VS + 512 * ktg + 64 * hh + 64],
                                aa[:, ats + 128 * k: ats + 128 * k + 128],
                                start=(k == 0), stop=(k == nkt_ - 1)),
                                inc="PE" if k == nkt_ - 1 else None)
                        bk.wrote_banks(mmv, 6)
                        att_pe[j] = mmv
                        bk.read("scalar", "ES", [6], lambda e, hh=hh, j=j, qt_=qt_: e.copy(
                            out=aa[64 * (hh % 2):64 * (hh % 2) + 64,
                                   YW + 512 * (hh // 2) + 128 * qt_: YW + 512 * (hh // 2) + 128 * qt_ + 128],
                            in_=pb[6][0:64, 128 * j:128 * j + 128]))

                for qt in range(4):
                    qi = 4 * w + qt
                    q0 = 128 * qi
                    k_lo = max(0, q0 - 512)
                    W = q0 + 128 - k_lo
                    nkt = W // 128
                    mof = 640 - W
                    for hp in range(4):
                        par = (qi * 4 + hp) % 2
                        atn0 = ATN + 1280 * par
                        bk.begin_write(0, 1, 2, 3)
                        for j in range(2):
                            hh = 2 * hp + j
                            ro = 64 * j
                            for j0 in range(0, W, 512):
                                nb2 = 2 * j + (1 if j0 else 0)
                                nn = min(512, W - j0)
                                mmv = p.emit("tensor", lambda e, hp=hp, ro=ro, q0=q0, k_lo=k_lo, j0=j0, nn=nn, nb2=nb2: e.matmul(
                                    pb[nb2][:, 0:nn],
                                    aa[ro:ro + 64, QT + 2048 * hp + q0: QT + 2048 * hp + q0 + 128],
                                    aa[ro:ro + 64, KT + 2048 * hp + k_lo + j0: KT + 2048 * hp + k_lo + j0 + nn],
                                    start=True, stop=True), inc="PE" if (j == 1 and j0 + 512 >= W) else None)
                        bk.wrote_banks(mmv, 0, 1, 2, 3)
                        if pend is not None:
                            emit_tail(pend)
                            pend = None
                        mk1 = bk.read("vector", "EV", [0, 1, 2, 3], lambda e, W=W, mof=mof: e.tensor_tensor(
                            out=fa[:, 0:1280].rearrange("p (h s) -> p h s", h=2)[:, :, 0:W],
                            in0=self.pfull[:, 0:2048].rearrange("p (h s) -> p h s", h=2)[:, :, 0:W],
                            in1=b["mask_l0"][:, mof:mof + W].unsqueeze(1).broadcast_to([128, 2, W]),
                            op=ALU.add))
                        rx_v = p.emit("vector", lambda e, W=W: e.reduce_max(
                            out=sm[:, 64:66], in_=fa[:, 0:1280].rearrange("p (h s) -> p h s", h=2)[:, :, 0:W],
                            axis=AX.X, negate=True))
                        p.waits("scalar", [("EV", rx_v), ("PE", atn_pe[par])])
                        ex_v = 0
                        for j in range(2):
                            ex_v = p.emit("scalar", lambda e, j=j, W=W, atn0=atn0: e.activation(
                                out=aa[:, atn0 + 640 * j: atn0 + 640 * j + W],
                                in_=fa[:, 640 * j:640 * j + W], func=AF.Exp,
                                bias=sm[:, 64 + j:65 + j], accum_out=sm[:, 66 + j:67 + j]))
                        p.waits("vector", [("ES", ex_v), ("PE", atn_pe[par])])
                        rc_v = p.emit("vector", lambda e: e.reciprocal(
                            out=sm[:, 68:70], in_=sm[:, 66:68]))
                        dg_v = 0
                        for j in range(2):
                            dg_v = p.emit("vector", lambda e, j=j, par=par: e.tensor_scalar(
                                out=aa[:, DG + 256 * par + 128 * j: DG + 256 * par + 128 * j + 128],
                                in0=b["ident"][:, :], scalar1=sm[:, 68 + j:69 + j],
                                scalar2=None, op0=ALU.mult))
                        pend = (hp, par, atn0, W, nkt, k_lo, qt, (ex_v, dg_v))
                emit_tail(pend)
                ow_es = p.cnt["ES"]
                p.waits("tensor", [("ES", ow_es)])
                for oc in range(8):
                    ob = oc % 2
                    bk.begin_write(ob)
                    for kc in range(4):
                        mmv = p.emit("tensor", lambda e, oc=oc, kc=kc, ob=ob: e.matmul(
                            pb[ob][:, :],
                            wa[:, 12288 + 1024 * kc + 128 * oc: 12288 + 1024 * kc + 128 * oc + 128],
                            aa[:, YW + 512 * kc: YW + 512 * kc + 512],
                            start=(kc == 0), stop=(kc == 3)),
                            inc="PE" if kc == 3 else None)
                    bk.wrote_banks(mmv, ob)
                    yw_pe = mmv
                    bk.read("vector", "EV", [ob], lambda e, oc=oc, w=w, ob=ob: e.tensor_add(
                        out=self.xsl(oc, 512 * w, 512), in0=pb[ob][:, :],
                        in1=self.xsl(oc, 512 * w, 512)))
            self.f_wqkv_pe = p.cnt["PE"]
            self.f_wo_pe = p.cnt["PE"]
            resid_ev = p.cnt["EV"]
            if upto == "attn":
                return []

            SB2, QB2, B1B, B0B = 5, 6, 3, 4
            pn1_pe = [self.f_wqkv_pe, self.f_wqkv_pe]
            pn2_pe = [self.f_wo_pe, self.f_wo_pe]
            h2n_pe = 0
            h3_pe = 0
            w2_ev = resid_ev
            mu2 = aa[0:1, 19456:20480].bitcast(F32)
            var2 = aa[0:1, 20480:21504].bitcast(F32)
            rstd2 = aa[0:1, 21504:22528].bitcast(F32)
            nm2 = aa[0:1, 22528:23552].bitcast(F32)
            for tc in range(4):
                t0 = 512 * tc
                def h2ap(oc, t0=t0):
                    return self.xsl(oc, t0, 512) if oc < 4 else self.xsl_shift(oc, t0, 512)
                w1l_v = [0] * 11
                for pnl in range(2):
                    p.waits("gpsimd", [("PE", pn1_pe[pnl % 2])])
                    w1l_v[pnl] = p.emit("gpsimd", lambda e, l=l, pnl=pnl: e.dma_start(
                        out=wa[:, 4096 * (pnl % 2):4096 * (pnl % 2) + 4096].rearrange("p (c m) -> p c m", c=8),
                        in_=dr["f_w1p"][l].rearrange("(c p) m -> p c m", p=128)[:, :, 512 * pnl:512 * pnl + 512]),
                        inc=f"W{pnl % 2}", n=16)
                p.waits("tensor", [("EV", w2_ev)])
                bk.begin_write(SB2, QB2)
                mmv2 = []
                for oc in range(8):
                    sq = 4096 + 512 * (oc % 2)
                    if oc >= 2:
                        p.waits("scalar", [("PE", mmv2[oc - 2])])
                    sqv = p.emit("scalar", lambda e, oc=oc, sq=sq, h2ap=h2ap: e.activation(
                        out=fa[:, sq:sq + 512], in_=h2ap(oc), func=AF.Square))
                    p.waits("tensor", [("ES", sqv)])
                    p.emit("tensor", lambda e, oc=oc, h2ap=h2ap: e.matmul(
                        pb[QB2][0:1, 0:512], b["ones_col_b"][:, 0:1], h2ap(oc),
                        start=(oc == 0), stop=(oc == 7)))
                    mmv2.append(p.emit("tensor", lambda e, oc=oc, sq=sq: e.matmul(
                        pb[SB2][0:1, 0:512], b["ones_col"][:, 0:1], fa[:, sq:sq + 512],
                        start=(oc == 0), stop=(oc == 7)), inc="PE"))
                smm = p.cnt["PE"]
                bk.wrote_banks(smm, SB2, QB2)
                bk.read("vector", "EV", [QB2], lambda e: e.tensor_scalar(
                    out=mu2, in0=pb[QB2][0:1, 0:512], scalar1=1.0 / D, scalar2=None, op0=ALU.mult))
                bk.read("vector", "EV", [SB2], lambda e: e.tensor_scalar(
                    out=var2, in0=pb[SB2][0:1, 0:512], scalar1=1.0 / D, scalar2=None, op0=ALU.mult))
                p.emit("vector", lambda e: e.scalar_tensor_tensor(
                    out=nm2, in0=mu2, scalar=-1.0, in1=mu2, op0=ALU.mult, op1=ALU.mult))
                vv = p.emit("vector", lambda e: e.tensor_add(out=var2, in0=var2, in1=nm2))
                p.waits("scalar", [("EV", vv)])
                p.emit("scalar", lambda e: e.activation(out=nm2, in_=var2, func=AF.Ln,
                                                        bias=b["small"][0:1, 255:256]))
                sv = p.emit("scalar", lambda e: e.activation(out=rstd2, in_=nm2, func=AF.Exp, scale=-0.5))
                p.waits("vector", [("ES", sv)])
                nv = p.emit("vector", lambda e: e.scalar_tensor_tensor(
                    out=nm2, in0=mu2, scalar=-1.0, in1=rstd2, op0=ALU.mult, op1=ALU.mult))
                p.waits("tensor", [("EV", nv)])
                bk.begin_write(B1B, B0B)
                b1mm = p.emit("tensor", lambda e: e.matmul(
                    pb[B1B][:, :], b["ones_row_f"][0:1, 0:128], rstd2, start=True, stop=True), inc="PE")
                b0mm = p.emit("tensor", lambda e: e.matmul(
                    pb[B0B][:, :], b["ones_row_f"][0:1, 0:128], nm2, start=True, stop=True), inc="PE")
                bk.wrote_banks(b1mm, B1B)
                bk.wrote_banks(b0mm, B0B)
                p.waits("vector", [("PE", max(h2n_pe, b0mm))])
                for oc in range(8):
                    tslot = 4096 + 512 * (oc % 2)
                    bk.read("vector", "EV", [B1B], lambda e, oc=oc, tslot=tslot, h2ap=h2ap: e.tensor_mul(
                        out=fa[:, tslot:tslot + 512], in0=h2ap(oc), in1=pb[B1B][:, :]))
                    bk.read("vector", "EV", [B0B], lambda e, oc=oc, tslot=tslot: e.tensor_add(
                        out=fa[:, tslot:tslot + 512], in0=fa[:, tslot:tslot + 512], in1=pb[B0B][:, :]))
                    p.emit("vector", lambda e, oc=oc, tslot=tslot: e.tensor_scalar(
                        out=aa[:, H2N + 512 * oc: H2N + 512 * oc + 512],
                        in0=fa[:, tslot:tslot + 512], scalar1=pp[:, 68 + oc:69 + oc],
                        scalar2=pp[:, 76 + oc:77 + oc], op0=ALU.mult, op1=ALU.add))
                h2n_ev = p.cnt["EV"]
                p.waits("tensor", [("EV", h2n_ev), ("W0", w1l_v[0]), ("PE", h3_pe)])
                gl_par = 0
                gl_ev = []
                for j in range(44):
                    pnl = j // 4
                    if j % 4 == 0 and 1 <= pnl < 10:
                        slot = (pnl + 1) % 2
                        p.waits("gpsimd", [("PE", pn1_pe[slot])])
                        w1l_v[pnl + 1] = p.emit("gpsimd", lambda e, l=l, pnl=pnl, slot=slot: e.dma_start(
                            out=wa[:, 4096 * slot:4096 * slot + 4096].rearrange("p (c m) -> p c m", c=8),
                            in_=dr["f_w1p"][l].rearrange("(c p) m -> p c m", p=128)[:, :, 512 * (pnl + 1):512 * (pnl + 1) + 512]),
                            inc=f"W{slot}", n=16)
                    if j % 4 == 0 and pnl > 0:
                        p.waits("tensor", [(f"W{pnl % 2}", w1l_v[pnl])])
                    hb = j % 2
                    slot = pnl % 2
                    bk.begin_write(hb)
                    for kc in range(8):
                        mmv = p.emit("tensor", lambda e, j=j, kc=kc, slot=slot, hb=hb: e.matmul(
                            pb[hb][:, :],
                            wa[:, 4096 * slot + 512 * kc + 128 * (j % 4): 4096 * slot + 512 * kc + 128 * (j % 4) + 128],
                            aa[:, H2N + 512 * kc: H2N + 512 * kc + 512],
                            start=(kc == 0), stop=(kc == 7)),
                            inc="PE" if kc == 7 else None)
                    bk.wrote_banks(mmv, hb)
                    if j % 4 == 3 or j == 43:
                        pn1_pe[slot] = mmv
                    glo = GL + 512 * gl_par
                    if j >= 2:
                        p.waits("scalar", [("EV", gl_ev[j - 2])])
                    gv2 = bk.read("scalar", "ES", [hb], lambda e, j=j, hb=hb, glo=glo: e.activation(
                        out=aa[0:64, glo:glo + 512], in_=pb[hb][64:128, :],
                        func=AF.Gelu, bias=pp[64:128, 16 + j:17 + j]))
                    p.waits("vector", [("ES", gv2)])
                    ro2 = 64 * (j % 2)
                    gl_ev.append(bk.read("vector", "EV", [hb], lambda e, j=j, hb=hb, glo=glo, ro2=ro2: e.scalar_tensor_tensor(
                        out=aa[ro2:ro2 + 64, H3 + 512 * (j // 2): H3 + 512 * (j // 2) + 512],
                        in0=pb[hb][0:64, :], scalar=pp[0:64, 16 + j:17 + j],
                        in1=aa[0:64, glo:glo + 512], op0=ALU.add, op1=ALU.mult)))
                    gl_par ^= 1
                h3_ev, h3_es = p.cnt["EV"], p.cnt["ES"]
                for pnl in range(2):
                    p.waits("gpsimd", [("PE", pn2_pe[pnl])])
                    w2l0 = p.emit("gpsimd", lambda e, l=l, pnl=pnl: e.dma_start(
                        out=wa[:, 8192 + 5632 * pnl:8192 + 5632 * pnl + 5632].rearrange("p (c m) -> p c m", c=22),
                        in_=dr["f_w2p"][l].rearrange("(c p) m -> p c m", p=128)[:, :, 256 * pnl:256 * pnl + 256]),
                        inc=f"W{2 + pnl}", n=16)
                    if pnl == 0:
                        w2l_v0 = w2l0
                    else:
                        w2l_v1 = w2l0
                p.waits("tensor", [("EV", h3_ev), ("ES", h3_es), ("W2", w2l_v0)])
                for oc in range(8):
                    pnl = oc // 4
                    pq = oc // 2
                    slot = pq % 2
                    lane = f"W{2 + slot}"
                    if oc == 2:
                        pass
                    ob = 2 + (oc % 2)
                    if pq >= 2 and oc % 2 == 0:
                        p.waits("gpsimd", [("PE", pn2_pe[slot])])
                        wv2 = p.emit("gpsimd", lambda e, l=l, pq=pq, slot=slot: e.dma_start(
                            out=wa[:, 8192 + 5632 * slot:8192 + 5632 * slot + 5632].rearrange("p (c m) -> p c m", c=22),
                            in_=dr["f_w2p"][l].rearrange("(c p) m -> p c m", p=128)[:, :, 256 * pq:256 * pq + 256]),
                            inc=lane, n=16)
                        p.waits("tensor", [(lane, wv2)])
                    elif oc == 2:
                        p.waits("tensor", [("W3", w2l_v1)])
                    bk.begin_write(ob)
                    for kc in range(22):
                        mmv = p.emit("tensor", lambda e, oc=oc, kc=kc, slot=slot, ob=ob: e.matmul(
                            pb[ob][:, :],
                            wa[:, 8192 + 5632 * slot + 256 * kc + 128 * (oc % 2): 8192 + 5632 * slot + 256 * kc + 128 * (oc % 2) + 128],
                            aa[:, H3 + 512 * kc: H3 + 512 * kc + 512],
                            start=(kc == 0), stop=(kc == 21)),
                            inc="PE" if kc == 21 else None)
                    bk.wrote_banks(mmv, ob)
                    if oc % 2 == 1:
                        pn2_pe[slot] = mmv
                    bk.read("vector", "EV", [ob], lambda e, oc=oc, t0=t0, ob=ob: e.scalar_tensor_tensor(
                        out=self.xsl(oc, t0, 512), in0=pb[ob][:, :],
                        scalar=pp[:, 60 + oc:61 + oc], in1=self.xsl(oc, t0, 512),
                        op0=ALU.add, op1=ALU.add))
                w2_ev = p.cnt["EV"]
                h2n_pe = p.cnt["PE"]
                h3_pe = h2n_pe
            self.f_pp_pe, self.f_pp_es, self.f_pp_ev = p.cnt["PE"], p.cnt["ES"], p.cnt["EV"]
            self.f_wqkv_pe = p.cnt["PE"]
            self.f_wo_pe = p.cnt["PE"]
            if upto == "ffn" + str(l):
                return []

        st_vals = []
        xin_pe = p.cnt["PE"]
        for tc in range(4):
            t0 = 512 * tc
            p.waits("sync", [("ST", self.f_xin_st), ("PE", xin_pe),
                             ("EV", w2_ev), ("ES", p.cnt["ES"] if tc == 0 else 0)])
            xi_v = p.emit("sync", lambda e, t0=t0: e.dma_start(
                out=aa[:, XIN:XIN + 4096].rearrange("p (c n) -> p c n", c=8),
                in_=dr["xin_d"].rearrange("c p n -> p c n")[:, :, t0:t0 + 512]), inc="GR", n=16)
            p.waits("vector", [("GR", xi_v), ("EV", w2_ev)])
            for oc in range(8):
                hv = p.emit("vector", lambda e, oc=oc, t0=t0: e.tensor_add(
                    out=fa[:, 512 * oc:512 * oc + 512],
                    in0=aa[:, XIN + 512 * oc: XIN + 512 * oc + 512],
                    in1=self.xsl(oc, t0, 512)))
            p.waits("scalar", [("EV", hv)])
            p.waits("tensor", [("EV", hv)])
            self._module_ln(tc, bi, final, st_vals, sb_base=STATS2, out_base=OUTB2)
            xin_pe = self.prog.cnt["PE"]
        self.wa_pe = p.cnt["PE"]
        return st_vals



def build_nc(stage_list, nb=NB, dump=False, upto=None):
    nc = bass.Bass("TRN2", target_bir_lowering=False, debug=False)
    mk = MK(nc, stage_list)
    mk.declare()
    if dump:
        mk.dram["dbg_aa"] = nc.dram_tensor("dbg_aa", [128, 34816], BF16, kind="ExternalOutput").ap()
        mk.dram["dbg_fa"] = nc.dram_tensor("dbg_fa", [128, 5632], F32, kind="ExternalOutput").ap()
        mk.dram["dbg_wa"] = nc.dram_tensor("dbg_wa", [128, 24576], BF16, kind="ExternalOutput").ap()
    for name in SEMS:
        cm = nc.semaphore(name)
        mk.prog.sems[name] = cm.__enter__()
        mk.ctxs.append(cm)
    mk.alloc()
    mk.ld_const()
    cl0, cg0, ev0 = mk.prog.cnt["CL"], mk.prog.cnt["CG"], mk.prog.cnt["EV"]
    for eng in ("tensor", "vector", "scalar"):
        mk.prog.waits(eng, [("CL", cl0), ("CG", cg0)])
    mk.prog.waits("tensor", [("EV", ev0)])

    for bi in range(nb):
        if bi > 0:
            mk.prog.waits("gpsimd", [("EV", mk.prog.cnt["EV"]),
                                     ("ES", mk.prog.cnt["ES"]),
                                     ("PE", mk.prog.cnt["PE"])])
        mk.x_ready = mk.ld_x(bi)
        st_vals = []
        for si, stg in enumerate(stage_list):
            final = (si == len(stage_list) - 1)
            if stg == "a":
                st_vals = mk.stage_a(bi, final=final, upto=upto)
            elif stg == "f":
                st_vals = mk.stage_f(bi, final=final, upto=upto)
            elif stg == "l1a":
                st_vals = mk.stage_l1a(bi, final=final, upto=upto)
            elif stg == "moe":
                st_vals = mk.stage_moe(bi, final=final)
            else:
                raise ValueError(stg)
            if si != len(stage_list) - 1:
                mk.prog.barrier()
        mk.prog.barrier()
    if dump:
        for nm2, bf2 in (("dbg_aa", "aa"), ("dbg_fa", "fa"), ("dbg_wa", "wa")):
            mk.prog.emit("sync", lambda e, nm2=nm2, bf2=bf2: e.dma_start(
                out=mk.dram[nm2][:, :], in_=mk.buf[bf2][:, :]), inc="ST", n=16)
    mk.prog.waits("sync", [(s, mk.prog.cnt[s]) for s in
                           ("SO0", "SO1", "ST") if mk.prog.cnt[s]])
    mk.prog.check_deadlock()
    with nc.Block() as block:
        mk.prog.replay(block)
    mk.close()
    return nc


def run_stages(inputs, stage_list, nc=None):
    common, shards = host_prepare(inputs)
    if nc is None:
        nc = build_nc(stage_list)
    in_maps = []
    for core in range(8):
        m = dict(common)
        m["x_t"] = shards[core]
        in_maps.append(m)
    res = run_bass_kernel_spmd(nc, in_maps, core_ids=list(range(8)))
    outs = []
    for core in range(8):
        ot = res.results[core]["out_t"]
        outs.append(np.ascontiguousarray(ot.transpose(0, 2, 1)))
    return np.concatenate(outs, axis=0)


def kernel(**inputs):
    return run_stages(inputs, ["a", "f", "l1a", "moe"]).astype(np.float32)



# version 16 (final)
# speedup vs baseline: 1.2863x; 1.0074x over previous
import sys
sys.path.insert(0, "/opt/trn_rl_repo")
import numpy as np
import ml_dtypes
import concourse.bass as bass
import concourse.mybir as mybir
from concourse.bass_utils import run_bass_kernel_spmd

F32 = mybir.dt.float32
BF16 = mybir.dt.bfloat16
AF = mybir.ActivationFunctionType
ALU = mybir.AluOpType
AX = mybir.AxisListType

T = 2048
D = 1024
PADT = 2052
NB = 4
NEG = -3.0e38

ENGS = ("tensor", "vector", "scalar", "gpsimd", "sync")


class Prog:

    OWN_SEM = {"tensor": "PE", "vector": "EV", "scalar": "ES"}

    def __init__(self, nc, semnames):
        self.nc = nc
        self.semnames = list(semnames)
        self.sems = {}
        self.cnt = {s: 0 for s in semnames}
        self.q = {e: [] for e in ENGS}
        self.ev = {e: [] for e in ENGS}
        self.self_waited = {e: 0 for e in ENGS}

    def wait(self, eng, sem, val):
        if val <= 0:
            return
        self.q[eng].append(("w", sem, val, None))
        self.ev[eng].append(("w", sem, val))

    def emit(self, eng, fn, inc=None, n=1):
        own = self.OWN_SEM.get(eng)
        if own is not None and self.cnt[own] > self.self_waited[eng]:
            self.self_waited[eng] = self.cnt[own]
            self.wait(eng, own, self.cnt[own])
        if inc is None and eng in ("vector", "scalar"):
            inc = own
        val = None
        if inc is not None:
            self.cnt[inc] += n
            val = self.cnt[inc]
        self.q[eng].append(("i", inc, n, fn))
        self.ev[eng].append(("i", inc, val))
        return val

    def waits(self, eng, pairs):
        best = {}
        for s, v in pairs:
            if v and v > best.get(s, 0):
                best[s] = v
        for s, v in best.items():
            self.wait(eng, s, v)

    def barrier(self, tag="", engines=("tensor", "vector", "scalar")):
        sem_of = {"tensor": "PE", "vector": "EV", "scalar": "ES",
                  "gpsimd": "EG", "sync": "EY"}
        vals = {}
        for e in engines:
            vals[e] = self.emit(e, lambda en: en.nop(), inc=sem_of[e])
        for e in engines:
            for o in engines:
                if o != e:
                    self.wait(e, sem_of[o], vals[o])

    def check_deadlock(self):
        ptr = {e: 0 for e in ENGS}
        cnt = {s: 0 for s in self.semnames}
        progress = True
        while progress:
            progress = False
            for e in ENGS:
                while ptr[e] < len(self.ev[e]):
                    kind, sem, val = self.ev[e][ptr[e]]
                    if kind == "w":
                        if cnt[sem] >= val:
                            ptr[e] += 1
                            progress = True
                        else:
                            break
                    else:
                        if sem is not None:
                            cnt[sem] = val if val > cnt[sem] else cnt[sem]
                        ptr[e] += 1
                        progress = True
        stuck = {e: ptr[e] for e in ENGS if ptr[e] < len(self.ev[e])}
        if stuck:
            info = []
            for e, p in stuck.items():
                kind, sem, val = self.ev[e][p]
                info.append(f"{e}@{p}/{len(self.ev[e])} waiting {sem}>={val} (now {cnt[sem]})")
            raise RuntimeError("DEADLOCK in schedule: " + "; ".join(info))

    def replay(self, block):
        def runner(stream):
            def go(eng):
                for item in stream:
                    kind, a, b, fn = item
                    if kind == "w":
                        eng.wait_ge(self.sems[a], b)
                    else:
                        ins = fn(eng)
                        if a is not None:
                            ins.then_inc(self.sems[a], b)
            return go
        block.tensor(runner(self.q["tensor"]))
        block.vector(runner(self.q["vector"]))
        block.scalar(runner(self.q["scalar"]))
        block.gpsimd(runner(self.q["gpsimd"]))
        block.sync(runner(self.q["sync"]))


class Banks:

    def __init__(self, prog):
        self.p = prog
        self.wrote = [0] * 8
        self.reads = [dict() for _ in range(8)]

    def begin_write(self, *banks):
        pairs = []
        for b in banks:
            pairs += list(self.reads[b].items())
            self.reads[b] = {}
        self.p.waits("tensor", pairs)

    def wrote_banks(self, pe_val, *banks):
        for b in banks:
            self.wrote[b] = pe_val

    def read(self, eng, sem, banks, fn, inc_n=1):
        self.p.waits(eng, [("PE", max(self.wrote[b] for b in banks))])
        v = self.p.emit(eng, fn, inc=sem, n=inc_n)
        for b in banks:
            self.reads[b][sem] = v
        return v



def _pc(vec, nchunk):
    return np.ascontiguousarray(vec.reshape(nchunk, 128).T).astype(np.float32)


def host_prepare(inputs):
    f = {k: np.asarray(v, np.float32) for k, v in inputs.items()}
    c = {}
    for nm in ("Wq", "Wk", "Wv", "Wo"):
        c["a_w" + nm[-1].lower()] = np.ascontiguousarray(f["L0_attn_" + nm].T)
    c["a_bq_pc"] = _pc(f["L0_attn_bq"], 8)
    c["a_bo_pc"] = _pc(f["L0_attn_bo"], 8)
    c["a_bkv_row"] = np.concatenate([f["L0_attn_bk"], f["L0_attn_bv"]])[None, :]

    HID = 2730
    HIDP = 2816
    NBLK = 44
    w1 = f["L0_ffn_w1"]
    b1 = f["L0_ffn_b1"]
    w2 = f["L0_ffn_w2"]
    w1a = np.zeros((2, 1024, HIDP), np.float32)
    w1g = np.zeros((2, 1024, HIDP), np.float32)
    w1a[:, :, :HID] = w1[:, :, :HID]
    w1g[:, :, :HID] = w1[:, :, HID:]
    b1a = np.zeros((2, HIDP), np.float32)
    b1g = np.zeros((2, HIDP), np.float32)
    b1a[:, :HID] = b1[:, :HID]
    b1g[:, :HID] = b1[:, HID:]
    w1p = np.zeros((2, 1024, 2 * HIDP), np.float32)
    b1_pc = np.zeros((2, 128, NBLK), np.float32)
    for j in range(NBLK):
        w1p[:, :, 128 * j:128 * j + 64] = w1a[:, :, 64 * j:64 * j + 64]
        w1p[:, :, 128 * j + 64:128 * j + 128] = w1g[:, :, 64 * j:64 * j + 64]
        b1_pc[:, 0:64, j] = b1a[:, 64 * j:64 * j + 64]
        b1_pc[:, 64:128, j] = b1g[:, 64 * j:64 * j + 64]
    w2p = np.zeros((2, HIDP, 1024), np.float32)
    w2p[:, :HID, :] = w2
    c["f_wqkv"] = f["L0_ffn_wqkv"]
    c["f_wo"] = f["L0_ffn_wo"]
    c["f_w1p"] = w1p
    c["f_w2p"] = w2p
    c["f_b1_pc"] = b1_pc
    c["f_b2_pc"] = np.stack([_pc(f["L0_ffn_b2"][l], 8) for l in range(2)])
    c["f_lng_pc"] = np.stack([_pc(f["L0_ffn_ln_g"][l], 8) for l in range(2)])
    c["f_lnb_pc"] = np.stack([_pc(f["L0_ffn_ln_b"][l], 8) for l in range(2)])
    c["f_qs_row"] = f["L0_ffn_q_scale"][:, None, :]
    c["f_ks_row"] = f["L0_ffn_k_scale"][:, None, :]

    DIM_HEAD, WINDOW, QK_SCALE = 64, 512, 8.0
    half = np.arange(0, DIM_HEAD, 2, dtype=np.float64)
    inv_freq = 1.0 / 10000.0 ** (half / DIM_HEAD)
    t = np.arange(T, dtype=np.float64)
    freqs = t[:, None] * inv_freq[None, :]
    freqs = np.concatenate([freqs, freqs], -1)
    base = (half + 0.4 * DIM_HEAD) / (1.4 * DIM_HEAD)
    power = (t - T // 2) / float(WINDOW // 2)
    scale = base[None, :] ** power[:, None]
    scale = np.concatenate([scale, scale], -1)
    cos, sin = np.cos(freqs), np.sin(freqs)
    qc = cos * scale * QK_SCALE
    qs_t = sin * scale * QK_SCALE
    kc = cos / scale
    ks_t = sin / scale
    def sgn(tab):
        out = tab.copy()
        out[:, :32] = -tab[:, :32]
        return out
    def tokmaj(tab):
        return np.ascontiguousarray(
            tab.reshape(16, 128, 64).transpose(1, 0, 2).reshape(128, 16 * 64)
        ).astype(np.float32)
    c["rt_qc"] = tokmaj(qc)
    c["rt_qs"] = tokmaj(sgn(qs_t))
    c["rt_kc"] = tokmaj(kc)
    c["rt_ks"] = tokmaj(sgn(ks_t))

    p = np.arange(128)[:, None]
    j = np.arange(640)[None, :]
    diff = p + 512 - j
    c["mask_l0"] = np.where((diff >= 0) & (diff <= 512), 0.0, NEG).astype(ml_dtypes.bfloat16)

    for nm in ("Wq", "Wk", "Wv", "Wo"):
        c["l1_w" + nm[-1].lower()] = np.ascontiguousarray(f["L1_attn_" + nm].T)
    BLK = 64
    qi = np.arange(128)[:, None]
    kj = np.arange(128)[None, :]
    local = np.where((qi // BLK == kj // BLK) & (kj <= qi), 0.0, NEG)
    m_a = np.concatenate([local, np.zeros((128, 16))], axis=1)
    gb = np.zeros((128, 16))
    gb[0:64, :] = NEG
    m_b = np.concatenate([local, gb], axis=1)
    c["mask_l1"] = np.concatenate([m_a, m_b], axis=1).astype(ml_dtypes.bfloat16)

    c["m_gate"] = f["L1_ffn_gate"]
    c["m_w1"] = f["L1_ffn_W1"]
    c["m_w2"] = f["L1_ffn_W2"]
    c["m_b1_pc2"] = np.concatenate(
        [_pc(f["L1_ffn_b1"][e], 16) for e in range(8)], axis=1)
    c["c_identf"] = np.eye(128, dtype=np.float32)
    c["c_iota256"] = np.tile(np.arange(256, dtype=np.float32), (128, 1))
    c["m_b2"] = f["L1_ffn_b2"]

    c["c_ident"] = np.eye(128, dtype=np.float32).astype(ml_dtypes.bfloat16)
    c["c_ones_row"] = np.ones((1, 128), dtype=ml_dtypes.bfloat16)
    c["c_ones_col"] = np.ones((128, 1), np.float32)
    c["c_ones_row_f"] = np.ones((1, 128), np.float32)
    c["c_ones_col_b"] = np.ones((128, 1), dtype=ml_dtypes.bfloat16)
    sel = np.zeros((2, 128), np.float32)
    sel[0, 0:64] = 1.0
    sel[1, 64:128] = 1.0
    c["c_sel2"] = sel.astype(ml_dtypes.bfloat16)

    x = np.asarray(inputs["L0_attn_x"], np.float32)
    shards = []
    for core in range(8):
        xb = x[core * NB:(core + 1) * NB]
        shards.append(np.ascontiguousarray(xb.transpose(0, 2, 1)))
    return c, shards



SEMS = ("CL", "CG", "XL", "W0", "W1", "W2", "W3", "W4", "W5", "GR",
        "SO0", "SO1", "ST", "PE", "EV", "ES", "EG", "EY")


class MK:

    def __init__(self, nc, stages, fused=True):
        self.nc = nc
        self.stages = stages
        self.fused = fused
        self.prog = Prog(nc, SEMS)
        self.dram = {}
        self.buf = {}
        self.psum = []
        self.ctxs = []
        self.wa_pe = 0
        self.moe_b1_pe = 0
        self.moe_b1_ev = 0
        self.moe_b1_es = 0
        self.f_wqkv_pe = 0
        self.f_wo_pe = 0
        self.f_rtf_ev = 0
        self.f_qsks_pe = 0
        self.f_qr_pe = 0
        self.f_pp_pe = 0
        self.f_pp_es = 0
        self.f_pp_ev = 0
        self.f_xin_st = 0

    def declare(self):
        nc = self.nc
        dr = self.dram
        dr["x_t"] = nc.dram_tensor("x_t", [NB, D, T], F32, kind="ExternalInput").ap()
        dr["gt_d"] = nc.dram_tensor("gt_d", [8, T], BF16, kind="Internal").ap()
        dr["xin_d"] = nc.dram_tensor("xin_d", [8, 128, T], BF16, kind="Internal").ap()
        dr["out_t"] = nc.dram_tensor("out_t", [NB, D, T], F32, kind="ExternalOutput").ap()
        shapes = {
            "a_wq": [D, D], "a_wk": [D, D], "a_wv": [D, D], "a_wo": [D, D],
            "a_bq_pc": [128, 8], "a_bo_pc": [128, 8], "a_bkv_row": [1, 2 * D],
            "f_wqkv": [2, D, 1536], "f_wo": [2, 512, D],
            "f_w1p": [2, D, 5632], "f_w2p": [2, 2816, D],
            "f_b1_pc": [2, 128, 44], "f_b2_pc": [2, 128, 8],
            "f_lng_pc": [2, 128, 8], "f_lnb_pc": [2, 128, 8],
            "f_qs_row": [2, 1, 64], "f_ks_row": [2, 1, 64],
            "rt_qc": [128, 1024], "rt_qs": [128, 1024],
            "rt_kc": [128, 1024], "rt_ks": [128, 1024],
            "l1_wq": [D, D], "l1_wk": [D, D], "l1_wv": [D, D], "l1_wo": [D, D],
            "m_gate": [D, 8], "m_w1": [8, D, 2048], "m_w2": [8, 2048, D],
            "m_b1_pc2": [128, 128], "m_b2": [8, D],
            "c_identf": [128, 128], "c_iota256": [128, 256],
            "c_ones_col": [128, 1], "c_ones_row_f": [1, 128],
        }
        bf_shapes = {
            "c_ident": [128, 128], "c_ones_row": [1, 128],
            "c_ones_col_b": [128, 1], "c_sel2": [2, 128],
            "mask_l0": [128, 640], "mask_l1": [128, 288],
        }
        for k, s in shapes.items():
            dr[k] = nc.dram_tensor(k, s, F32, kind="ExternalInput").ap()
        for k, s in bf_shapes.items():
            dr[k] = nc.dram_tensor(k, s, BF16, kind="ExternalInput").ap()

    def alloc(self):
        nc = self.nc
        def sb(name, shape, dt=BF16):
            cm = nc.sbuf_tensor(name, shape, dt)
            t = cm.__enter__()
            self.ctxs.append(cm)
            return t
        b = self.buf
        b["xs"] = sb("xs", [128, 8 * PADT])
        b["wa"] = sb("wa", [128, 24576])
        b["aa"] = sb("aa", [128, 34816])
        b["fa"] = sb("fa", [128, 5632], F32)
        b["small"] = sb("small", [128, 256], F32)
        b["ident"] = sb("ident", [128, 128])
        b["identf"] = sb("identf", [128, 128], F32)
        b["iota256"] = sb("iota256", [128, 256], F32)
        b["ones_row"] = sb("ones_row", [1, 128])
        b["ones_col"] = sb("ones_col", [128, 1], F32)
        b["ones_row_f"] = sb("ones_row_f", [1, 128], F32)
        b["ones_col_b"] = sb("ones_col_b", [128, 1])
        b["sel2"] = sb("sel2", [2, 128])
        b["mask_l0"] = sb("s_mask_l0", [128, 640])
        b["mask_l1"] = sb("s_mask_l1", [128, 288])
        b["rt"] = sb("rt", [128, 4096], F32)
        b["rtf"] = sb("rtf", [128, 4096])
        b["pp"] = sb("pp", [128, 160], F32)
        b["rows"] = sb("rows", [1, 2048])
        b["mb2"] = sb("mb2", [8, D])
        b["qsks"] = sb("qsks", [1, 128], F32)
        cm = nc.psum_tensor("pbig", [128, 3584], F32)
        pf = cm.__enter__()
        self.ctxs.append(cm)
        self.pfull = pf
        for i in range(7):
            self.psum.append(pf[:, 512 * i:512 * i + 512])
        cm = nc.psum_tensor("pb7", [128, 1024], BF16)
        self.psum.append(cm.__enter__())
        self.ctxs.append(cm)
        self.banks = Banks(self.prog)

    def close(self):
        for cm in reversed(self.ctxs):
            cm.__exit__(None, None, None)

    def ld_const(self):
        p, dr, b = self.prog, self.dram, self.buf
        def g(eng, dst, src):
            return p.emit(eng, lambda e, d=dst, s=src: e.dma_start(out=d, in_=s),
                          inc="CL" if eng == "sync" else "CG", n=16)
        g("sync", b["ident"][:, :], dr["c_ident"][:, :])
        g("sync", b["ones_row"][0:1, :], dr["c_ones_row"][0:1, :])
        g("sync", b["ones_col"][:, :], dr["c_ones_col"][:, :])
        g("sync", b["ones_row_f"][0:1, :], dr["c_ones_row_f"][0:1, :])
        g("sync", b["identf"][:, :], dr["c_identf"][:, :])
        g("sync", b["iota256"][:, :], dr["c_iota256"][:, :])
        g("sync", b["ones_col_b"][:, :], dr["c_ones_col_b"][:, :])
        g("sync", b["sel2"][0:2, :], dr["c_sel2"][0:2, :])
        g("sync", b["mask_l0"][:, :], dr["mask_l0"][:, :])
        g("sync", b["mask_l1"][:, :], dr["mask_l1"][:, :])
        rt = b["rt"]
        g("sync", rt[:, 0:1024], dr["rt_qc"][:, :])
        g("sync", rt[:, 1024:2048], dr["rt_qs"][:, :])
        g("sync", rt[:, 2048:3072], dr["rt_kc"][:, :])
        g("sync", rt[:, 3072:4096], dr["rt_ks"][:, :])
        pp = b["pp"]
        g("sync", pp[:, 0:8], dr["a_bq_pc"][:, :])
        g("sync", pp[:, 8:16], dr["a_bo_pc"][:, :])
        g("gpsimd", b["rows"][0:1, 0:2048], dr["a_bkv_row"][0:1, :])
        g("gpsimd", b["mb2"][0:8, :], dr["m_b2"][0:8, :])
        self.ld_sem_vals = (self.prog.cnt["CL"], self.prog.cnt["CG"])
        p.emit("vector", lambda e: e.memset(self.buf["small"][:, 255:256], 1e-5), inc="EV")
        for c in range(8):
            p.emit("vector", lambda e, c=c: e.memset(self.buf["xs"][:, c * PADT:c * PADT + 1], 0.0), inc="EV")

    def ld_x(self, bi):
        p, dr, b = self.prog, self.dram, self.buf
        vals = []
        for c in range(8):
            v = p.emit("gpsimd",
                       lambda e, c=c, bi=bi: e.dma_start(
                           out=b["xs"][:, c * PADT + 1: c * PADT + 1 + T],
                           in_=dr["x_t"][bi, c * 128:(c + 1) * 128, :]),
                       inc="XL", n=16)
            vals.append(v)
        return vals[-1]

    def st_out(self, bi, src_f32):
        p, dr = self.prog, self.dram
        vals = []
        for c in range(8):
            v = p.emit("sync",
                       lambda e, c=c, bi=bi: e.dma_start(
                           out=dr["out_t"][bi, c * 128:(c + 1) * 128, :],
                           in_=src_f32(c)),
                       inc="SO0", n=16)
            vals.append(v)
        return vals[-1]

    def xsl(self, c, j0, n):
        return self.buf["xs"][:, c * PADT + 1 + j0: c * PADT + 1 + j0 + n]

    def xsl_shift(self, c, j0, n):
        return self.buf["xs"][:, c * PADT + j0: c * PADT + j0 + n]

    def _module_ln(self, tc, bi, final, st_vals, sb_base=12288, out_base=8192,
                   stats_only=False):
        p, b, bk, pb = self.prog, self.buf, self.banks, self.psum
        aa, fa = b["aa"], b["fa"]
        t0 = 512 * tc
        SB, QB = 5, 6
        B1, B0 = 3, 4
        mu = aa[0:1, sb_base:sb_base + 1024].bitcast(F32)
        var = aa[0:1, sb_base + 1024:sb_base + 2048].bitcast(F32)
        rstd = aa[0:1, sb_base + 2048:sb_base + 3072].bitcast(F32)
        nm = aa[0:1, sb_base + 3072:sb_base + 4096].bitcast(F32)

        bk.begin_write(SB, QB)
        sqb = aa[:, out_base:out_base + 4096]
        sqv = p.emit("scalar", lambda e, sqb=sqb: e.activation(
            out=sqb, in_=fa[:, 0:4096], func=AF.Square), inc="ES")
        p.waits("tensor", [("ES", sqv)])
        for oc in range(8):
            p.emit("tensor", lambda e, oc=oc: e.matmul(
                pb[QB][0:1, 0:512], b["ones_col"][:, 0:1],
                fa[:, 512 * oc:512 * oc + 512], start=(oc == 0), stop=(oc == 7)))
            p.emit("tensor", lambda e, oc=oc, sqb=sqb: e.matmul(
                pb[SB][0:1, 0:512], b["ones_col_b"][:, 0:1],
                sqb[:, 512 * oc:512 * oc + 512], start=(oc == 0), stop=(oc == 7)),
                inc="PE" if oc == 7 else None)
        smm = p.cnt["PE"]
        bk.wrote_banks(smm, SB, QB)
        bk.read("vector", "EV", [QB], lambda e: e.tensor_scalar(
            out=mu, in0=pb[QB][0:1, 0:512], scalar1=1.0 / D, scalar2=None, op0=ALU.mult))
        bk.read("vector", "EV", [SB], lambda e: e.tensor_scalar(
            out=var, in0=pb[SB][0:1, 0:512], scalar1=1.0 / D, scalar2=None, op0=ALU.mult))
        p.emit("vector", lambda e: e.scalar_tensor_tensor(
            out=nm, in0=mu, scalar=-1.0, in1=mu, op0=ALU.mult, op1=ALU.mult), inc="EV")
        vv = p.emit("vector", lambda e: e.tensor_add(out=var, in0=var, in1=nm), inc="EV")
        p.waits("scalar", [("EV", vv)])
        p.emit("scalar", lambda e: e.activation(out=nm, in_=var, func=AF.Ln,
                                                bias=b["small"][0:1, 255:256]), inc="ES")
        sv = p.emit("scalar", lambda e: e.activation(out=rstd, in_=nm, func=AF.Exp, scale=-0.5), inc="ES")
        p.waits("vector", [("ES", sv)])
        nv = p.emit("vector", lambda e: e.scalar_tensor_tensor(
            out=nm, in0=mu, scalar=-1.0, in1=rstd, op0=ALU.mult, op1=ALU.mult), inc="EV")
        if stats_only is True:
            return
        p.waits("tensor", [("EV", nv)])
        bk.begin_write(B1, B0)
        b1mm = p.emit("tensor", lambda e: e.matmul(
            pb[B1][:, :], b["ones_row_f"][0:1, 0:128], rstd, start=True, stop=True),
            inc="PE")
        b0mm = p.emit("tensor", lambda e: e.matmul(
            pb[B0][:, :], b["ones_row_f"][0:1, 0:128], nm, start=True, stop=True),
            inc="PE")
        bk.wrote_banks(b1mm, B1)
        bk.wrote_banks(b0mm, B0)
        if stats_only == "bcast":
            bk.read("scalar", "ES", [B1], lambda e: e.copy(
                out=fa[:, 4096:4608], in_=pb[B1][:, :]))
            bk.read("scalar", "ES", [B0], lambda e: e.copy(
                out=fa[:, 4608:5120], in_=pb[B0][:, :]))
            return
        for oc in range(8):
            tslot = 4096 + 512 * (oc % 2)
            t1v = bk.read("vector", "EV", [B1], lambda e, oc=oc, tslot=tslot: e.tensor_mul(
                out=fa[:, tslot:tslot + 512], in0=fa[:, 512 * oc:512 * oc + 512],
                in1=pb[B1][:, :]))
            if not final:
                wv = bk.read("vector", "EV", [B0], lambda e, oc=oc, tslot=tslot, t0=t0: e.tensor_add(
                    out=self.xsl(oc, t0, 512), in0=fa[:, tslot:tslot + 512],
                    in1=pb[B0][:, :]))
            else:
                wv = bk.read("vector", "EV", [B0], lambda e, oc=oc, tslot=tslot: e.tensor_add(
                    out=fa[:, 512 * oc:512 * oc + 512],
                    in0=fa[:, tslot:tslot + 512], in1=pb[B0][:, :]))
                p.waits("sync", [("EV", wv)])
                lane = "SO0" if oc % 2 == 0 else "SO1"
                sval = p.emit("sync", lambda e, oc=oc, t0=t0, bi=bi: e.dma_start(
                    out=self.dram["out_t"][bi, oc * 128:(oc + 1) * 128, t0:t0 + 512],
                    in_=fa[:, 512 * oc:512 * oc + 512]),
                    inc=lane, n=16)
                st_vals.append((lane, sval))
        return p.cnt["EV"]

    def stage_a(self, bi, final=False, upto=None):
        p, b, dr, bk = self.prog, self.buf, self.dram, self.banks
        aa, fa, wa, sm, pb = b["aa"], b["fa"], b["wa"], b["small"], self.psum
        KK, VV = 0, 16384
        KVS, KSOF = 33024, 34048
        QK, YF, REC = 0, 4096, 8192

        def wload(name, off, lane):
            return p.emit("gpsimd", lambda e, n=name, o=off: e.dma_start(
                out=wa[:, o:o + 8192].rearrange("p (c m) -> p c m", c=8),
                in_=dr[n].rearrange("(c p) m -> p c m", p=128)), inc=lane, n=16)
        p.waits("gpsimd", [("PE", self.wa_pe)])
        wk_v = wload("a_wk", 0, "W0")
        wv_v = wload("a_wv", 8192, "W1")

        if True:
            p.emit("vector", lambda e: e.memset(
                aa[:, VV:VV + 16640].rearrange("p (t h s) -> p t h s", t=16, h=16)[:, :, :, 64:65],
                1.0), inc="EV")
            p.emit("vector", lambda e: e.memset(aa[:, KSOF:KSOF + 16], 0.0), inc="EV")
            p.emit("vector", lambda e: e.memset(aa[:, KVS:KVS + 1024], 0.0), inc="EV")
        vx_ready = p.cnt["EV"]

        p.waits("tensor", [("W0", wk_v), ("W1", wv_v), ("XL", self.x_ready),
                           ("EV", vx_ready)])
        exp_done = 0
        vcp_done = 0
        red_prev = 0
        for t in range(16):
            kb = (0, 1) if t % 2 == 0 else (2, 3)
            vb = (4, 5)
            bk.begin_write(*vb)
            for h2 in range(2):
                for c in range(8):
                    p.emit("tensor", lambda e, t=t, c=c, h2=h2, vb=vb: e.matmul(
                        pb[vb[h2]][:, :], self.xsl(c, 128 * t, 128),
                        wa[:, 8192 + c * 1024 + h2 * 512: 8192 + c * 1024 + (h2 + 1) * 512],
                        start=(c == 0), stop=False))
                vmm = p.emit("tensor", lambda e, h2=h2, vb=vb: e.matmul(
                    pb[vb[h2]][:, :], b["ones_row"][0:1, 0:128],
                    b["rows"][0:1, 1024 + h2 * 512: 1024 + (h2 + 1) * 512],
                    start=False, stop=True), inc="PE")
            bk.wrote_banks(vmm, *vb)
            bk.begin_write(*kb)
            for h2 in range(2):
                for c in range(8):
                    p.emit("tensor", lambda e, t=t, c=c, h2=h2, kb=kb: e.matmul(
                        pb[kb[h2]][:, :], self.xsl(c, 128 * t, 128),
                        wa[:, c * 1024 + h2 * 512: c * 1024 + (h2 + 1) * 512],
                        start=(c == 0), stop=False))
                kmm = p.emit("tensor", lambda e, h2=h2, kb=kb: e.matmul(
                    pb[kb[h2]][:, :], b["ones_row"][0:1, 0:128],
                    b["rows"][0:1, h2 * 512:(h2 + 1) * 512],
                    start=False, stop=True), inc="PE")
            bk.wrote_banks(kmm, *kb)

            for i, bvv in enumerate(vb):
                vcp_done = bk.read("vector", "EV", [bvv], lambda e, t=t, i=i, bvv=bvv: e.tensor_copy(
                    out=aa[:, VV + 1040 * t: VV + 1040 * t + 1040].rearrange(
                        "p (h s) -> p h s", h=16)[:, 8 * i:8 * i + 8, 0:64],
                    in_=pb[bvv][:, :].rearrange("p (h s) -> p h s", h=8)))
            p.waits("vector", [("ES", exp_done)])
            red = bk.read("vector", "EV", list(kb), lambda e, kb=kb: e.reduce_max(
                out=sm[:, 0:16],
                in_=self.pfull[:, 512 * kb[0]:512 * kb[0] + 1024].rearrange(
                    "p (h d) -> p h d", h=16),
                axis=AX.X, negate=True))
            sub = bk.read("vector", "EV", list(kb), lambda e, kb=kb: e.scalar_tensor_tensor(
                out=fa[:, 4096:5120].rearrange("p (h d) -> p h d", h=16),
                in0=self.pfull[:, 512 * kb[0]:512 * kb[0] + 1024].rearrange(
                    "p (h d) -> p h d", h=16),
                scalar=1.0,
                in1=sm[:, 0:16].unsqueeze(2).broadcast_to([128, 16, 64]),
                op0=ALU.mult, op1=ALU.add))
            p.waits("scalar", [("EV", sub)])
            exp_done = p.emit("scalar", lambda e, t=t: e.activation(
                out=aa[:, KK + 1024 * t: KK + 1024 * t + 1024],
                in_=fa[:, 4096:5120], func=AF.Exp))

        if upto == "p1":
            return []
        p.waits("tensor", [("ES", exp_done), ("EV", vcp_done)])
        kv_es, kv_ev = 0, 0
        for H in range(16):
            bno = H % 7
            bk.begin_write(bno)
            for t in range(16):
                kvm = p.emit("tensor", lambda e, t=t, H=H, bno=bno: e.matmul(
                    pb[bno][0:64, 0:65],
                    aa[:, KK + 1024 * t + 64 * H: KK + 1024 * t + 64 * H + 64],
                    aa[:, VV + 1040 * t + 65 * H: VV + 1040 * t + 65 * H + 65],
                    start=(t == 0), stop=(t == 15)),
                    inc="PE" if t == 15 else None)
            bk.wrote_banks(kvm, bno)
            pr, ro = H // 2, 64 * (H % 2)
            kv_es = bk.read("scalar", "ES", [bno], lambda e, pr=pr, ro=ro, bno=bno: e.copy(
                out=aa[ro:ro + 64, KVS + 128 * pr + ro: KVS + 128 * pr + ro + 64],
                in_=pb[bno][0:64, 0:64]))
            kv_ev = bk.read("vector", "EV", [bno], lambda e, H=H, ro=ro, bno=bno: e.tensor_copy(
                out=aa[ro:ro + 64, KSOF + H: KSOF + H + 1],
                in_=pb[bno][0:64, 64:65]))

        p.waits("gpsimd", [("PE", p.cnt["PE"])])
        wq_v = wload("a_wq", 0, "W0")
        wo_v = wload("a_wo", 8192, "W1")

        if upto == "p15":
            return []
        p.waits("tensor", [("ES", kv_es), ("EV", kv_ev), ("W0", wq_v), ("W1", wo_v)])
        st_vals = []
        for tc in range(4):
            t0 = 512 * tc
            for c in range(8):
                qb = c % 2
                bk.begin_write(qb)
                for kc in range(8):
                    qmm = p.emit("tensor", lambda e, c=c, kc=kc, t0=t0, qb=qb: e.matmul(
                        pb[qb][:, :], wa[:, kc * 1024 + c * 128: kc * 1024 + c * 128 + 128],
                        self.xsl(kc, t0, 512), start=(kc == 0), stop=(kc == 7)),
                        inc="PE" if kc == 7 else None)
                bk.wrote_banks(qmm, qb)
                bk.read("scalar", "ES", [qb], lambda e, c=c, qb=qb: e.activation(
                    out=aa[:, QK + 512 * c: QK + 512 * c + 512], in_=pb[qb][:, :],
                    func=AF.Exp, bias=b["pp"][:, c:c + 1], scale=1.0))
            qk_done = p.cnt["ES"]
            p.waits("tensor", [("ES", qk_done)])
            p.waits("vector", [("SO0", p.cnt["SO0"]), ("SO1", p.cnt["SO1"])])
            for c in range(8):
                db = 2 + (c % 2)
                bk.begin_write(db)
                dmm = p.emit("tensor", lambda e, c=c, db=db: e.matmul(
                    pb[db][0:2, 0:512], aa[:, KSOF + 2 * c: KSOF + 2 * c + 2],
                    aa[:, QK + 512 * c: QK + 512 * c + 512], start=True, stop=True),
                    inc="PE")
                bk.wrote_banks(dmm, db)
                bk.read("vector", "EV", [db], lambda e, c=c, db=db: e.tensor_scalar(
                    out=fa[0:2, 512 * (c % 2): 512 * (c % 2) + 512], in0=pb[db][0:2, 0:512],
                    scalar1=1e-6, scalar2=None, op0=ALU.add))
                p.emit("vector", lambda e, c=c: e.reciprocal(
                    out=fa[0:2, 1024 + 512 * (c % 2): 1024 + 512 * (c % 2) + 512],
                    in_=fa[0:2, 512 * (c % 2): 512 * (c % 2) + 512]))
                p.emit("vector", lambda e, c=c: e.tensor_copy(
                    out=aa[0:2, REC + 512 * c: REC + 512 * c + 512],
                    in_=fa[0:2, 1024 + 512 * (c % 2): 1024 + 512 * (c % 2) + 512]), inc="EV")
            rec_done = p.cnt["EV"]
            p.waits("tensor", [("EV", rec_done)])
            for c in range(8):
                yb = 4 + (c % 2)
                bb6 = 6
                bk.begin_write(yb)
                ymm = p.emit("tensor", lambda e, c=c, yb=yb: e.matmul(
                    pb[yb][:, :], aa[:, KVS + 128 * c: KVS + 128 * c + 128],
                    aa[:, QK + 512 * c: QK + 512 * c + 512], start=True, stop=True),
                    inc="PE")
                bk.wrote_banks(ymm, yb)
                bk.begin_write(bb6)
                bmm = p.emit("tensor", lambda e, c=c, bb6=bb6: e.matmul(
                    pb[bb6][:, :], b["sel2"][0:2, :],
                    aa[0:2, REC + 512 * c: REC + 512 * c + 512],
                    start=True, stop=True), inc="PE")
                bk.wrote_banks(bmm, bb6)
                bc_ev = bk.read("scalar", "ES", [bb6], lambda e, c=c, bb6=bb6: e.copy(
                    out=fa[:, 4096 + 512 * (c % 2): 4096 + 512 * (c % 2) + 512],
                    in_=pb[bb6][:, :]))
                p.waits("vector", [("ES", bc_ev)])
                bk.read("vector", "EV", [yb], lambda e, c=c, yb=yb: e.tensor_mul(
                    out=aa[:, YF + 512 * c: YF + 512 * c + 512],
                    in0=pb[yb][:, :],
                    in1=fa[:, 4096 + 512 * (c % 2): 4096 + 512 * (c % 2) + 512]))
            yf_done = p.cnt["EV"]
            p.waits("tensor", [("EV", yf_done)])
            for oc in range(8):
                ob = (0, 1)[oc % 2]
                bk.begin_write(ob)
                for kc in range(8):
                    omm = p.emit("tensor", lambda e, oc=oc, kc=kc, ob=ob: e.matmul(
                        pb[ob][:, :], wa[:, 8192 + kc * 1024 + oc * 128: 8192 + kc * 1024 + oc * 128 + 128],
                        aa[:, YF + 512 * kc: YF + 512 * kc + 512],
                        start=(kc == 0), stop=(kc == 7)),
                        inc="PE" if kc == 7 else None)
                bk.wrote_banks(omm, ob)
                bk.read("vector", "EV", [ob], lambda e, oc=oc, ob=ob, t0=t0: e.scalar_tensor_tensor(
                    out=fa[:, 512 * oc:512 * oc + 512], in0=pb[ob][:, :],
                    scalar=b["pp"][:, 8 + oc:9 + oc], in1=self.xsl(oc, t0, 512),
                    op0=ALU.add, op1=ALU.add))
            h_done = p.cnt["EV"]
            if upto == "p2h":
                return []
            p.waits("scalar", [("EV", h_done)])
            p.waits("tensor", [("EV", h_done)])
            so = {"p2s": True, "p2b": "bcast"}.get(upto, False)
            self._module_ln(tc, bi, final, st_vals, stats_only=so)
            if upto in ("p2s", "p2b"):
                return []
        self.wa_pe = p.cnt["PE"]
        return st_vals

    def stage_l1a(self, bi, final=False, upto=None):
        p, b, dr, bk = self.prog, self.buf, self.dram, self.banks
        aa, fa, wa, sm, pb = b["aa"], b["fa"], b["wa"], b["small"], self.psum
        QT, KT, VS = 0, 4096, 8192
        AT, ATT, ATG = 12288, 14592, 15616
        KG, VG, YT = 16640, 16768, 17792

        def wload(name, off, lane):
            return p.emit("gpsimd", lambda e, n=name, o=off: e.dma_start(
                out=wa[:, o:o + 8192].rearrange("p (c m) -> p c m", c=8),
                in_=dr[n].rearrange("(c p) m -> p c m", p=128)), inc=lane, n=16)
        p.waits("gpsimd", [("PE", self.wa_pe)])
        wq_v = wload("l1_wq", 0, "W0")
        wk_v = wload("l1_wk", 8192, "W1")
        wv_v = wload("l1_wv", 16384, "W2")
        p.waits("vector", [("SO0", p.cnt["SO0"]), ("SO1", p.cnt["SO1"])])
        p.waits("scalar", [("SO0", p.cnt["SO0"]), ("SO1", p.cnt["SO1"])])
        p.waits("tensor", [("W0", wq_v), ("W1", wk_v), ("W2", wv_v),
                           ("XL", self.x_ready)])

        atl_pe = [0, 0]
        self._l1a_att2_pe = 0
        pend2 = None

        def l1a_tail(st):
            g_, tp_, t0_, at0_, par_, sc_ev_ = st
            p.waits("tensor", [("EV", sc_ev_)])
            bk.begin_write(7)
            for h8 in range(8):
                mmv = p.emit("tensor", lambda e, h8=h8, at0_=at0_: e.transpose(
                    out=pb[7][:, 128 * h8:128 * h8 + 128],
                    in_=aa[:, at0_ + 144 * h8: at0_ + 144 * h8 + 128],
                    identity=b["ident"][:, :]), inc="PE" if h8 == 7 else None)
            bk.wrote_banks(mmv, 7)
            p.waits("vector", [("PE", self._l1a_att2_pe)])
            bk.read("vector", "EV", [7], lambda e: e.tensor_copy(
                out=aa[:, ATT:ATT + 1024], in_=pb[7][:, :]))
            bk.begin_write(7)
            for h8 in range(8):
                mmv = p.emit("tensor", lambda e, h8=h8, at0_=at0_: e.transpose(
                    out=pb[7][0:16, 128 * h8:128 * h8 + 128],
                    in_=aa[:, at0_ + 144 * h8 + 128: at0_ + 144 * h8 + 144],
                    identity=b["ident"][:, :]), inc="PE" if h8 == 7 else None)
            bk.wrote_banks(mmv, 7)
            atl_pe[par_] = mmv
            tg_ev = bk.read("vector", "EV", [7], lambda e: e.tensor_copy(
                out=aa[0:16, ATG:ATG + 1024], in_=pb[7][0:16, :]))
            p.waits("tensor", [("EV", tg_ev)])
            for h8 in range(8):
                h = 8 * g_ + h8
                yb = 4 + h8 % 2
                bk.begin_write(yb)
                p.emit("tensor", lambda e, h=h, h8=h8, tp_=tp_, yb=yb: e.matmul(
                    pb[yb][0:64, 0:128], aa[:, VS + 1024 * tp_ + 64 * h: VS + 1024 * tp_ + 64 * h + 64],
                    aa[:, ATT + 128 * h8: ATT + 128 * h8 + 128], start=True, stop=False))
                mmv = p.emit("tensor", lambda e, h=h, h8=h8, yb=yb: e.matmul(
                    pb[yb][0:64, 0:128], aa[0:16, VG + 64 * h: VG + 64 * h + 64],
                    aa[0:16, ATG + 128 * h8: ATG + 128 * h8 + 128],
                    start=False, stop=True), inc="PE")
                bk.wrote_banks(mmv, yb)
                self._l1a_att2_pe = mmv
                co, ro = h // 2, 64 * (h % 2)
                bk.read("scalar", "ES", [yb], lambda e, co=co, ro=ro, t0_=t0_, tp_=tp_, yb=yb: e.copy(
                    out=aa[ro:ro + 64, YT + 2048 * co + t0_ + 128 * tp_: YT + 2048 * co + t0_ + 128 * tp_ + 128],
                    in_=pb[yb][0:64, 0:128]))

        last_qkv_mm = 0
        for tc in range(4):
            t0 = 512 * tc
            for which, base, wofs in (("q", QT, 0), ("k", KT, 8192)):
                for c in range(8):
                    qb = c % 2
                    bk.begin_write(qb)
                    for kc in range(8):
                        mmv = p.emit("tensor", lambda e, c=c, kc=kc, t0=t0, qb=qb, wofs=wofs: e.matmul(
                            pb[qb][:, :], wa[:, wofs + kc * 1024 + c * 128: wofs + kc * 1024 + c * 128 + 128],
                            self.xsl(kc, t0, 512), start=(kc == 0), stop=(kc == 7)),
                            inc="PE" if kc == 7 else None)
                    bk.wrote_banks(mmv, qb)
                    last_qkv_mm = mmv
                    if which == "q":
                        bk.read("vector", "EV", [qb], lambda e, c=c, qb=qb, base=base: e.tensor_scalar(
                            out=aa[:, base + 512 * c: base + 512 * c + 512],
                            in0=pb[qb][:, :], scalar1=0.125, scalar2=None, op0=ALU.mult))
                    else:
                        bk.read("scalar", "ES", [qb], lambda e, c=c, qb=qb, base=base: e.copy(
                            out=aa[:, base + 512 * c: base + 512 * c + 512],
                            in_=pb[qb][:, :]))
            for tt in range(4):
                for half in range(2):
                    vb2 = 2 + (2 * tt + half) % 2
                    bk.begin_write(vb2)
                    for kc in range(8):
                        mmv = p.emit("tensor", lambda e, tt=tt, half=half, kc=kc, t0=t0, vb2=vb2: e.matmul(
                            pb[vb2][:, :], self.xsl(kc, t0 + 128 * tt, 128),
                            wa[:, 16384 + kc * 1024 + half * 512: 16384 + kc * 1024 + half * 512 + 512],
                            start=(kc == 0), stop=(kc == 7)),
                            inc="PE" if kc == 7 else None)
                    bk.wrote_banks(mmv, vb2)
                    last_qkv_mm = mmv
                    bk.read("vector", "EV", [vb2], lambda e, tt=tt, half=half, vb2=vb2: e.tensor_copy(
                        out=aa[:, VS + 1024 * tt + 512 * half: VS + 1024 * tt + 512 * half + 512],
                        in_=pb[vb2][:, :]))
            qkv_ev, qkv_es = p.cnt["EV"], p.cnt["ES"]
            if tc == 0:
                p.waits("vector", [("ES", qkv_es)])
                p.emit("vector", lambda e: e.tensor_copy(
                    out=aa[:, KG:KG + 128].rearrange("p (c s) -> p c s", c=8),
                    in_=aa[:, KT:KT + 4096].rearrange("p (c s) -> p c s", c=8)[:, :, 0:16]))
                qkv_ev = p.emit("vector", lambda e: e.tensor_copy(
                    out=aa[0:16, VG:VG + 1024], in_=aa[0:16, VS:VS + 1024]), inc="EV")

            for tp in range(4):
                pglob = tc * 4 + tp
                mvar = 144 if pglob == 0 else 0
                for g in range(2):
                    par = (2 * pglob + g) % 2
                    at0 = AT + 1152 * par
                    p.waits("tensor", [("EV", qkv_ev), ("ES", qkv_es)])
                    bk.begin_write(0, 1, 2, 3)
                    for h8 in (0, 2, 4, 6, 1, 3, 5, 7):
                        h = 8 * g + h8
                        c, ro = h // 2, 64 * (h % 2)
                        bankno, so = h8 // 2, 256 * (h8 % 2)
                        p.emit("tensor", lambda e, c=c, ro=ro, bankno=bankno, so=so, t0=t0, tp=tp: e.matmul(
                            pb[bankno][:, so:so + 128],
                            aa[ro:ro + 64, QT + 512 * c + 128 * tp: QT + 512 * c + 128 * tp + 128],
                            aa[ro:ro + 64, KT + 512 * c + 128 * tp: KT + 512 * c + 128 * tp + 128],
                            start=True, stop=True))
                        mmv = p.emit("tensor", lambda e, c=c, ro=ro, bankno=bankno, so=so, tp=tp: e.matmul(
                            pb[bankno][:, so + 128:so + 144],
                            aa[ro:ro + 64, QT + 512 * c + 128 * tp: QT + 512 * c + 128 * tp + 128],
                            aa[ro:ro + 64, KG + 16 * c: KG + 16 * c + 16],
                            start=True, stop=True), inc="PE" if h8 == 7 else None)
                    bk.wrote_banks(mmv, 0, 1, 2, 3)
                    if pend2 is not None:
                        l1a_tail(pend2)
                        pend2 = None
                    mk_ev = bk.read("vector", "EV", [0, 1, 2, 3], lambda e, mvar=mvar: e.tensor_tensor(
                        out=fa[:, 0:1152].rearrange("p (h s) -> p h s", h=8),
                        in0=self.pfull[:, 0:2048].rearrange("p (h s) -> p h s", h=8)[:, :, 0:144],
                        in1=b["mask_l1"][:, mvar:mvar + 144].unsqueeze(1).broadcast_to([128, 8, 144]),
                        op=ALU.add))
                    rx = p.emit("vector", lambda e: e.reduce_max(
                        out=sm[:, 0:8], in_=fa[:, 0:1152].rearrange("p (h s) -> p h s", h=8),
                        axis=AX.X, negate=True), inc="EV")
                    p.waits("scalar", [("EV", rx), ("PE", atl_pe[par])])
                    exp_last = 0
                    for h8 in range(8):
                        exp_last = p.emit("scalar", lambda e, h8=h8, at0=at0: e.activation(
                            out=aa[:, at0 + 144 * h8: at0 + 144 * h8 + 144],
                            in_=fa[:, 144 * h8: 144 * h8 + 144], func=AF.Exp,
                            bias=sm[:, h8:h8 + 1], scale=1.0,
                            accum_out=sm[:, 8 + h8:9 + h8]), inc="ES")
                    p.waits("vector", [("ES", exp_last)])
                    p.emit("vector", lambda e: e.reciprocal(out=sm[:, 16:24], in_=sm[:, 8:16]))
                    sc_ev = p.emit("vector", lambda e, at0=at0: e.tensor_mul(
                        out=aa[:, at0:at0 + 1152].rearrange("p (h s) -> p h s", h=8),
                        in0=aa[:, at0:at0 + 1152].rearrange("p (h s) -> p h s", h=8),
                        in1=sm[:, 16:24].unsqueeze(2).broadcast_to([128, 8, 144])), inc="EV")
                    pend2 = (g, tp, t0, at0, par, sc_ev)
            l1a_tail(pend2)
            pend2 = None

        if upto is not None:
            return []
        p.waits("gpsimd", [("PE", p.cnt["PE"])])
        wo_v = wload("l1_wo", 0, "W0")
        p.waits("tensor", [("W0", wo_v), ("ES", p.cnt["ES"])])
        st_vals = []
        for tc in range(4):
            t0 = 512 * tc
            for oc in range(8):
                ob = oc % 2
                bk.begin_write(ob)
                for kc in range(8):
                    omm = p.emit("tensor", lambda e, oc=oc, kc=kc, t0=t0, ob=ob: e.matmul(
                        pb[ob][:, :], wa[:, kc * 1024 + oc * 128: kc * 1024 + oc * 128 + 128],
                        aa[:, YT + 2048 * kc + t0: YT + 2048 * kc + t0 + 512],
                        start=(kc == 0), stop=(kc == 7)),
                        inc="PE" if kc == 7 else None)
                bk.wrote_banks(omm, ob)
                bk.read("vector", "EV", [ob], lambda e, oc=oc, ob=ob, t0=t0: e.tensor_add(
                    out=fa[:, 512 * oc:512 * oc + 512], in0=pb[ob][:, :],
                    in1=self.xsl(oc, t0, 512)))
            h_done = p.cnt["EV"]
            p.waits("scalar", [("EV", h_done)])
            p.waits("tensor", [("EV", h_done)])
            self._module_ln(tc, bi, final, st_vals)
        self.wa_pe = p.cnt["PE"]
        return st_vals


    def stage_moe(self, bi, final=False):
        p, b, dr, bk = self.prog, self.buf, self.dram, self.banks
        aa, fa, wa, sm, pb = b["aa"], b["fa"], b["wa"], b["small"], self.psum
        ACC = 0
        XTM, XG, HP = 8192, 12288, 14336
        OG, OGT, PU, PG, PT = 18432, 20480, 22528, 23552, 24576
        B1C, GT = 25600, 26624
        GROWG, GTMF, POSM, MTM = 28672, 28928, 29184, 29248
        OUTB, STATS = 8192, 12288
        accf = aa[:, 0:8192].bitcast(F32)
        C = 256

        p.waits("tensor", [("XL", self.x_ready), ("CG", self.ld_sem_vals[1])])
        p.waits("gpsimd", [("PE", self.wa_pe)])
        gv = p.emit("gpsimd", lambda e: e.dma_start(
            out=wa[:, 0:64].rearrange("p (c m) -> p c m", c=8),
            in_=dr["m_gate"].rearrange("(c p) m -> p c m", p=128)), inc="W3", n=16)
        p.waits("sync", [("PE", p.cnt["PE"]), ("EV", p.cnt["EV"]),
                         ("ES", p.cnt["ES"])])
        b1v = p.emit("sync", lambda e: e.dma_start(
            out=aa[:, B1C:B1C + 512].bitcast(F32)[:, 0:128],
            in_=dr["m_b1_pc2"][:, :]), inc="W5", n=16)
        p.waits("tensor", [("W3", gv)])
        p.waits("vector", [("SO0", p.cnt["SO0"]), ("SO1", p.cnt["SO1"])])
        p.waits("scalar", [("SO0", p.cnt["SO0"]), ("SO1", p.cnt["SO1"])])
        for t in range(16):
            gb2 = t % 2
            bk.begin_write(gb2)
            for kc in range(8):
                mmv = p.emit("tensor", lambda e, t=t, kc=kc, gb2=gb2: e.matmul(
                    pb[gb2][:, 0:8], self.xsl(kc, 128 * t, 128),
                    wa[:, 8 * kc: 8 * kc + 8],
                    start=(kc == 0), stop=(kc == 7)), inc="PE" if kc == 7 else None)
            bk.wrote_banks(mmv, gb2)
            bk.read("vector", "EV", [gb2], lambda e, gb2=gb2: e.tensor_copy(
                out=fa[:, 0:8], in_=pb[gb2][:, 0:8]))
            p.emit("vector", lambda e: e.reduce_max(
                out=sm[:, 40:41], in_=fa[:, 0:8], axis=AX.X, negate=True))
            p.emit("vector", lambda e: e.tensor_scalar(
                out=sm[:, 41:42], in0=sm[:, 40:41], scalar1=-1.0, scalar2=None, op0=ALU.mult))
            p.emit("vector", lambda e: e.tensor_scalar(
                out=fa[:, 8:16], in0=fa[:, 0:8], scalar1=sm[:, 41:42], scalar2=None, op0=ALU.is_equal))
            p.emit("vector", lambda e: e.scalar_tensor_tensor(
                out=fa[:, 8:16], in0=fa[:, 8:16], scalar=-1e30, in1=fa[:, 0:8],
                op0=ALU.mult, op1=ALU.add))
            p.emit("vector", lambda e: e.reduce_max(
                out=sm[:, 42:43], in_=fa[:, 8:16], axis=AX.X))
            msk_v = p.emit("vector", lambda e: e.tensor_scalar(
                out=fa[:, 8:16], in0=fa[:, 0:8], scalar1=sm[:, 42:43], scalar2=None, op0=ALU.is_ge))
            p.waits("scalar", [("EV", msk_v)])
            ex_v = p.emit("scalar", lambda e: e.activation(
                out=fa[:, 16:24], in_=fa[:, 0:8], func=AF.Exp, bias=sm[:, 40:41]))
            p.waits("vector", [("ES", ex_v)])
            p.emit("vector", lambda e: e.tensor_mul(
                out=fa[:, 16:24], in0=fa[:, 16:24], in1=fa[:, 8:16]))
            p.emit("vector", lambda e: e.reduce_sum(
                out=sm[:, 43:44], in_=fa[:, 16:24], axis=AX.X))
            p.emit("vector", lambda e: e.reciprocal(
                out=sm[:, 44:45], in_=sm[:, 43:44]))
            p.emit("vector", lambda e, t=t: e.tensor_scalar(
                out=aa[:, GTMF:GTMF + 512].bitcast(F32)[:, 8 * t:8 * t + 8],
                in0=fa[:, 16:24], scalar1=sm[:, 44:45], scalar2=None, op0=ALU.mult))
            gt_v = p.emit("vector", lambda e, t=t: e.tensor_copy(
                out=fa[:, 24:32].bitcast(BF16)[:, 0:8],
                in_=aa[:, GTMF:GTMF + 512].bitcast(F32)[:, 8 * t:8 * t + 8]))
            p.waits("tensor", [("EV", gt_v)])
            bk.begin_write(7)
            tv = p.emit("tensor", lambda e: e.transpose(
                out=pb[7][0:8, 0:128], in_=fa[:, 24:32].bitcast(BF16)[:, 0:8],
                identity=b["ident"][:, :]), inc="PE")
            bk.wrote_banks(tv, 7)
            bk.read("vector", "EV", [7], lambda e, t=t: e.tensor_copy(
                out=aa[0:8, GT + 128 * t: GT + 128 * t + 128], in_=pb[7][0:8, 0:128]))
        gt_done = p.cnt["EV"]

        pslot_pe = [p.cnt["PE"], 0, 0]
        pi = 0
        st_vals = []
        xtm_pe = 0
        pup_pe = [0, 0]
        mst = {"ogt_pe": 0, "ptbuf_pe": [0, 0]}
        for tc in range(4):
            t0 = 512 * tc
            p.waits("tensor", [("EV", gt_done)])
            for tt in range(4):
                bk.begin_write(7)
                for dc in range(8):
                    tv = p.emit("tensor", lambda e, tt=tt, dc=dc, t0=t0: e.transpose(
                        out=pb[7][:, 128 * dc:128 * dc + 128],
                        in_=self.xsl(dc, t0 + 128 * tt, 128),
                        identity=b["ident"][:, :]), inc="PE" if dc == 7 else None)
                bk.wrote_banks(tv, 7)
                p.waits("vector", [("PE", xtm_pe)])
                bk.read("vector", "EV", [7], lambda e, tt=tt: e.tensor_copy(
                    out=aa[:, XTM + 1024 * tt: XTM + 1024 * tt + 1024], in_=pb[7][:, :]))
            mp_ev = p.emit("vector", lambda e, tc=tc: e.tensor_scalar(
                out=fa[0:8, 4096:4608], in0=aa[0:8, GT + 512 * tc: GT + 512 * tc + 512],
                scalar1=0.0, scalar2=None, op0=ALU.is_gt))
            mp_ev = p.emit("vector", lambda e: e.tensor_tensor_scan(
                out=fa[0:8, 4608:5120], data0=fa[0:8, 4096:4608],
                data1=b["ones_col"][0:8, 0:1].broadcast_to([8, 512]),
                initial=0.0, op0=ALU.add, op1=ALU.mult))
            mp_ev = p.emit("vector", lambda e: e.tensor_sub(
                out=fa[0:8, 4608:5120], in0=fa[0:8, 4608:5120], in1=fa[0:8, 4096:4608]))
            mt_ev = p.emit("vector", lambda e, tc=tc: e.tensor_scalar(
                out=aa[:, MTM:MTM + 128].bitcast(F32)[:, 0:32],
                in0=aa[:, GTMF:GTMF + 512].bitcast(F32)[:, 32 * tc:32 * tc + 32],
                scalar1=0.0, scalar2=None, op0=ALU.is_gt))
            p.waits("tensor", [("EV", mp_ev)])
            bk.begin_write(6)
            for tt in range(4):
                tv = p.emit("tensor", lambda e, tt=tt: e.transpose(
                    out=pb[6].bitcast(F32)[0:0 + 128, 8 * tt:8 * tt + 8] if False else pb[6][:, 8 * tt:8 * tt + 8],
                    in_=fa[0:8, 4608 + 128 * tt:4608 + 128 * tt + 128],
                    identity=b["identf"][0:8, 0:8]), inc="PE" if tt == 3 else None)
            bk.wrote_banks(tv, 6)
            ps_ev = bk.read("vector", "EV", [6], lambda e: e.tensor_copy(
                out=aa[:, POSM:POSM + 64].bitcast(F32)[:, 0:32], in_=pb[6][:, 0:32]))
            for oc in range(8):
                ib = 2 + (oc % 2)
                bk.begin_write(ib)
                mmv = p.emit("tensor", lambda e, oc=oc, t0=t0, ib=ib: e.matmul(
                    pb[ib][:, :], b["mb2"][0:8, 128 * oc:128 * oc + 128],
                    aa[0:8, GT + t0: GT + t0 + 512],
                    start=True, stop=True), inc="PE")
                bk.wrote_banks(mmv, ib)
                bk.read("vector", "EV", [ib], lambda e, oc=oc, ib=ib: e.tensor_copy(
                    out=accf[:, 512 * oc: 512 * oc + 512], in_=pb[ib][:, :]))
            pend3 = None

            def emit_scatter(par, oev, oes):
                PTb = PT if par == 0 else OG
                p.waits("tensor", [("EV", oev), ("ES", oes)])
                for dc in range(8):
                    sb3 = 4 + (dc % 2)
                    bk.begin_write(sb3)
                    for st2 in range(2):
                        mmv = p.emit("tensor", lambda e, dc=dc, st2=st2, sb3=sb3, PTb=PTb: e.matmul(
                            pb[sb3][:, :],
                            aa[:, OGT + 1024 * st2 + 128 * dc: OGT + 1024 * st2 + 128 * dc + 128],
                            aa[:, PTb + 512 * st2: PTb + 512 * st2 + 512],
                            start=(st2 == 0), stop=(st2 == 1)),
                            inc="PE" if st2 == 1 else None)
                    bk.wrote_banks(mmv, sb3)
                    mst["ogt_pe"] = mmv
                    mst["ptbuf_pe"][par] = mmv
                    bk.read("vector", "EV", [sb3], lambda e, dc=dc, sb3=sb3: e.tensor_add(
                        out=accf[:, 512 * dc: 512 * dc + 512],
                        in0=pb[sb3][:, :],
                        in1=accf[:, 512 * dc: 512 * dc + 512]))

            for e_i in range(8):
                pls = []
                for pnl in range(3):
                    slot = pi % 3
                    lane = f"W{slot}"
                    p.waits("gpsimd", [("PE", pslot_pe[slot])])
                    if pnl < 2:
                        v = p.emit("gpsimd", lambda e, e_i=e_i, pnl=pnl, slot=slot: e.dma_start(
                            out=wa[:, 8192 * slot:8192 * slot + 8192].rearrange("p (c m) -> p c m", c=8),
                            in_=dr["m_w1"][e_i].rearrange("(c p) m -> p c m", p=128)[:, :, 1024 * pnl:1024 * pnl + 1024]),
                            inc=lane, n=16)
                    else:
                        v = p.emit("gpsimd", lambda e, e_i=e_i, pnl=pnl, slot=slot: e.dma_start(
                            out=wa[:, 8192 * slot:8192 * slot + 8192].rearrange("p (c m) -> p c m", c=16),
                            in_=dr["m_w2"][e_i].rearrange("(c p) m -> p c m", p=128)[:, :, 0:512]),
                            inc=lane, n=16)
                    pls.append((slot, lane, v))
                    pi += 1
                p.waits("vector", [("EV", ps_ev), ("PE", max(pup_pe))])
                pv = 0
                for tt in range(4):
                    gtt = 4 * tc + tt
                    pv = p.emit("vector", lambda e, tt=tt, e_i=e_i: e.tensor_scalar(
                        out=aa[:, PU + 256 * tt: PU + 256 * tt + 256],
                        in0=b["iota256"][:, :],
                        scalar1=aa[:, POSM:POSM + 64].bitcast(F32)[:, 8 * tt + e_i:8 * tt + e_i + 1],
                        scalar2=aa[:, MTM:MTM + 128].bitcast(F32)[:, 8 * tt + e_i:8 * tt + e_i + 1],
                        op0=ALU.is_equal, op1=ALU.mult))
                    pv = p.emit("vector", lambda e, tt=tt, e_i=e_i, gtt=gtt: e.tensor_scalar(
                        out=aa[:, PG + 256 * tt: PG + 256 * tt + 256],
                        in0=b["iota256"][:, :],
                        scalar1=aa[:, POSM:POSM + 64].bitcast(F32)[:, 8 * tt + e_i:8 * tt + e_i + 1],
                        scalar2=aa[:, GTMF:GTMF + 512].bitcast(F32)[:, 8 * gtt + e_i:8 * gtt + e_i + 1],
                        op0=ALU.is_equal, op1=ALU.mult))
                p.waits("tensor", [("EV", pv)])
                for dc in range(8):
                    gb3 = dc % 2
                    bk.begin_write(gb3)
                    for tt in range(4):
                        mmv = p.emit("tensor", lambda e, dc=dc, tt=tt, gb3=gb3: e.matmul(
                            pb[gb3][:, 0:256],
                            aa[:, XTM + 1024 * tt + 128 * dc: XTM + 1024 * tt + 128 * dc + 128],
                            aa[:, PU + 256 * tt: PU + 256 * tt + 256],
                            start=(tt == 0), stop=(tt == 3)),
                            inc="PE" if tt == 3 else None)
                    bk.wrote_banks(mmv, gb3)
                    if dc >= 6:
                        xtm_pe = mmv
                    pup_pe[0] = mmv
                    eng = ("vector", "EV") if dc % 2 == 0 else ("scalar", "ES")
                    if eng[0] == "vector":
                        bk.read("vector", "EV", [gb3], lambda e, dc=dc, gb3=gb3: e.tensor_copy(
                            out=aa[:, XG + 256 * dc: XG + 256 * dc + 256], in_=pb[gb3][:, 0:256]))
                    else:
                        bk.read("scalar", "ES", [gb3], lambda e, dc=dc, gb3=gb3: e.copy(
                            out=aa[:, XG + 256 * dc: XG + 256 * dc + 256], in_=pb[gb3][:, 0:256]))
                xg_ev, xg_es = p.cnt["EV"], p.cnt["ES"]
                par = e_i % 2
                PTb = PT if par == 0 else OG
                p.waits("tensor", [("EV", pv)])
                p.waits("vector", [("PE", mst["ptbuf_pe"][par])])
                for st2 in range(2):
                    bk.begin_write(7)
                    for tt in range(4):
                        tv = p.emit("tensor", lambda e, st2=st2, tt=tt: e.transpose(
                            out=pb[7][:, 128 * tt:128 * tt + 128],
                            in_=aa[:, PG + 256 * tt + 128 * st2: PG + 256 * tt + 128 * st2 + 128],
                            identity=b["ident"][:, :]), inc="PE" if tt == 3 else None)
                    bk.wrote_banks(tv, 7)
                    pup_pe[1] = tv
                    bk.read("vector", "EV", [7], lambda e, st2=st2, PTb=PTb: e.tensor_copy(
                        out=aa[:, PTb + 512 * st2: PTb + 512 * st2 + 512], in_=pb[7][:, 0:512]))
                if pend3 is not None:
                    emit_scatter(*pend3)
                    pend3 = None
                p.waits("tensor", [(pls[0][1], pls[0][2]), ("EV", xg_ev), ("ES", xg_es),
                                   ("W5", b1v)])
                for hdc in range(16):
                    hb = hdc % 4
                    slot = pls[0][0] if hdc < 8 else pls[1][0]
                    if hdc == 8:
                        p.waits("tensor", [(pls[1][1], pls[1][2])])
                    bk.begin_write(hb)
                    for kc in range(8):
                        mmv = p.emit("tensor", lambda e, hdc=hdc, kc=kc, slot=slot, hb=hb: e.matmul(
                            pb[hb][:, 0:256],
                            wa[:, 8192 * slot + 1024 * kc + 128 * (hdc % 8): 8192 * slot + 1024 * kc + 128 * (hdc % 8) + 128],
                            aa[:, XG + 256 * kc: XG + 256 * kc + 256],
                            start=(kc == 0), stop=(kc == 7)),
                            inc="PE" if kc == 7 else None)
                    bk.wrote_banks(mmv, hb)
                    if hdc == 7:
                        pslot_pe[pls[0][0]] = mmv
                    if hdc == 15:
                        pslot_pe[pls[1][0]] = mmv
                    if hdc % 2 == 0:
                        bk.read("vector", "EV", [hb], lambda e, hdc=hdc, hb=hb, e_i=e_i: e.tensor_scalar(
                            out=aa[:, HP + 256 * hdc: HP + 256 * hdc + 256],
                            in0=pb[hb][:, 0:256],
                            scalar1=aa[:, B1C:B1C + 512].bitcast(F32)[:, 16 * e_i + hdc:16 * e_i + hdc + 1],
                            scalar2=0.0, op0=ALU.add, op1=ALU.max))
                    else:
                        bk.read("scalar", "ES", [hb], lambda e, hdc=hdc, hb=hb, e_i=e_i: e.activation(
                            out=aa[:, HP + 256 * hdc: HP + 256 * hdc + 256],
                            in_=pb[hb][:, 0:256], func=AF.Relu,
                            bias=aa[:, B1C:B1C + 512].bitcast(F32)[:, 16 * e_i + hdc:16 * e_i + hdc + 1]))
                hp_ev, hp_es = p.cnt["EV"], p.cnt["ES"]
                slot = pi % 3
                lane = f"W{slot}"
                p.waits("gpsimd", [("PE", pslot_pe[slot])])
                v = p.emit("gpsimd", lambda e, e_i=e_i, slot=slot: e.dma_start(
                    out=wa[:, 8192 * slot:8192 * slot + 8192].rearrange("p (c m) -> p c m", c=16),
                    in_=dr["m_w2"][e_i].rearrange("(c p) m -> p c m", p=128)[:, :, 512:1024]),
                    inc=lane, n=16)
                pls.append((slot, lane, v))
                pi += 1
                p.waits("tensor", [(pls[2][1], pls[2][2]), ("EV", hp_ev), ("ES", hp_es)])
                for cb in range(4):
                    nh, st2 = cb // 2, cb % 2
                    ob = 2 + cb
                    pl = pls[2] if nh == 0 else pls[3]
                    if cb == 2:
                        p.waits("tensor", [(pls[3][1], pls[3][2])])
                    bk.begin_write(ob)
                    for kc in range(16):
                        mmv = p.emit("tensor", lambda e, kc=kc, st2=st2, slot=pl[0], ob=ob: e.matmul(
                            pb[ob][:, 0:512],
                            aa[:, HP + 256 * kc + 128 * st2: HP + 256 * kc + 128 * st2 + 128],
                            wa[:, 8192 * slot + 512 * kc: 8192 * slot + 512 * kc + 512],
                            start=(kc == 0), stop=(kc == 15)),
                            inc="PE" if kc == 15 else None)
                    bk.wrote_banks(mmv, ob)
                    if cb == 1:
                        pslot_pe[pls[2][0]] = mmv
                    if cb == 3:
                        pslot_pe[pls[3][0]] = mmv
                    if cb == 0:
                        p.waits("vector", [("PE", mst["ogt_pe"])])
                    if cb == 1:
                        p.waits("scalar", [("PE", mst["ogt_pe"])])
                    if cb % 2 == 0:
                        bk.read("vector", "EV", [ob], lambda e, nh=nh, st2=st2, ob=ob: e.tensor_copy(
                            out=aa[:, OGT + 1024 * st2 + 512 * nh: OGT + 1024 * st2 + 512 * nh + 512],
                            in_=pb[ob][:, 0:512]))
                    else:
                        bk.read("scalar", "ES", [ob], lambda e, nh=nh, st2=st2, ob=ob: e.copy(
                            out=aa[:, OGT + 1024 * st2 + 512 * nh: OGT + 1024 * st2 + 512 * nh + 512],
                            in_=pb[ob][:, 0:512]))
                ogt_ev, ogt_es = p.cnt["EV"], p.cnt["ES"]
                pend3 = (par, ogt_ev, ogt_es)
            emit_scatter(*pend3)
            pend3 = None
            acc_done = p.cnt["EV"]
            p.waits("vector", [("SO0", p.cnt["SO0"]), ("SO1", p.cnt["SO1"])])
            for oc in range(8):
                hv = p.emit("vector", lambda e, oc=oc, t0=t0: e.tensor_add(
                    out=fa[:, 512 * oc:512 * oc + 512],
                    in0=accf[:, 512 * oc: 512 * oc + 512],
                    in1=self.xsl(oc, t0, 512)))
            p.waits("scalar", [("EV", hv)])
            p.waits("tensor", [("EV", hv)])
            self._module_ln(tc, bi, final, st_vals, sb_base=STATS, out_base=OUTB)
        self.wa_pe = p.cnt["PE"]
        self.moe_b1_pe = p.cnt["PE"]
        self.moe_b1_ev = p.cnt["EV"]
        self.moe_b1_es = p.cnt["ES"]
        return st_vals

    def stage_f(self, bi, final=False, upto=None):
        p, b, dr, bk = self.prog, self.buf, self.dram, self.banks
        aa, fa, wa, sm, pb = b["aa"], b["fa"], b["wa"], b["small"], self.psum
        VS, QT, KT, YW = 0, 8192, 16384, 24576
        ATN, ATT, QR = 26624, 29184, 30464
        DG = 32512
        H2N, GL, H3 = 0, 4096, 8192
        XIN, STATS2, OUTB2 = 20480, 26624, 30720
        pp = b["pp"]

        p.waits("vector", [("SO0", p.cnt["SO0"]), ("SO1", p.cnt["SO1"])])
        p.waits("scalar", [("SO0", p.cnt["SO0"]), ("SO1", p.cnt["SO1"])])
        p.waits("sync", [("XL", self.x_ready), ("ST", getattr(self, "f_xin_st", 0)),
                         ("EV", p.cnt["EV"])])
        for c in range(8):
            xin_st = p.emit("sync", lambda e, c=c: e.dma_start(
                out=dr["xin_d"][c, :, :], in_=self.xsl(c, 0, T)), inc="ST", n=16)
        self.f_xin_st = xin_st

        for l in range(2):
            p.waits("gpsimd", [("PE", max(self.f_wqkv_pe, self.wa_pe))])
            wqkv_v = p.emit("gpsimd", lambda e, l=l: e.dma_start(
                out=wa[:, 0:12288].rearrange("p (c m) -> p c m", c=8),
                in_=dr["f_wqkv"][l].rearrange("(c p) m -> p c m", p=128)), inc="W0", n=16)
            p.waits("gpsimd", [("PE", self.f_wo_pe)])
            wo_v = p.emit("gpsimd", lambda e, l=l: e.dma_start(
                out=wa[:, 12288:16384].rearrange("p (c m) -> p c m", c=4),
                in_=dr["f_wo"][l].rearrange("(c p) m -> p c m", p=128)), inc="W1", n=16)
            p.waits("sync", [("PE", self.f_pp_pe), ("ES", self.f_pp_es),
                             ("EV", self.f_pp_ev)])
            p.emit("sync", lambda e, l=l: e.dma_start(
                out=pp[:, 16:60], in_=dr["f_b1_pc"][l]), inc="W5", n=16)
            p.emit("sync", lambda e, l=l: e.dma_start(
                out=pp[:, 60:68], in_=dr["f_b2_pc"][l]), inc="W5", n=16)
            p.emit("sync", lambda e, l=l: e.dma_start(
                out=pp[:, 68:76], in_=dr["f_lng_pc"][l]), inc="W5", n=16)
            pp_v = p.emit("sync", lambda e, l=l: e.dma_start(
                out=pp[:, 76:84], in_=dr["f_lnb_pc"][l]), inc="W5", n=16)
            qs_v = p.emit("sync", lambda e, l=l: e.dma_start(
                out=b["qsks"][0:1, 0:64], in_=dr["f_qs_row"][l][0:1, :]), inc="W5", n=16)
            ks_v = p.emit("sync", lambda e, l=l: e.dma_start(
                out=b["qsks"][0:1, 64:128], in_=dr["f_ks_row"][l][0:1, :]), inc="W5", n=16)
            p.waits("gpsimd", [("EV", self.f_rtf_ev)])
            for ti, nm in enumerate(("rt_qc", "rt_qs", "rt_kc", "rt_ks")):
                rtl_v = p.emit("gpsimd", lambda e, nm=nm, ti=ti: e.dma_start(
                    out=b["rtf"][:, 1024 * ti:1024 * ti + 1024],
                    in_=dr[nm][:, :]), inc="W4", n=16)
            p.waits("tensor", [("W5", ks_v), ("PE", self.f_qsks_pe)])
            sb6 = 6
            bk.begin_write(sb6)
            scv = p.emit("tensor", lambda e: e.matmul(
                pb[sb6][:, 0:128], b["ones_row_f"][0:1, 0:128], b["qsks"][0:1, 0:128],
                start=True, stop=True), inc="PE")
            bk.wrote_banks(scv, sb6)
            self.f_qsks_pe = scv
            sc_ev = bk.read("vector", "EV", [sb6], lambda e: e.tensor_copy(
                out=fa[:, 5120:5248], in_=pb[sb6][:, 0:128]))
            p.waits("vector", [("W4", rtl_v)])
            for ti in range(4):
                off = 0 if ti < 2 else 64
                p.emit("vector", lambda e, ti=ti, off=off: e.tensor_mul(
                    out=b["rtf"][:, 1024 * ti:1024 * ti + 1024].rearrange("p (a d) -> p a d", a=16),
                    in0=b["rtf"][:, 1024 * ti:1024 * ti + 1024].rearrange("p (a d) -> p a d", a=16),
                    in1=fa[:, 5120 + off:5120 + off + 64].unsqueeze(1).broadcast_to([128, 16, 64])))
            rtf_ev = p.cnt["EV"]

            def rtfs(ti, t):
                return b["rtf"][:, 1024 * ti + 64 * t: 1024 * ti + 64 * t + 64]

            p.waits("tensor", [("W0", wqkv_v), ("XL", self.x_ready), ("EV", rtf_ev),
                               ("W5", ks_v)])
            pend_tr = []
            qr_pe = [0, 0]
            for t in range(16):
                bq3 = (0, 1, 2) if t % 2 == 0 else (3, 4, 5)
                for pi3 in range(3):
                    bno = bq3[pi3]
                    bk.begin_write(bno)
                    for kc in range(8):
                        src = self.xsl(kc, 128 * t, 128) if kc < 4 else self.xsl_shift(kc, 128 * t, 128)
                        mmv = p.emit("tensor", lambda e, kc=kc, pi3=pi3, bno=bno, src=src: e.matmul(
                            pb[bno][:, :], src,
                            wa[:, 1536 * kc + 512 * pi3: 1536 * kc + 512 * pi3 + 512],
                            start=(kc == 0), stop=(kc == 7)),
                            inc="PE" if kc == 7 else None)
                    bk.wrote_banks(mmv, bno)
                for qro, base_, t_, qr_v in pend_tr:
                    p.waits("tensor", [("EV", qr_v)])
                    bk.begin_write(7)
                    for dc in range(4):
                        tv = p.emit("tensor", lambda e, dc=dc, qro=qro: e.transpose(
                            out=pb[7][:, 128 * dc:128 * dc + 128],
                            in_=aa[:, qro + 128 * dc: qro + 128 * dc + 128],
                            identity=b["ident"][:, :]), inc="PE" if dc == 3 else None)
                    bk.wrote_banks(tv, 7)
                    qr_pe[t_ % 2] = tv
                    bk.read("vector", "EV", [7], lambda e, base_=base_, t_=t_: e.tensor_copy(
                        out=aa[:, base_:base_ + 8192].rearrange("p (c n) -> p c n", c=4)[:, :, 128 * t_:128 * t_ + 128],
                        in_=pb[7][:, 0:512].rearrange("p (c n) -> p c n", c=4)))
                pend_tr = []
                vb = bq3[2]
                bk.read("vector", "EV", [vb], lambda e, t=t, vb=vb: e.tensor_copy(
                    out=aa[:, VS + 512 * t: VS + 512 * t + 512], in_=pb[vb][:, :]))
                for pi3, (base, tq, ts_) in enumerate(((QT, 0, 1), (KT, 2, 3))):
                    bno = bq3[pi3]
                    ft = 512 * pi3
                    sq_v = bk.read("scalar", "ES", [bno], lambda e, bno=bno, ft=ft: e.activation(
                        out=fa[:, 1024 + ft:1024 + ft + 512], in_=pb[bno][:, :],
                        func=AF.Square))
                    p.waits("vector", [("ES", sq_v)])
                    p.emit("vector", lambda e, ft=ft: e.reduce_sum(
                        out=sm[:, 48:56], in_=fa[:, 1024 + ft:1024 + ft + 512].rearrange(
                            "p (h d) -> p h d", h=8), axis=AX.X))
                    ssc_v = p.emit("vector", lambda e: e.tensor_scalar(
                        out=sm[:, 48:56], in0=sm[:, 48:56], scalar1=1e-24, scalar2=None,
                        op0=ALU.max))
                    p.waits("scalar", [("EV", ssc_v)])
                    p.emit("scalar", lambda e: e.activation(
                        out=sm[:, 56:64], in_=sm[:, 48:56], func=AF.Ln))
                    rsq_v = p.emit("scalar", lambda e: e.activation(
                        out=sm[:, 56:64], in_=sm[:, 56:64], func=AF.Exp, scale=-0.5))
                    p.waits("vector", [("ES", rsq_v)])
                    bk.read("vector", "EV", [bno], lambda e, bno=bno, ft=ft: e.tensor_mul(
                        out=fa[:, 2048 + ft:2048 + ft + 512].rearrange("p (h d) -> p h d", h=8),
                        in0=pb[bno][:, :].rearrange("p (h d) -> p h d", h=8),
                        in1=sm[:, 56:64].unsqueeze(2).broadcast_to([128, 8, 64])))
                    qn = fa[:, 2048 + ft:2048 + ft + 512]
                    t1 = fa[:, 3072:3584]
                    uu = fa[:, 3584:4096]
                    p.waits("vector", [("PE", self.f_qr_pe)])
                    p.emit("vector", lambda e, qn=qn, t1=t1, tq=tq, t=t: e.tensor_mul(
                        out=t1.rearrange("p (h d) -> p h d", h=8),
                        in0=qn.rearrange("p (h d) -> p h d", h=8),
                        in1=rtfs(tq, t).unsqueeze(1).broadcast_to([128, 8, 64])))
                    p.emit("vector", lambda e, qn=qn, uu=uu, ts_=ts_, t=t: e.tensor_mul(
                        out=uu.rearrange("p (h d) -> p h d", h=8)[:, :, 0:32],
                        in0=qn.rearrange("p (h d) -> p h d", h=8)[:, :, 32:64],
                        in1=rtfs(ts_, t)[:, 0:32].unsqueeze(1).broadcast_to([128, 8, 32])))
                    p.emit("vector", lambda e, qn=qn, uu=uu, ts_=ts_, t=t: e.tensor_mul(
                        out=uu.rearrange("p (h d) -> p h d", h=8)[:, :, 32:64],
                        in0=qn.rearrange("p (h d) -> p h d", h=8)[:, :, 0:32],
                        in1=rtfs(ts_, t)[:, 32:64].unsqueeze(1).broadcast_to([128, 8, 32])))
                    qro = QR + 1024 * (t % 2) + 512 * pi3
                    p.waits("vector", [("PE", qr_pe[t % 2])])
                    qr_v = p.emit("vector", lambda e, t1=t1, uu=uu, qro=qro: e.tensor_add(
                        out=aa[:, qro:qro + 512], in0=t1, in1=uu))
                    pend_tr.append((qro, base, t, qr_v))
            for qro, base_, t_, qr_v in pend_tr:
                p.waits("tensor", [("EV", qr_v)])
                bk.begin_write(7)
                for dc in range(4):
                    tv = p.emit("tensor", lambda e, dc=dc, qro=qro: e.transpose(
                        out=pb[7][:, 128 * dc:128 * dc + 128],
                        in_=aa[:, qro + 128 * dc: qro + 128 * dc + 128],
                        identity=b["ident"][:, :]), inc="PE" if dc == 3 else None)
                bk.wrote_banks(tv, 7)
                qr_pe[t_ % 2] = tv
                bk.read("vector", "EV", [7], lambda e, base_=base_, t_=t_: e.tensor_copy(
                    out=aa[:, base_:base_ + 8192].rearrange("p (c n) -> p c n", c=4)[:, :, 128 * t_:128 * t_ + 128],
                    in_=pb[7][:, 0:512].rearrange("p (c n) -> p c n", c=4)))
            pend_tr = []
            qkv_ev, qkv_es = p.cnt["EV"], p.cnt["ES"]
            rtf_last_ev = qkv_ev
            self.f_rtf_ev = rtf_last_ev
            if upto == "qkv":
                return []

            p.waits("tensor", [("W1", wo_v), ("EV", qkv_ev), ("ES", qkv_es)])
            p.waits("vector", [("ST", self.f_xin_st)])
            atn_pe = [0, 0]
            att_pe = [0, 0]
            yw_pe = 0
            for w in range(4):
                p.waits("scalar", [("PE", yw_pe)])
                pend = None

                def emit_tail(st):
                    hp_, par_, atn0_, W_, nkt_, k_lo_, qt_, exdg = st
                    p.waits("tensor", [("ES", exdg[0]), ("EV", exdg[1])])
                    for j in range(2):
                        hh = 2 * hp_ + j
                        ats = ATT + 640 * j
                        bk.begin_write(4, 5)
                        for k in range(nkt_):
                            tv = p.emit("tensor", lambda e, k=k, j=j, atn0_=atn0_, par_=par_: e.matmul(
                                self.pfull[:, 2048 + 128 * k:2048 + 128 * k + 128],
                                aa[:, atn0_ + 640 * j + 128 * k: atn0_ + 640 * j + 128 * k + 128],
                                aa[:, DG + 256 * par_ + 128 * j: DG + 256 * par_ + 128 * j + 128],
                                start=True, stop=True),
                                inc="PE" if k == nkt_ - 1 else None)
                        bk.wrote_banks(tv, 4, 5)
                        atn_pe[par_] = tv
                        p.waits("scalar", [("PE", att_pe[j])])
                        att_v = bk.read("scalar", "ES", [4, 5], lambda e, nkt_=nkt_, ats=ats: e.copy(
                            out=aa[:, ats:ats + 128 * nkt_],
                            in_=self.pfull[:, 2048:2048 + 128 * nkt_]))
                        p.waits("tensor", [("ES", att_v)])
                        bk.begin_write(6)
                        for k in range(nkt_):
                            ktg = k_lo_ // 128 + k
                            mmv = p.emit("tensor", lambda e, k=k, ktg=ktg, hh=hh, j=j, nkt_=nkt_, ats=ats: e.matmul(
                                pb[6][0:64, 128 * j:128 * j + 128],
                                aa[:, VS + 512 * ktg + 64 * hh: VS + 512 * ktg + 64 * hh + 64],
                                aa[:, ats + 128 * k: ats + 128 * k + 128],
                                start=(k == 0), stop=(k == nkt_ - 1)),
                                inc="PE" if k == nkt_ - 1 else None)
                        bk.wrote_banks(mmv, 6)
                        att_pe[j] = mmv
                        bk.read("scalar", "ES", [6], lambda e, hh=hh, j=j, qt_=qt_: e.copy(
                            out=aa[64 * (hh % 2):64 * (hh % 2) + 64,
                                   YW + 512 * (hh // 2) + 128 * qt_: YW + 512 * (hh // 2) + 128 * qt_ + 128],
                            in_=pb[6][0:64, 128 * j:128 * j + 128]))

                for qt in range(4):
                    qi = 4 * w + qt
                    q0 = 128 * qi
                    k_lo = max(0, q0 - 512)
                    W = q0 + 128 - k_lo
                    nkt = W // 128
                    mof = 640 - W
                    for hp in range(4):
                        par = (qi * 4 + hp) % 2
                        atn0 = ATN + 1280 * par
                        bk.begin_write(0, 1, 2, 3)
                        for j in range(2):
                            hh = 2 * hp + j
                            ro = 64 * j
                            for j0 in range(0, W, 512):
                                nb2 = 2 * j + (1 if j0 else 0)
                                nn = min(512, W - j0)
                                mmv = p.emit("tensor", lambda e, hp=hp, ro=ro, q0=q0, k_lo=k_lo, j0=j0, nn=nn, nb2=nb2: e.matmul(
                                    pb[nb2][:, 0:nn],
                                    aa[ro:ro + 64, QT + 2048 * hp + q0: QT + 2048 * hp + q0 + 128],
                                    aa[ro:ro + 64, KT + 2048 * hp + k_lo + j0: KT + 2048 * hp + k_lo + j0 + nn],
                                    start=True, stop=True), inc="PE" if (j == 1 and j0 + 512 >= W) else None)
                        bk.wrote_banks(mmv, 0, 1, 2, 3)
                        if pend is not None:
                            emit_tail(pend)
                            pend = None
                        mk1 = bk.read("vector", "EV", [0, 1, 2, 3], lambda e, W=W, mof=mof: e.tensor_tensor(
                            out=fa[:, 0:1280].rearrange("p (h s) -> p h s", h=2)[:, :, 0:W],
                            in0=self.pfull[:, 0:2048].rearrange("p (h s) -> p h s", h=2)[:, :, 0:W],
                            in1=b["mask_l0"][:, mof:mof + W].unsqueeze(1).broadcast_to([128, 2, W]),
                            op=ALU.add))
                        rx_v = p.emit("vector", lambda e, W=W: e.reduce_max(
                            out=sm[:, 64:66], in_=fa[:, 0:1280].rearrange("p (h s) -> p h s", h=2)[:, :, 0:W],
                            axis=AX.X, negate=True))
                        p.waits("scalar", [("EV", rx_v), ("PE", atn_pe[par])])
                        ex_v = 0
                        for j in range(2):
                            ex_v = p.emit("scalar", lambda e, j=j, W=W, atn0=atn0: e.activation(
                                out=aa[:, atn0 + 640 * j: atn0 + 640 * j + W],
                                in_=fa[:, 640 * j:640 * j + W], func=AF.Exp,
                                bias=sm[:, 64 + j:65 + j], accum_out=sm[:, 66 + j:67 + j]))
                        p.waits("vector", [("ES", ex_v), ("PE", atn_pe[par])])
                        rc_v = p.emit("vector", lambda e: e.reciprocal(
                            out=sm[:, 68:70], in_=sm[:, 66:68]))
                        dg_v = 0
                        for j in range(2):
                            dg_v = p.emit("vector", lambda e, j=j, par=par: e.tensor_scalar(
                                out=aa[:, DG + 256 * par + 128 * j: DG + 256 * par + 128 * j + 128],
                                in0=b["ident"][:, :], scalar1=sm[:, 68 + j:69 + j],
                                scalar2=None, op0=ALU.mult))
                        pend = (hp, par, atn0, W, nkt, k_lo, qt, (ex_v, dg_v))
                emit_tail(pend)
                ow_es = p.cnt["ES"]
                p.waits("tensor", [("ES", ow_es)])
                for oc in range(8):
                    ob = oc % 2
                    bk.begin_write(ob)
                    for kc in range(4):
                        mmv = p.emit("tensor", lambda e, oc=oc, kc=kc, ob=ob: e.matmul(
                            pb[ob][:, :],
                            wa[:, 12288 + 1024 * kc + 128 * oc: 12288 + 1024 * kc + 128 * oc + 128],
                            aa[:, YW + 512 * kc: YW + 512 * kc + 512],
                            start=(kc == 0), stop=(kc == 3)),
                            inc="PE" if kc == 3 else None)
                    bk.wrote_banks(mmv, ob)
                    yw_pe = mmv
                    bk.read("vector", "EV", [ob], lambda e, oc=oc, w=w, ob=ob: e.tensor_add(
                        out=self.xsl(oc, 512 * w, 512), in0=pb[ob][:, :],
                        in1=self.xsl(oc, 512 * w, 512)))
            self.f_wqkv_pe = p.cnt["PE"]
            self.f_wo_pe = p.cnt["PE"]
            resid_ev = p.cnt["EV"]
            if upto == "attn":
                return []

            SB2, QB2, B1B, B0B = 5, 6, 3, 4
            pn1_pe = [self.f_wqkv_pe, self.f_wqkv_pe]
            pn2_pe = [self.f_wo_pe, self.f_wo_pe]
            h2n_pe = 0
            h3_pe = 0
            w2_ev = resid_ev
            mu2 = aa[0:1, 19456:20480].bitcast(F32)
            var2 = aa[0:1, 20480:21504].bitcast(F32)
            rstd2 = aa[0:1, 21504:22528].bitcast(F32)
            nm2 = aa[0:1, 22528:23552].bitcast(F32)
            for tc in range(4):
                t0 = 512 * tc
                def h2ap(oc, t0=t0):
                    return self.xsl(oc, t0, 512) if oc < 4 else self.xsl_shift(oc, t0, 512)
                w1l_v = [0] * 11
                for pnl in range(2):
                    p.waits("gpsimd", [("PE", pn1_pe[pnl % 2])])
                    w1l_v[pnl] = p.emit("gpsimd", lambda e, l=l, pnl=pnl: e.dma_start(
                        out=wa[:, 4096 * (pnl % 2):4096 * (pnl % 2) + 4096].rearrange("p (c m) -> p c m", c=8),
                        in_=dr["f_w1p"][l].rearrange("(c p) m -> p c m", p=128)[:, :, 512 * pnl:512 * pnl + 512]),
                        inc=f"W{pnl % 2}", n=16)
                p.waits("tensor", [("EV", w2_ev)])
                bk.begin_write(SB2, QB2)
                mmv2 = []
                for oc in range(8):
                    sq = 4096 + 512 * (oc % 2)
                    if oc >= 2:
                        p.waits("scalar", [("PE", mmv2[oc - 2])])
                    sqv = p.emit("scalar", lambda e, oc=oc, sq=sq, h2ap=h2ap: e.activation(
                        out=fa[:, sq:sq + 512], in_=h2ap(oc), func=AF.Square))
                    p.waits("tensor", [("ES", sqv)])
                    p.emit("tensor", lambda e, oc=oc, h2ap=h2ap: e.matmul(
                        pb[QB2][0:1, 0:512], b["ones_col_b"][:, 0:1], h2ap(oc),
                        start=(oc == 0), stop=(oc == 7)))
                    mmv2.append(p.emit("tensor", lambda e, oc=oc, sq=sq: e.matmul(
                        pb[SB2][0:1, 0:512], b["ones_col"][:, 0:1], fa[:, sq:sq + 512],
                        start=(oc == 0), stop=(oc == 7)), inc="PE"))
                smm = p.cnt["PE"]
                bk.wrote_banks(smm, SB2, QB2)
                bk.read("vector", "EV", [QB2], lambda e: e.tensor_scalar(
                    out=mu2, in0=pb[QB2][0:1, 0:512], scalar1=1.0 / D, scalar2=None, op0=ALU.mult))
                bk.read("vector", "EV", [SB2], lambda e: e.tensor_scalar(
                    out=var2, in0=pb[SB2][0:1, 0:512], scalar1=1.0 / D, scalar2=None, op0=ALU.mult))
                p.emit("vector", lambda e: e.scalar_tensor_tensor(
                    out=nm2, in0=mu2, scalar=-1.0, in1=mu2, op0=ALU.mult, op1=ALU.mult))
                vv = p.emit("vector", lambda e: e.tensor_add(out=var2, in0=var2, in1=nm2))
                p.waits("scalar", [("EV", vv)])
                p.emit("scalar", lambda e: e.activation(out=nm2, in_=var2, func=AF.Ln,
                                                        bias=b["small"][0:1, 255:256]))
                sv = p.emit("scalar", lambda e: e.activation(out=rstd2, in_=nm2, func=AF.Exp, scale=-0.5))
                p.waits("vector", [("ES", sv)])
                nv = p.emit("vector", lambda e: e.scalar_tensor_tensor(
                    out=nm2, in0=mu2, scalar=-1.0, in1=rstd2, op0=ALU.mult, op1=ALU.mult))
                p.waits("tensor", [("EV", nv)])
                bk.begin_write(B1B, B0B)
                b1mm = p.emit("tensor", lambda e: e.matmul(
                    pb[B1B][:, :], b["ones_row_f"][0:1, 0:128], rstd2, start=True, stop=True), inc="PE")
                b0mm = p.emit("tensor", lambda e: e.matmul(
                    pb[B0B][:, :], b["ones_row_f"][0:1, 0:128], nm2, start=True, stop=True), inc="PE")
                bk.wrote_banks(b1mm, B1B)
                bk.wrote_banks(b0mm, B0B)
                p.waits("vector", [("PE", max(h2n_pe, b0mm))])
                for oc in range(8):
                    tslot = 4096 + 512 * (oc % 2)
                    bk.read("vector", "EV", [B1B], lambda e, oc=oc, tslot=tslot, h2ap=h2ap: e.tensor_mul(
                        out=fa[:, tslot:tslot + 512], in0=h2ap(oc), in1=pb[B1B][:, :]))
                    bk.read("vector", "EV", [B0B], lambda e, oc=oc, tslot=tslot: e.tensor_add(
                        out=fa[:, tslot:tslot + 512], in0=fa[:, tslot:tslot + 512], in1=pb[B0B][:, :]))
                    p.emit("vector", lambda e, oc=oc, tslot=tslot: e.tensor_scalar(
                        out=aa[:, H2N + 512 * oc: H2N + 512 * oc + 512],
                        in0=fa[:, tslot:tslot + 512], scalar1=pp[:, 68 + oc:69 + oc],
                        scalar2=pp[:, 76 + oc:77 + oc], op0=ALU.mult, op1=ALU.add))
                h2n_ev = p.cnt["EV"]
                p.waits("tensor", [("EV", h2n_ev), ("W0", w1l_v[0]), ("PE", h3_pe)])
                gl_par = 0
                gl_ev = []
                for j in range(44):
                    pnl = j // 4
                    if j % 4 == 0 and 1 <= pnl < 10:
                        slot = (pnl + 1) % 2
                        p.waits("gpsimd", [("PE", pn1_pe[slot])])
                        w1l_v[pnl + 1] = p.emit("gpsimd", lambda e, l=l, pnl=pnl, slot=slot: e.dma_start(
                            out=wa[:, 4096 * slot:4096 * slot + 4096].rearrange("p (c m) -> p c m", c=8),
                            in_=dr["f_w1p"][l].rearrange("(c p) m -> p c m", p=128)[:, :, 512 * (pnl + 1):512 * (pnl + 1) + 512]),
                            inc=f"W{slot}", n=16)
                    if j % 4 == 0 and pnl > 0:
                        p.waits("tensor", [(f"W{pnl % 2}", w1l_v[pnl])])
                    hb = j % 2
                    slot = pnl % 2
                    bk.begin_write(hb)
                    for kc in range(8):
                        mmv = p.emit("tensor", lambda e, j=j, kc=kc, slot=slot, hb=hb: e.matmul(
                            pb[hb][:, :],
                            wa[:, 4096 * slot + 512 * kc + 128 * (j % 4): 4096 * slot + 512 * kc + 128 * (j % 4) + 128],
                            aa[:, H2N + 512 * kc: H2N + 512 * kc + 512],
                            start=(kc == 0), stop=(kc == 7)),
                            inc="PE" if kc == 7 else None)
                    bk.wrote_banks(mmv, hb)
                    if j % 4 == 3 or j == 43:
                        pn1_pe[slot] = mmv
                    glo = GL + 512 * gl_par
                    if j >= 2:
                        p.waits("scalar", [("EV", gl_ev[j - 2])])
                    gv2 = bk.read("scalar", "ES", [hb], lambda e, j=j, hb=hb, glo=glo: e.activation(
                        out=aa[0:64, glo:glo + 512], in_=pb[hb][64:128, :],
                        func=AF.Gelu, bias=pp[64:128, 16 + j:17 + j]))
                    p.waits("vector", [("ES", gv2)])
                    ro2 = 64 * (j % 2)
                    gl_ev.append(bk.read("vector", "EV", [hb], lambda e, j=j, hb=hb, glo=glo, ro2=ro2: e.scalar_tensor_tensor(
                        out=aa[ro2:ro2 + 64, H3 + 512 * (j // 2): H3 + 512 * (j // 2) + 512],
                        in0=pb[hb][0:64, :], scalar=pp[0:64, 16 + j:17 + j],
                        in1=aa[0:64, glo:glo + 512], op0=ALU.add, op1=ALU.mult)))
                    gl_par ^= 1
                h3_ev, h3_es = p.cnt["EV"], p.cnt["ES"]
                for pnl in range(2):
                    p.waits("gpsimd", [("PE", pn2_pe[pnl])])
                    w2l0 = p.emit("gpsimd", lambda e, l=l, pnl=pnl: e.dma_start(
                        out=wa[:, 8192 + 5632 * pnl:8192 + 5632 * pnl + 5632].rearrange("p (c m) -> p c m", c=22),
                        in_=dr["f_w2p"][l].rearrange("(c p) m -> p c m", p=128)[:, :, 256 * pnl:256 * pnl + 256]),
                        inc=f"W{2 + pnl}", n=16)
                    if pnl == 0:
                        w2l_v0 = w2l0
                    else:
                        w2l_v1 = w2l0
                p.waits("tensor", [("EV", h3_ev), ("ES", h3_es), ("W2", w2l_v0)])
                for oc in range(8):
                    pnl = oc // 4
                    pq = oc // 2
                    slot = pq % 2
                    lane = f"W{2 + slot}"
                    if oc == 2:
                        pass
                    ob = 2 + (oc % 2)
                    if pq >= 2 and oc % 2 == 0:
                        p.waits("gpsimd", [("PE", pn2_pe[slot])])
                        wv2 = p.emit("gpsimd", lambda e, l=l, pq=pq, slot=slot: e.dma_start(
                            out=wa[:, 8192 + 5632 * slot:8192 + 5632 * slot + 5632].rearrange("p (c m) -> p c m", c=22),
                            in_=dr["f_w2p"][l].rearrange("(c p) m -> p c m", p=128)[:, :, 256 * pq:256 * pq + 256]),
                            inc=lane, n=16)
                        p.waits("tensor", [(lane, wv2)])
                    elif oc == 2:
                        p.waits("tensor", [("W3", w2l_v1)])
                    bk.begin_write(ob)
                    for kc in range(22):
                        mmv = p.emit("tensor", lambda e, oc=oc, kc=kc, slot=slot, ob=ob: e.matmul(
                            pb[ob][:, :],
                            wa[:, 8192 + 5632 * slot + 256 * kc + 128 * (oc % 2): 8192 + 5632 * slot + 256 * kc + 128 * (oc % 2) + 128],
                            aa[:, H3 + 512 * kc: H3 + 512 * kc + 512],
                            start=(kc == 0), stop=(kc == 21)),
                            inc="PE" if kc == 21 else None)
                    bk.wrote_banks(mmv, ob)
                    if oc % 2 == 1:
                        pn2_pe[slot] = mmv
                    bk.read("vector", "EV", [ob], lambda e, oc=oc, t0=t0, ob=ob: e.scalar_tensor_tensor(
                        out=self.xsl(oc, t0, 512), in0=pb[ob][:, :],
                        scalar=pp[:, 60 + oc:61 + oc], in1=self.xsl(oc, t0, 512),
                        op0=ALU.add, op1=ALU.add))
                w2_ev = p.cnt["EV"]
                h2n_pe = p.cnt["PE"]
                h3_pe = h2n_pe
            self.f_pp_pe, self.f_pp_es, self.f_pp_ev = p.cnt["PE"], p.cnt["ES"], p.cnt["EV"]
            self.f_wqkv_pe = p.cnt["PE"]
            self.f_wo_pe = p.cnt["PE"]
            if upto == "ffn" + str(l):
                return []

        st_vals = []
        xin_pe = p.cnt["PE"]
        for tc in range(4):
            t0 = 512 * tc
            p.waits("sync", [("ST", self.f_xin_st), ("PE", xin_pe),
                             ("EV", w2_ev), ("ES", p.cnt["ES"] if tc == 0 else 0)])
            xi_v = p.emit("sync", lambda e, t0=t0: e.dma_start(
                out=aa[:, XIN:XIN + 4096].rearrange("p (c n) -> p c n", c=8),
                in_=dr["xin_d"].rearrange("c p n -> p c n")[:, :, t0:t0 + 512]), inc="GR", n=16)
            p.waits("vector", [("GR", xi_v), ("EV", w2_ev)])
            for oc in range(8):
                hv = p.emit("vector", lambda e, oc=oc, t0=t0: e.tensor_add(
                    out=fa[:, 512 * oc:512 * oc + 512],
                    in0=aa[:, XIN + 512 * oc: XIN + 512 * oc + 512],
                    in1=self.xsl(oc, t0, 512)))
            p.waits("scalar", [("EV", hv)])
            p.waits("tensor", [("EV", hv)])
            self._module_ln(tc, bi, final, st_vals, sb_base=STATS2, out_base=OUTB2)
            xin_pe = self.prog.cnt["PE"]
        self.wa_pe = p.cnt["PE"]
        return st_vals



def build_nc(stage_list, nb=NB, dump=False, upto=None):
    nc = bass.Bass("TRN2", target_bir_lowering=False, debug=False)
    mk = MK(nc, stage_list)
    mk.declare()
    if dump:
        mk.dram["dbg_aa"] = nc.dram_tensor("dbg_aa", [128, 34816], BF16, kind="ExternalOutput").ap()
        mk.dram["dbg_fa"] = nc.dram_tensor("dbg_fa", [128, 5632], F32, kind="ExternalOutput").ap()
        mk.dram["dbg_wa"] = nc.dram_tensor("dbg_wa", [128, 24576], BF16, kind="ExternalOutput").ap()
    for name in SEMS:
        cm = nc.semaphore(name)
        mk.prog.sems[name] = cm.__enter__()
        mk.ctxs.append(cm)
    mk.alloc()
    mk.ld_const()
    cl0, cg0, ev0 = mk.prog.cnt["CL"], mk.prog.cnt["CG"], mk.prog.cnt["EV"]
    for eng in ("tensor", "vector", "scalar"):
        mk.prog.waits(eng, [("CL", cl0), ("CG", cg0)])
    mk.prog.waits("tensor", [("EV", ev0)])

    for bi in range(nb):
        if bi > 0:
            mk.prog.waits("gpsimd", [("EV", mk.prog.cnt["EV"]),
                                     ("ES", mk.prog.cnt["ES"]),
                                     ("PE", mk.prog.cnt["PE"])])
        mk.x_ready = mk.ld_x(bi)
        st_vals = []
        for si, stg in enumerate(stage_list):
            final = (si == len(stage_list) - 1)
            if stg == "a":
                st_vals = mk.stage_a(bi, final=final, upto=upto)
            elif stg == "f":
                st_vals = mk.stage_f(bi, final=final, upto=upto)
            elif stg == "l1a":
                st_vals = mk.stage_l1a(bi, final=final, upto=upto)
            elif stg == "moe":
                st_vals = mk.stage_moe(bi, final=final)
            else:
                raise ValueError(stg)
            if si != len(stage_list) - 1:
                mk.prog.barrier()
        mk.prog.barrier()
    if dump:
        for nm2, bf2 in (("dbg_aa", "aa"), ("dbg_fa", "fa"), ("dbg_wa", "wa")):
            mk.prog.emit("sync", lambda e, nm2=nm2, bf2=bf2: e.dma_start(
                out=mk.dram[nm2][:, :], in_=mk.buf[bf2][:, :]), inc="ST", n=16)
    mk.prog.waits("sync", [(s, mk.prog.cnt[s]) for s in
                           ("SO0", "SO1", "ST") if mk.prog.cnt[s]])
    mk.prog.check_deadlock()
    with nc.Block() as block:
        mk.prog.replay(block)
    mk.close()
    return nc


def run_stages(inputs, stage_list, nc=None):
    common, shards = host_prepare(inputs)
    if nc is None:
        nc = build_nc(stage_list)
    in_maps = []
    for core in range(8):
        m = dict(common)
        m["x_t"] = shards[core]
        in_maps.append(m)
    res = run_bass_kernel_spmd(nc, in_maps, core_ids=list(range(8)))
    outs = []
    for core in range(8):
        ot = res.results[core]["out_t"]
        outs.append(np.ascontiguousarray(ot.transpose(0, 2, 1)))
    return np.concatenate(outs, axis=0)


def kernel(**inputs):
    return run_stages(inputs, ["a", "f", "l1a", "moe"]).astype(np.float32)

